# version 06
# baseline (speedup 1.0000x reference)
import numpy as np
import ml_dtypes
from contextlib import ExitStack

import concourse.bass as bass
import concourse.mybir as mybir
from concourse.bass import IndirectOffsetOnAxis
from concourse.bass_utils import run_bass_kernel_spmd
from concourse.aot_env import allow_ici_all_to_all

F32 = mybir.dt.float32
BF16 = mybir.dt.bfloat16
U32 = mybir.dt.uint32
AF = mybir.ActivationFunctionType
ALU = mybir.AluOpType
AX = mybir.AxisListType.X
BF16NP = ml_dtypes.bfloat16

D = 1024
T = 4096
H = 16
NC = 8
E = 16
CAP = 2048
LCAP = 1024
FH = 4096
NCH = D // 128
NTT = T // 512
NST = T // 128
BIG = float(1 << 26)
BIGIX = 1.0e6
NROW = E * LCAP


class Prog:
    def __init__(self, nc, es, sems):
        self.nc = nc
        self.c = {}
        self.sem = {}
        for name in sems:
            self.sem[name] = es.enter_context(nc.semaphore(name))
            self.c[name] = 0

    def inc(self, instr, name, k=1):
        self.c[name] += k
        instr.then_inc(self.sem[name], k)
        return self.c[name]

    def wait(self, eng, name, v=None):
        eng.wait_ge(self.sem[name], self.c[name] if v is None else v)


SEMS = ["LX", "LS", "WH0", "WH1", "PE", "ACT", "VEC", "ST",
        "SX0", "SX1", "SST", "SEXP", "SPV",
        "CN", "CCA", "CCF", "CCR",
        "BT0", "BT1", "W10", "W11", "W20", "W21", "SY0", "SY1",
        "LT0", "LT1", "GB0", "GB1", "SO"]


def alloc_l1(nc, es):
    sb = lambda name, shape, dt: es.enter_context(nc.sbuf_tensor(name, shape, dt))
    ps = lambda name, shape, dt: es.enter_context(nc.psum_tensor(name, shape, dt))
    Z = {}
    Z["x_sb"] = sb("x_sb", [128, NCH * T], BF16)
    Z["o_sb"] = sb("o_sb", [128, NCH * T], BF16)
    Z["wo_sb"] = sb("wo_sb", [128, NCH * D], BF16)
    Z["wh_sb"] = [sb(f"wh_sb{i}", [128, 3 * D], BF16) for i in range(2)]
    Z["wg_sb"] = sb("wg_sb", [128, NCH * E], BF16)
    Z["boc_sb"] = sb("boc_sb", [128, NCH], F32)
    Z["bgc_sb"] = sb("bgc_sb", [16, 1], F32)
    Z["idbf_sb"] = sb("idbf_sb", [128, 128], BF16)
    Z["id16_sb"] = sb("id16_sb", [16, 16], F32)
    Z["iota_sb"] = sb("iota_sb", [128, 16], F32)
    Z["r16m_sb"] = sb("r16m_sb", [128, 16], F32)
    Z["ones_sb"] = sb("ones_sb", [128, 128], BF16)
    Z["onesf_sb"] = sb("onesf_sb", [128, 128], F32)
    Z["k2_sb"] = sb("k2_sb", [128, T], BF16)
    Z["q2_sb"] = sb("q2_sb", [128, T], BF16)
    Z["v2tmp"] = sb("v2tmp", [128, 512], BF16)
    Z["vaugA"] = sb("vaugA", [128, 65 * 32], BF16)
    Z["vaugB"] = sb("vaugB", [128, 65 * 32], BF16)
    Z["expS"] = [sb(f"expS{i}", [128, 1024], BF16) for i in range(2)]
    Z["rden_bf"] = sb("rden_bf", [65, 512], BF16)
    Z["bc_sb"] = sb("bc_sb", [128, 512], F32)
    Z["ysq_sb"] = sb("ysq_sb", [128, 512], BF16)
    Z["mean_sb"] = sb("mean_sb", [1, 512], F32)
    Z["var_sb"] = sb("var_sb", [1, 512], F32)
    Z["mean_bf"] = sb("mean_bf", [1, 512], BF16)
    Z["rstd_bf"] = sb("rstd_bf", [1, 512], BF16)
    Z["rbc_sb"] = sb("rbc_sb", [128, 512], F32)
    Z["lg_tt"] = sb("lg_tt", [16, 512], F32)
    Z["msq_sb"] = Z["lg_tt"][0:1, 0:512]
    Z["mbc_sb"] = Z["bc_sb"]
    Z["mx_sb"] = sb("mx_sb", [128, 1], F32)
    Z["nmx_sb"] = sb("nmx_sb", [128, 1], F32)
    Z["sm_sb"] = sb("sm_sb", [128, 1], F32)
    Z["ejunk_sb"] = sb("ejunk_sb", [128, 16], F32)
    Z["eq_sb"] = sb("eq_sb", [128, 16], F32)
    Z["cand_sb"] = sb("cand_sb", [128, 16], F32)
    Z["t1_sb"] = sb("t1_sb", [128, 1], F32)

    Z["ohb_sb"] = sb("ohb_sb", [128, 16], BF16)
    Z["cnt_sb"] = sb("cnt_sb", [16, 8], F32)
    Z["sel_sb"] = sb("sel_sb", [128, 16], F32)
    Z["cs_sb"] = sb("cs_sb", [128, 1], F32)

    Z["c5_sb"] = sb("c5_sb", [128, 1], F32)
    Z["c8_sb"] = sb("c8_sb", [128, 1], F32)
    Z["csum_sb"] = Z["o_sb"][0:16, 0:8192].bitcast(F32)
    Z["ohfm_sb"] = Z["o_sb"][0:16, 8192:12288]
    Z["lg_tm"] = Z["o_sb"][:, 12288:13312].bitcast(F32)
    Z["ohf_all"] = Z["o_sb"][:, 13312:14336].bitcast(F32)
    Z["meta_sb"] = Z["o_sb"][:, 14336:14592].bitcast(F32)
    Z["idx_all"] = Z["o_sb"][:, 14592:14656].bitcast(F32)
    Z["x2st"] = [Z["o_sb"][:, 16384:17408], Z["o_sb"][:, 17408:18432]]
    Z["fif"] = sb("fif", [128, 32], F32)
    Z["fiu"] = sb("fiu", [128, 32], U32)
    Z["gif"] = sb("gif", [128, 32], F32)
    Z["giu"] = sb("giu", [128, 32], U32)
    Z["cnts8_sb"] = sb("cnts8_sb", [8, 16], F32)
    Z["tri8_sb"] = sb("tri8_sb", [8, 8], F32)
    Z["eh01_sb"] = sb("eh01_sb", [16, 2], F32)
    Z["pmask_sb"] = sb("pmask_sb", [8, 1], F32)
    Z["iota128_sb"] = sb("iota128_sb", [128, 1], F32)
    Z["baseT_sb"] = sb("baseT_sb", [16, 8], F32)
    Z["rowsel_sb"] = sb("rowsel_sb", [1, 16], F32)
    Z["gbrow_sb"] = sb("gbrow_sb", [1, 16], F32)
    Z["bbc_sb"] = sb("bbc_sb", [128, 16], F32)
    Z["gbb_sb"] = sb("gbb_sb", [128, 16], F32)
    Z["pc_sb"] = sb("pc_sb", [128, 1], F32)
    Z["cmp8_sb"] = sb("cmp8_sb", [128, 8], F32)
    Z["bsc_sb"] = sb("bsc_sb", [128, 8], F32)
    Z["nle_sb"] = sb("nle_sb", [128, 1], F32)
    Z["rr_sb"] = sb("rr_sb", [128, 1], F32)
    Z["ovg_sb"] = sb("ovg_sb", [128, 1], F32)
    Z["b1c_sb"] = sb("b1c_sb", [128, 64], F32)
    Z["b2c_sb"] = sb("b2c_sb", [128, 16], F32)
    Z["ln2_sb"] = sb("ln2_sb", [128, 8], F32)
    Z["st_ps"] = [ps("st_ps0", [128, 1024], F32), ps("st_ps1", [128, 1024], F32)]
    Z["pvA_ps"] = ps("pvA_ps", [128, 512], F32)
    Z["pvB_ps"] = ps("pvB_ps", [128, 512], F32)
    Z["bc_ps"] = ps("bc_ps", [128, 512], F32)
    Z["trb_ps"] = ps("trb_ps", [128, 1024], BF16)
    return Z


def alloc_overlays(Z):
    ZF = {}
    x, o = Z["x_sb"], Z["o_sb"]
    ZF["buf_fm"] = x[:, 0:16384]
    ZF["h_sb"] = x[:, 16384:32768]
    ZF["yfm_sb"] = o[:, 0:4096]
    ZF["bt_sb"] = [o[:, 4096:5120], o[:, 5120:6144]]
    ZF["yt_sb"] = [o[:, 6144:7168], o[:, 7168:8192]]
    ZF["w1_sb"] = [o[:, 8192:9216], o[:, 9216:10240]]
    ZF["w2_sb"] = [o[:, 10240:14336], o[:, 14336:18432]]
    k2, q2 = Z["k2_sb"], Z["q2_sb"]
    ZF["x2t2"] = [k2[:, 0:1024], k2[:, 1024:2048]]
    ZF["yb2"] = [k2[:, 2048:3072], k2[:, 3072:4096]]
    ZF["mt2"] = [q2[:, 0:8].bitcast(F32), q2[:, 8:16].bitcast(F32)]
    ZF["z_sb"] = Z["vaugA"][:, 0:1024]
    ZF["zc_sb"] = Z["vaugA"][:, 1024:2048]
    ZF["zsq_sb"] = Z["vaugB"][:, 0:1024]
    ZF["o2_sb"] = q2[:, 2048:4096].bitcast(F32)
    return ZF


def new_l1_state():
    return {
        "gg": 0,
        "vec_bc_free": 0,
        "vec_pvA_free": 0,
        "vec_pvB_free": 0,
        "wh_ready": [0, 0],
        "wh_freed": [0, 0],
    }


def emit_l1_init(nc, P, Z):
    v, s, t = nc.vector, nc.scalar, nc.tensor
    P.wait(v, "LS")
    P.inc(v.memset(Z["vaugA"][:, :], 1.0), "VEC")
    P.inc(v.memset(Z["vaugB"][:, :], 1.0), "VEC")
    P.inc(v.memset(Z["c5_sb"][:, :], 1e-5), "VEC")
    P.inc(v.memset(Z["c8_sb"][:, :], 1e-8), "VEC")
    P.wait(s, "VEC")
    P.wait(t, "VEC")


def emit_l1_batch(nc, P, Z, S, prefetch, x2tm_ap, meta_ap, cnt_col, lx_ready,
                  fsrc_ap, bchk):
    g, sy, t, v, s = nc.gpsimd, nc.sync, nc.tensor, nc.vector, nc.scalar
    st_ps, pvA_ps, pvB_ps = Z["st_ps"], Z["pvA_ps"], Z["pvB_ps"]
    bc_ps, trb_ps = Z["bc_ps"], Z["trb_ps"]
    x_sb, o_sb = Z["x_sb"], Z["o_sb"]
    k2_sb, q2_sb = Z["k2_sb"], Z["q2_sb"]
    expS = Z["expS"]
    ones_sb = Z["ones_sb"]

    for hp in range(8):
        bsel = hp % 2
        wh_sem = "WH0" if bsel == 0 else "WH1"
        prefetch(hp)
        t.wait_ge(P.sem[wh_sem], S["wh_ready"][bsel])
        P.wait(t, "LX", lx_ready)
        P.wait(t, "LS")
        for mi, dst_sb in ((0, k2_sb), (1, q2_sb)):
            for tt in range(NTT):
                pp = bc_ps if tt % 2 == 0 else pvA_ps
                P.wait(t, "VEC",
                       S["vec_bc_free"] if tt % 2 == 0 else S["vec_pvA_free"])
                for c in range(NCH):
                    mm = t.matmul(pp[:, :],
                                  Z["wh_sb"][bsel][:, D * mi + 128 * c:
                                                   D * mi + 128 * (c + 1)],
                                  x_sb[:, T * c + 512 * tt: T * c + 512 * (tt + 1)],
                                  start=(c == 0), stop=(c == NCH - 1))
                P.inc(mm, "PE")
                P.wait(v, "PE")
                vi = P.inc(v.tensor_copy(dst_sb[:, 512 * tt:512 * (tt + 1)],
                                         pp[:, :]), "VEC")
                if tt % 2 == 0:
                    S["vec_bc_free"] = vi
                else:
                    S["vec_pvA_free"] = vi
        for tt in range(NTT):
            pp = bc_ps if tt % 2 == 0 else pvA_ps
            P.wait(t, "VEC",
                   S["vec_bc_free"] if tt % 2 == 0 else S["vec_pvA_free"])
            for c in range(NCH):
                mm = t.matmul(pp[:, :],
                              Z["wh_sb"][bsel][:, 2 * D + 128 * c:
                                               2 * D + 128 * (c + 1)],
                              x_sb[:, T * c + 512 * tt: T * c + 512 * (tt + 1)],
                              start=(c == 0), stop=(c == NCH - 1))
            P.inc(mm, "PE")
            P.wait(v, "PE")
            vi = P.inc(v.tensor_copy(Z["v2tmp"][:, :], pp[:, :]), "VEC")
            if tt % 2 == 0:
                S["vec_bc_free"] = vi
            else:
                S["vec_pvA_free"] = vi
            for j in range(4):
                sc = 4 * tt + j
                P.wait(t, "VEC")
                mm = t.transpose(trb_ps[0:128, 0:128],
                                 Z["v2tmp"][:, 128 * j:128 * (j + 1)],
                                 Z["idbf_sb"][:, :])
                P.inc(mm, "PE")
                P.wait(v, "PE")
                P.inc(v.tensor_copy(Z["vaugA"][:, 65 * sc:65 * sc + 64],
                                    trb_ps[0:128, 0:64]), "VEC")
                P.inc(v.tensor_copy(Z["vaugB"][:, 65 * sc:65 * sc + 64],
                                    trb_ps[0:128, 64:128]), "VEC")
        S["wh_freed"][bsel] = P.c["PE"]
        vec_vaug = P.c["VEC"]

        for tt in range(NTT):
            g0 = S["gg"]
            tsl = slice(512 * tt, 512 * (tt + 1))
            for lg in range(32):
                cgg = g0 + lg
                if cgg >= 2:
                    P.wait(t, "SEXP", cgg - 1)
                if lg == 0:
                    P.wait(t, "VEC", vec_vaug)
                mm = t.matmul(st_ps[cgg % 2][:, 0:512],
                              k2_sb[0:64, 128 * lg:128 * (lg + 1)],
                              q2_sb[0:64, tsl], start=True, stop=True,
                              tile_position=(0, 0))
                mm = t.matmul(st_ps[cgg % 2][:, 512:1024],
                              k2_sb[64:128, 128 * lg:128 * (lg + 1)],
                              q2_sb[64:128, tsl], start=True, stop=True,
                              tile_position=(64, 0))
                P.inc(mm, "SST")
                if lg >= 1:
                    P.wait(t, "SEXP", cgg)
                    pvg = lg - 1
                    if pvg == 0:
                        P.wait(t, "VEC", S["vec_pvA_free"])
                        P.wait(t, "VEC", S["vec_pvB_free"])
                    mm = t.matmul(pvA_ps[0:65, :],
                                  Z["vaugA"][:, 65 * pvg:65 * pvg + 65],
                                  expS[(g0 + pvg) % 2][:, 0:512],
                                  start=(pvg == 0), stop=False,
                                  skip_group_check=True)
                    mm = t.matmul(pvB_ps[0:65, :],
                                  Z["vaugB"][:, 65 * pvg:65 * pvg + 65],
                                  expS[(g0 + pvg) % 2][:, 512:1024],
                                  start=(pvg == 0), stop=False,
                                  skip_group_check=True)
                    P.inc(mm, "SPV")
            P.wait(t, "SEXP", g0 + 32)
            mm = t.matmul(pvA_ps[0:65, :], Z["vaugA"][:, 65 * 31:65 * 31 + 65],
                          expS[(g0 + 31) % 2][:, 0:512],
                          start=False, stop=True, skip_group_check=True)
            mm = t.matmul(pvB_ps[0:65, :], Z["vaugB"][:, 65 * 31:65 * 31 + 65],
                          expS[(g0 + 31) % 2][:, 512:1024],
                          start=False, stop=True, skip_group_check=True)
            P.inc(mm, "SPV")
            for lg in range(32):
                cgg = g0 + lg
                P.wait(s, "SST", cgg + 1)
                if cgg >= 2:
                    P.wait(s, "SPV", cgg - 1)
                si = s.activation(expS[cgg % 2][:, :], st_ps[cgg % 2][:, :],
                                  AF.Exp, bias=0.0, scale=0.125)
                P.inc(si, "SEXP")
            S["gg"] = g0 + 32
            for head, pv in ((0, pvA_ps), (1, pvB_ps)):
                bcx = Z["bc_sb"] if head == 0 else Z["rbc_sb"]
                dr = slice(0, 1) if head == 0 else slice(64, 65)
                po = head * 64
                ocols = slice(T * hp + 512 * tt, T * hp + 512 * (tt + 1))
                P.wait(v, "SPV", S["gg"])
                with nc.allow_low_precision(reason="bf16 stage for matmul"):
                    P.inc(v.tensor_copy(Z["rden_bf"][dr, :], pv[64:65, :]),
                          "VEC")
                with nc.allow_low_precision(reason="unnormalised bf16 stage"):
                    vi = P.inc(v.tensor_copy(o_sb[po:po + 64, ocols],
                                             pv[0:64, :]), "VEC")
                if head == 0:
                    S["vec_pvA_free"] = vi
                else:
                    S["vec_pvB_free"] = vi
                P.wait(t, "VEC")
                P.wait(t, "VEC", S["vec_bc_free"])
                mm = t.matmul(bc_ps[:, :], ones_sb[dr, 0:128],
                              Z["rden_bf"][dr, :], start=True, stop=True,
                              skip_group_check=True)
                P.inc(mm, "PE")
                P.wait(v, "PE")
                P.wait(v, "VEC")
                P.inc(v.tensor_scalar(bcx[:, :], bc_ps[:, :],
                                      1e-8, None, ALU.add), "VEC")
                S["vec_bc_free"] = P.c["VEC"]
                P.wait(v, "VEC")
                P.inc(v.reciprocal(bcx[:, :], bcx[:, :]), "VEC")
                P.wait(v, "VEC")
                P.inc(v.tensor_tensor(o_sb[po:po + 64, ocols],
                                      o_sb[po:po + 64, ocols],
                                      bcx[po:po + 64, :], ALU.mult), "VEC")

    P.wait(t, "SEXP", S["gg"])
    P.wait(t, "VEC", S["vec_pvA_free"])
    P.wait(t, "VEC", S["vec_pvB_free"])
    for tt in range(NTT):
        for dc in range(NCH):
            wv = st_ps[0][:, 0:512] if dc % 2 == 0 else st_ps[0][:, 512:1024]
            P.wait(t, "VEC")
            for r in range(NCH):
                mm = t.matmul(wv,
                              Z["wo_sb"][:, D * r + 128 * dc: D * r + 128 * (dc + 1)],
                              o_sb[:, T * r + 512 * tt: T * r + 512 * (tt + 1)],
                              start=(r == 0), stop=(r == NCH - 1),
                              skip_group_check=True)
            P.inc(mm, "PE")
            P.wait(v, "PE")
            xv = x_sb[:, T * dc + 512 * tt: T * dc + 512 * (tt + 1)]
            P.inc(v.scalar_tensor_tensor(xv, wv, Z["boc_sb"][:, dc:dc + 1], xv,
                                         ALU.add, ALU.add), "VEC")
            P.wait(t, "VEC")
            mm = t.matmul(pvA_ps[0:1, :], ones_sb[:, 0:1], xv,
                          start=(dc == 0), stop=(dc == NCH - 1),
                          skip_group_check=True)
            P.inc(mm, "PE")
            P.wait(s, "VEC")
            P.inc(s.activation(Z["ysq_sb"][:, :], xv, AF.Square), "ACT")
            P.wait(t, "ACT")
            mm = t.matmul(pvA_ps[32:33, :], ones_sb[:, 0:1], Z["ysq_sb"][:, :],
                          start=(dc == 0), stop=(dc == NCH - 1),
                          skip_group_check=True)
            P.inc(mm, "PE")
        P.wait(v, "PE")
        P.inc(v.tensor_scalar(Z["mean_sb"][0:1, :], pvA_ps[0:1, :],
                              1.0 / D, None, ALU.mult), "VEC")
        P.inc(v.tensor_scalar(Z["msq_sb"][:, :], pvA_ps[32:33, :],
                              1.0 / D, None, ALU.mult), "VEC")
        P.wait(v, "VEC")
        P.inc(v.tensor_tensor(Z["var_sb"][0:1, :], Z["mean_sb"][0:1, :],
                              Z["mean_sb"][0:1, :], ALU.mult), "VEC")
        P.wait(v, "VEC")
        P.inc(v.tensor_sub(Z["var_sb"][0:1, :], Z["msq_sb"][:, :],
                           Z["var_sb"][0:1, :]), "VEC")
        P.wait(s, "VEC")
        P.inc(s.activation(Z["var_sb"][0:1, :], Z["var_sb"][0:1, :],
                           AF.Sqrt, bias=Z["c5_sb"][0:1, 0:1], scale=1.0),
              "ACT")
        P.wait(v, "ACT")
        P.inc(v.reciprocal(Z["var_sb"][0:1, :], Z["var_sb"][0:1, :]), "VEC")
        P.wait(v, "VEC")
        with nc.allow_low_precision(reason="bf16 broadcast operands"):
            P.inc(v.tensor_copy(Z["mean_bf"][0:1, :], Z["mean_sb"][0:1, :]),
                  "VEC")
            P.inc(v.tensor_copy(Z["rstd_bf"][0:1, :], Z["var_sb"][0:1, :]),
                  "VEC")
        P.wait(t, "VEC")
        t.matmul(st_ps[1][:, 0:512], ones_sb[0:1, 0:128], Z["mean_bf"][0:1, :],
                 start=True, stop=True, skip_group_check=True)
        mm = t.matmul(st_ps[1][:, 512:1024], ones_sb[0:1, 0:128],
                      Z["rstd_bf"][0:1, :], start=True, stop=True,
                      skip_group_check=True)
        P.inc(mm, "PE")
        P.wait(v, "PE")
        P.inc(v.tensor_copy(Z["mbc_sb"][:, :], st_ps[1][:, 0:512]), "VEC")
        P.inc(v.tensor_copy(Z["rbc_sb"][:, :], st_ps[1][:, 512:1024]), "VEC")
        P.wait(v, "VEC")
        for dc in range(NCH):
            xv = x_sb[:, T * dc + 512 * tt: T * dc + 512 * (tt + 1)]
            P.inc(v.tensor_sub(xv, xv, Z["mbc_sb"][:, :]), "VEC")
            P.wait(v, "VEC")
            P.inc(v.tensor_tensor(xv, xv, Z["rbc_sb"][:, :], ALU.mult), "VEC")
            P.wait(v, "VEC")
    vec_x2 = P.c["VEC"]

    for tt in range(NTT):
        P.wait(t, "VEC", vec_x2)
        for c in range(NCH):
            mm = t.matmul(st_ps[0][0:16, 512:1024],
                          Z["wg_sb"][:, 16 * c:16 * (c + 1)],
                          x_sb[:, T * c + 512 * tt: T * c + 512 * (tt + 1)],
                          start=(c == 0), stop=(c == NCH - 1),
                          skip_group_check=True)
        P.inc(mm, "PE")
        P.wait(s, "PE")
        P.wait(s, "VEC")
        si = s.activation(Z["lg_tt"][0:16, :], st_ps[0][0:16, 512:1024],
                          AF.Identity, bias=Z["bgc_sb"][0:16, 0:1], scale=1.0)
        P.inc(si, "ACT")
        P.wait(t, "ACT")
        for j in range(4):
            i = 4 * tt + j
            P.wait(t, "VEC")
            mm = t.transpose(pvB_ps[0:128, 0:16],
                             Z["lg_tt"][0:16, 128 * j:128 * (j + 1)],
                             Z["id16_sb"][:, :])
            P.inc(mm, "PE")
            P.wait(v, "PE")
            P.inc(v.tensor_copy(Z["lg_tm"][:, 16 * i:16 * (i + 1)],
                                pvB_ps[0:128, 0:16]), "VEC")
        P.wait(t, "VEC")
    for i in range(NST):
        ltile = Z["lg_tm"][:, 16 * i:16 * (i + 1)]
        P.inc(v.reduce_max(Z["mx_sb"][:, :], ltile, axis=AX), "VEC")
        P.inc(v.reduce_max(Z["nmx_sb"][:, :], ltile, axis=AX, negate=True),
              "VEC")
        P.wait(s, "VEC")
        si = s.activation(Z["ejunk_sb"][:, :], ltile, AF.Exp,
                          bias=Z["nmx_sb"][:, 0:1], scale=1.0,
                          accum_out=Z["sm_sb"][:, :])
        P.inc(si, "ACT")
        P.wait(v, "ACT")
        P.inc(v.reciprocal(Z["meta_sb"][:, 4 * i + 2:4 * i + 3],
                           Z["sm_sb"][:, :]), "VEC")
        P.inc(v.tensor_scalar(Z["eq_sb"][:, :], ltile, Z["mx_sb"][:, 0:1],
                              None, ALU.is_equal), "VEC")
        P.wait(v, "VEC")
        P.inc(v.tensor_tensor(Z["cand_sb"][:, :], Z["r16m_sb"][:, :],
                              Z["eq_sb"][:, :], ALU.mult), "VEC")
        P.wait(v, "VEC")
        P.inc(v.reduce_max(Z["t1_sb"][:, :], Z["cand_sb"][:, :], axis=AX),
              "VEC")
        P.wait(v, "VEC")
        P.inc(v.tensor_scalar(Z["idx_all"][:, i:i + 1], Z["t1_sb"][:, :],
                              -1.0, 16.0, ALU.mult, ALU.add), "VEC")
        P.wait(v, "VEC")
        P.inc(v.tensor_copy(Z["meta_sb"][:, 4 * i:4 * i + 1],
                            Z["idx_all"][:, i:i + 1]), "VEC")
        P.inc(v.tensor_scalar(Z["ohf_all"][:, 16 * i:16 * (i + 1)],
                              Z["iota_sb"][:, :], Z["idx_all"][:, i:i + 1],
                              None, ALU.is_equal), "VEC")
        with nc.allow_low_precision(reason="onehot bf16"):
            P.inc(v.tensor_scalar(Z["ohb_sb"][:, :], Z["iota_sb"][:, :],
                                  Z["idx_all"][:, i:i + 1], None,
                                  ALU.is_equal), "VEC")
        P.wait(t, "VEC")
        mm = t.transpose(trb_ps[0:16, 0:128], Z["ohb_sb"][:, :],
                         Z["idbf_sb"][:, :])
        P.inc(mm, "PE")
        P.wait(v, "PE")
        P.inc(v.tensor_copy(Z["ohfm_sb"][0:16, 128 * i:128 * (i + 1)],
                            trb_ps[0:16, 0:128]), "VEC")
        P.wait(v, "VEC")
    P.wait(v, "VEC")
    P.inc(v.tensor_tensor_scan(Z["csum_sb"][:, :], Z["ohfm_sb"][:, :],
                               Z["ohfm_sb"][:, :], 0.0, ALU.add, ALU.bypass),
          "VEC")
    P.wait(v, "VEC")
    P.inc(v.tensor_copy(Z["cnt_sb"][:, cnt_col:cnt_col + 1],
                        Z["csum_sb"][:, T - 1:T]), "VEC")
    for i in range(NST):
        P.wait(t, "VEC")
        mm = t.transpose(pvB_ps[0:128, 0:16],
                         Z["csum_sb"][0:16, 128 * i:128 * (i + 1)],
                         Z["id16_sb"][:, :])
        P.inc(mm, "PE")
        P.wait(v, "PE")
        P.inc(v.tensor_tensor(Z["sel_sb"][:, :], pvB_ps[0:128, 0:16],
                              Z["ohf_all"][:, 16 * i:16 * (i + 1)], ALU.mult),
              "VEC")
        P.wait(v, "VEC")
        P.inc(v.reduce_sum(Z["cs_sb"][:, :], Z["sel_sb"][:, :], axis=AX),
              "VEC")
        P.wait(v, "VEC")
        P.inc(v.tensor_scalar(Z["meta_sb"][:, 4 * i + 1:4 * i + 2],
                              Z["cs_sb"][:, :], -1.0, None, ALU.add), "VEC")
        P.inc(v.memset(Z["meta_sb"][:, 4 * i + 3:4 * i + 4], 0.0), "VEC")
        P.wait(v, "VEC")
        P.inc(v.tensor_scalar(Z["fif"][:, i:i + 1], Z["idx_all"][:, i:i + 1],
                              float(LCAP), None, ALU.mult), "VEC")
        P.inc(v.tensor_scalar(Z["ovg_sb"][:, :],
                              Z["meta_sb"][:, 4 * i + 1:4 * i + 2],
                              float(LCAP), BIGIX, ALU.is_ge, ALU.mult), "VEC")
        P.wait(v, "VEC")
        P.inc(v.tensor_tensor(Z["fif"][:, i:i + 1], Z["fif"][:, i:i + 1],
                              Z["meta_sb"][:, 4 * i + 1:4 * i + 2], ALU.add),
              "VEC")
        P.wait(v, "VEC")
        P.inc(v.tensor_tensor(Z["fif"][:, i:i + 1], Z["fif"][:, i:i + 1],
                              Z["ovg_sb"][:, :], ALU.add), "VEC")
        P.wait(v, "VEC")
        P.inc(v.tensor_copy(Z["fiu"][:, i:i + 1], Z["fif"][:, i:i + 1]),
              "VEC")
        P.wait(v, "VEC")
    P.wait(sy, "VEC")
    P.inc(sy.dma_start(out=meta_ap.rearrange("(i p) c -> p i c", p=128),
                       in_=Z["meta_sb"].rearrange("p (i c) -> p i c", c=4)),
          "ST", 16)
    for i in range(NST):
        b = i % 2
        sxs = "SX0" if b == 0 else "SX1"
        buf = Z["x2st"][b]
        first = True
        for dc in range(NCH):
            P.wait(t, "VEC")
            mm = t.transpose(trb_ps[0:128, 0:128],
                             x_sb[:, T * dc + 128 * i: T * dc + 128 * (i + 1)],
                             Z["idbf_sb"][:, :])
            P.inc(mm, "PE")
            P.wait(v, "PE")
            if first:
                P.wait(v, sxs)
            first = False
            P.inc(v.tensor_copy(buf[:, 128 * dc:128 * (dc + 1)],
                                trb_ps[0:128, 0:128]), "VEC")
        P.wait(sy, "VEC")
        P.inc(sy.dma_start(out=x2tm_ap[128 * i:128 * (i + 1), :], in_=buf[:, :]),
              sxs, 16)
        P.wait(g, "VEC")
        P.inc(g.indirect_dma_start(
            out=fsrc_ap,
            out_offset=IndirectOffsetOnAxis(Z["fiu"][:, i:i + 1], 0),
            in_=buf[:, :], in_offset=None,
            bounds_check=bchk, oob_is_err=False), sxs, 16)


def emit_l1_const_loads(nc, P, Z, aps):
    g, sy = nc.gpsimd, nc.sync
    for c in range(NCH):
        P.inc(g.dma_start(out=Z["wo_sb"][:, D * c: D * (c + 1)],
                          in_=aps["woT"][128 * c:128 * (c + 1), :]), "LX", 16)
    P.inc(g.dma_start(out=Z["wg_sb"].ap().rearrange("p (c e) -> p c e", c=NCH),
                      in_=aps["wg"].rearrange("(c p) e -> p c e", p=128)),
          "LX", 16)
    for dk, sk in (("boc_sb", "boc"), ("bgc_sb", "bgc"), ("idbf_sb", "idbf"),
                   ("id16_sb", "id16"), ("iota_sb", "iota16b"),
                   ("r16m_sb", "r16m"), ("ones_sb", "onesb"),
                   ("onesf_sb", "onesf"), ("tri8_sb", "tri8"),
                   ("eh01_sb", "eh01"), ("pmask_sb", "pmask"),
                   ("iota128_sb", "iota128"), ("b1c_sb", "b1c"),
                   ("b2c_sb", "b2c")):
        P.inc(sy.dma_start(out=Z[dk][:, :], in_=aps[sk][:, :]), "LS", 16)


def make_headpair_prefetch(nc, P, Z, S, wq_ap, wk_ap, wv_ap, npasses=1):
    g = nc.gpsimd

    def load_pair(hp, bsel):
        sem = "WH0" if bsel == 0 else "WH1"
        P.wait(g, "PE", S["wh_freed"][bsel])
        for mi, wsrc in enumerate((wk_ap, wq_ap, wv_ap)):
            ins = g.dma_start(
                out=Z["wh_sb"][bsel][:, D * mi:D * (mi + 1)].rearrange(
                    "p (c f) -> p c f", c=NCH),
                in_=wsrc[:, 128 * hp:128 * (hp + 1)].rearrange(
                    "(c p) f -> p c f", p=128))
            P.inc(ins, sem, 16)
        S["wh_ready"][bsel] = P.c[sem]

    seq = {"next": 1}

    def prefetch(_hp):
        nxt = seq["next"]
        if nxt < 8 * npasses:
            load_pair(nxt % 8, nxt % 2)
            seq["next"] = nxt + 1

    load_pair(0, 0)
    return prefetch



def emit_prolog_b(nc, P, Z, cnts1_ap, cnts8_ap):
    g, sy, t, v, s = nc.gpsimd, nc.sync, nc.tensor, nc.vector, nc.scalar
    bc_ps = Z["bc_ps"]
    P.wait(sy, "VEC")
    P.inc(sy.dma_start(out=cnts1_ap.rearrange("one e -> e one"),
                       in_=Z["cnt_sb"][:, 0:1]), "CN", 16)
    P.wait(g, "CN")
    cc = g.collective_compute(
        "AllGather", ALU.bypass, replica_groups=[list(range(NC))],
        ins=[cnts1_ap.opt()], outs=[cnts8_ap.opt()])
    P.inc(cc, "CCA", 1)
    P.wait(g, "CCA")
    P.inc(g.dma_start(out=Z["cnts8_sb"][:, :], in_=cnts8_ap), "CN", 16)
    pe0, vec0 = P.c["PE"], P.c["VEC"]
    P.wait(t, "CN")
    P.wait(t, "VEC", vec0)
    mm = t.matmul(bc_ps[0:16, 0:8], Z["cnts8_sb"][:, :], Z["tri8_sb"][:, :],
                  start=True, stop=True, skip_group_check=True)
    P.inc(mm, "PE")
    P.wait(v, "PE")
    P.inc(v.tensor_copy(Z["baseT_sb"][:, :], bc_ps[0:16, 0:8]), "VEC")
    P.wait(t, "VEC")
    mm = t.matmul(bc_ps[0:1, 16:32], Z["pmask_sb"][:, :], Z["cnts8_sb"][:, :],
                  start=True, stop=True, skip_group_check=True)
    P.inc(mm, "PE")
    P.wait(v, "PE")
    P.inc(v.tensor_copy(Z["gbrow_sb"][:, :], bc_ps[0:1, 16:32]), "VEC")
    P.wait(t, "VEC")
    for k in range(2):
        mm = t.matmul(bc_ps[0:1, 32 + 8 * k:40 + 8 * k],
                      Z["eh01_sb"][:, k:k + 1], Z["baseT_sb"][:, :],
                      start=True, stop=True, skip_group_check=True)
        P.inc(mm, "PE")
    P.wait(v, "PE")
    P.inc(v.tensor_copy(Z["rowsel_sb"][:, 0:8], bc_ps[0:1, 32:40]), "VEC")
    P.inc(v.tensor_copy(Z["rowsel_sb"][:, 8:16], bc_ps[0:1, 40:48]), "VEC")
    P.wait(t, "VEC")
    mm = t.matmul(bc_ps[:, 48:64], Z["onesf_sb"][0:1, :],
                  Z["rowsel_sb"][0:1, :], start=True, stop=True,
                  skip_group_check=True)
    P.inc(mm, "PE")
    mm = t.matmul(bc_ps[:, 64:80], Z["onesf_sb"][0:1, :],
                  Z["gbrow_sb"][0:1, :], start=True, stop=True,
                  skip_group_check=True)
    P.inc(mm, "PE")
    P.wait(v, "PE")
    P.inc(v.tensor_copy(Z["bbc_sb"][:, :], bc_ps[:, 48:64]), "VEC")
    P.inc(v.tensor_copy(Z["gbb_sb"][:, :], bc_ps[:, 64:80]), "VEC")
    P.wait(v, "VEC")
    P.wait(v, "PE")
    k2 = Z["k2_sb"]
    pc_all = k2[:, 0:64].bitcast(F32)
    cmp_all = k2[:, 64:576].bitcast(F32)
    nle_all = k2[:, 576:640].bitcast(F32)
    rr_all = k2[:, 640:704].bitcast(F32)
    ovg_all = k2[:, 704:768].bitcast(F32)
    cols = [(ep * 16 + i, ep, i) for ep in range(2) for i in range(CAP // 128)]
    for col, ep, i in cols:
        P.inc(v.tensor_scalar(pc_all[:, col:col + 1], Z["iota128_sb"][:, :],
                              float(128 * i), None, ALU.add), "VEC")
    P.wait(v, "VEC")
    for col, ep, i in cols:
        P.inc(v.tensor_scalar(cmp_all[:, 8 * col:8 * col + 8],
                              Z["bbc_sb"][:, 8 * ep:8 * ep + 8],
                              pc_all[:, col:col + 1], None, ALU.is_le), "VEC")
    P.wait(v, "VEC")
    for col, ep, i in cols:
        P.inc(v.reduce_sum(nle_all[:, col:col + 1],
                           cmp_all[:, 8 * col:8 * col + 8], axis=AX), "VEC")
    P.wait(v, "VEC")
    for col, ep, i in cols:
        P.inc(v.tensor_tensor(cmp_all[:, 8 * col:8 * col + 8],
                              cmp_all[:, 8 * col:8 * col + 8],
                              Z["bbc_sb"][:, 8 * ep:8 * ep + 8], ALU.mult),
              "VEC")
    P.wait(v, "VEC")
    for col, ep, i in cols:
        P.inc(v.reduce_max(rr_all[:, col:col + 1],
                           cmp_all[:, 8 * col:8 * col + 8], axis=AX), "VEC")
    P.wait(v, "VEC")
    for col, ep, i in cols:
        P.inc(v.tensor_sub(rr_all[:, col:col + 1], pc_all[:, col:col + 1],
                           rr_all[:, col:col + 1]), "VEC")
    P.wait(v, "VEC")
    for col, ep, i in cols:
        P.inc(v.tensor_scalar(ovg_all[:, col:col + 1], rr_all[:, col:col + 1],
                              float(LCAP), BIGIX, ALU.is_ge, ALU.mult), "VEC")
        P.inc(v.tensor_scalar(Z["gif"][:, col:col + 1], nle_all[:, col:col + 1],
                              2048.0, float(1024 * ep - 2048), ALU.mult,
                              ALU.add), "VEC")
    P.wait(v, "VEC")
    for col, ep, i in cols:
        P.inc(v.tensor_tensor(Z["gif"][:, col:col + 1],
                              Z["gif"][:, col:col + 1], rr_all[:, col:col + 1],
                              ALU.add), "VEC")
    P.wait(v, "VEC")
    for col, ep, i in cols:
        P.inc(v.tensor_tensor(Z["gif"][:, col:col + 1],
                              Z["gif"][:, col:col + 1], ovg_all[:, col:col + 1],
                              ALU.add), "VEC")
    P.wait(v, "VEC")
    for col, ep, i in cols:
        P.inc(v.tensor_copy(Z["giu"][:, col:col + 1],
                            Z["gif"][:, col:col + 1]), "VEC")
    P.wait(v, "VEC")



def make_ffn_loaders(nc, P, ZF, w1_ap, w2_ap):
    g = nc.gpsimd
    w1_sb, w2_sb = ZF["w1_sb"], ZF["w2_sb"]
    FS = {"w1_ready": [0, 0], "w1_freed": [0, 0],
          "w2_ready": [0, 0], "w2_freed": [0, 0]}

    def load_w1(bsel, e, hc):
        P.wait(g, "PE", FS["w1_freed"][bsel])
        ins = g.dma_start(
            out=w1_sb[bsel].rearrange("p (c f) -> p c f", c=NCH),
            in_=w1_ap[e, :, 128 * hc:128 * (hc + 1)].rearrange(
                "(c p) f -> p c f", p=128))
        FS["w1_ready"][bsel] = P.inc(ins, "W10" if bsel == 0 else "W11", 16)

    def load_w2(bsel, e, dc):
        P.wait(g, "PE", FS["w2_freed"][bsel])
        ins = g.dma_start(
            out=w2_sb[bsel].rearrange("p (c f) -> p c f", c=32),
            in_=w2_ap[e, :, 128 * dc:128 * (dc + 1)].rearrange(
                "(c p) f -> p c f", p=128))
        FS["w2_ready"][bsel] = P.inc(ins, "W20" if bsel == 0 else "W21", 16)

    return FS, load_w1, load_w2


def emit_ffn(nc, P, Z, ZF, loaders, fdst_ap, rsrc_ap, giu_done, bchk):
    g, sy, t, v, s = nc.gpsimd, nc.sync, nc.tensor, nc.vector, nc.scalar
    buf_fm = ZF["buf_fm"]
    bt_sb, yt_sb = ZF["bt_sb"], ZF["yt_sb"]
    w1_sb, w2_sb = ZF["w1_sb"], ZF["w2_sb"]
    h_sb, yfm_sb = ZF["h_sb"], ZF["yfm_sb"]
    h_ps, y_ps, tr_ps = Z["pvA_ps"], Z["pvB_ps"], Z["trb_ps"]
    b1_sb, b2_sb, idbf_sb = Z["b1c_sb"], Z["b2c_sb"], Z["idbf_sb"]
    FS, load_w1, load_w2 = loaders
    w1_ready, w1_freed = FS["w1_ready"], FS["w1_freed"]
    w2_ready, w2_freed = FS["w2_ready"], FS["w2_freed"]
    tr_after = {}
    vec_strip = {}

    for e in range(2):
        for i in range(CAP // 128):
            b = i % 2
            bts = "BT0" if b == 0 else "BT1"
            if i >= 2 or e > 0:
                P.wait(g, "PE", tr_after.get((e, i - 2), tr_after.get(
                    (e - 1, CAP // 128 + i - 2), 0)))
            P.wait(g, "VEC", giu_done)
            col = e * 16 + i
            P.inc(g.indirect_dma_start(
                out=bt_sb[b][:, :], out_offset=None,
                in_=fdst_ap,
                in_offset=IndirectOffsetOnAxis(Z["giu"][:, col:col + 1], 0),
                bounds_check=bchk, oob_is_err=False), bts, 16)
            P.wait(t, bts)
            P.wait(t, "LS", P.c["LS"])
            P.wait(t, "VEC")
            for dc in range(NCH):
                P.wait(t, "VEC")
                mm = t.transpose(tr_ps[0:128, 0:128],
                                 bt_sb[b][:, 128 * dc:128 * (dc + 1)],
                                 idbf_sb[:, :])
                P.inc(mm, "PE")
                P.wait(v, "PE")
                P.inc(v.tensor_copy(
                    buf_fm[:, CAP * dc + 128 * i:CAP * dc + 128 * (i + 1)],
                    tr_ps[0:128, 0:128]), "VEC")
            tr_after[(e, i)] = P.c["PE"]
            if i % 4 == 3:
                vec_strip[(e, i // 4)] = P.c["VEC"]
        if e > 0:
            load_w1(0, e, 0)
        for st in range(4):
            act_of = {}
            P.wait(t, "ACT")
            for hc in range(32):
                bsel = hc % 2
                if hc + 1 < 32:
                    load_w1((hc + 1) % 2, e, hc + 1)
                elif st + 1 < 4:
                    load_w1((hc + 1) % 2, e, 0)
                t.wait_ge(P.sem["W10" if bsel == 0 else "W11"], w1_ready[bsel])
                P.wait(t, "VEC", vec_strip[(e, st)])
                hps = h_ps if hc % 2 == 0 else y_ps
                if hc >= 2:
                    P.wait(t, "ACT", act_of.get(hc - 2, 0))
                for c in range(NCH):
                    mm = t.matmul(hps[:, :],
                                  w1_sb[bsel][:, 128 * c:128 * (c + 1)],
                                  buf_fm[:, CAP * c + 512 * st:
                                         CAP * c + 512 * (st + 1)],
                                  start=(c == 0), stop=(c == NCH - 1),
                                  skip_group_check=True)
                P.inc(mm, "PE")
                w1_freed[bsel] = P.c["PE"]
                P.wait(s, "PE")
                si = s.activation(h_sb[:, 512 * hc:512 * (hc + 1)], hps[:, :],
                                  AF.Relu,
                                  bias=b1_sb[:, 32 * e + hc:32 * e + hc + 1],
                                  scale=1.0)
                act_of[hc] = P.inc(si, "ACT")
            load_w2(0, e, 0)
            yact = {}
            for dc in range(NCH):
                bsel = dc % 2
                if dc + 1 < NCH:
                    load_w2((dc + 1) % 2, e, dc + 1)
                t.wait_ge(P.sem["W20" if bsel == 0 else "W21"], w2_ready[bsel])
                yps = h_ps if dc % 2 == 0 else y_ps
                if dc < 2:
                    P.wait(t, "ACT", act_of[31])
                else:
                    P.wait(t, "ACT", yact[dc - 2])
                for hc in range(32):
                    mm = t.matmul(yps[:, :],
                                  w2_sb[bsel][:, 128 * hc:128 * (hc + 1)],
                                  h_sb[:, 512 * hc:512 * (hc + 1)],
                                  start=(hc == 0), stop=(hc == 31),
                                  skip_group_check=True)
                P.inc(mm, "PE")
                w2_freed[bsel] = P.c["PE"]
                P.wait(s, "PE")
                si = s.activation(yfm_sb[:, 512 * dc:512 * (dc + 1)], yps[:, :],
                                  AF.Identity,
                                  bias=b2_sb[:, 8 * e + dc:8 * e + dc + 1],
                                  scale=1.0)
                yact[dc] = P.inc(si, "ACT")
            P.wait(t, "ACT")
            for j in range(4):
                jb = j % 2
                sys_ = "SY0" if jb == 0 else "SY1"
                buf = yt_sb[jb]
                first = True
                for dc in range(NCH):
                    P.wait(t, "VEC")
                    mm = t.transpose(tr_ps[0:128, 0:128],
                                     yfm_sb[:, 512 * dc + 128 * j:
                                            512 * dc + 128 * (j + 1)],
                                     idbf_sb[:, :])
                    P.inc(mm, "PE")
                    P.wait(v, "PE")
                    if first:
                        P.wait(v, sys_)
                    first = False
                    P.inc(v.tensor_copy(buf[:, 128 * dc:128 * (dc + 1)],
                                        tr_ps[0:128, 0:128]), "VEC")
                P.wait(g, "VEC")
                col = e * 16 + 4 * st + j
                P.inc(g.indirect_dma_start(
                    out=rsrc_ap,
                    out_offset=IndirectOffsetOnAxis(Z["giu"][:, col:col + 1], 0),
                    in_=buf[:, :], in_offset=None,
                    bounds_check=bchk, oob_is_err=False), sys_, 16)



def emit_phase_c(nc, P, Z, ZF, x2tm_ap, meta_ap, rdst_ap, out_ap, bchk,
                 pe_attn):
    g, sy, t, v, s = nc.gpsimd, nc.sync, nc.tensor, nc.vector, nc.scalar
    ybig = Z["x_sb"]
    x2all = Z["o_sb"]
    q2, k2 = Z["q2_sb"], Z["k2_sb"]
    eqall = q2[:, 0:1024].bitcast(F32)
    selall = q2[:, 1024:2048].bitcast(F32)
    lnall = q2[:, 2048:2560].bitcast(F32)
    mt_all = q2[:, 2560:2816].bitcast(F32)
    o2b = [k2[:, 0:2048].bitcast(F32), k2[:, 2048:4096].bitcast(F32)]
    zb = [Z["vaugA"][:, 0:1024], Z["vaugA"][:, 1024:2048],
          Z["vaugB"][:, 0:1024], Z["vaugB"][:, 1024:2048]]
    zdump = Z["expS"]

    P.wait(sy, "SX0")
    P.wait(sy, "SX1")
    P.wait(sy, "SY0")
    P.wait(sy, "SY1")
    P.wait(sy, "ST")
    P.wait(sy, "PE")
    P.wait(sy, "VEC")
    P.wait(sy, "ACT")
    for i in range(0, 9):
        P.inc(sy.dma_start(out=x2all[:, 1024 * i:1024 * (i + 1)],
                           in_=x2tm_ap[128 * i:128 * (i + 1), :]), "LT0", 16)
    P.wait(v, "PE", pe_attn)
    P.wait(v, "LT0", 16)
    P.wait(v, "VEC")
    for i in range(NST):
        P.inc(v.tensor_scalar(eqall[:, 16 * i:16 * (i + 1)], Z["iota_sb"][:, :],
                              mt_all[:, 4 * i:4 * i + 1], None, ALU.is_equal),
              "VEC")
    P.wait(v, "VEC")
    for i in range(NST):
        P.inc(v.tensor_tensor(selall[:, 16 * i:16 * (i + 1)],
                              eqall[:, 16 * i:16 * (i + 1)], Z["gbb_sb"][:, :],
                              ALU.mult), "VEC")
    P.wait(v, "VEC")
    for i in range(NST):
        P.inc(v.reduce_sum(lnall[:, 8 * i + 6:8 * i + 7],
                           selall[:, 16 * i:16 * (i + 1)], axis=AX), "VEC")
    P.wait(v, "VEC")
    for i in range(NST):
        P.inc(v.tensor_tensor(lnall[:, 8 * i + 6:8 * i + 7],
                              lnall[:, 8 * i + 6:8 * i + 7],
                              mt_all[:, 4 * i + 1:4 * i + 2], ALU.add), "VEC")
    P.wait(v, "VEC")
    for i in range(NST):
        P.inc(v.tensor_scalar(lnall[:, 8 * i + 7:8 * i + 8],
                              lnall[:, 8 * i + 6:8 * i + 7], float(CAP),
                              BIGIX, ALU.is_ge, ALU.mult), "VEC")
    P.wait(v, "VEC")
    for i in range(NST):
        P.inc(v.tensor_tensor(Z["fif"][:, i:i + 1], Z["fif"][:, i:i + 1],
                              lnall[:, 8 * i + 7:8 * i + 8], ALU.add), "VEC")
    P.wait(v, "VEC")
    for i in range(NST):
        P.inc(v.tensor_copy(Z["fiu"][:, i:i + 1], Z["fif"][:, i:i + 1]),
              "VEC")
    P.inc(v.memset(ybig[:, :], 0.0), "VEC")
    P.wait(g, "PE")
    P.wait(g, "VEC")
    for i in range(9, 18):
        P.inc(g.dma_start(out=x2all[:, 1024 * i:1024 * (i + 1)],
                          in_=x2tm_ap[128 * i:128 * (i + 1), :]), "LT1", 16)
    P.wait(g, "CCR")
    for i in range(NST):
        P.inc(g.indirect_dma_start(
            out=ybig[:, 1024 * i:1024 * (i + 1)], out_offset=None,
            in_=rdst_ap,
            in_offset=IndirectOffsetOnAxis(Z["fiu"][:, i:i + 1], 0),
            bounds_check=bchk, oob_is_err=False), "GB0", 16)
    P.wait(v, "GB0")
    P.wait(v, "LT0")
    P.wait(v, "LT1")
    zv = {}
    sums = {}
    o2v = {}
    for i in range(NST):
        zbi = zb[i % 4]
        obi = o2b[i % 2]
        if i >= 4:
            P.wait(v, "VEC", o2v[i - 4])
        zv[i] = P.inc(v.scalar_tensor_tensor(
            zbi[:, :], ybig[:, 1024 * i:1024 * (i + 1)],
            mt_all[:, 4 * i + 2:4 * i + 3],
            x2all[:, 1024 * i:1024 * (i + 1)], ALU.mult, ALU.add), "VEC")
        s.wait_ge(P.sem["VEC"], zv[i])
        P.inc(s.activation(zdump[0][:, :], zbi[:, :], AF.Identity,
                           accum_out=lnall[:, 8 * i:8 * i + 1]), "ACT")
        sums[i] = P.inc(s.activation(zdump[1][:, :], zbi[:, :], AF.Square,
                                     accum_out=lnall[:, 8 * i + 1:8 * i + 2]),
                        "ACT")
        v.wait_ge(P.sem["ACT"], sums[i])
        P.inc(v.tensor_scalar(lnall[:, 8 * i + 2:8 * i + 3],
                              lnall[:, 8 * i:8 * i + 1], 1.0 / D, None,
                              ALU.mult), "VEC")
        P.wait(v, "VEC")
        P.inc(v.tensor_scalar(lnall[:, 8 * i + 4:8 * i + 5],
                              lnall[:, 8 * i + 2:8 * i + 3],
                              lnall[:, 8 * i + 2:8 * i + 3], None, ALU.mult),
              "VEC")
        P.wait(v, "VEC")
        vv = P.inc(v.tensor_scalar(lnall[:, 8 * i + 3:8 * i + 4],
                                   lnall[:, 8 * i + 1:8 * i + 2], 1.0 / D,
                                   lnall[:, 8 * i + 4:8 * i + 5], ALU.mult,
                                   ALU.subtract), "VEC")
        s.wait_ge(P.sem["VEC"], vv)
        sq = P.inc(s.activation(lnall[:, 8 * i + 5:8 * i + 6],
                                lnall[:, 8 * i + 3:8 * i + 4], AF.Sqrt,
                                bias=Z["c5_sb"][:, 0:1], scale=1.0), "ACT")
        v.wait_ge(P.sem["ACT"], sq)
        P.inc(v.reciprocal(lnall[:, 8 * i + 5:8 * i + 6],
                           lnall[:, 8 * i + 5:8 * i + 6]), "VEC")
        P.wait(v, "VEC")
        if i >= 2:
            P.wait(v, "SO", 16 * (i - 1))
        o2v[i] = P.inc(v.tensor_scalar(obi[:, :], zbi[:, :],
                                       lnall[:, 8 * i + 2:8 * i + 3],
                                       lnall[:, 8 * i + 5:8 * i + 6],
                                       ALU.subtract, ALU.mult), "VEC")
        sy.wait_ge(P.sem["VEC"], o2v[i])
        P.inc(sy.dma_start(out=out_ap[128 * i:128 * (i + 1), :],
                           in_=obi[:, :]), "SO", 16)
    P.wait(sy, "SO")



def build_fused():
    nc = bass.Bass("TRN2", target_bir_lowering=False, debug=False)
    xT = nc.dram_tensor("xT", [D, T], F32, kind="ExternalInput").ap()
    wqT = nc.dram_tensor("wqT", [D, D], F32, kind="ExternalInput").ap()
    wkT = nc.dram_tensor("wkT", [D, D], F32, kind="ExternalInput").ap()
    wvT = nc.dram_tensor("wvT", [D, D], F32, kind="ExternalInput").ap()
    woT = nc.dram_tensor("woT", [D, D], F32, kind="ExternalInput").ap()
    boc = nc.dram_tensor("boc", [128, NCH], F32, kind="ExternalInput").ap()
    wg = nc.dram_tensor("wg", [D, E], F32, kind="ExternalInput").ap()
    bgc = nc.dram_tensor("bgc", [E, 1], F32, kind="ExternalInput").ap()
    idbf = nc.dram_tensor("idbf", [128, 128], BF16, kind="ExternalInput").ap()
    id16 = nc.dram_tensor("id16", [16, 16], F32, kind="ExternalInput").ap()
    iota16b = nc.dram_tensor("iota16b", [128, 16], F32, kind="ExternalInput").ap()
    r16m = nc.dram_tensor("r16m", [128, 16], F32, kind="ExternalInput").ap()
    onesb = nc.dram_tensor("onesb", [128, 128], BF16, kind="ExternalInput").ap()
    onesf = nc.dram_tensor("onesf", [128, 128], F32, kind="ExternalInput").ap()
    tri8 = nc.dram_tensor("tri8", [8, 8], F32, kind="ExternalInput").ap()
    eh01 = nc.dram_tensor("eh01", [16, 2], F32, kind="ExternalInput").ap()
    pmask = nc.dram_tensor("pmask", [8, 1], F32, kind="ExternalInput").ap()
    iota128 = nc.dram_tensor("iota128", [128, 1], F32, kind="ExternalInput").ap()
    w1 = nc.dram_tensor("w1", [2, D, FH], F32, kind="ExternalInput").ap()
    w2 = nc.dram_tensor("w2", [2, FH, D], F32, kind="ExternalInput").ap()
    b1c = nc.dram_tensor("b1c", [128, 64], F32, kind="ExternalInput").ap()
    b2c = nc.dram_tensor("b2c", [128, 16], F32, kind="ExternalInput").ap()
    out = nc.dram_tensor("out", [T, D], F32, kind="ExternalOutput").ap()
    x2tm = nc.dram_tensor("x2tm", [T, D], BF16).ap()
    meta = nc.dram_tensor("meta", [T, 4], F32).ap()
    cnts1 = nc.dram_tensor("cnts1", [1, E], F32).ap()
    cnts8 = nc.dram_tensor("cnts8", [8, E], F32).ap()
    fsrc = nc.dram_tensor("fsrc", [NROW, D], BF16).ap()
    fdst = nc.dram_tensor("fdst", [NROW, D], BF16).ap()
    rsrc = nc.dram_tensor("rsrc", [NROW, D], BF16).ap()
    rdst = nc.dram_tensor("rdst", [NROW, D], BF16).ap()

    es = ExitStack()
    Z = alloc_l1(nc, es)
    ZF = alloc_overlays(Z)
    P = Prog(nc, es, SEMS)
    g, sy = nc.gpsimd, nc.sync
    bchk = nc.alloc_register(mybir.EngineType.Pool, "bchk")
    g.reg_mov(bchk, NROW - 1)

    for c in range(NCH):
        P.inc(g.dma_start(out=Z["x_sb"][:, T * c: T * (c + 1)],
                          in_=xT[128 * c:128 * (c + 1), :]), "LX", 16)
    emit_l1_const_loads(nc, P, Z, dict(
        woT=woT, wg=wg, boc=boc, bgc=bgc, idbf=idbf, id16=id16,
        iota16b=iota16b, r16m=r16m, onesb=onesb, onesf=onesf, tri8=tri8,
        eh01=eh01, pmask=pmask, iota128=iota128, b1c=b1c, b2c=b2c))
    lx_ready = P.c["LX"]
    S = new_l1_state()
    prefetch = make_headpair_prefetch(nc, P, Z, S, wqT, wkT, wvT)
    emit_l1_init(nc, P, Z)
    with nc.named_scope("l1"):
        emit_l1_batch(nc, P, Z, S, prefetch, x2tm, meta, 0, lx_ready, fsrc,
                      bchk)
    pe_attn = P.c["PE"]

    with nc.named_scope("prolog_b"):
        emit_prolog_b(nc, P, Z, cnts1, cnts8)
    giu_done = P.c["VEC"]

    P.wait(g, "SX0")
    P.wait(g, "SX1")
    P.wait(g, "ST")
    P.wait(g, "PE")
    P.wait(g, "VEC")
    loaders = make_ffn_loaders(nc, P, ZF, w1, w2)
    loaders[1](0, 0, 0)
    P.wait(sy, "SX0")
    P.wait(sy, "SX1")
    P.wait(sy, "ST")
    P.wait(sy, "PE")
    P.wait(sy, "VEC")
    mt_all_early = Z["q2_sb"][:, 2560:2816].bitcast(F32)
    P.inc(sy.dma_start(out=mt_all_early.rearrange("p (i c) -> p i c", c=4),
                       in_=meta.rearrange("(i p) c -> p i c", p=128)),
          "LT0", 16)
    for i in range(18, NST):
        P.inc(sy.dma_start(out=Z["o_sb"][:, 1024 * i:1024 * (i + 1)],
                           in_=x2tm[128 * i:128 * (i + 1), :]), "LT0", 16)

    with nc.named_scope("a2a_fwd"):
        with allow_ici_all_to_all():
            cc = g.collective_compute(
                "AllToAll", ALU.bypass, replica_groups=[list(range(NC))],
                ins=[fsrc.opt()], outs=[fdst.opt()])
        P.inc(cc, "CCF", 1)
        P.wait(g, "CCF")

    with nc.named_scope("ffn"):
        emit_ffn(nc, P, Z, ZF, loaders, fdst, rsrc, giu_done, bchk)

    with nc.named_scope("a2a_ret"):
        P.wait(g, "SY0")
        P.wait(g, "SY1")
        with allow_ici_all_to_all():
            cc = g.collective_compute(
                "AllToAll", ALU.bypass, replica_groups=[list(range(NC))],
                ins=[rsrc.opt()], outs=[rdst.opt()])
        P.inc(cc, "CCR", 1)

    with nc.named_scope("phase_c"):
        emit_phase_c(nc, P, Z, ZF, x2tm, meta, rdst, out, bchk, pe_attn)

    es.close()
    return nc



_NC_CACHE = {}


def _get(name, builder):
    if name not in _NC_CACHE:
        _NC_CACHE[name] = builder()
    return _NC_CACHE[name]


def kernel(L0_attn_x, L0_attn_Wq, L0_attn_Wk, L0_attn_Wv, L0_attn_Wo,
           L0_attn_bo, L0_ffn_Wg, L0_ffn_bg, L0_ffn_W1, L0_ffn_b1,
           L0_ffn_W2, L0_ffn_b2):
    x = np.asarray(L0_attn_x, dtype=np.float32)
    wqT = np.ascontiguousarray(np.asarray(L0_attn_Wq, np.float32).T)
    wkT = np.ascontiguousarray(np.asarray(L0_attn_Wk, np.float32).T)
    wvT = np.ascontiguousarray(np.asarray(L0_attn_Wv, np.float32).T)
    woT = np.ascontiguousarray(np.asarray(L0_attn_Wo, np.float32).T)
    bo = np.asarray(L0_attn_bo, np.float32)
    wg = np.ascontiguousarray(np.asarray(L0_ffn_Wg, np.float32))
    bg = np.asarray(L0_ffn_bg, np.float32)
    W1 = np.asarray(L0_ffn_W1, np.float32)
    b1 = np.asarray(L0_ffn_b1, np.float32)
    W2 = np.asarray(L0_ffn_W2, np.float32)
    b2 = np.asarray(L0_ffn_b2, np.float32)

    boc = np.ascontiguousarray(bo.reshape(NCH, 128).T)
    bgc = np.ascontiguousarray(bg.reshape(E, 1))
    idbf = np.eye(128, dtype=np.float32).astype(BF16NP)
    id16 = np.eye(16, dtype=np.float32)
    iota16b = np.ascontiguousarray(np.tile(np.arange(16, dtype=np.float32),
                                           (128, 1)))
    r16m = np.ascontiguousarray(16.0 - iota16b)
    onesb = np.ones((128, 128), np.float32).astype(BF16NP)
    onesf = np.ones((128, 128), np.float32)
    tri8 = np.triu(np.ones((8, 8), np.float32), 1)
    iota128 = np.arange(128, dtype=np.float32).reshape(128, 1)

    nc1 = _get("fused", build_fused)
    in_maps = []
    for c in range(NC):
        eh01 = np.zeros((16, 2), np.float32)
        eh01[2 * c, 0] = 1.0
        eh01[2 * c + 1, 1] = 1.0
        b1cc = np.ascontiguousarray(
            np.concatenate([b1[2 * c + e].reshape(32, 128).T for e in range(2)],
                           axis=1))
        b2cc = np.ascontiguousarray(
            np.concatenate([b2[2 * c + e].reshape(NCH, 128).T for e in range(2)],
                           axis=1))
        in_maps.append({
            "xT": np.ascontiguousarray(x[c].T),
            "wqT": wqT, "wkT": wkT, "wvT": wvT, "woT": woT,
            "boc": boc, "wg": wg, "bgc": bgc,
            "idbf": idbf, "id16": id16, "iota16b": iota16b, "r16m": r16m,
            "onesb": onesb, "onesf": onesf, "tri8": tri8, "eh01": eh01,
            "pmask": (np.arange(8) < c).astype(np.float32).reshape(8, 1),
            "iota128": iota128,
            "w1": np.ascontiguousarray(W1[2 * c:2 * c + 2]),
            "w2": np.ascontiguousarray(W2[2 * c:2 * c + 2]),
            "b1c": b1cc, "b2c": b2cc,
        })
    res = run_bass_kernel_spmd(nc1, in_maps, core_ids=list(range(NC))).results
    return np.stack([res[c]["out"] for c in range(NC)], axis=0)


# version 07
# speedup vs baseline: 1.0056x; 1.0056x over previous
import numpy as np
import ml_dtypes
from contextlib import ExitStack

import concourse.bass as bass
import concourse.mybir as mybir
from concourse.bass import IndirectOffsetOnAxis
from concourse.bass_utils import run_bass_kernel_spmd
from concourse.aot_env import allow_ici_all_to_all

F32 = mybir.dt.float32
BF16 = mybir.dt.bfloat16
U32 = mybir.dt.uint32
AF = mybir.ActivationFunctionType
ALU = mybir.AluOpType
AX = mybir.AxisListType.X
BF16NP = ml_dtypes.bfloat16

D = 1024
T = 4096
H = 16
NC = 8
E = 16
CAP = 2048
LCAP = 1024
FH = 4096
NCH = D // 128
NTT = T // 512
NST = T // 128
BIG = float(1 << 26)
BIGIX = 1.0e6
NROW = E * LCAP


class Prog:
    def __init__(self, nc, es, sems):
        self.nc = nc
        self.c = {}
        self.sem = {}
        for name in sems:
            self.sem[name] = es.enter_context(nc.semaphore(name))
            self.c[name] = 0

    def inc(self, instr, name, k=1):
        self.c[name] += k
        instr.then_inc(self.sem[name], k)
        return self.c[name]

    def wait(self, eng, name, v=None):
        eng.wait_ge(self.sem[name], self.c[name] if v is None else v)


SEMS = ["LX", "LS", "WH0", "WH1", "PE", "ACT", "VEC", "ST",
        "SX0", "SX1", "SST", "SEXP", "SPV",
        "CN", "CCA", "CCF", "CCR",
        "BT0", "BT1", "W10", "W11", "W20", "W21", "SY0", "SY1",
        "LT0", "LT1", "GB0", "GB1", "SO"]


def alloc_l1(nc, es):
    sb = lambda name, shape, dt: es.enter_context(nc.sbuf_tensor(name, shape, dt))
    ps = lambda name, shape, dt: es.enter_context(nc.psum_tensor(name, shape, dt))
    Z = {}
    Z["x_sb"] = sb("x_sb", [128, NCH * T], BF16)
    Z["o_sb"] = sb("o_sb", [128, NCH * T], BF16)
    Z["wo_sb"] = sb("wo_sb", [128, NCH * D], BF16)
    Z["wh_sb"] = [sb(f"wh_sb{i}", [128, 3 * D], BF16) for i in range(2)]
    Z["wg_sb"] = sb("wg_sb", [128, NCH * E], BF16)
    Z["boc_sb"] = sb("boc_sb", [128, NCH], F32)
    Z["bgc_sb"] = sb("bgc_sb", [16, 1], F32)
    Z["idbf_sb"] = sb("idbf_sb", [128, 128], BF16)
    Z["id16_sb"] = sb("id16_sb", [16, 16], F32)
    Z["iota_sb"] = sb("iota_sb", [128, 16], F32)
    Z["r16m_sb"] = sb("r16m_sb", [128, 16], F32)
    Z["ones_sb"] = sb("ones_sb", [128, 128], BF16)
    Z["onesf_sb"] = sb("onesf_sb", [128, 128], F32)
    Z["k2_sb"] = sb("k2_sb", [128, T], BF16)
    Z["q2_sb"] = sb("q2_sb", [128, T], BF16)
    Z["v2tmp"] = sb("v2tmp", [128, 512], BF16)
    Z["vaugA"] = sb("vaugA", [128, 65 * 32], BF16)
    Z["vaugB"] = sb("vaugB", [128, 65 * 32], BF16)
    Z["expS"] = [sb(f"expS{i}", [128, 1024], BF16) for i in range(2)]
    Z["rden_bf"] = sb("rden_bf", [65, 512], BF16)
    Z["bc_sb"] = sb("bc_sb", [128, 512], F32)
    Z["ysq_sb"] = sb("ysq_sb", [128, 512], BF16)
    Z["mean_sb"] = sb("mean_sb", [1, 512], F32)
    Z["var_sb"] = sb("var_sb", [1, 512], F32)
    Z["mean_bf"] = sb("mean_bf", [1, 512], BF16)
    Z["rstd_bf"] = sb("rstd_bf", [1, 512], BF16)
    Z["rbc_sb"] = sb("rbc_sb", [128, 512], F32)
    Z["lg_tt"] = sb("lg_tt", [16, 512], F32)
    Z["msq_sb"] = Z["lg_tt"][0:1, 0:512]
    Z["mbc_sb"] = Z["bc_sb"]
    Z["mx_sb"] = sb("mx_sb", [128, 1], F32)
    Z["nmx_sb"] = sb("nmx_sb", [128, 1], F32)
    Z["sm_sb"] = sb("sm_sb", [128, 1], F32)
    Z["ejunk_sb"] = sb("ejunk_sb", [128, 16], F32)
    Z["eq_sb"] = sb("eq_sb", [128, 16], F32)
    Z["cand_sb"] = sb("cand_sb", [128, 16], F32)
    Z["t1_sb"] = sb("t1_sb", [128, 1], F32)

    Z["ohb_sb"] = sb("ohb_sb", [128, 16], BF16)
    Z["cnt_sb"] = sb("cnt_sb", [16, 8], F32)
    Z["sel_sb"] = sb("sel_sb", [128, 16], F32)
    Z["cs_sb"] = sb("cs_sb", [128, 1], F32)

    Z["c5_sb"] = sb("c5_sb", [128, 1], F32)
    Z["c8_sb"] = sb("c8_sb", [128, 1], F32)
    Z["csum_sb"] = Z["o_sb"][0:16, 0:8192].bitcast(F32)
    Z["ohfm_sb"] = Z["o_sb"][0:16, 8192:12288]
    Z["lg_tm"] = Z["o_sb"][:, 12288:13312].bitcast(F32)
    Z["ohf_all"] = Z["o_sb"][:, 13312:14336].bitcast(F32)
    Z["meta_sb"] = Z["o_sb"][:, 14336:14592].bitcast(F32)
    Z["idx_all"] = Z["o_sb"][:, 14592:14656].bitcast(F32)
    Z["x2st"] = [Z["o_sb"][:, 16384:17408], Z["o_sb"][:, 17408:18432]]
    Z["fif"] = sb("fif", [128, 32], F32)
    Z["fiu"] = sb("fiu", [128, 32], U32)
    Z["gif"] = sb("gif", [128, 32], F32)
    Z["giu"] = sb("giu", [128, 32], U32)
    Z["cnts8_sb"] = sb("cnts8_sb", [8, 16], F32)
    Z["tri8_sb"] = sb("tri8_sb", [8, 8], F32)
    Z["eh01_sb"] = sb("eh01_sb", [16, 2], F32)
    Z["pmask_sb"] = sb("pmask_sb", [8, 1], F32)
    Z["iota128_sb"] = sb("iota128_sb", [128, 1], F32)
    Z["baseT_sb"] = sb("baseT_sb", [16, 8], F32)
    Z["rowsel_sb"] = sb("rowsel_sb", [1, 16], F32)
    Z["gbrow_sb"] = sb("gbrow_sb", [1, 16], F32)
    Z["bbc_sb"] = sb("bbc_sb", [128, 16], F32)
    Z["gbb_sb"] = sb("gbb_sb", [128, 16], F32)
    Z["pc_sb"] = sb("pc_sb", [128, 1], F32)
    Z["cmp8_sb"] = sb("cmp8_sb", [128, 8], F32)
    Z["bsc_sb"] = sb("bsc_sb", [128, 8], F32)
    Z["nle_sb"] = sb("nle_sb", [128, 1], F32)
    Z["rr_sb"] = sb("rr_sb", [128, 1], F32)
    Z["ovg_sb"] = sb("ovg_sb", [128, 1], F32)
    Z["b1c_sb"] = sb("b1c_sb", [128, 64], F32)
    Z["b2c_sb"] = sb("b2c_sb", [128, 16], F32)
    Z["ln2_sb"] = sb("ln2_sb", [128, 8], F32)
    Z["st_ps"] = [ps("st_ps0", [128, 1024], F32), ps("st_ps1", [128, 1024], F32)]
    Z["pvA_ps"] = ps("pvA_ps", [128, 512], F32)
    Z["pvB_ps"] = ps("pvB_ps", [128, 512], F32)
    Z["bc_ps"] = ps("bc_ps", [128, 512], F32)
    Z["trb_ps"] = ps("trb_ps", [128, 1024], BF16)
    return Z


def alloc_overlays(Z):
    ZF = {}
    x, o = Z["x_sb"], Z["o_sb"]
    ZF["buf_fm"] = x[:, 0:16384]
    ZF["h_sb"] = x[:, 16384:32768]
    ZF["yfm_sb"] = o[:, 0:4096]
    ZF["bt_sb"] = [o[:, 4096:5120], o[:, 5120:6144]]
    ZF["yt_sb"] = [o[:, 6144:7168], o[:, 7168:8192]]
    ZF["w1_sb"] = [o[:, 8192:9216], o[:, 9216:10240]]
    ZF["w2_sb"] = [o[:, 10240:14336], o[:, 14336:18432]]
    k2, q2 = Z["k2_sb"], Z["q2_sb"]
    ZF["x2t2"] = [k2[:, 0:1024], k2[:, 1024:2048]]
    ZF["yb2"] = [k2[:, 2048:3072], k2[:, 3072:4096]]
    ZF["mt2"] = [q2[:, 0:8].bitcast(F32), q2[:, 8:16].bitcast(F32)]
    ZF["z_sb"] = Z["vaugA"][:, 0:1024]
    ZF["zc_sb"] = Z["vaugA"][:, 1024:2048]
    ZF["zsq_sb"] = Z["vaugB"][:, 0:1024]
    ZF["o2_sb"] = q2[:, 2048:4096].bitcast(F32)
    return ZF


def new_l1_state():
    return {
        "gg": 0,
        "vec_bc_free": 0,
        "vec_pvA_free": 0,
        "vec_pvB_free": 0,
        "wh_ready": [0, 0],
        "wh_freed": [0, 0],
    }


def emit_l1_init(nc, P, Z):
    v, s, t = nc.vector, nc.scalar, nc.tensor
    P.wait(v, "LS")
    P.inc(v.memset(Z["vaugA"][:, :], 1.0), "VEC")
    P.inc(v.memset(Z["vaugB"][:, :], 1.0), "VEC")
    P.inc(v.memset(Z["c5_sb"][:, :], 1e-5), "VEC")
    P.inc(v.memset(Z["c8_sb"][:, :], 1e-8), "VEC")
    P.wait(s, "VEC")
    P.wait(t, "VEC")


def emit_l1_batch(nc, P, Z, S, prefetch, x2tm_ap, meta_ap, cnt_col, lx_ready,
                  fsrc_ap, bchk):
    g, sy, t, v, s = nc.gpsimd, nc.sync, nc.tensor, nc.vector, nc.scalar
    st_ps, pvA_ps, pvB_ps = Z["st_ps"], Z["pvA_ps"], Z["pvB_ps"]
    bc_ps, trb_ps = Z["bc_ps"], Z["trb_ps"]
    x_sb, o_sb = Z["x_sb"], Z["o_sb"]
    k2_sb, q2_sb = Z["k2_sb"], Z["q2_sb"]
    expS = Z["expS"]
    ones_sb = Z["ones_sb"]

    for hp in range(8):
        bsel = hp % 2
        wh_sem = "WH0" if bsel == 0 else "WH1"
        prefetch(hp)
        t.wait_ge(P.sem[wh_sem], S["wh_ready"][bsel])
        P.wait(t, "LX", lx_ready)
        P.wait(t, "LS")
        for mi, dst_sb in ((0, k2_sb), (1, q2_sb)):
            for tt in range(NTT):
                pp = bc_ps if tt % 2 == 0 else pvA_ps
                P.wait(t, "VEC",
                       S["vec_bc_free"] if tt % 2 == 0 else S["vec_pvA_free"])
                for c in range(NCH):
                    mm = t.matmul(pp[:, :],
                                  Z["wh_sb"][bsel][:, D * mi + 128 * c:
                                                   D * mi + 128 * (c + 1)],
                                  x_sb[:, T * c + 512 * tt: T * c + 512 * (tt + 1)],
                                  start=(c == 0), stop=(c == NCH - 1))
                P.inc(mm, "PE")
                P.wait(v, "PE")
                vi = P.inc(v.tensor_copy(dst_sb[:, 512 * tt:512 * (tt + 1)],
                                         pp[:, :]), "VEC")
                if tt % 2 == 0:
                    S["vec_bc_free"] = vi
                else:
                    S["vec_pvA_free"] = vi
        for tt in range(NTT):
            pp = bc_ps if tt % 2 == 0 else pvA_ps
            P.wait(t, "VEC",
                   S["vec_bc_free"] if tt % 2 == 0 else S["vec_pvA_free"])
            for c in range(NCH):
                mm = t.matmul(pp[:, :],
                              Z["wh_sb"][bsel][:, 2 * D + 128 * c:
                                               2 * D + 128 * (c + 1)],
                              x_sb[:, T * c + 512 * tt: T * c + 512 * (tt + 1)],
                              start=(c == 0), stop=(c == NCH - 1))
            P.inc(mm, "PE")
            P.wait(v, "PE")
            vi = P.inc(v.tensor_copy(Z["v2tmp"][:, :], pp[:, :]), "VEC")
            if tt % 2 == 0:
                S["vec_bc_free"] = vi
            else:
                S["vec_pvA_free"] = vi
            for j in range(4):
                sc = 4 * tt + j
                P.wait(t, "VEC")
                mm = t.transpose(trb_ps[0:128, 0:128],
                                 Z["v2tmp"][:, 128 * j:128 * (j + 1)],
                                 Z["idbf_sb"][:, :])
                P.inc(mm, "PE")
                P.wait(v, "PE")
                P.inc(v.tensor_copy(Z["vaugA"][:, 65 * sc:65 * sc + 64],
                                    trb_ps[0:128, 0:64]), "VEC")
                P.inc(v.tensor_copy(Z["vaugB"][:, 65 * sc:65 * sc + 64],
                                    trb_ps[0:128, 64:128]), "VEC")
        S["wh_freed"][bsel] = P.c["PE"]
        vec_vaug = P.c["VEC"]

        for tt in range(NTT):
            g0 = S["gg"]
            tsl = slice(512 * tt, 512 * (tt + 1))
            for lg in range(32):
                cgg = g0 + lg
                if cgg >= 2:
                    P.wait(t, "SEXP", cgg - 1)
                if lg == 0:
                    P.wait(t, "VEC", vec_vaug)
                mm = t.matmul(st_ps[cgg % 2][:, 0:512],
                              k2_sb[0:64, 128 * lg:128 * (lg + 1)],
                              q2_sb[0:64, tsl], start=True, stop=True,
                              tile_position=(0, 0))
                mm = t.matmul(st_ps[cgg % 2][:, 512:1024],
                              k2_sb[64:128, 128 * lg:128 * (lg + 1)],
                              q2_sb[64:128, tsl], start=True, stop=True,
                              tile_position=(64, 0))
                P.inc(mm, "SST")
                if lg >= 1:
                    P.wait(t, "SEXP", cgg)
                    pvg = lg - 1
                    if pvg == 0:
                        P.wait(t, "VEC", S["vec_pvA_free"])
                        P.wait(t, "VEC", S["vec_pvB_free"])
                    mm = t.matmul(pvA_ps[0:65, :],
                                  Z["vaugA"][:, 65 * pvg:65 * pvg + 65],
                                  expS[(g0 + pvg) % 2][:, 0:512],
                                  start=(pvg == 0), stop=False,
                                  skip_group_check=True)
                    mm = t.matmul(pvB_ps[0:65, :],
                                  Z["vaugB"][:, 65 * pvg:65 * pvg + 65],
                                  expS[(g0 + pvg) % 2][:, 512:1024],
                                  start=(pvg == 0), stop=False,
                                  skip_group_check=True)
                    P.inc(mm, "SPV")
            P.wait(t, "SEXP", g0 + 32)
            mm = t.matmul(pvA_ps[0:65, :], Z["vaugA"][:, 65 * 31:65 * 31 + 65],
                          expS[(g0 + 31) % 2][:, 0:512],
                          start=False, stop=True, skip_group_check=True)
            mm = t.matmul(pvB_ps[0:65, :], Z["vaugB"][:, 65 * 31:65 * 31 + 65],
                          expS[(g0 + 31) % 2][:, 512:1024],
                          start=False, stop=True, skip_group_check=True)
            P.inc(mm, "SPV")
            for lg in range(32):
                cgg = g0 + lg
                P.wait(s, "SST", cgg + 1)
                if cgg >= 2:
                    P.wait(s, "SPV", cgg - 1)
                si = s.activation(expS[cgg % 2][:, :], st_ps[cgg % 2][:, :],
                                  AF.Exp, bias=0.0, scale=0.125)
                P.inc(si, "SEXP")
            S["gg"] = g0 + 32
            for head, pv in ((0, pvA_ps), (1, pvB_ps)):
                bcx = Z["bc_sb"] if head == 0 else Z["rbc_sb"]
                dr = slice(0, 1) if head == 0 else slice(64, 65)
                po = head * 64
                ocols = slice(T * hp + 512 * tt, T * hp + 512 * (tt + 1))
                P.wait(v, "SPV", S["gg"])
                with nc.allow_low_precision(reason="bf16 stage for matmul"):
                    den_cast = P.inc(v.tensor_copy(Z["rden_bf"][dr, :],
                                                   pv[64:65, :]), "VEC")
                with nc.allow_low_precision(reason="unnormalised bf16 stage"):
                    vi = P.inc(v.tensor_copy(o_sb[po:po + 64, ocols],
                                             pv[0:64, :]), "VEC")
                if head == 0:
                    S["vec_pvA_free"] = vi
                else:
                    S["vec_pvB_free"] = vi
                P.wait(t, "VEC", den_cast)
                P.wait(t, "VEC", S["vec_bc_free"])
                mm = t.matmul(bc_ps[:, :], ones_sb[dr, 0:128],
                              Z["rden_bf"][dr, :], start=True, stop=True,
                              skip_group_check=True)
                P.inc(mm, "PE")
                P.wait(v, "PE")
                P.wait(v, "VEC")
                P.inc(v.tensor_scalar(bcx[:, :], bc_ps[:, :],
                                      1e-8, None, ALU.add), "VEC")
                S["vec_bc_free"] = P.c["VEC"]
                P.wait(v, "VEC")
                P.inc(v.reciprocal(bcx[:, :], bcx[:, :]), "VEC")
                P.wait(v, "VEC")
                P.inc(v.tensor_tensor(o_sb[po:po + 64, ocols],
                                      o_sb[po:po + 64, ocols],
                                      bcx[po:po + 64, :], ALU.mult), "VEC")

    P.wait(t, "SEXP", S["gg"])
    P.wait(t, "VEC", S["vec_pvA_free"])
    P.wait(t, "VEC", S["vec_pvB_free"])
    for tt in range(NTT):
        for dc in range(NCH):
            wv = st_ps[0][:, 0:512] if dc % 2 == 0 else st_ps[0][:, 512:1024]
            P.wait(t, "VEC")
            for r in range(NCH):
                mm = t.matmul(wv,
                              Z["wo_sb"][:, D * r + 128 * dc: D * r + 128 * (dc + 1)],
                              o_sb[:, T * r + 512 * tt: T * r + 512 * (tt + 1)],
                              start=(r == 0), stop=(r == NCH - 1),
                              skip_group_check=True)
            P.inc(mm, "PE")
            P.wait(v, "PE")
            xv = x_sb[:, T * dc + 512 * tt: T * dc + 512 * (tt + 1)]
            P.inc(v.scalar_tensor_tensor(xv, wv, Z["boc_sb"][:, dc:dc + 1], xv,
                                         ALU.add, ALU.add), "VEC")
            P.wait(t, "VEC")
            mm = t.matmul(pvA_ps[0:1, :], ones_sb[:, 0:1], xv,
                          start=(dc == 0), stop=(dc == NCH - 1),
                          skip_group_check=True)
            P.inc(mm, "PE")
            P.wait(s, "VEC")
            P.inc(s.activation(Z["ysq_sb"][:, :], xv, AF.Square), "ACT")
            P.wait(t, "ACT")
            mm = t.matmul(pvA_ps[32:33, :], ones_sb[:, 0:1], Z["ysq_sb"][:, :],
                          start=(dc == 0), stop=(dc == NCH - 1),
                          skip_group_check=True)
            P.inc(mm, "PE")
        P.wait(v, "PE")
        P.inc(v.tensor_scalar(Z["mean_sb"][0:1, :], pvA_ps[0:1, :],
                              1.0 / D, None, ALU.mult), "VEC")
        P.inc(v.tensor_scalar(Z["msq_sb"][:, :], pvA_ps[32:33, :],
                              1.0 / D, None, ALU.mult), "VEC")
        P.wait(v, "VEC")
        P.inc(v.tensor_tensor(Z["var_sb"][0:1, :], Z["mean_sb"][0:1, :],
                              Z["mean_sb"][0:1, :], ALU.mult), "VEC")
        P.wait(v, "VEC")
        P.inc(v.tensor_sub(Z["var_sb"][0:1, :], Z["msq_sb"][:, :],
                           Z["var_sb"][0:1, :]), "VEC")
        P.wait(s, "VEC")
        P.inc(s.activation(Z["var_sb"][0:1, :], Z["var_sb"][0:1, :],
                           AF.Sqrt, bias=Z["c5_sb"][0:1, 0:1], scale=1.0),
              "ACT")
        P.wait(v, "ACT")
        P.inc(v.reciprocal(Z["var_sb"][0:1, :], Z["var_sb"][0:1, :]), "VEC")
        P.wait(v, "VEC")
        with nc.allow_low_precision(reason="bf16 broadcast operands"):
            P.inc(v.tensor_copy(Z["mean_bf"][0:1, :], Z["mean_sb"][0:1, :]),
                  "VEC")
            P.inc(v.tensor_copy(Z["rstd_bf"][0:1, :], Z["var_sb"][0:1, :]),
                  "VEC")
        P.wait(t, "VEC")
        t.matmul(st_ps[1][:, 0:512], ones_sb[0:1, 0:128], Z["mean_bf"][0:1, :],
                 start=True, stop=True, skip_group_check=True)
        mm = t.matmul(st_ps[1][:, 512:1024], ones_sb[0:1, 0:128],
                      Z["rstd_bf"][0:1, :], start=True, stop=True,
                      skip_group_check=True)
        P.inc(mm, "PE")
        P.wait(v, "PE")
        P.inc(v.tensor_copy(Z["mbc_sb"][:, :], st_ps[1][:, 0:512]), "VEC")
        P.inc(v.tensor_copy(Z["rbc_sb"][:, :], st_ps[1][:, 512:1024]), "VEC")
        P.wait(v, "VEC")
        for dc in range(NCH):
            xv = x_sb[:, T * dc + 512 * tt: T * dc + 512 * (tt + 1)]
            P.inc(v.tensor_sub(xv, xv, Z["mbc_sb"][:, :]), "VEC")
            P.wait(v, "VEC")
            P.inc(v.tensor_tensor(xv, xv, Z["rbc_sb"][:, :], ALU.mult), "VEC")
            P.wait(v, "VEC")
    vec_x2 = P.c["VEC"]

    for tt in range(NTT):
        P.wait(t, "VEC", vec_x2)
        for c in range(NCH):
            mm = t.matmul(st_ps[0][0:16, 512:1024],
                          Z["wg_sb"][:, 16 * c:16 * (c + 1)],
                          x_sb[:, T * c + 512 * tt: T * c + 512 * (tt + 1)],
                          start=(c == 0), stop=(c == NCH - 1),
                          skip_group_check=True)
        P.inc(mm, "PE")
        P.wait(s, "PE")
        P.wait(s, "VEC")
        si = s.activation(Z["lg_tt"][0:16, :], st_ps[0][0:16, 512:1024],
                          AF.Identity, bias=Z["bgc_sb"][0:16, 0:1], scale=1.0)
        P.inc(si, "ACT")
        P.wait(t, "ACT")
        for j in range(4):
            i = 4 * tt + j
            P.wait(t, "VEC")
            mm = t.transpose(pvB_ps[0:128, 0:16],
                             Z["lg_tt"][0:16, 128 * j:128 * (j + 1)],
                             Z["id16_sb"][:, :])
            P.inc(mm, "PE")
            P.wait(v, "PE")
            P.inc(v.tensor_copy(Z["lg_tm"][:, 16 * i:16 * (i + 1)],
                                pvB_ps[0:128, 0:16]), "VEC")
        P.wait(t, "VEC")
    for i in range(NST):
        ltile = Z["lg_tm"][:, 16 * i:16 * (i + 1)]
        P.inc(v.reduce_max(Z["mx_sb"][:, :], ltile, axis=AX), "VEC")
        P.inc(v.reduce_max(Z["nmx_sb"][:, :], ltile, axis=AX, negate=True),
              "VEC")
        P.wait(s, "VEC")
        si = s.activation(Z["ejunk_sb"][:, :], ltile, AF.Exp,
                          bias=Z["nmx_sb"][:, 0:1], scale=1.0,
                          accum_out=Z["sm_sb"][:, :])
        P.inc(si, "ACT")
        P.wait(v, "ACT")
        P.inc(v.reciprocal(Z["meta_sb"][:, 4 * i + 2:4 * i + 3],
                           Z["sm_sb"][:, :]), "VEC")
        P.inc(v.tensor_scalar(Z["eq_sb"][:, :], ltile, Z["mx_sb"][:, 0:1],
                              None, ALU.is_equal), "VEC")
        P.wait(v, "VEC")
        P.inc(v.tensor_tensor(Z["cand_sb"][:, :], Z["r16m_sb"][:, :],
                              Z["eq_sb"][:, :], ALU.mult), "VEC")
        P.wait(v, "VEC")
        P.inc(v.reduce_max(Z["t1_sb"][:, :], Z["cand_sb"][:, :], axis=AX),
              "VEC")
        P.wait(v, "VEC")
        P.inc(v.tensor_scalar(Z["idx_all"][:, i:i + 1], Z["t1_sb"][:, :],
                              -1.0, 16.0, ALU.mult, ALU.add), "VEC")
        P.wait(v, "VEC")
        P.inc(v.tensor_copy(Z["meta_sb"][:, 4 * i:4 * i + 1],
                            Z["idx_all"][:, i:i + 1]), "VEC")
        P.inc(v.tensor_scalar(Z["ohf_all"][:, 16 * i:16 * (i + 1)],
                              Z["iota_sb"][:, :], Z["idx_all"][:, i:i + 1],
                              None, ALU.is_equal), "VEC")
        with nc.allow_low_precision(reason="onehot bf16"):
            P.inc(v.tensor_scalar(Z["ohb_sb"][:, :], Z["iota_sb"][:, :],
                                  Z["idx_all"][:, i:i + 1], None,
                                  ALU.is_equal), "VEC")
        P.wait(t, "VEC")
        mm = t.transpose(trb_ps[0:16, 0:128], Z["ohb_sb"][:, :],
                         Z["idbf_sb"][:, :])
        P.inc(mm, "PE")
        P.wait(v, "PE")
        P.inc(v.tensor_copy(Z["ohfm_sb"][0:16, 128 * i:128 * (i + 1)],
                            trb_ps[0:16, 0:128]), "VEC")
        P.wait(v, "VEC")
    P.wait(v, "VEC")
    P.inc(v.tensor_tensor_scan(Z["csum_sb"][:, :], Z["ohfm_sb"][:, :],
                               Z["ohfm_sb"][:, :], 0.0, ALU.add, ALU.bypass),
          "VEC")
    P.wait(v, "VEC")
    P.inc(v.tensor_copy(Z["cnt_sb"][:, cnt_col:cnt_col + 1],
                        Z["csum_sb"][:, T - 1:T]), "VEC")
    for i in range(NST):
        P.wait(t, "VEC")
        mm = t.transpose(pvB_ps[0:128, 0:16],
                         Z["csum_sb"][0:16, 128 * i:128 * (i + 1)],
                         Z["id16_sb"][:, :])
        P.inc(mm, "PE")
        P.wait(v, "PE")
        P.inc(v.tensor_tensor(Z["sel_sb"][:, :], pvB_ps[0:128, 0:16],
                              Z["ohf_all"][:, 16 * i:16 * (i + 1)], ALU.mult),
              "VEC")
        P.wait(v, "VEC")
        P.inc(v.reduce_sum(Z["cs_sb"][:, :], Z["sel_sb"][:, :], axis=AX),
              "VEC")
        P.wait(v, "VEC")
        P.inc(v.tensor_scalar(Z["meta_sb"][:, 4 * i + 1:4 * i + 2],
                              Z["cs_sb"][:, :], -1.0, None, ALU.add), "VEC")
        P.inc(v.memset(Z["meta_sb"][:, 4 * i + 3:4 * i + 4], 0.0), "VEC")
        P.wait(v, "VEC")
        P.inc(v.tensor_scalar(Z["fif"][:, i:i + 1], Z["idx_all"][:, i:i + 1],
                              float(LCAP), None, ALU.mult), "VEC")
        P.inc(v.tensor_scalar(Z["ovg_sb"][:, :],
                              Z["meta_sb"][:, 4 * i + 1:4 * i + 2],
                              float(LCAP), BIGIX, ALU.is_ge, ALU.mult), "VEC")
        P.wait(v, "VEC")
        P.inc(v.tensor_tensor(Z["fif"][:, i:i + 1], Z["fif"][:, i:i + 1],
                              Z["meta_sb"][:, 4 * i + 1:4 * i + 2], ALU.add),
              "VEC")
        P.wait(v, "VEC")
        P.inc(v.tensor_tensor(Z["fif"][:, i:i + 1], Z["fif"][:, i:i + 1],
                              Z["ovg_sb"][:, :], ALU.add), "VEC")
        P.wait(v, "VEC")
        P.inc(v.tensor_copy(Z["fiu"][:, i:i + 1], Z["fif"][:, i:i + 1]),
              "VEC")
        P.wait(v, "VEC")
    P.wait(sy, "VEC")
    P.inc(sy.dma_start(out=meta_ap.rearrange("(i p) c -> p i c", p=128),
                       in_=Z["meta_sb"].rearrange("p (i c) -> p i c", c=4)),
          "ST", 16)
    for i in range(NST):
        b = i % 2
        sxs = "SX0" if b == 0 else "SX1"
        buf = Z["x2st"][b]
        first = True
        for dc in range(NCH):
            P.wait(t, "VEC")
            mm = t.transpose(trb_ps[0:128, 0:128],
                             x_sb[:, T * dc + 128 * i: T * dc + 128 * (i + 1)],
                             Z["idbf_sb"][:, :])
            P.inc(mm, "PE")
            P.wait(v, "PE")
            if first:
                P.wait(v, sxs)
            first = False
            P.inc(v.tensor_copy(buf[:, 128 * dc:128 * (dc + 1)],
                                trb_ps[0:128, 0:128]), "VEC")
        P.wait(sy, "VEC")
        P.inc(sy.dma_start(out=x2tm_ap[128 * i:128 * (i + 1), :], in_=buf[:, :]),
              sxs, 16)
        P.wait(g, "VEC")
        P.inc(g.indirect_dma_start(
            out=fsrc_ap,
            out_offset=IndirectOffsetOnAxis(Z["fiu"][:, i:i + 1], 0),
            in_=buf[:, :], in_offset=None,
            bounds_check=bchk, oob_is_err=False), sxs, 16)


def emit_l1_const_loads(nc, P, Z, aps):
    g, sy = nc.gpsimd, nc.sync
    for c in range(NCH):
        P.inc(g.dma_start(out=Z["wo_sb"][:, D * c: D * (c + 1)],
                          in_=aps["woT"][128 * c:128 * (c + 1), :]), "LX", 16)
    P.inc(g.dma_start(out=Z["wg_sb"].ap().rearrange("p (c e) -> p c e", c=NCH),
                      in_=aps["wg"].rearrange("(c p) e -> p c e", p=128)),
          "LX", 16)
    for dk, sk in (("boc_sb", "boc"), ("bgc_sb", "bgc"), ("idbf_sb", "idbf"),
                   ("id16_sb", "id16"), ("iota_sb", "iota16b"),
                   ("r16m_sb", "r16m"), ("ones_sb", "onesb"),
                   ("onesf_sb", "onesf"), ("tri8_sb", "tri8"),
                   ("eh01_sb", "eh01"), ("pmask_sb", "pmask"),
                   ("iota128_sb", "iota128"), ("b1c_sb", "b1c"),
                   ("b2c_sb", "b2c")):
        P.inc(sy.dma_start(out=Z[dk][:, :], in_=aps[sk][:, :]), "LS", 16)


def make_headpair_prefetch(nc, P, Z, S, wq_ap, wk_ap, wv_ap, npasses=1):
    g = nc.gpsimd

    def load_pair(hp, bsel):
        sem = "WH0" if bsel == 0 else "WH1"
        P.wait(g, "PE", S["wh_freed"][bsel])
        for mi, wsrc in enumerate((wk_ap, wq_ap, wv_ap)):
            ins = g.dma_start(
                out=Z["wh_sb"][bsel][:, D * mi:D * (mi + 1)].rearrange(
                    "p (c f) -> p c f", c=NCH),
                in_=wsrc[:, 128 * hp:128 * (hp + 1)].rearrange(
                    "(c p) f -> p c f", p=128))
            P.inc(ins, sem, 16)
        S["wh_ready"][bsel] = P.c[sem]

    seq = {"next": 1}

    def prefetch(_hp):
        nxt = seq["next"]
        if nxt < 8 * npasses:
            load_pair(nxt % 8, nxt % 2)
            seq["next"] = nxt + 1

    load_pair(0, 0)
    return prefetch



def emit_prolog_b(nc, P, Z, cnts1_ap, cnts8_ap):
    g, sy, t, v, s = nc.gpsimd, nc.sync, nc.tensor, nc.vector, nc.scalar
    bc_ps = Z["bc_ps"]
    P.wait(sy, "VEC")
    P.inc(sy.dma_start(out=cnts1_ap.rearrange("one e -> e one"),
                       in_=Z["cnt_sb"][:, 0:1]), "CN", 16)
    P.wait(g, "CN")
    cc = g.collective_compute(
        "AllGather", ALU.bypass, replica_groups=[list(range(NC))],
        ins=[cnts1_ap.opt()], outs=[cnts8_ap.opt()])
    P.inc(cc, "CCA", 1)
    P.wait(g, "CCA")
    P.inc(g.dma_start(out=Z["cnts8_sb"][:, :], in_=cnts8_ap), "CN", 16)
    pe0, vec0 = P.c["PE"], P.c["VEC"]
    P.wait(t, "CN")
    P.wait(t, "VEC", vec0)
    mm = t.matmul(bc_ps[0:16, 0:8], Z["cnts8_sb"][:, :], Z["tri8_sb"][:, :],
                  start=True, stop=True, skip_group_check=True)
    P.inc(mm, "PE")
    P.wait(v, "PE")
    P.inc(v.tensor_copy(Z["baseT_sb"][:, :], bc_ps[0:16, 0:8]), "VEC")
    P.wait(t, "VEC")
    mm = t.matmul(bc_ps[0:1, 16:32], Z["pmask_sb"][:, :], Z["cnts8_sb"][:, :],
                  start=True, stop=True, skip_group_check=True)
    P.inc(mm, "PE")
    P.wait(v, "PE")
    P.inc(v.tensor_copy(Z["gbrow_sb"][:, :], bc_ps[0:1, 16:32]), "VEC")
    P.wait(t, "VEC")
    for k in range(2):
        mm = t.matmul(bc_ps[0:1, 32 + 8 * k:40 + 8 * k],
                      Z["eh01_sb"][:, k:k + 1], Z["baseT_sb"][:, :],
                      start=True, stop=True, skip_group_check=True)
        P.inc(mm, "PE")
    P.wait(v, "PE")
    P.inc(v.tensor_copy(Z["rowsel_sb"][:, 0:8], bc_ps[0:1, 32:40]), "VEC")
    P.inc(v.tensor_copy(Z["rowsel_sb"][:, 8:16], bc_ps[0:1, 40:48]), "VEC")
    P.wait(t, "VEC")
    mm = t.matmul(bc_ps[:, 48:64], Z["onesf_sb"][0:1, :],
                  Z["rowsel_sb"][0:1, :], start=True, stop=True,
                  skip_group_check=True)
    P.inc(mm, "PE")
    mm = t.matmul(bc_ps[:, 64:80], Z["onesf_sb"][0:1, :],
                  Z["gbrow_sb"][0:1, :], start=True, stop=True,
                  skip_group_check=True)
    P.inc(mm, "PE")
    P.wait(v, "PE")
    P.inc(v.tensor_copy(Z["bbc_sb"][:, :], bc_ps[:, 48:64]), "VEC")
    P.inc(v.tensor_copy(Z["gbb_sb"][:, :], bc_ps[:, 64:80]), "VEC")
    P.wait(v, "VEC")
    P.wait(v, "PE")
    k2 = Z["k2_sb"]
    pc_all = k2[:, 0:64].bitcast(F32)
    cmp_all = k2[:, 64:576].bitcast(F32)
    nle_all = k2[:, 576:640].bitcast(F32)
    rr_all = k2[:, 640:704].bitcast(F32)
    ovg_all = k2[:, 704:768].bitcast(F32)
    cols = [(ep * 16 + i, ep, i) for ep in range(2) for i in range(CAP // 128)]
    for col, ep, i in cols:
        P.inc(v.tensor_scalar(pc_all[:, col:col + 1], Z["iota128_sb"][:, :],
                              float(128 * i), None, ALU.add), "VEC")
    P.wait(v, "VEC")
    for col, ep, i in cols:
        P.inc(v.tensor_scalar(cmp_all[:, 8 * col:8 * col + 8],
                              Z["bbc_sb"][:, 8 * ep:8 * ep + 8],
                              pc_all[:, col:col + 1], None, ALU.is_le), "VEC")
    P.wait(v, "VEC")
    for col, ep, i in cols:
        P.inc(v.reduce_sum(nle_all[:, col:col + 1],
                           cmp_all[:, 8 * col:8 * col + 8], axis=AX), "VEC")
    P.wait(v, "VEC")
    for col, ep, i in cols:
        P.inc(v.tensor_tensor(cmp_all[:, 8 * col:8 * col + 8],
                              cmp_all[:, 8 * col:8 * col + 8],
                              Z["bbc_sb"][:, 8 * ep:8 * ep + 8], ALU.mult),
              "VEC")
    P.wait(v, "VEC")
    for col, ep, i in cols:
        P.inc(v.reduce_max(rr_all[:, col:col + 1],
                           cmp_all[:, 8 * col:8 * col + 8], axis=AX), "VEC")
    P.wait(v, "VEC")
    for col, ep, i in cols:
        P.inc(v.tensor_sub(rr_all[:, col:col + 1], pc_all[:, col:col + 1],
                           rr_all[:, col:col + 1]), "VEC")
    P.wait(v, "VEC")
    for col, ep, i in cols:
        P.inc(v.tensor_scalar(ovg_all[:, col:col + 1], rr_all[:, col:col + 1],
                              float(LCAP), BIGIX, ALU.is_ge, ALU.mult), "VEC")
        P.inc(v.tensor_scalar(Z["gif"][:, col:col + 1], nle_all[:, col:col + 1],
                              2048.0, float(1024 * ep - 2048), ALU.mult,
                              ALU.add), "VEC")
    P.wait(v, "VEC")
    for col, ep, i in cols:
        P.inc(v.tensor_tensor(Z["gif"][:, col:col + 1],
                              Z["gif"][:, col:col + 1], rr_all[:, col:col + 1],
                              ALU.add), "VEC")
    P.wait(v, "VEC")
    for col, ep, i in cols:
        P.inc(v.tensor_tensor(Z["gif"][:, col:col + 1],
                              Z["gif"][:, col:col + 1], ovg_all[:, col:col + 1],
                              ALU.add), "VEC")
    P.wait(v, "VEC")
    for col, ep, i in cols:
        P.inc(v.tensor_copy(Z["giu"][:, col:col + 1],
                            Z["gif"][:, col:col + 1]), "VEC")
    P.wait(v, "VEC")



def make_ffn_loaders(nc, P, ZF, w1_ap, w2_ap):
    g = nc.gpsimd
    w1_sb, w2_sb = ZF["w1_sb"], ZF["w2_sb"]
    FS = {"w1_ready": [0, 0], "w1_freed": [0, 0],
          "w2_ready": [0, 0], "w2_freed": [0, 0]}

    def load_w1(bsel, e, hc):
        P.wait(g, "PE", FS["w1_freed"][bsel])
        ins = g.dma_start(
            out=w1_sb[bsel].rearrange("p (c f) -> p c f", c=NCH),
            in_=w1_ap[e, :, 128 * hc:128 * (hc + 1)].rearrange(
                "(c p) f -> p c f", p=128))
        FS["w1_ready"][bsel] = P.inc(ins, "W10" if bsel == 0 else "W11", 16)

    def load_w2(bsel, e, dc):
        P.wait(g, "PE", FS["w2_freed"][bsel])
        ins = g.dma_start(
            out=w2_sb[bsel].rearrange("p (c f) -> p c f", c=32),
            in_=w2_ap[e, :, 128 * dc:128 * (dc + 1)].rearrange(
                "(c p) f -> p c f", p=128))
        FS["w2_ready"][bsel] = P.inc(ins, "W20" if bsel == 0 else "W21", 16)

    return FS, load_w1, load_w2


def emit_ffn(nc, P, Z, ZF, loaders, fdst_ap, rsrc_ap, giu_done, bchk):
    g, sy, t, v, s = nc.gpsimd, nc.sync, nc.tensor, nc.vector, nc.scalar
    buf_fm = ZF["buf_fm"]
    bt_sb, yt_sb = ZF["bt_sb"], ZF["yt_sb"]
    w1_sb, w2_sb = ZF["w1_sb"], ZF["w2_sb"]
    h_sb, yfm_sb = ZF["h_sb"], ZF["yfm_sb"]
    h_ps, y_ps, tr_ps = Z["pvA_ps"], Z["pvB_ps"], Z["trb_ps"]
    b1_sb, b2_sb, idbf_sb = Z["b1c_sb"], Z["b2c_sb"], Z["idbf_sb"]
    FS, load_w1, load_w2 = loaders
    w1_ready, w1_freed = FS["w1_ready"], FS["w1_freed"]
    w2_ready, w2_freed = FS["w2_ready"], FS["w2_freed"]
    tr_after = {}
    vec_strip = {}

    for e in range(2):
        for i in range(CAP // 128):
            b = i % 2
            bts = "BT0" if b == 0 else "BT1"
            if i >= 2 or e > 0:
                P.wait(g, "PE", tr_after.get((e, i - 2), tr_after.get(
                    (e - 1, CAP // 128 + i - 2), 0)))
            P.wait(g, "VEC", giu_done)
            col = e * 16 + i
            P.inc(g.indirect_dma_start(
                out=bt_sb[b][:, :], out_offset=None,
                in_=fdst_ap,
                in_offset=IndirectOffsetOnAxis(Z["giu"][:, col:col + 1], 0),
                bounds_check=bchk, oob_is_err=False), bts, 16)
            P.wait(t, bts)
            P.wait(t, "LS", P.c["LS"])
            P.wait(t, "VEC")
            for dc in range(NCH):
                P.wait(t, "VEC")
                mm = t.transpose(tr_ps[0:128, 0:128],
                                 bt_sb[b][:, 128 * dc:128 * (dc + 1)],
                                 idbf_sb[:, :])
                P.inc(mm, "PE")
                P.wait(v, "PE")
                P.inc(v.tensor_copy(
                    buf_fm[:, CAP * dc + 128 * i:CAP * dc + 128 * (i + 1)],
                    tr_ps[0:128, 0:128]), "VEC")
            tr_after[(e, i)] = P.c["PE"]
            if i % 4 == 3:
                vec_strip[(e, i // 4)] = P.c["VEC"]
        if e > 0:
            load_w1(0, e, 0)
        for st in range(4):
            act_of = {}
            P.wait(t, "ACT")
            for hc in range(32):
                bsel = hc % 2
                if hc + 1 < 32:
                    load_w1((hc + 1) % 2, e, hc + 1)
                elif st + 1 < 4:
                    load_w1((hc + 1) % 2, e, 0)
                t.wait_ge(P.sem["W10" if bsel == 0 else "W11"], w1_ready[bsel])
                P.wait(t, "VEC", vec_strip[(e, st)])
                hps = h_ps if hc % 2 == 0 else y_ps
                if hc >= 2:
                    P.wait(t, "ACT", act_of.get(hc - 2, 0))
                for c in range(NCH):
                    mm = t.matmul(hps[:, :],
                                  w1_sb[bsel][:, 128 * c:128 * (c + 1)],
                                  buf_fm[:, CAP * c + 512 * st:
                                         CAP * c + 512 * (st + 1)],
                                  start=(c == 0), stop=(c == NCH - 1),
                                  skip_group_check=True)
                P.inc(mm, "PE")
                w1_freed[bsel] = P.c["PE"]
                P.wait(s, "PE")
                si = s.activation(h_sb[:, 512 * hc:512 * (hc + 1)], hps[:, :],
                                  AF.Relu,
                                  bias=b1_sb[:, 32 * e + hc:32 * e + hc + 1],
                                  scale=1.0)
                act_of[hc] = P.inc(si, "ACT")
            load_w2(0, e, 0)
            yact = {}
            for dc in range(NCH):
                bsel = dc % 2
                if dc + 1 < NCH:
                    load_w2((dc + 1) % 2, e, dc + 1)
                t.wait_ge(P.sem["W20" if bsel == 0 else "W21"], w2_ready[bsel])
                yps = h_ps if dc % 2 == 0 else y_ps
                if dc < 2:
                    P.wait(t, "ACT", act_of[31])
                else:
                    P.wait(t, "ACT", yact[dc - 2])
                for hc in range(32):
                    mm = t.matmul(yps[:, :],
                                  w2_sb[bsel][:, 128 * hc:128 * (hc + 1)],
                                  h_sb[:, 512 * hc:512 * (hc + 1)],
                                  start=(hc == 0), stop=(hc == 31),
                                  skip_group_check=True)
                P.inc(mm, "PE")
                w2_freed[bsel] = P.c["PE"]
                P.wait(s, "PE")
                si = s.activation(yfm_sb[:, 512 * dc:512 * (dc + 1)], yps[:, :],
                                  AF.Identity,
                                  bias=b2_sb[:, 8 * e + dc:8 * e + dc + 1],
                                  scale=1.0)
                yact[dc] = P.inc(si, "ACT")
            P.wait(t, "ACT")
            for j in range(4):
                jb = j % 2
                sys_ = "SY0" if jb == 0 else "SY1"
                buf = yt_sb[jb]
                first = True
                for dc in range(NCH):
                    P.wait(t, "VEC")
                    mm = t.transpose(tr_ps[0:128, 0:128],
                                     yfm_sb[:, 512 * dc + 128 * j:
                                            512 * dc + 128 * (j + 1)],
                                     idbf_sb[:, :])
                    P.inc(mm, "PE")
                    P.wait(v, "PE")
                    if first:
                        P.wait(v, sys_)
                    first = False
                    P.inc(v.tensor_copy(buf[:, 128 * dc:128 * (dc + 1)],
                                        tr_ps[0:128, 0:128]), "VEC")
                P.wait(g, "VEC")
                col = e * 16 + 4 * st + j
                P.inc(g.indirect_dma_start(
                    out=rsrc_ap,
                    out_offset=IndirectOffsetOnAxis(Z["giu"][:, col:col + 1], 0),
                    in_=buf[:, :], in_offset=None,
                    bounds_check=bchk, oob_is_err=False), sys_, 16)



def emit_phase_c(nc, P, Z, ZF, x2tm_ap, meta_ap, rdst_ap, out_ap, bchk,
                 pe_attn):
    g, sy, t, v, s = nc.gpsimd, nc.sync, nc.tensor, nc.vector, nc.scalar
    ybig = Z["x_sb"]
    x2all = Z["o_sb"]
    q2, k2 = Z["q2_sb"], Z["k2_sb"]
    eqall = q2[:, 0:1024].bitcast(F32)
    selall = q2[:, 1024:2048].bitcast(F32)
    lnall = q2[:, 2048:2560].bitcast(F32)
    mt_all = q2[:, 2560:2816].bitcast(F32)
    o2b = [k2[:, 0:2048].bitcast(F32), k2[:, 2048:4096].bitcast(F32)]
    zb = [Z["vaugA"][:, 0:1024], Z["vaugA"][:, 1024:2048],
          Z["vaugB"][:, 0:1024], Z["vaugB"][:, 1024:2048]]
    zdump = Z["expS"]

    P.wait(sy, "SX0")
    P.wait(sy, "SX1")
    P.wait(sy, "SY0")
    P.wait(sy, "SY1")
    P.wait(sy, "ST")
    P.wait(sy, "PE")
    P.wait(sy, "VEC")
    P.wait(sy, "ACT")
    for i in range(0, 9):
        P.inc(sy.dma_start(out=x2all[:, 1024 * i:1024 * (i + 1)],
                           in_=x2tm_ap[128 * i:128 * (i + 1), :]), "LT0", 16)
    P.wait(v, "PE", pe_attn)
    P.wait(v, "LT0", 16)
    P.wait(v, "VEC")
    for i in range(NST):
        P.inc(v.tensor_scalar(eqall[:, 16 * i:16 * (i + 1)], Z["iota_sb"][:, :],
                              mt_all[:, 4 * i:4 * i + 1], None, ALU.is_equal),
              "VEC")
    P.wait(v, "VEC")
    for i in range(NST):
        P.inc(v.tensor_tensor(selall[:, 16 * i:16 * (i + 1)],
                              eqall[:, 16 * i:16 * (i + 1)], Z["gbb_sb"][:, :],
                              ALU.mult), "VEC")
    P.wait(v, "VEC")
    for i in range(NST):
        P.inc(v.reduce_sum(lnall[:, 8 * i + 6:8 * i + 7],
                           selall[:, 16 * i:16 * (i + 1)], axis=AX), "VEC")
    P.wait(v, "VEC")
    for i in range(NST):
        P.inc(v.tensor_tensor(lnall[:, 8 * i + 6:8 * i + 7],
                              lnall[:, 8 * i + 6:8 * i + 7],
                              mt_all[:, 4 * i + 1:4 * i + 2], ALU.add), "VEC")
    P.wait(v, "VEC")
    for i in range(NST):
        P.inc(v.tensor_scalar(lnall[:, 8 * i + 7:8 * i + 8],
                              lnall[:, 8 * i + 6:8 * i + 7], float(CAP),
                              BIGIX, ALU.is_ge, ALU.mult), "VEC")
    P.wait(v, "VEC")
    for i in range(NST):
        P.inc(v.tensor_tensor(Z["fif"][:, i:i + 1], Z["fif"][:, i:i + 1],
                              lnall[:, 8 * i + 7:8 * i + 8], ALU.add), "VEC")
    P.wait(v, "VEC")
    for i in range(NST):
        P.inc(v.tensor_copy(Z["fiu"][:, i:i + 1], Z["fif"][:, i:i + 1]),
              "VEC")
    P.inc(v.memset(ybig[:, :], 0.0), "VEC")
    P.wait(g, "PE")
    P.wait(g, "VEC")
    for i in range(9, 18):
        P.inc(g.dma_start(out=x2all[:, 1024 * i:1024 * (i + 1)],
                          in_=x2tm_ap[128 * i:128 * (i + 1), :]), "LT1", 16)
    P.wait(g, "CCR")
    for i in range(NST):
        P.inc(g.indirect_dma_start(
            out=ybig[:, 1024 * i:1024 * (i + 1)], out_offset=None,
            in_=rdst_ap,
            in_offset=IndirectOffsetOnAxis(Z["fiu"][:, i:i + 1], 0),
            bounds_check=bchk, oob_is_err=False), "GB0", 16)
    P.wait(v, "GB0")
    P.wait(v, "LT0")
    P.wait(v, "LT1")
    zv = {}
    sums = {}
    o2v = {}
    for i in range(NST):
        zbi = zb[i % 4]
        obi = o2b[i % 2]
        if i >= 4:
            P.wait(v, "VEC", o2v[i - 4])
        zv[i] = P.inc(v.scalar_tensor_tensor(
            zbi[:, :], ybig[:, 1024 * i:1024 * (i + 1)],
            mt_all[:, 4 * i + 2:4 * i + 3],
            x2all[:, 1024 * i:1024 * (i + 1)], ALU.mult, ALU.add), "VEC")
        s.wait_ge(P.sem["VEC"], zv[i])
        P.inc(s.activation(zdump[0][:, :], zbi[:, :], AF.Identity,
                           accum_out=lnall[:, 8 * i:8 * i + 1]), "ACT")
        sums[i] = P.inc(s.activation(zdump[1][:, :], zbi[:, :], AF.Square,
                                     accum_out=lnall[:, 8 * i + 1:8 * i + 2]),
                        "ACT")
        v.wait_ge(P.sem["ACT"], sums[i])
        P.inc(v.tensor_scalar(lnall[:, 8 * i + 2:8 * i + 3],
                              lnall[:, 8 * i:8 * i + 1], 1.0 / D, None,
                              ALU.mult), "VEC")
        P.wait(v, "VEC")
        P.inc(v.tensor_scalar(lnall[:, 8 * i + 4:8 * i + 5],
                              lnall[:, 8 * i + 2:8 * i + 3],
                              lnall[:, 8 * i + 2:8 * i + 3], None, ALU.mult),
              "VEC")
        P.wait(v, "VEC")
        vv = P.inc(v.tensor_scalar(lnall[:, 8 * i + 3:8 * i + 4],
                                   lnall[:, 8 * i + 1:8 * i + 2], 1.0 / D,
                                   lnall[:, 8 * i + 4:8 * i + 5], ALU.mult,
                                   ALU.subtract), "VEC")
        s.wait_ge(P.sem["VEC"], vv)
        sq = P.inc(s.activation(lnall[:, 8 * i + 5:8 * i + 6],
                                lnall[:, 8 * i + 3:8 * i + 4], AF.Sqrt,
                                bias=Z["c5_sb"][:, 0:1], scale=1.0), "ACT")
        v.wait_ge(P.sem["ACT"], sq)
        P.inc(v.reciprocal(lnall[:, 8 * i + 5:8 * i + 6],
                           lnall[:, 8 * i + 5:8 * i + 6]), "VEC")
        P.wait(v, "VEC")
        if i >= 2:
            P.wait(v, "SO", 16 * (i - 1))
        o2v[i] = P.inc(v.tensor_scalar(obi[:, :], zbi[:, :],
                                       lnall[:, 8 * i + 2:8 * i + 3],
                                       lnall[:, 8 * i + 5:8 * i + 6],
                                       ALU.subtract, ALU.mult), "VEC")
        sy.wait_ge(P.sem["VEC"], o2v[i])
        P.inc(sy.dma_start(out=out_ap[128 * i:128 * (i + 1), :],
                           in_=obi[:, :]), "SO", 16)
    P.wait(sy, "SO")



def build_fused():
    nc = bass.Bass("TRN2", target_bir_lowering=False, debug=False)
    xT = nc.dram_tensor("xT", [D, T], F32, kind="ExternalInput").ap()
    wqT = nc.dram_tensor("wqT", [D, D], F32, kind="ExternalInput").ap()
    wkT = nc.dram_tensor("wkT", [D, D], F32, kind="ExternalInput").ap()
    wvT = nc.dram_tensor("wvT", [D, D], F32, kind="ExternalInput").ap()
    woT = nc.dram_tensor("woT", [D, D], F32, kind="ExternalInput").ap()
    boc = nc.dram_tensor("boc", [128, NCH], F32, kind="ExternalInput").ap()
    wg = nc.dram_tensor("wg", [D, E], F32, kind="ExternalInput").ap()
    bgc = nc.dram_tensor("bgc", [E, 1], F32, kind="ExternalInput").ap()
    idbf = nc.dram_tensor("idbf", [128, 128], BF16, kind="ExternalInput").ap()
    id16 = nc.dram_tensor("id16", [16, 16], F32, kind="ExternalInput").ap()
    iota16b = nc.dram_tensor("iota16b", [128, 16], F32, kind="ExternalInput").ap()
    r16m = nc.dram_tensor("r16m", [128, 16], F32, kind="ExternalInput").ap()
    onesb = nc.dram_tensor("onesb", [128, 128], BF16, kind="ExternalInput").ap()
    onesf = nc.dram_tensor("onesf", [128, 128], F32, kind="ExternalInput").ap()
    tri8 = nc.dram_tensor("tri8", [8, 8], F32, kind="ExternalInput").ap()
    eh01 = nc.dram_tensor("eh01", [16, 2], F32, kind="ExternalInput").ap()
    pmask = nc.dram_tensor("pmask", [8, 1], F32, kind="ExternalInput").ap()
    iota128 = nc.dram_tensor("iota128", [128, 1], F32, kind="ExternalInput").ap()
    w1 = nc.dram_tensor("w1", [2, D, FH], F32, kind="ExternalInput").ap()
    w2 = nc.dram_tensor("w2", [2, FH, D], F32, kind="ExternalInput").ap()
    b1c = nc.dram_tensor("b1c", [128, 64], F32, kind="ExternalInput").ap()
    b2c = nc.dram_tensor("b2c", [128, 16], F32, kind="ExternalInput").ap()
    out = nc.dram_tensor("out", [T, D], F32, kind="ExternalOutput").ap()
    x2tm = nc.dram_tensor("x2tm", [T, D], BF16).ap()
    meta = nc.dram_tensor("meta", [T, 4], F32).ap()
    cnts1 = nc.dram_tensor("cnts1", [1, E], F32).ap()
    cnts8 = nc.dram_tensor("cnts8", [8, E], F32).ap()
    fsrc = nc.dram_tensor("fsrc", [NROW, D], BF16).ap()
    fdst = nc.dram_tensor("fdst", [NROW, D], BF16).ap()
    rsrc = nc.dram_tensor("rsrc", [NROW, D], BF16).ap()
    rdst = nc.dram_tensor("rdst", [NROW, D], BF16).ap()

    es = ExitStack()
    Z = alloc_l1(nc, es)
    ZF = alloc_overlays(Z)
    P = Prog(nc, es, SEMS)
    g, sy = nc.gpsimd, nc.sync
    bchk = nc.alloc_register(mybir.EngineType.Pool, "bchk")
    g.reg_mov(bchk, NROW - 1)

    for c in range(NCH):
        P.inc(g.dma_start(out=Z["x_sb"][:, T * c: T * (c + 1)],
                          in_=xT[128 * c:128 * (c + 1), :]), "LX", 16)
    emit_l1_const_loads(nc, P, Z, dict(
        woT=woT, wg=wg, boc=boc, bgc=bgc, idbf=idbf, id16=id16,
        iota16b=iota16b, r16m=r16m, onesb=onesb, onesf=onesf, tri8=tri8,
        eh01=eh01, pmask=pmask, iota128=iota128, b1c=b1c, b2c=b2c))
    lx_ready = P.c["LX"]
    S = new_l1_state()
    prefetch = make_headpair_prefetch(nc, P, Z, S, wqT, wkT, wvT)
    emit_l1_init(nc, P, Z)
    with nc.named_scope("l1"):
        emit_l1_batch(nc, P, Z, S, prefetch, x2tm, meta, 0, lx_ready, fsrc,
                      bchk)
    pe_attn = P.c["PE"]

    with nc.named_scope("prolog_b"):
        emit_prolog_b(nc, P, Z, cnts1, cnts8)
    giu_done = P.c["VEC"]

    P.wait(g, "SX0")
    P.wait(g, "SX1")
    P.wait(g, "ST")
    P.wait(g, "PE")
    P.wait(g, "VEC")
    loaders = make_ffn_loaders(nc, P, ZF, w1, w2)
    loaders[1](0, 0, 0)
    P.wait(sy, "SX0")
    P.wait(sy, "SX1")
    P.wait(sy, "ST")
    P.wait(sy, "PE")
    P.wait(sy, "VEC")
    mt_all_early = Z["q2_sb"][:, 2560:2816].bitcast(F32)
    P.inc(sy.dma_start(out=mt_all_early.rearrange("p (i c) -> p i c", c=4),
                       in_=meta.rearrange("(i p) c -> p i c", p=128)),
          "LT0", 16)
    for i in range(18, NST):
        P.inc(sy.dma_start(out=Z["o_sb"][:, 1024 * i:1024 * (i + 1)],
                           in_=x2tm[128 * i:128 * (i + 1), :]), "LT0", 16)

    with nc.named_scope("a2a_fwd"):
        with allow_ici_all_to_all():
            cc = g.collective_compute(
                "AllToAll", ALU.bypass, replica_groups=[list(range(NC))],
                ins=[fsrc.opt()], outs=[fdst.opt()])
        P.inc(cc, "CCF", 1)
        P.wait(g, "CCF")

    with nc.named_scope("ffn"):
        emit_ffn(nc, P, Z, ZF, loaders, fdst, rsrc, giu_done, bchk)

    with nc.named_scope("a2a_ret"):
        P.wait(g, "SY0")
        P.wait(g, "SY1")
        with allow_ici_all_to_all():
            cc = g.collective_compute(
                "AllToAll", ALU.bypass, replica_groups=[list(range(NC))],
                ins=[rsrc.opt()], outs=[rdst.opt()])
        P.inc(cc, "CCR", 1)

    with nc.named_scope("phase_c"):
        emit_phase_c(nc, P, Z, ZF, x2tm, meta, rdst, out, bchk, pe_attn)

    es.close()
    return nc



_NC_CACHE = {}


def _get(name, builder):
    if name not in _NC_CACHE:
        _NC_CACHE[name] = builder()
    return _NC_CACHE[name]


def kernel(L0_attn_x, L0_attn_Wq, L0_attn_Wk, L0_attn_Wv, L0_attn_Wo,
           L0_attn_bo, L0_ffn_Wg, L0_ffn_bg, L0_ffn_W1, L0_ffn_b1,
           L0_ffn_W2, L0_ffn_b2):
    x = np.asarray(L0_attn_x, dtype=np.float32)
    wqT = np.ascontiguousarray(np.asarray(L0_attn_Wq, np.float32).T)
    wkT = np.ascontiguousarray(np.asarray(L0_attn_Wk, np.float32).T)
    wvT = np.ascontiguousarray(np.asarray(L0_attn_Wv, np.float32).T)
    woT = np.ascontiguousarray(np.asarray(L0_attn_Wo, np.float32).T)
    bo = np.asarray(L0_attn_bo, np.float32)
    wg = np.ascontiguousarray(np.asarray(L0_ffn_Wg, np.float32))
    bg = np.asarray(L0_ffn_bg, np.float32)
    W1 = np.asarray(L0_ffn_W1, np.float32)
    b1 = np.asarray(L0_ffn_b1, np.float32)
    W2 = np.asarray(L0_ffn_W2, np.float32)
    b2 = np.asarray(L0_ffn_b2, np.float32)

    boc = np.ascontiguousarray(bo.reshape(NCH, 128).T)
    bgc = np.ascontiguousarray(bg.reshape(E, 1))
    idbf = np.eye(128, dtype=np.float32).astype(BF16NP)
    id16 = np.eye(16, dtype=np.float32)
    iota16b = np.ascontiguousarray(np.tile(np.arange(16, dtype=np.float32),
                                           (128, 1)))
    r16m = np.ascontiguousarray(16.0 - iota16b)
    onesb = np.ones((128, 128), np.float32).astype(BF16NP)
    onesf = np.ones((128, 128), np.float32)
    tri8 = np.triu(np.ones((8, 8), np.float32), 1)
    iota128 = np.arange(128, dtype=np.float32).reshape(128, 1)

    nc1 = _get("fused", build_fused)
    in_maps = []
    for c in range(NC):
        eh01 = np.zeros((16, 2), np.float32)
        eh01[2 * c, 0] = 1.0
        eh01[2 * c + 1, 1] = 1.0
        b1cc = np.ascontiguousarray(
            np.concatenate([b1[2 * c + e].reshape(32, 128).T for e in range(2)],
                           axis=1))
        b2cc = np.ascontiguousarray(
            np.concatenate([b2[2 * c + e].reshape(NCH, 128).T for e in range(2)],
                           axis=1))
        in_maps.append({
            "xT": np.ascontiguousarray(x[c].T),
            "wqT": wqT, "wkT": wkT, "wvT": wvT, "woT": woT,
            "boc": boc, "wg": wg, "bgc": bgc,
            "idbf": idbf, "id16": id16, "iota16b": iota16b, "r16m": r16m,
            "onesb": onesb, "onesf": onesf, "tri8": tri8, "eh01": eh01,
            "pmask": (np.arange(8) < c).astype(np.float32).reshape(8, 1),
            "iota128": iota128,
            "w1": np.ascontiguousarray(W1[2 * c:2 * c + 2]),
            "w2": np.ascontiguousarray(W2[2 * c:2 * c + 2]),
            "b1c": b1cc, "b2c": b2cc,
        })
    res = run_bass_kernel_spmd(nc1, in_maps, core_ids=list(range(NC))).results
    return np.stack([res[c]["out"] for c in range(NC)], axis=0)


# version 08
# speedup vs baseline: 1.0172x; 1.0116x over previous
import numpy as np
import ml_dtypes
from contextlib import ExitStack

import concourse.bass as bass
import concourse.mybir as mybir
from concourse.bass import IndirectOffsetOnAxis
from concourse.bass_utils import run_bass_kernel_spmd
from concourse.aot_env import allow_ici_all_to_all

F32 = mybir.dt.float32
BF16 = mybir.dt.bfloat16
U32 = mybir.dt.uint32
AF = mybir.ActivationFunctionType
ALU = mybir.AluOpType
AX = mybir.AxisListType.X
BF16NP = ml_dtypes.bfloat16

D = 1024
T = 4096
H = 16
NC = 8
E = 16
CAP = 2048
LCAP = 1024
FH = 4096
NCH = D // 128
NTT = T // 512
NST = T // 128
BIG = float(1 << 26)
BIGIX = 1.0e6
NROW = E * LCAP


class Prog:
    def __init__(self, nc, es, sems):
        self.nc = nc
        self.c = {}
        self.sem = {}
        for name in sems:
            self.sem[name] = es.enter_context(nc.semaphore(name))
            self.c[name] = 0

    def inc(self, instr, name, k=1):
        self.c[name] += k
        instr.then_inc(self.sem[name], k)
        return self.c[name]

    def wait(self, eng, name, v=None):
        eng.wait_ge(self.sem[name], self.c[name] if v is None else v)


SEMS = ["LX", "LS", "WH0", "WH1", "PE", "ACT", "VEC", "ST",
        "SX0", "SX1", "SST", "SEXP", "SPV",
        "CN", "CCA", "CCF", "CCR",
        "BT0", "BT1", "W10", "W11", "W20", "W21", "SY0", "SY1",
        "LT0", "LT1", "GB0", "GB1", "SO"]


def alloc_l1(nc, es):
    sb = lambda name, shape, dt: es.enter_context(nc.sbuf_tensor(name, shape, dt))
    ps = lambda name, shape, dt: es.enter_context(nc.psum_tensor(name, shape, dt))
    Z = {}
    Z["x_sb"] = sb("x_sb", [128, NCH * T], BF16)
    Z["o_sb"] = sb("o_sb", [128, NCH * T], BF16)
    Z["wo_sb"] = sb("wo_sb", [128, NCH * D], BF16)
    Z["wh_sb"] = [sb(f"wh_sb{i}", [128, 3 * D], BF16) for i in range(2)]
    Z["wg_sb"] = sb("wg_sb", [128, NCH * E], BF16)
    Z["boc_sb"] = sb("boc_sb", [128, NCH], F32)
    Z["bgc_sb"] = sb("bgc_sb", [16, 1], F32)
    Z["idbf_sb"] = sb("idbf_sb", [128, 128], BF16)
    Z["id16_sb"] = sb("id16_sb", [16, 16], F32)
    Z["iota_sb"] = sb("iota_sb", [128, 16], F32)
    Z["r16m_sb"] = sb("r16m_sb", [128, 16], F32)
    Z["ones_sb"] = sb("ones_sb", [128, 128], BF16)
    Z["onesf_sb"] = sb("onesf_sb", [128, 128], F32)
    Z["k2_sb"] = sb("k2_sb", [128, T], BF16)
    Z["q2_sb"] = sb("q2_sb", [128, T], BF16)
    Z["v2tmp"] = sb("v2tmp", [128, 512], BF16)
    Z["vaugA"] = sb("vaugA", [128, 65 * 32], BF16)
    Z["vaugB"] = sb("vaugB", [128, 65 * 32], BF16)
    Z["expS"] = [sb(f"expS{i}", [128, 1024], BF16) for i in range(2)]
    Z["rden_bf"] = sb("rden_bf", [65, 512], BF16)
    Z["bc_sb"] = sb("bc_sb", [128, 512], F32)
    Z["ysq_sb"] = sb("ysq_sb", [128, 512], BF16)
    Z["mean_sb"] = sb("mean_sb", [1, 512], F32)
    Z["var_sb"] = sb("var_sb", [1, 512], F32)
    Z["mean_bf"] = sb("mean_bf", [1, 512], BF16)
    Z["rstd_bf"] = sb("rstd_bf", [1, 512], BF16)
    Z["rbc_sb"] = sb("rbc_sb", [128, 512], F32)
    Z["lg_tt"] = sb("lg_tt", [16, 512], F32)
    Z["msq_sb"] = Z["lg_tt"][0:1, 0:512]
    Z["mbc_sb"] = Z["bc_sb"]
    Z["mx_sb"] = sb("mx_sb", [128, 1], F32)
    Z["nmx_sb"] = sb("nmx_sb", [128, 1], F32)
    Z["sm_sb"] = sb("sm_sb", [128, 1], F32)
    Z["ejunk_sb"] = sb("ejunk_sb", [128, 16], F32)
    Z["eq_sb"] = sb("eq_sb", [128, 16], F32)
    Z["cand_sb"] = sb("cand_sb", [128, 16], F32)
    Z["t1_sb"] = sb("t1_sb", [128, 1], F32)

    Z["ohb_sb"] = sb("ohb_sb", [128, 16], BF16)
    Z["cnt_sb"] = sb("cnt_sb", [16, 8], F32)
    Z["sel_sb"] = sb("sel_sb", [128, 16], F32)
    Z["cs_sb"] = sb("cs_sb", [128, 1], F32)

    Z["c5_sb"] = sb("c5_sb", [128, 1], F32)
    Z["c8_sb"] = sb("c8_sb", [128, 1], F32)
    Z["csum_sb"] = Z["o_sb"][0:16, 0:8192].bitcast(F32)
    Z["ohfm_sb"] = Z["o_sb"][0:16, 8192:12288]
    Z["lg_tm"] = Z["o_sb"][:, 12288:13312].bitcast(F32)
    Z["ohf_all"] = Z["o_sb"][:, 13312:14336].bitcast(F32)
    Z["meta_sb"] = Z["o_sb"][:, 14336:14592].bitcast(F32)
    Z["idx_all"] = Z["o_sb"][:, 14592:14656].bitcast(F32)
    Z["x2st"] = [Z["o_sb"][:, 16384:17408], Z["o_sb"][:, 17408:18432]]
    Z["fif"] = sb("fif", [128, 32], F32)
    Z["fiu"] = sb("fiu", [128, 32], U32)
    Z["gif"] = sb("gif", [128, 32], F32)
    Z["giu"] = sb("giu", [128, 32], U32)
    Z["cnts8_sb"] = sb("cnts8_sb", [8, 16], F32)
    Z["tri8_sb"] = sb("tri8_sb", [8, 8], F32)
    Z["eh01_sb"] = sb("eh01_sb", [16, 2], F32)
    Z["pmask_sb"] = sb("pmask_sb", [8, 1], F32)
    Z["iota128_sb"] = sb("iota128_sb", [128, 1], F32)
    Z["baseT_sb"] = sb("baseT_sb", [16, 8], F32)
    Z["rowsel_sb"] = sb("rowsel_sb", [1, 16], F32)
    Z["gbrow_sb"] = sb("gbrow_sb", [1, 16], F32)
    Z["bbc_sb"] = sb("bbc_sb", [128, 16], F32)
    Z["gbb_sb"] = sb("gbb_sb", [128, 16], F32)
    Z["pc_sb"] = sb("pc_sb", [128, 1], F32)
    Z["cmp8_sb"] = sb("cmp8_sb", [128, 8], F32)
    Z["bsc_sb"] = sb("bsc_sb", [128, 8], F32)
    Z["nle_sb"] = sb("nle_sb", [128, 1], F32)
    Z["rr_sb"] = sb("rr_sb", [128, 1], F32)
    Z["ovg_sb"] = sb("ovg_sb", [128, 1], F32)
    Z["b1c_sb"] = sb("b1c_sb", [128, 64], F32)
    Z["b2c_sb"] = sb("b2c_sb", [128, 16], F32)
    Z["ln2_sb"] = sb("ln2_sb", [128, 8], F32)
    Z["st_ps"] = [ps("st_ps0", [128, 1024], F32), ps("st_ps1", [128, 1024], F32)]
    Z["pvA_ps"] = ps("pvA_ps", [128, 512], F32)
    Z["pvB_ps"] = ps("pvB_ps", [128, 512], F32)
    Z["bc_ps"] = ps("bc_ps", [128, 512], F32)
    Z["trb_ps"] = ps("trb_ps", [128, 1024], BF16)
    return Z


def alloc_overlays(Z):
    ZF = {}
    x, o = Z["x_sb"], Z["o_sb"]
    ZF["buf_fm"] = x[:, 0:16384]
    ZF["h_sb"] = x[:, 16384:32768]
    ZF["yfm_sb"] = o[:, 0:4096]
    ZF["bt_sb"] = [o[:, 4096:5120], o[:, 5120:6144]]
    ZF["yt_sb"] = [o[:, 6144:7168], o[:, 7168:8192]]
    ZF["w1_sb"] = [o[:, 8192:9216], o[:, 9216:10240]]
    ZF["w2_sb"] = [o[:, 10240:14336], o[:, 14336:18432]]
    k2, q2 = Z["k2_sb"], Z["q2_sb"]
    ZF["x2t2"] = [k2[:, 0:1024], k2[:, 1024:2048]]
    ZF["yb2"] = [k2[:, 2048:3072], k2[:, 3072:4096]]
    ZF["mt2"] = [q2[:, 0:8].bitcast(F32), q2[:, 8:16].bitcast(F32)]
    ZF["z_sb"] = Z["vaugA"][:, 0:1024]
    ZF["zc_sb"] = Z["vaugA"][:, 1024:2048]
    ZF["zsq_sb"] = Z["vaugB"][:, 0:1024]
    ZF["o2_sb"] = q2[:, 2048:4096].bitcast(F32)
    return ZF


def new_l1_state():
    return {
        "gg": 0,
        "vec_bc_free": 0,
        "vec_pvA_free": 0,
        "vec_pvB_free": 0,
        "wh_ready": [0, 0],
        "wh_freed": [0, 0],
    }


def emit_l1_init(nc, P, Z):
    v, s, t = nc.vector, nc.scalar, nc.tensor
    P.wait(v, "LS")
    P.inc(v.memset(Z["vaugA"][:, :], 1.0), "VEC")
    P.inc(v.memset(Z["vaugB"][:, :], 1.0), "VEC")
    P.inc(v.memset(Z["c5_sb"][:, :], 1e-5), "VEC")
    P.inc(v.memset(Z["c8_sb"][:, :], 1e-8), "VEC")
    P.wait(s, "VEC")
    P.wait(t, "VEC")


def emit_l1_batch(nc, P, Z, S, prefetch, x2tm_ap, meta_ap, cnt_col, lx_ready,
                  fsrc_ap, bchk):
    g, sy, t, v, s = nc.gpsimd, nc.sync, nc.tensor, nc.vector, nc.scalar
    st_ps, pvA_ps, pvB_ps = Z["st_ps"], Z["pvA_ps"], Z["pvB_ps"]
    bc_ps, trb_ps = Z["bc_ps"], Z["trb_ps"]
    x_sb, o_sb = Z["x_sb"], Z["o_sb"]
    k2_sb, q2_sb = Z["k2_sb"], Z["q2_sb"]
    expS = Z["expS"]
    ones_sb = Z["ones_sb"]

    for hp in range(8):
        bsel = hp % 2
        wh_sem = "WH0" if bsel == 0 else "WH1"
        prefetch(hp)
        t.wait_ge(P.sem[wh_sem], S["wh_ready"][bsel])
        P.wait(t, "LX", lx_ready)
        P.wait(t, "LS")
        for mi, dst_sb in ((0, k2_sb), (1, q2_sb)):
            for tt in range(NTT):
                pp = bc_ps if tt % 2 == 0 else pvA_ps
                P.wait(t, "VEC",
                       S["vec_bc_free"] if tt % 2 == 0 else S["vec_pvA_free"])
                for c in range(NCH):
                    mm = t.matmul(pp[:, :],
                                  Z["wh_sb"][bsel][:, D * mi + 128 * c:
                                                   D * mi + 128 * (c + 1)],
                                  x_sb[:, T * c + 512 * tt: T * c + 512 * (tt + 1)],
                                  start=(c == 0), stop=(c == NCH - 1))
                P.inc(mm, "PE")
                P.wait(v, "PE")
                vi = P.inc(v.tensor_copy(dst_sb[:, 512 * tt:512 * (tt + 1)],
                                         pp[:, :]), "VEC")
                if tt % 2 == 0:
                    S["vec_bc_free"] = vi
                else:
                    S["vec_pvA_free"] = vi
        for tt in range(NTT):
            pp = bc_ps if tt % 2 == 0 else pvA_ps
            P.wait(t, "VEC",
                   S["vec_bc_free"] if tt % 2 == 0 else S["vec_pvA_free"])
            for c in range(NCH):
                mm = t.matmul(pp[:, :],
                              Z["wh_sb"][bsel][:, 2 * D + 128 * c:
                                               2 * D + 128 * (c + 1)],
                              x_sb[:, T * c + 512 * tt: T * c + 512 * (tt + 1)],
                              start=(c == 0), stop=(c == NCH - 1))
            P.inc(mm, "PE")
            P.wait(v, "PE")
            vi = P.inc(v.tensor_copy(Z["v2tmp"][:, :], pp[:, :]), "VEC")
            if tt % 2 == 0:
                S["vec_bc_free"] = vi
            else:
                S["vec_pvA_free"] = vi
            for j in range(4):
                sc = 4 * tt + j
                P.wait(t, "VEC")
                mm = t.transpose(trb_ps[0:128, 0:128],
                                 Z["v2tmp"][:, 128 * j:128 * (j + 1)],
                                 Z["idbf_sb"][:, :])
                P.inc(mm, "PE")
                P.wait(v, "PE")
                P.inc(v.tensor_copy(Z["vaugA"][:, 65 * sc:65 * sc + 64],
                                    trb_ps[0:128, 0:64]), "VEC")
                P.inc(v.tensor_copy(Z["vaugB"][:, 65 * sc:65 * sc + 64],
                                    trb_ps[0:128, 64:128]), "VEC")
        S["wh_freed"][bsel] = P.c["PE"]
        vec_vaug = P.c["VEC"]

        for tt in range(NTT):
            g0 = S["gg"]
            tsl = slice(512 * tt, 512 * (tt + 1))
            for lg in range(32):
                cgg = g0 + lg
                if cgg >= 2:
                    P.wait(t, "SEXP", cgg - 1)
                if lg == 0:
                    P.wait(t, "VEC", vec_vaug)
                mm = t.matmul(st_ps[cgg % 2][:, 0:512],
                              k2_sb[0:64, 128 * lg:128 * (lg + 1)],
                              q2_sb[0:64, tsl], start=True, stop=True,
                              tile_position=(0, 0))
                mm = t.matmul(st_ps[cgg % 2][:, 512:1024],
                              k2_sb[64:128, 128 * lg:128 * (lg + 1)],
                              q2_sb[64:128, tsl], start=True, stop=True,
                              tile_position=(64, 0))
                P.inc(mm, "SST")
                if lg >= 1:
                    P.wait(t, "SEXP", cgg)
                    pvg = lg - 1
                    if pvg == 0:
                        P.wait(t, "VEC", S["vec_pvA_free"])
                        P.wait(t, "VEC", S["vec_pvB_free"])
                    mm = t.matmul(pvA_ps[0:65, :],
                                  Z["vaugA"][:, 65 * pvg:65 * pvg + 65],
                                  expS[(g0 + pvg) % 2][:, 0:512],
                                  start=(pvg == 0), stop=False,
                                  skip_group_check=True)
                    mm = t.matmul(pvB_ps[0:65, :],
                                  Z["vaugB"][:, 65 * pvg:65 * pvg + 65],
                                  expS[(g0 + pvg) % 2][:, 512:1024],
                                  start=(pvg == 0), stop=False,
                                  skip_group_check=True)
                    P.inc(mm, "SPV")
            P.wait(t, "SEXP", g0 + 32)
            mm = t.matmul(pvA_ps[0:65, :], Z["vaugA"][:, 65 * 31:65 * 31 + 65],
                          expS[(g0 + 31) % 2][:, 0:512],
                          start=False, stop=True, skip_group_check=True)
            mm = t.matmul(pvB_ps[0:65, :], Z["vaugB"][:, 65 * 31:65 * 31 + 65],
                          expS[(g0 + 31) % 2][:, 512:1024],
                          start=False, stop=True, skip_group_check=True)
            P.inc(mm, "SPV")
            for lg in range(32):
                cgg = g0 + lg
                P.wait(s, "SST", cgg + 1)
                if cgg >= 2:
                    P.wait(s, "SPV", cgg - 1)
                si = s.activation(expS[cgg % 2][:, :], st_ps[cgg % 2][:, :],
                                  AF.Exp, bias=0.0, scale=0.125)
                P.inc(si, "SEXP")
            S["gg"] = g0 + 32
            for head, pv in ((0, pvA_ps), (1, pvB_ps)):
                bcx = Z["bc_sb"] if head == 0 else Z["rbc_sb"]
                dr = slice(0, 1) if head == 0 else slice(64, 65)
                po = head * 64
                ocols = slice(T * hp + 512 * tt, T * hp + 512 * (tt + 1))
                P.wait(v, "SPV", S["gg"])
                with nc.allow_low_precision(reason="bf16 stage for matmul"):
                    den_cast = P.inc(v.tensor_copy(Z["rden_bf"][dr, :],
                                                   pv[64:65, :]), "VEC")
                with nc.allow_low_precision(reason="unnormalised bf16 stage"):
                    vi = P.inc(v.tensor_copy(o_sb[po:po + 64, ocols],
                                             pv[0:64, :]), "VEC")
                if head == 0:
                    S["vec_pvA_free"] = vi
                else:
                    S["vec_pvB_free"] = vi
                P.wait(t, "VEC", den_cast)
                P.wait(t, "VEC", S["vec_bc_free"])
                mm = t.matmul(bc_ps[:, :], ones_sb[dr, 0:128],
                              Z["rden_bf"][dr, :], start=True, stop=True,
                              skip_group_check=True)
                P.inc(mm, "PE")
                P.wait(v, "PE")
                P.wait(v, "VEC")
                P.inc(v.tensor_scalar(bcx[:, :], bc_ps[:, :],
                                      1e-8, None, ALU.add), "VEC")
                S["vec_bc_free"] = P.c["VEC"]
                P.wait(v, "VEC")
                P.inc(v.reciprocal(bcx[:, :], bcx[:, :]), "VEC")
                P.wait(v, "VEC")
                P.inc(v.tensor_tensor(o_sb[po:po + 64, ocols],
                                      o_sb[po:po + 64, ocols],
                                      bcx[po:po + 64, :], ALU.mult), "VEC")

    P.wait(t, "SEXP", S["gg"])
    P.wait(t, "VEC", S["vec_pvA_free"])
    P.wait(t, "VEC", S["vec_pvB_free"])
    for tt in range(NTT):
        for dc in range(NCH):
            wv = st_ps[0][:, 0:512] if dc % 2 == 0 else st_ps[0][:, 512:1024]
            P.wait(t, "VEC")
            for r in range(NCH):
                mm = t.matmul(wv,
                              Z["wo_sb"][:, D * r + 128 * dc: D * r + 128 * (dc + 1)],
                              o_sb[:, T * r + 512 * tt: T * r + 512 * (tt + 1)],
                              start=(r == 0), stop=(r == NCH - 1),
                              skip_group_check=True)
            P.inc(mm, "PE")
            P.wait(v, "PE")
            xv = x_sb[:, T * dc + 512 * tt: T * dc + 512 * (tt + 1)]
            P.inc(v.scalar_tensor_tensor(xv, wv, Z["boc_sb"][:, dc:dc + 1], xv,
                                         ALU.add, ALU.add), "VEC")
            P.wait(t, "VEC")
            mm = t.matmul(pvA_ps[0:1, :], ones_sb[:, 0:1], xv,
                          start=(dc == 0), stop=(dc == NCH - 1),
                          skip_group_check=True)
            P.inc(mm, "PE")
            P.wait(s, "VEC")
            P.inc(s.activation(Z["ysq_sb"][:, :], xv, AF.Square), "ACT")
            P.wait(t, "ACT")
            mm = t.matmul(pvA_ps[32:33, :], ones_sb[:, 0:1], Z["ysq_sb"][:, :],
                          start=(dc == 0), stop=(dc == NCH - 1),
                          skip_group_check=True)
            P.inc(mm, "PE")
        P.wait(v, "PE")
        P.inc(v.tensor_scalar(Z["mean_sb"][0:1, :], pvA_ps[0:1, :],
                              1.0 / D, None, ALU.mult), "VEC")
        P.inc(v.tensor_scalar(Z["msq_sb"][:, :], pvA_ps[32:33, :],
                              1.0 / D, None, ALU.mult), "VEC")
        P.wait(v, "VEC")
        P.inc(v.tensor_tensor(Z["var_sb"][0:1, :], Z["mean_sb"][0:1, :],
                              Z["mean_sb"][0:1, :], ALU.mult), "VEC")
        P.wait(v, "VEC")
        P.inc(v.tensor_sub(Z["var_sb"][0:1, :], Z["msq_sb"][:, :],
                           Z["var_sb"][0:1, :]), "VEC")
        P.wait(s, "VEC")
        P.inc(s.activation(Z["var_sb"][0:1, :], Z["var_sb"][0:1, :],
                           AF.Sqrt, bias=Z["c5_sb"][0:1, 0:1], scale=1.0),
              "ACT")
        P.wait(v, "ACT")
        P.inc(v.reciprocal(Z["var_sb"][0:1, :], Z["var_sb"][0:1, :]), "VEC")
        P.wait(v, "VEC")
        with nc.allow_low_precision(reason="bf16 broadcast operands"):
            P.inc(v.tensor_copy(Z["mean_bf"][0:1, :], Z["mean_sb"][0:1, :]),
                  "VEC")
            P.inc(v.tensor_copy(Z["rstd_bf"][0:1, :], Z["var_sb"][0:1, :]),
                  "VEC")
        P.wait(t, "VEC")
        t.matmul(st_ps[1][:, 0:512], ones_sb[0:1, 0:128], Z["mean_bf"][0:1, :],
                 start=True, stop=True, skip_group_check=True)
        mm = t.matmul(st_ps[1][:, 512:1024], ones_sb[0:1, 0:128],
                      Z["rstd_bf"][0:1, :], start=True, stop=True,
                      skip_group_check=True)
        P.inc(mm, "PE")
        P.wait(v, "PE")
        P.inc(v.tensor_copy(Z["mbc_sb"][:, :], st_ps[1][:, 0:512]), "VEC")
        P.inc(v.tensor_copy(Z["rbc_sb"][:, :], st_ps[1][:, 512:1024]), "VEC")
        P.wait(v, "VEC")
        for dc in range(NCH):
            xv = x_sb[:, T * dc + 512 * tt: T * dc + 512 * (tt + 1)]
            P.inc(v.tensor_sub(xv, xv, Z["mbc_sb"][:, :]), "VEC")
            P.wait(v, "VEC")
            P.inc(v.tensor_tensor(xv, xv, Z["rbc_sb"][:, :], ALU.mult), "VEC")
            P.wait(v, "VEC")
    vec_x2 = P.c["VEC"]

    for tt in range(NTT):
        P.wait(t, "VEC", vec_x2)
        for c in range(NCH):
            mm = t.matmul(st_ps[0][0:16, 512:1024],
                          Z["wg_sb"][:, 16 * c:16 * (c + 1)],
                          x_sb[:, T * c + 512 * tt: T * c + 512 * (tt + 1)],
                          start=(c == 0), stop=(c == NCH - 1),
                          skip_group_check=True)
        P.inc(mm, "PE")
        P.wait(s, "PE")
        P.wait(s, "VEC")
        si = s.activation(Z["lg_tt"][0:16, :], st_ps[0][0:16, 512:1024],
                          AF.Identity, bias=Z["bgc_sb"][0:16, 0:1], scale=1.0)
        P.inc(si, "ACT")
        P.wait(t, "ACT")
        for j in range(4):
            i = 4 * tt + j
            P.wait(t, "VEC")
            mm = t.transpose(pvB_ps[0:128, 0:16],
                             Z["lg_tt"][0:16, 128 * j:128 * (j + 1)],
                             Z["id16_sb"][:, :])
            P.inc(mm, "PE")
            P.wait(v, "PE")
            P.inc(v.tensor_copy(Z["lg_tm"][:, 16 * i:16 * (i + 1)],
                                pvB_ps[0:128, 0:16]), "VEC")
        P.wait(t, "VEC")
    k2r = Z["k2_sb"]
    mxa = k2r[:, 0:64].bitcast(F32)
    nmxa = k2r[:, 64:128].bitcast(F32)
    sma = k2r[:, 128:192].bitcast(F32)
    eqa = k2r[:, 192:1216].bitcast(F32)
    cana = k2r[:, 1216:2240].bitcast(F32)
    t1a = k2r[:, 2240:2304].bitcast(F32)
    ohba = k2r[:, 2304:2816]
    sela = k2r[:, 2816:3840].bitcast(F32)
    csa = k2r[:, 3840:3904].bitcast(F32)
    ovga = k2r[:, 3904:3968].bitcast(F32)
    P.wait(v, "VEC")
    for i in range(NST):
        ltile = Z["lg_tm"][:, 16 * i:16 * (i + 1)]
        P.inc(v.reduce_max(mxa[:, i:i + 1], ltile, axis=AX), "VEC")
        P.inc(v.reduce_max(nmxa[:, i:i + 1], ltile, axis=AX, negate=True),
              "VEC")
    P.wait(s, "VEC")
    for i in range(NST):
        ltile = Z["lg_tm"][:, 16 * i:16 * (i + 1)]
        si = s.activation(Z["ejunk_sb"][:, :], ltile, AF.Exp,
                          bias=nmxa[:, i:i + 1], scale=1.0,
                          accum_out=sma[:, i:i + 1])
        P.inc(si, "ACT")
    P.wait(v, "VEC")
    for i in range(NST):
        ltile = Z["lg_tm"][:, 16 * i:16 * (i + 1)]
        P.inc(v.tensor_scalar(eqa[:, 16 * i:16 * (i + 1)], ltile,
                              mxa[:, i:i + 1], None, ALU.is_equal), "VEC")
    P.wait(v, "ACT")
    for i in range(NST):
        P.inc(v.reciprocal(Z["meta_sb"][:, 4 * i + 2:4 * i + 3],
                           sma[:, i:i + 1]), "VEC")
    P.wait(v, "VEC")
    for i in range(NST):
        P.inc(v.tensor_tensor(cana[:, 16 * i:16 * (i + 1)], Z["r16m_sb"][:, :],
                              eqa[:, 16 * i:16 * (i + 1)], ALU.mult), "VEC")
    P.wait(v, "VEC")
    for i in range(NST):
        P.inc(v.reduce_max(t1a[:, i:i + 1], cana[:, 16 * i:16 * (i + 1)],
                           axis=AX), "VEC")
    P.wait(v, "VEC")
    for i in range(NST):
        P.inc(v.tensor_scalar(Z["idx_all"][:, i:i + 1], t1a[:, i:i + 1],
                              -1.0, 16.0, ALU.mult, ALU.add), "VEC")
    P.wait(v, "VEC")
    for i in range(NST):
        P.inc(v.tensor_copy(Z["meta_sb"][:, 4 * i:4 * i + 1],
                            Z["idx_all"][:, i:i + 1]), "VEC")
        P.inc(v.tensor_scalar(Z["ohf_all"][:, 16 * i:16 * (i + 1)],
                              Z["iota_sb"][:, :], Z["idx_all"][:, i:i + 1],
                              None, ALU.is_equal), "VEC")
        with nc.allow_low_precision(reason="onehot bf16"):
            P.inc(v.tensor_scalar(ohba[:, 16 * i:16 * (i + 1)],
                                  Z["iota_sb"][:, :], Z["idx_all"][:, i:i + 1],
                                  None, ALU.is_equal), "VEC")
    for r in range(NST // 8):
        P.wait(t, "VEC")
        for j in range(8):
            i = 8 * r + j
            mm = t.transpose(trb_ps[0:16, 128 * j:128 * (j + 1)],
                             ohba[:, 16 * i:16 * (i + 1)], Z["idbf_sb"][:, :])
            P.inc(mm, "PE")
        P.wait(v, "PE")
        for j in range(8):
            i = 8 * r + j
            P.inc(v.tensor_copy(Z["ohfm_sb"][0:16, 128 * i:128 * (i + 1)],
                                trb_ps[0:16, 128 * j:128 * (j + 1)]), "VEC")
    P.wait(v, "VEC")
    P.inc(v.tensor_tensor_scan(Z["csum_sb"][:, :], Z["ohfm_sb"][:, :],
                               Z["ohfm_sb"][:, :], 0.0, ALU.add, ALU.bypass),
          "VEC")
    P.wait(v, "VEC")
    P.inc(v.tensor_copy(Z["cnt_sb"][:, cnt_col:cnt_col + 1],
                        Z["csum_sb"][:, T - 1:T]), "VEC")
    P.wait(t, "VEC")
    for i in range(NST):
        mm = t.transpose(pvB_ps[0:128, 16 * i:16 * (i + 1)],
                         Z["csum_sb"][0:16, 128 * i:128 * (i + 1)],
                         Z["id16_sb"][:, :])
        P.inc(mm, "PE")
    P.wait(v, "PE")
    for i in range(NST):
        P.inc(v.tensor_tensor(sela[:, 16 * i:16 * (i + 1)],
                              pvB_ps[0:128, 16 * i:16 * (i + 1)],
                              Z["ohf_all"][:, 16 * i:16 * (i + 1)], ALU.mult),
              "VEC")
    P.wait(v, "VEC")
    for i in range(NST):
        P.inc(v.reduce_sum(csa[:, i:i + 1], sela[:, 16 * i:16 * (i + 1)],
                           axis=AX), "VEC")
    P.wait(v, "VEC")
    for i in range(NST):
        P.inc(v.tensor_scalar(Z["meta_sb"][:, 4 * i + 1:4 * i + 2],
                              csa[:, i:i + 1], -1.0, None, ALU.add), "VEC")
        P.inc(v.memset(Z["meta_sb"][:, 4 * i + 3:4 * i + 4], 0.0), "VEC")
        P.inc(v.tensor_scalar(Z["fif"][:, i:i + 1], Z["idx_all"][:, i:i + 1],
                              float(LCAP), None, ALU.mult), "VEC")
    P.wait(v, "VEC")
    for i in range(NST):
        P.inc(v.tensor_scalar(ovga[:, i:i + 1],
                              Z["meta_sb"][:, 4 * i + 1:4 * i + 2],
                              float(LCAP), BIGIX, ALU.is_ge, ALU.mult), "VEC")
        P.inc(v.tensor_tensor(Z["fif"][:, i:i + 1], Z["fif"][:, i:i + 1],
                              Z["meta_sb"][:, 4 * i + 1:4 * i + 2], ALU.add),
              "VEC")
    P.wait(v, "VEC")
    for i in range(NST):
        P.inc(v.tensor_tensor(Z["fif"][:, i:i + 1], Z["fif"][:, i:i + 1],
                              ovga[:, i:i + 1], ALU.add), "VEC")
    P.wait(v, "VEC")
    for i in range(NST):
        P.inc(v.tensor_copy(Z["fiu"][:, i:i + 1], Z["fif"][:, i:i + 1]),
              "VEC")
    P.wait(v, "VEC")
    P.wait(sy, "VEC")
    P.inc(sy.dma_start(out=meta_ap.rearrange("(i p) c -> p i c", p=128),
                       in_=Z["meta_sb"].rearrange("p (i c) -> p i c", c=4)),
          "ST", 16)
    for i in range(NST):
        b = i % 2
        sxs = "SX0" if b == 0 else "SX1"
        buf = Z["x2st"][b]
        first = True
        for dc in range(NCH):
            P.wait(t, "VEC")
            mm = t.transpose(trb_ps[0:128, 0:128],
                             x_sb[:, T * dc + 128 * i: T * dc + 128 * (i + 1)],
                             Z["idbf_sb"][:, :])
            P.inc(mm, "PE")
            P.wait(v, "PE")
            if first:
                P.wait(v, sxs)
            first = False
            P.inc(v.tensor_copy(buf[:, 128 * dc:128 * (dc + 1)],
                                trb_ps[0:128, 0:128]), "VEC")
        P.wait(sy, "VEC")
        P.inc(sy.dma_start(out=x2tm_ap[128 * i:128 * (i + 1), :], in_=buf[:, :]),
              sxs, 16)
        P.wait(g, "VEC")
        P.inc(g.indirect_dma_start(
            out=fsrc_ap,
            out_offset=IndirectOffsetOnAxis(Z["fiu"][:, i:i + 1], 0),
            in_=buf[:, :], in_offset=None,
            bounds_check=bchk, oob_is_err=False), sxs, 16)


def emit_l1_const_loads(nc, P, Z, aps):
    g, sy = nc.gpsimd, nc.sync
    for c in range(NCH):
        P.inc(g.dma_start(out=Z["wo_sb"][:, D * c: D * (c + 1)],
                          in_=aps["woT"][128 * c:128 * (c + 1), :]), "LX", 16)
    P.inc(g.dma_start(out=Z["wg_sb"].ap().rearrange("p (c e) -> p c e", c=NCH),
                      in_=aps["wg"].rearrange("(c p) e -> p c e", p=128)),
          "LX", 16)
    for dk, sk in (("boc_sb", "boc"), ("bgc_sb", "bgc"), ("idbf_sb", "idbf"),
                   ("id16_sb", "id16"), ("iota_sb", "iota16b"),
                   ("r16m_sb", "r16m"), ("ones_sb", "onesb"),
                   ("onesf_sb", "onesf"), ("tri8_sb", "tri8"),
                   ("eh01_sb", "eh01"), ("pmask_sb", "pmask"),
                   ("iota128_sb", "iota128"), ("b1c_sb", "b1c"),
                   ("b2c_sb", "b2c")):
        P.inc(sy.dma_start(out=Z[dk][:, :], in_=aps[sk][:, :]), "LS", 16)


def make_headpair_prefetch(nc, P, Z, S, wq_ap, wk_ap, wv_ap, npasses=1):
    g = nc.gpsimd

    def load_pair(hp, bsel):
        sem = "WH0" if bsel == 0 else "WH1"
        P.wait(g, "PE", S["wh_freed"][bsel])
        for mi, wsrc in enumerate((wk_ap, wq_ap, wv_ap)):
            ins = g.dma_start(
                out=Z["wh_sb"][bsel][:, D * mi:D * (mi + 1)].rearrange(
                    "p (c f) -> p c f", c=NCH),
                in_=wsrc[:, 128 * hp:128 * (hp + 1)].rearrange(
                    "(c p) f -> p c f", p=128))
            P.inc(ins, sem, 16)
        S["wh_ready"][bsel] = P.c[sem]

    seq = {"next": 1}

    def prefetch(_hp):
        nxt = seq["next"]
        if nxt < 8 * npasses:
            load_pair(nxt % 8, nxt % 2)
            seq["next"] = nxt + 1

    load_pair(0, 0)
    return prefetch



def emit_prolog_b(nc, P, Z, cnts1_ap, cnts8_ap):
    g, sy, t, v, s = nc.gpsimd, nc.sync, nc.tensor, nc.vector, nc.scalar
    bc_ps = Z["bc_ps"]
    P.wait(sy, "VEC")
    P.inc(sy.dma_start(out=cnts1_ap.rearrange("one e -> e one"),
                       in_=Z["cnt_sb"][:, 0:1]), "CN", 16)
    P.wait(g, "CN")
    cc = g.collective_compute(
        "AllGather", ALU.bypass, replica_groups=[list(range(NC))],
        ins=[cnts1_ap.opt()], outs=[cnts8_ap.opt()])
    P.inc(cc, "CCA", 1)
    P.wait(g, "CCA")
    P.inc(g.dma_start(out=Z["cnts8_sb"][:, :], in_=cnts8_ap), "CN", 16)
    pe0, vec0 = P.c["PE"], P.c["VEC"]
    P.wait(t, "CN")
    P.wait(t, "VEC", vec0)
    mm = t.matmul(bc_ps[0:16, 0:8], Z["cnts8_sb"][:, :], Z["tri8_sb"][:, :],
                  start=True, stop=True, skip_group_check=True)
    P.inc(mm, "PE")
    P.wait(v, "PE")
    P.inc(v.tensor_copy(Z["baseT_sb"][:, :], bc_ps[0:16, 0:8]), "VEC")
    P.wait(t, "VEC")
    mm = t.matmul(bc_ps[0:1, 16:32], Z["pmask_sb"][:, :], Z["cnts8_sb"][:, :],
                  start=True, stop=True, skip_group_check=True)
    P.inc(mm, "PE")
    P.wait(v, "PE")
    P.inc(v.tensor_copy(Z["gbrow_sb"][:, :], bc_ps[0:1, 16:32]), "VEC")
    P.wait(t, "VEC")
    for k in range(2):
        mm = t.matmul(bc_ps[0:1, 32 + 8 * k:40 + 8 * k],
                      Z["eh01_sb"][:, k:k + 1], Z["baseT_sb"][:, :],
                      start=True, stop=True, skip_group_check=True)
        P.inc(mm, "PE")
    P.wait(v, "PE")
    P.inc(v.tensor_copy(Z["rowsel_sb"][:, 0:8], bc_ps[0:1, 32:40]), "VEC")
    P.inc(v.tensor_copy(Z["rowsel_sb"][:, 8:16], bc_ps[0:1, 40:48]), "VEC")
    P.wait(t, "VEC")
    mm = t.matmul(bc_ps[:, 48:64], Z["onesf_sb"][0:1, :],
                  Z["rowsel_sb"][0:1, :], start=True, stop=True,
                  skip_group_check=True)
    P.inc(mm, "PE")
    mm = t.matmul(bc_ps[:, 64:80], Z["onesf_sb"][0:1, :],
                  Z["gbrow_sb"][0:1, :], start=True, stop=True,
                  skip_group_check=True)
    P.inc(mm, "PE")
    P.wait(v, "PE")
    P.inc(v.tensor_copy(Z["bbc_sb"][:, :], bc_ps[:, 48:64]), "VEC")
    P.inc(v.tensor_copy(Z["gbb_sb"][:, :], bc_ps[:, 64:80]), "VEC")
    P.wait(v, "VEC")
    P.wait(v, "PE")
    k2 = Z["k2_sb"]
    pc_all = k2[:, 0:64].bitcast(F32)
    cmp_all = k2[:, 64:576].bitcast(F32)
    nle_all = k2[:, 576:640].bitcast(F32)
    rr_all = k2[:, 640:704].bitcast(F32)
    ovg_all = k2[:, 704:768].bitcast(F32)
    cols = [(ep * 16 + i, ep, i) for ep in range(2) for i in range(CAP // 128)]
    for col, ep, i in cols:
        P.inc(v.tensor_scalar(pc_all[:, col:col + 1], Z["iota128_sb"][:, :],
                              float(128 * i), None, ALU.add), "VEC")
    P.wait(v, "VEC")
    for col, ep, i in cols:
        P.inc(v.tensor_scalar(cmp_all[:, 8 * col:8 * col + 8],
                              Z["bbc_sb"][:, 8 * ep:8 * ep + 8],
                              pc_all[:, col:col + 1], None, ALU.is_le), "VEC")
    P.wait(v, "VEC")
    for col, ep, i in cols:
        P.inc(v.reduce_sum(nle_all[:, col:col + 1],
                           cmp_all[:, 8 * col:8 * col + 8], axis=AX), "VEC")
    P.wait(v, "VEC")
    for col, ep, i in cols:
        P.inc(v.tensor_tensor(cmp_all[:, 8 * col:8 * col + 8],
                              cmp_all[:, 8 * col:8 * col + 8],
                              Z["bbc_sb"][:, 8 * ep:8 * ep + 8], ALU.mult),
              "VEC")
    P.wait(v, "VEC")
    for col, ep, i in cols:
        P.inc(v.reduce_max(rr_all[:, col:col + 1],
                           cmp_all[:, 8 * col:8 * col + 8], axis=AX), "VEC")
    P.wait(v, "VEC")
    for col, ep, i in cols:
        P.inc(v.tensor_sub(rr_all[:, col:col + 1], pc_all[:, col:col + 1],
                           rr_all[:, col:col + 1]), "VEC")
    P.wait(v, "VEC")
    for col, ep, i in cols:
        P.inc(v.tensor_scalar(ovg_all[:, col:col + 1], rr_all[:, col:col + 1],
                              float(LCAP), BIGIX, ALU.is_ge, ALU.mult), "VEC")
        P.inc(v.tensor_scalar(Z["gif"][:, col:col + 1], nle_all[:, col:col + 1],
                              2048.0, float(1024 * ep - 2048), ALU.mult,
                              ALU.add), "VEC")
    P.wait(v, "VEC")
    for col, ep, i in cols:
        P.inc(v.tensor_tensor(Z["gif"][:, col:col + 1],
                              Z["gif"][:, col:col + 1], rr_all[:, col:col + 1],
                              ALU.add), "VEC")
    P.wait(v, "VEC")
    for col, ep, i in cols:
        P.inc(v.tensor_tensor(Z["gif"][:, col:col + 1],
                              Z["gif"][:, col:col + 1], ovg_all[:, col:col + 1],
                              ALU.add), "VEC")
    P.wait(v, "VEC")
    for col, ep, i in cols:
        P.inc(v.tensor_copy(Z["giu"][:, col:col + 1],
                            Z["gif"][:, col:col + 1]), "VEC")
    P.wait(v, "VEC")



def make_ffn_loaders(nc, P, ZF, w1_ap, w2_ap):
    g = nc.gpsimd
    w1_sb, w2_sb = ZF["w1_sb"], ZF["w2_sb"]
    FS = {"w1_ready": [0, 0], "w1_freed": [0, 0],
          "w2_ready": [0, 0], "w2_freed": [0, 0]}

    def load_w1(bsel, e, hc):
        P.wait(g, "PE", FS["w1_freed"][bsel])
        ins = g.dma_start(
            out=w1_sb[bsel].rearrange("p (c f) -> p c f", c=NCH),
            in_=w1_ap[e, :, 128 * hc:128 * (hc + 1)].rearrange(
                "(c p) f -> p c f", p=128))
        FS["w1_ready"][bsel] = P.inc(ins, "W10" if bsel == 0 else "W11", 16)

    def load_w2(bsel, e, dc):
        P.wait(g, "PE", FS["w2_freed"][bsel])
        ins = g.dma_start(
            out=w2_sb[bsel].rearrange("p (c f) -> p c f", c=32),
            in_=w2_ap[e, :, 128 * dc:128 * (dc + 1)].rearrange(
                "(c p) f -> p c f", p=128))
        FS["w2_ready"][bsel] = P.inc(ins, "W20" if bsel == 0 else "W21", 16)

    return FS, load_w1, load_w2


def emit_ffn(nc, P, Z, ZF, loaders, fdst_ap, rsrc_ap, giu_done, bchk):
    g, sy, t, v, s = nc.gpsimd, nc.sync, nc.tensor, nc.vector, nc.scalar
    buf_fm = ZF["buf_fm"]
    bt_sb, yt_sb = ZF["bt_sb"], ZF["yt_sb"]
    w1_sb, w2_sb = ZF["w1_sb"], ZF["w2_sb"]
    h_sb, yfm_sb = ZF["h_sb"], ZF["yfm_sb"]
    h_ps, y_ps, tr_ps = Z["pvA_ps"], Z["pvB_ps"], Z["trb_ps"]
    b1_sb, b2_sb, idbf_sb = Z["b1c_sb"], Z["b2c_sb"], Z["idbf_sb"]
    FS, load_w1, load_w2 = loaders
    w1_ready, w1_freed = FS["w1_ready"], FS["w1_freed"]
    w2_ready, w2_freed = FS["w2_ready"], FS["w2_freed"]
    tr_after = {}
    vec_strip = {}

    for e in range(2):
        for i in range(CAP // 128):
            b = i % 2
            bts = "BT0" if b == 0 else "BT1"
            if i >= 2 or e > 0:
                P.wait(g, "PE", tr_after.get((e, i - 2), tr_after.get(
                    (e - 1, CAP // 128 + i - 2), 0)))
            P.wait(g, "VEC", giu_done)
            col = e * 16 + i
            P.inc(g.indirect_dma_start(
                out=bt_sb[b][:, :], out_offset=None,
                in_=fdst_ap,
                in_offset=IndirectOffsetOnAxis(Z["giu"][:, col:col + 1], 0),
                bounds_check=bchk, oob_is_err=False), bts, 16)
            P.wait(t, bts)
            P.wait(t, "LS", P.c["LS"])
            P.wait(t, "VEC")
            for dc in range(NCH):
                P.wait(t, "VEC")
                mm = t.transpose(tr_ps[0:128, 0:128],
                                 bt_sb[b][:, 128 * dc:128 * (dc + 1)],
                                 idbf_sb[:, :])
                P.inc(mm, "PE")
                P.wait(v, "PE")
                P.inc(v.tensor_copy(
                    buf_fm[:, CAP * dc + 128 * i:CAP * dc + 128 * (i + 1)],
                    tr_ps[0:128, 0:128]), "VEC")
            tr_after[(e, i)] = P.c["PE"]
            if i % 4 == 3:
                vec_strip[(e, i // 4)] = P.c["VEC"]
        if e > 0:
            load_w1(0, e, 0)
        for st in range(4):
            act_of = {}
            P.wait(t, "ACT")
            for hc in range(32):
                bsel = hc % 2
                if hc + 1 < 32:
                    load_w1((hc + 1) % 2, e, hc + 1)
                elif st + 1 < 4:
                    load_w1((hc + 1) % 2, e, 0)
                t.wait_ge(P.sem["W10" if bsel == 0 else "W11"], w1_ready[bsel])
                P.wait(t, "VEC", vec_strip[(e, st)])
                hps = h_ps if hc % 2 == 0 else y_ps
                if hc >= 2:
                    P.wait(t, "ACT", act_of.get(hc - 2, 0))
                for c in range(NCH):
                    mm = t.matmul(hps[:, :],
                                  w1_sb[bsel][:, 128 * c:128 * (c + 1)],
                                  buf_fm[:, CAP * c + 512 * st:
                                         CAP * c + 512 * (st + 1)],
                                  start=(c == 0), stop=(c == NCH - 1),
                                  skip_group_check=True)
                P.inc(mm, "PE")
                w1_freed[bsel] = P.c["PE"]
                P.wait(s, "PE")
                si = s.activation(h_sb[:, 512 * hc:512 * (hc + 1)], hps[:, :],
                                  AF.Relu,
                                  bias=b1_sb[:, 32 * e + hc:32 * e + hc + 1],
                                  scale=1.0)
                act_of[hc] = P.inc(si, "ACT")
            load_w2(0, e, 0)
            yact = {}
            for dc in range(NCH):
                bsel = dc % 2
                if dc + 1 < NCH:
                    load_w2((dc + 1) % 2, e, dc + 1)
                t.wait_ge(P.sem["W20" if bsel == 0 else "W21"], w2_ready[bsel])
                yps = h_ps if dc % 2 == 0 else y_ps
                if dc < 2:
                    P.wait(t, "ACT", act_of[31])
                else:
                    P.wait(t, "ACT", yact[dc - 2])
                for hc in range(32):
                    mm = t.matmul(yps[:, :],
                                  w2_sb[bsel][:, 128 * hc:128 * (hc + 1)],
                                  h_sb[:, 512 * hc:512 * (hc + 1)],
                                  start=(hc == 0), stop=(hc == 31),
                                  skip_group_check=True)
                P.inc(mm, "PE")
                w2_freed[bsel] = P.c["PE"]
                P.wait(s, "PE")
                si = s.activation(yfm_sb[:, 512 * dc:512 * (dc + 1)], yps[:, :],
                                  AF.Identity,
                                  bias=b2_sb[:, 8 * e + dc:8 * e + dc + 1],
                                  scale=1.0)
                yact[dc] = P.inc(si, "ACT")
            P.wait(t, "ACT")
            for j in range(4):
                jb = j % 2
                sys_ = "SY0" if jb == 0 else "SY1"
                buf = yt_sb[jb]
                first = True
                for dc in range(NCH):
                    P.wait(t, "VEC")
                    mm = t.transpose(tr_ps[0:128, 0:128],
                                     yfm_sb[:, 512 * dc + 128 * j:
                                            512 * dc + 128 * (j + 1)],
                                     idbf_sb[:, :])
                    P.inc(mm, "PE")
                    P.wait(v, "PE")
                    if first:
                        P.wait(v, sys_)
                    first = False
                    P.inc(v.tensor_copy(buf[:, 128 * dc:128 * (dc + 1)],
                                        tr_ps[0:128, 0:128]), "VEC")
                P.wait(g, "VEC")
                col = e * 16 + 4 * st + j
                P.inc(g.indirect_dma_start(
                    out=rsrc_ap,
                    out_offset=IndirectOffsetOnAxis(Z["giu"][:, col:col + 1], 0),
                    in_=buf[:, :], in_offset=None,
                    bounds_check=bchk, oob_is_err=False), sys_, 16)



def emit_phase_c(nc, P, Z, ZF, x2tm_ap, meta_ap, rdst_ap, out_ap, bchk,
                 pe_attn):
    g, sy, t, v, s = nc.gpsimd, nc.sync, nc.tensor, nc.vector, nc.scalar
    ybig = Z["x_sb"]
    x2all = Z["o_sb"]
    q2, k2 = Z["q2_sb"], Z["k2_sb"]
    eqall = q2[:, 0:1024].bitcast(F32)
    selall = q2[:, 1024:2048].bitcast(F32)
    lnall = q2[:, 2048:2560].bitcast(F32)
    mt_all = q2[:, 2560:2816].bitcast(F32)
    o2b = [k2[:, 0:2048].bitcast(F32), k2[:, 2048:4096].bitcast(F32)]
    zb = [Z["vaugA"][:, 0:1024], Z["vaugA"][:, 1024:2048],
          Z["vaugB"][:, 0:1024], Z["vaugB"][:, 1024:2048]]
    zdump = Z["expS"]

    P.wait(sy, "SX0")
    P.wait(sy, "SX1")
    P.wait(sy, "SY0")
    P.wait(sy, "SY1")
    P.wait(sy, "ST")
    P.wait(sy, "PE")
    P.wait(sy, "VEC")
    P.wait(sy, "ACT")
    for i in range(0, 9):
        P.inc(sy.dma_start(out=x2all[:, 1024 * i:1024 * (i + 1)],
                           in_=x2tm_ap[128 * i:128 * (i + 1), :]), "LT0", 16)
    P.wait(v, "PE", pe_attn)
    P.wait(v, "LT0", 16)
    P.wait(v, "VEC")
    for i in range(NST):
        P.inc(v.tensor_scalar(eqall[:, 16 * i:16 * (i + 1)], Z["iota_sb"][:, :],
                              mt_all[:, 4 * i:4 * i + 1], None, ALU.is_equal),
              "VEC")
    P.wait(v, "VEC")
    for i in range(NST):
        P.inc(v.tensor_tensor(selall[:, 16 * i:16 * (i + 1)],
                              eqall[:, 16 * i:16 * (i + 1)], Z["gbb_sb"][:, :],
                              ALU.mult), "VEC")
    P.wait(v, "VEC")
    for i in range(NST):
        P.inc(v.reduce_sum(lnall[:, 8 * i + 6:8 * i + 7],
                           selall[:, 16 * i:16 * (i + 1)], axis=AX), "VEC")
    P.wait(v, "VEC")
    for i in range(NST):
        P.inc(v.tensor_tensor(lnall[:, 8 * i + 6:8 * i + 7],
                              lnall[:, 8 * i + 6:8 * i + 7],
                              mt_all[:, 4 * i + 1:4 * i + 2], ALU.add), "VEC")
    P.wait(v, "VEC")
    for i in range(NST):
        P.inc(v.tensor_scalar(lnall[:, 8 * i + 7:8 * i + 8],
                              lnall[:, 8 * i + 6:8 * i + 7], float(CAP),
                              BIGIX, ALU.is_ge, ALU.mult), "VEC")
    P.wait(v, "VEC")
    for i in range(NST):
        P.inc(v.tensor_tensor(Z["fif"][:, i:i + 1], Z["fif"][:, i:i + 1],
                              lnall[:, 8 * i + 7:8 * i + 8], ALU.add), "VEC")
    P.wait(v, "VEC")
    for i in range(NST):
        P.inc(v.tensor_copy(Z["fiu"][:, i:i + 1], Z["fif"][:, i:i + 1]),
              "VEC")
    P.inc(v.memset(ybig[:, :], 0.0), "VEC")
    P.wait(g, "PE")
    P.wait(g, "VEC")
    for i in range(9, 18):
        P.inc(g.dma_start(out=x2all[:, 1024 * i:1024 * (i + 1)],
                          in_=x2tm_ap[128 * i:128 * (i + 1), :]), "LT1", 16)
    P.wait(g, "CCR")
    for i in range(NST):
        P.inc(g.indirect_dma_start(
            out=ybig[:, 1024 * i:1024 * (i + 1)], out_offset=None,
            in_=rdst_ap,
            in_offset=IndirectOffsetOnAxis(Z["fiu"][:, i:i + 1], 0),
            bounds_check=bchk, oob_is_err=False), "GB0", 16)
    P.wait(v, "GB0")
    P.wait(v, "LT0")
    P.wait(v, "LT1")
    zv = {}
    sums = {}
    o2v = {}
    for i in range(NST):
        zbi = zb[i % 4]
        obi = o2b[i % 2]
        if i >= 4:
            P.wait(v, "VEC", o2v[i - 4])
        zv[i] = P.inc(v.scalar_tensor_tensor(
            zbi[:, :], ybig[:, 1024 * i:1024 * (i + 1)],
            mt_all[:, 4 * i + 2:4 * i + 3],
            x2all[:, 1024 * i:1024 * (i + 1)], ALU.mult, ALU.add), "VEC")
        s.wait_ge(P.sem["VEC"], zv[i])
        P.inc(s.activation(zdump[0][:, :], zbi[:, :], AF.Identity,
                           accum_out=lnall[:, 8 * i:8 * i + 1]), "ACT")
        sums[i] = P.inc(s.activation(zdump[1][:, :], zbi[:, :], AF.Square,
                                     accum_out=lnall[:, 8 * i + 1:8 * i + 2]),
                        "ACT")
        v.wait_ge(P.sem["ACT"], sums[i])
        P.inc(v.tensor_scalar(lnall[:, 8 * i + 2:8 * i + 3],
                              lnall[:, 8 * i:8 * i + 1], 1.0 / D, None,
                              ALU.mult), "VEC")
        P.wait(v, "VEC")
        P.inc(v.tensor_scalar(lnall[:, 8 * i + 4:8 * i + 5],
                              lnall[:, 8 * i + 2:8 * i + 3],
                              lnall[:, 8 * i + 2:8 * i + 3], None, ALU.mult),
              "VEC")
        P.wait(v, "VEC")
        vv = P.inc(v.tensor_scalar(lnall[:, 8 * i + 3:8 * i + 4],
                                   lnall[:, 8 * i + 1:8 * i + 2], 1.0 / D,
                                   lnall[:, 8 * i + 4:8 * i + 5], ALU.mult,
                                   ALU.subtract), "VEC")
        s.wait_ge(P.sem["VEC"], vv)
        sq = P.inc(s.activation(lnall[:, 8 * i + 5:8 * i + 6],
                                lnall[:, 8 * i + 3:8 * i + 4], AF.Sqrt,
                                bias=Z["c5_sb"][:, 0:1], scale=1.0), "ACT")
        v.wait_ge(P.sem["ACT"], sq)
        P.inc(v.reciprocal(lnall[:, 8 * i + 5:8 * i + 6],
                           lnall[:, 8 * i + 5:8 * i + 6]), "VEC")
        P.wait(v, "VEC")
        if i >= 2:
            P.wait(v, "SO", 16 * (i - 1))
        o2v[i] = P.inc(v.tensor_scalar(obi[:, :], zbi[:, :],
                                       lnall[:, 8 * i + 2:8 * i + 3],
                                       lnall[:, 8 * i + 5:8 * i + 6],
                                       ALU.subtract, ALU.mult), "VEC")
        sy.wait_ge(P.sem["VEC"], o2v[i])
        P.inc(sy.dma_start(out=out_ap[128 * i:128 * (i + 1), :],
                           in_=obi[:, :]), "SO", 16)
    P.wait(sy, "SO")



def build_fused():
    nc = bass.Bass("TRN2", target_bir_lowering=False, debug=False)
    xT = nc.dram_tensor("xT", [D, T], F32, kind="ExternalInput").ap()
    wqT = nc.dram_tensor("wqT", [D, D], F32, kind="ExternalInput").ap()
    wkT = nc.dram_tensor("wkT", [D, D], F32, kind="ExternalInput").ap()
    wvT = nc.dram_tensor("wvT", [D, D], F32, kind="ExternalInput").ap()
    woT = nc.dram_tensor("woT", [D, D], F32, kind="ExternalInput").ap()
    boc = nc.dram_tensor("boc", [128, NCH], F32, kind="ExternalInput").ap()
    wg = nc.dram_tensor("wg", [D, E], F32, kind="ExternalInput").ap()
    bgc = nc.dram_tensor("bgc", [E, 1], F32, kind="ExternalInput").ap()
    idbf = nc.dram_tensor("idbf", [128, 128], BF16, kind="ExternalInput").ap()
    id16 = nc.dram_tensor("id16", [16, 16], F32, kind="ExternalInput").ap()
    iota16b = nc.dram_tensor("iota16b", [128, 16], F32, kind="ExternalInput").ap()
    r16m = nc.dram_tensor("r16m", [128, 16], F32, kind="ExternalInput").ap()
    onesb = nc.dram_tensor("onesb", [128, 128], BF16, kind="ExternalInput").ap()
    onesf = nc.dram_tensor("onesf", [128, 128], F32, kind="ExternalInput").ap()
    tri8 = nc.dram_tensor("tri8", [8, 8], F32, kind="ExternalInput").ap()
    eh01 = nc.dram_tensor("eh01", [16, 2], F32, kind="ExternalInput").ap()
    pmask = nc.dram_tensor("pmask", [8, 1], F32, kind="ExternalInput").ap()
    iota128 = nc.dram_tensor("iota128", [128, 1], F32, kind="ExternalInput").ap()
    w1 = nc.dram_tensor("w1", [2, D, FH], F32, kind="ExternalInput").ap()
    w2 = nc.dram_tensor("w2", [2, FH, D], F32, kind="ExternalInput").ap()
    b1c = nc.dram_tensor("b1c", [128, 64], F32, kind="ExternalInput").ap()
    b2c = nc.dram_tensor("b2c", [128, 16], F32, kind="ExternalInput").ap()
    out = nc.dram_tensor("out", [T, D], F32, kind="ExternalOutput").ap()
    x2tm = nc.dram_tensor("x2tm", [T, D], BF16).ap()
    meta = nc.dram_tensor("meta", [T, 4], F32).ap()
    cnts1 = nc.dram_tensor("cnts1", [1, E], F32).ap()
    cnts8 = nc.dram_tensor("cnts8", [8, E], F32).ap()
    fsrc = nc.dram_tensor("fsrc", [NROW, D], BF16).ap()
    fdst = nc.dram_tensor("fdst", [NROW, D], BF16).ap()
    rsrc = nc.dram_tensor("rsrc", [NROW, D], BF16).ap()
    rdst = nc.dram_tensor("rdst", [NROW, D], BF16).ap()

    es = ExitStack()
    Z = alloc_l1(nc, es)
    ZF = alloc_overlays(Z)
    P = Prog(nc, es, SEMS)
    g, sy = nc.gpsimd, nc.sync
    bchk = nc.alloc_register(mybir.EngineType.Pool, "bchk")
    g.reg_mov(bchk, NROW - 1)

    for c in range(NCH):
        P.inc(g.dma_start(out=Z["x_sb"][:, T * c: T * (c + 1)],
                          in_=xT[128 * c:128 * (c + 1), :]), "LX", 16)
    emit_l1_const_loads(nc, P, Z, dict(
        woT=woT, wg=wg, boc=boc, bgc=bgc, idbf=idbf, id16=id16,
        iota16b=iota16b, r16m=r16m, onesb=onesb, onesf=onesf, tri8=tri8,
        eh01=eh01, pmask=pmask, iota128=iota128, b1c=b1c, b2c=b2c))
    lx_ready = P.c["LX"]
    S = new_l1_state()
    prefetch = make_headpair_prefetch(nc, P, Z, S, wqT, wkT, wvT)
    emit_l1_init(nc, P, Z)
    with nc.named_scope("l1"):
        emit_l1_batch(nc, P, Z, S, prefetch, x2tm, meta, 0, lx_ready, fsrc,
                      bchk)
    pe_attn = P.c["PE"]

    with nc.named_scope("prolog_b"):
        emit_prolog_b(nc, P, Z, cnts1, cnts8)
    giu_done = P.c["VEC"]

    P.wait(g, "SX0")
    P.wait(g, "SX1")
    P.wait(g, "ST")
    P.wait(g, "PE")
    P.wait(g, "VEC")
    loaders = make_ffn_loaders(nc, P, ZF, w1, w2)
    loaders[1](0, 0, 0)
    P.wait(sy, "SX0")
    P.wait(sy, "SX1")
    P.wait(sy, "ST")
    P.wait(sy, "PE")
    P.wait(sy, "VEC")
    mt_all_early = Z["q2_sb"][:, 2560:2816].bitcast(F32)
    P.inc(sy.dma_start(out=mt_all_early.rearrange("p (i c) -> p i c", c=4),
                       in_=meta.rearrange("(i p) c -> p i c", p=128)),
          "LT0", 16)
    for i in range(18, NST):
        P.inc(sy.dma_start(out=Z["o_sb"][:, 1024 * i:1024 * (i + 1)],
                           in_=x2tm[128 * i:128 * (i + 1), :]), "LT0", 16)

    with nc.named_scope("a2a_fwd"):
        with allow_ici_all_to_all():
            cc = g.collective_compute(
                "AllToAll", ALU.bypass, replica_groups=[list(range(NC))],
                ins=[fsrc.opt()], outs=[fdst.opt()])
        P.inc(cc, "CCF", 1)
        P.wait(g, "CCF")

    with nc.named_scope("ffn"):
        emit_ffn(nc, P, Z, ZF, loaders, fdst, rsrc, giu_done, bchk)

    with nc.named_scope("a2a_ret"):
        P.wait(g, "SY0")
        P.wait(g, "SY1")
        with allow_ici_all_to_all():
            cc = g.collective_compute(
                "AllToAll", ALU.bypass, replica_groups=[list(range(NC))],
                ins=[rsrc.opt()], outs=[rdst.opt()])
        P.inc(cc, "CCR", 1)

    with nc.named_scope("phase_c"):
        emit_phase_c(nc, P, Z, ZF, x2tm, meta, rdst, out, bchk, pe_attn)

    es.close()
    return nc



_NC_CACHE = {}


def _get(name, builder):
    if name not in _NC_CACHE:
        _NC_CACHE[name] = builder()
    return _NC_CACHE[name]


def kernel(L0_attn_x, L0_attn_Wq, L0_attn_Wk, L0_attn_Wv, L0_attn_Wo,
           L0_attn_bo, L0_ffn_Wg, L0_ffn_bg, L0_ffn_W1, L0_ffn_b1,
           L0_ffn_W2, L0_ffn_b2):
    x = np.asarray(L0_attn_x, dtype=np.float32)
    wqT = np.ascontiguousarray(np.asarray(L0_attn_Wq, np.float32).T)
    wkT = np.ascontiguousarray(np.asarray(L0_attn_Wk, np.float32).T)
    wvT = np.ascontiguousarray(np.asarray(L0_attn_Wv, np.float32).T)
    woT = np.ascontiguousarray(np.asarray(L0_attn_Wo, np.float32).T)
    bo = np.asarray(L0_attn_bo, np.float32)
    wg = np.ascontiguousarray(np.asarray(L0_ffn_Wg, np.float32))
    bg = np.asarray(L0_ffn_bg, np.float32)
    W1 = np.asarray(L0_ffn_W1, np.float32)
    b1 = np.asarray(L0_ffn_b1, np.float32)
    W2 = np.asarray(L0_ffn_W2, np.float32)
    b2 = np.asarray(L0_ffn_b2, np.float32)

    boc = np.ascontiguousarray(bo.reshape(NCH, 128).T)
    bgc = np.ascontiguousarray(bg.reshape(E, 1))
    idbf = np.eye(128, dtype=np.float32).astype(BF16NP)
    id16 = np.eye(16, dtype=np.float32)
    iota16b = np.ascontiguousarray(np.tile(np.arange(16, dtype=np.float32),
                                           (128, 1)))
    r16m = np.ascontiguousarray(16.0 - iota16b)
    onesb = np.ones((128, 128), np.float32).astype(BF16NP)
    onesf = np.ones((128, 128), np.float32)
    tri8 = np.triu(np.ones((8, 8), np.float32), 1)
    iota128 = np.arange(128, dtype=np.float32).reshape(128, 1)

    nc1 = _get("fused", build_fused)
    in_maps = []
    for c in range(NC):
        eh01 = np.zeros((16, 2), np.float32)
        eh01[2 * c, 0] = 1.0
        eh01[2 * c + 1, 1] = 1.0
        b1cc = np.ascontiguousarray(
            np.concatenate([b1[2 * c + e].reshape(32, 128).T for e in range(2)],
                           axis=1))
        b2cc = np.ascontiguousarray(
            np.concatenate([b2[2 * c + e].reshape(NCH, 128).T for e in range(2)],
                           axis=1))
        in_maps.append({
            "xT": np.ascontiguousarray(x[c].T),
            "wqT": wqT, "wkT": wkT, "wvT": wvT, "woT": woT,
            "boc": boc, "wg": wg, "bgc": bgc,
            "idbf": idbf, "id16": id16, "iota16b": iota16b, "r16m": r16m,
            "onesb": onesb, "onesf": onesf, "tri8": tri8, "eh01": eh01,
            "pmask": (np.arange(8) < c).astype(np.float32).reshape(8, 1),
            "iota128": iota128,
            "w1": np.ascontiguousarray(W1[2 * c:2 * c + 2]),
            "w2": np.ascontiguousarray(W2[2 * c:2 * c + 2]),
            "b1c": b1cc, "b2c": b2cc,
        })
    res = run_bass_kernel_spmd(nc1, in_maps, core_ids=list(range(NC))).results
    return np.stack([res[c]["out"] for c in range(NC)], axis=0)


# version 09
# speedup vs baseline: 1.0301x; 1.0126x over previous
import numpy as np
import ml_dtypes
from contextlib import ExitStack

import concourse.bass as bass
import concourse.mybir as mybir
from concourse.bass import IndirectOffsetOnAxis
from concourse.bass_utils import run_bass_kernel_spmd
from concourse.aot_env import allow_ici_all_to_all

F32 = mybir.dt.float32
BF16 = mybir.dt.bfloat16
U32 = mybir.dt.uint32
AF = mybir.ActivationFunctionType
ALU = mybir.AluOpType
AX = mybir.AxisListType.X
BF16NP = ml_dtypes.bfloat16

D = 1024
T = 4096
H = 16
NC = 8
E = 16
CAP = 2048
LCAP = 1024
FH = 4096
NCH = D // 128
NTT = T // 512
NST = T // 128
BIG = float(1 << 26)
BIGIX = 1.0e6
NROW = E * LCAP


class Prog:
    def __init__(self, nc, es, sems):
        self.nc = nc
        self.c = {}
        self.sem = {}
        for name in sems:
            self.sem[name] = es.enter_context(nc.semaphore(name))
            self.c[name] = 0

    def inc(self, instr, name, k=1):
        self.c[name] += k
        instr.then_inc(self.sem[name], k)
        return self.c[name]

    def wait(self, eng, name, v=None):
        eng.wait_ge(self.sem[name], self.c[name] if v is None else v)


SEMS = ["LX", "LS", "WH0", "WH1", "PE", "ACT", "VEC", "ST",
        "SX0", "SX1", "SST", "SEXP", "SPV",
        "CN", "CCA", "CCF", "CCR",
        "BT0", "BT1", "W10", "W11", "W20", "W21", "SY0", "SY1",
        "LT0", "LT1", "GB0", "GB1", "SO"]


def alloc_l1(nc, es):
    sb = lambda name, shape, dt: es.enter_context(nc.sbuf_tensor(name, shape, dt))
    ps = lambda name, shape, dt: es.enter_context(nc.psum_tensor(name, shape, dt))
    Z = {}
    Z["x_sb"] = sb("x_sb", [128, NCH * T], BF16)
    Z["o_sb"] = sb("o_sb", [128, NCH * T], BF16)
    Z["wo_sb"] = sb("wo_sb", [128, NCH * D], BF16)
    Z["wh_sb"] = [sb(f"wh_sb{i}", [128, 3 * D], BF16) for i in range(2)]
    Z["wg_sb"] = sb("wg_sb", [128, NCH * E], BF16)
    Z["boc_sb"] = sb("boc_sb", [128, NCH], F32)
    Z["bgc_sb"] = sb("bgc_sb", [16, 1], F32)
    Z["idbf_sb"] = sb("idbf_sb", [128, 128], BF16)
    Z["id16_sb"] = sb("id16_sb", [16, 16], F32)
    Z["iota_sb"] = sb("iota_sb", [128, 16], F32)
    Z["r16m_sb"] = sb("r16m_sb", [128, 16], F32)
    Z["ones_sb"] = sb("ones_sb", [128, 128], BF16)
    Z["onesf_sb"] = sb("onesf_sb", [128, 128], F32)
    Z["k2_sb"] = sb("k2_sb", [128, T], BF16)
    Z["q2_sb"] = sb("q2_sb", [128, T], BF16)
    Z["v2tmp"] = sb("v2tmp", [128, 512], BF16)
    Z["vaugA"] = sb("vaugA", [128, 65 * 32], BF16)
    Z["vaugB"] = sb("vaugB", [128, 65 * 32], BF16)
    Z["expS"] = [sb(f"expS{i}", [128, 1024], BF16) for i in range(2)]
    Z["rden_bf"] = sb("rden_bf", [65, 512], BF16)
    Z["bc_sb"] = sb("bc_sb", [128, 512], F32)
    Z["ysq_sb"] = sb("ysq_sb", [128, 512], BF16)
    Z["mean_sb"] = sb("mean_sb", [1, 512], F32)
    Z["var_sb"] = sb("var_sb", [1, 512], F32)
    Z["mean_bf"] = sb("mean_bf", [1, 512], BF16)
    Z["rstd_bf"] = sb("rstd_bf", [1, 512], BF16)
    Z["rbc_sb"] = sb("rbc_sb", [128, 512], F32)
    Z["lg_tt"] = sb("lg_tt", [16, 512], F32)
    Z["msq_sb"] = Z["lg_tt"][0:1, 0:512]
    Z["mbc_sb"] = Z["bc_sb"]
    Z["mx_sb"] = sb("mx_sb", [128, 1], F32)
    Z["nmx_sb"] = sb("nmx_sb", [128, 1], F32)
    Z["sm_sb"] = sb("sm_sb", [128, 1], F32)
    Z["ejunk_sb"] = sb("ejunk_sb", [128, 16], F32)
    Z["eq_sb"] = sb("eq_sb", [128, 16], F32)
    Z["cand_sb"] = sb("cand_sb", [128, 16], F32)
    Z["t1_sb"] = sb("t1_sb", [128, 1], F32)

    Z["ohb_sb"] = sb("ohb_sb", [128, 16], BF16)
    Z["cnt_sb"] = sb("cnt_sb", [16, 8], F32)
    Z["sel_sb"] = sb("sel_sb", [128, 16], F32)
    Z["cs_sb"] = sb("cs_sb", [128, 1], F32)

    Z["c5_sb"] = sb("c5_sb", [128, 1], F32)
    Z["c8_sb"] = sb("c8_sb", [128, 1], F32)
    Z["csum_sb"] = Z["o_sb"][0:16, 0:8192].bitcast(F32)
    Z["ohfm_sb"] = Z["o_sb"][0:16, 8192:12288]
    Z["lg_tm"] = Z["o_sb"][:, 12288:13312].bitcast(F32)
    Z["ohf_all"] = Z["o_sb"][:, 13312:14336].bitcast(F32)
    Z["meta_sb"] = Z["o_sb"][:, 14336:14592].bitcast(F32)
    Z["idx_all"] = Z["o_sb"][:, 14592:14656].bitcast(F32)
    Z["x2st"] = [Z["o_sb"][:, 16384:17408], Z["o_sb"][:, 17408:18432]]
    Z["fif"] = sb("fif", [128, 32], F32)
    Z["fiu"] = sb("fiu", [128, 32], U32)
    Z["gif"] = sb("gif", [128, 32], F32)
    Z["giu"] = sb("giu", [128, 32], U32)
    Z["cnts8_sb"] = sb("cnts8_sb", [8, 16], F32)
    Z["tri8_sb"] = sb("tri8_sb", [8, 8], F32)
    Z["eh01_sb"] = sb("eh01_sb", [16, 2], F32)
    Z["pmask_sb"] = sb("pmask_sb", [8, 1], F32)
    Z["iota128_sb"] = sb("iota128_sb", [128, 1], F32)
    Z["baseT_sb"] = sb("baseT_sb", [16, 8], F32)
    Z["rowsel_sb"] = sb("rowsel_sb", [1, 16], F32)
    Z["gbrow_sb"] = sb("gbrow_sb", [1, 16], F32)
    Z["bbc_sb"] = sb("bbc_sb", [128, 16], F32)
    Z["gbb_sb"] = sb("gbb_sb", [128, 16], F32)
    Z["pc_sb"] = sb("pc_sb", [128, 1], F32)
    Z["cmp8_sb"] = sb("cmp8_sb", [128, 8], F32)
    Z["bsc_sb"] = sb("bsc_sb", [128, 8], F32)
    Z["nle_sb"] = sb("nle_sb", [128, 1], F32)
    Z["rr_sb"] = sb("rr_sb", [128, 1], F32)
    Z["ovg_sb"] = sb("ovg_sb", [128, 1], F32)
    Z["b1c_sb"] = sb("b1c_sb", [128, 64], F32)
    Z["b2c_sb"] = sb("b2c_sb", [128, 16], F32)
    Z["ln2_sb"] = sb("ln2_sb", [128, 8], F32)
    Z["st_ps"] = [ps("st_ps0", [128, 1024], F32), ps("st_ps1", [128, 1024], F32)]
    Z["pvA_ps"] = ps("pvA_ps", [128, 512], F32)
    Z["pvB_ps"] = ps("pvB_ps", [128, 512], F32)
    Z["bc_ps"] = ps("bc_ps", [128, 512], F32)
    Z["trb_ps"] = ps("trb_ps", [128, 1024], BF16)
    return Z


def alloc_overlays(Z):
    ZF = {}
    x, o = Z["x_sb"], Z["o_sb"]
    ZF["buf_fm"] = x[:, 0:16384]
    ZF["h_sb"] = x[:, 16384:32768]
    ZF["yfm_sb"] = o[:, 0:4096]
    ZF["bt_sb"] = [o[:, 4096:5120], o[:, 5120:6144]]
    ZF["yt_sb"] = [o[:, 6144:7168], o[:, 7168:8192]]
    ZF["w1_sb"] = [o[:, 8192:9216], o[:, 9216:10240]]
    ZF["w2_sb"] = [o[:, 10240:14336], o[:, 14336:18432]]
    k2, q2 = Z["k2_sb"], Z["q2_sb"]
    ZF["x2t2"] = [k2[:, 0:1024], k2[:, 1024:2048]]
    ZF["yb2"] = [k2[:, 2048:3072], k2[:, 3072:4096]]
    ZF["mt2"] = [q2[:, 0:8].bitcast(F32), q2[:, 8:16].bitcast(F32)]
    ZF["z_sb"] = Z["vaugA"][:, 0:1024]
    ZF["zc_sb"] = Z["vaugA"][:, 1024:2048]
    ZF["zsq_sb"] = Z["vaugB"][:, 0:1024]
    ZF["o2_sb"] = q2[:, 2048:4096].bitcast(F32)
    return ZF


def new_l1_state():
    return {
        "gg": 0,
        "vec_bc_free": 0,
        "vec_pvA_free": 0,
        "vec_pvB_free": 0,
        "wh_ready": [0, 0],
        "wh_freed": [0, 0],
    }


def emit_l1_init(nc, P, Z):
    v, s, t = nc.vector, nc.scalar, nc.tensor
    P.wait(v, "LS")
    P.inc(v.memset(Z["vaugA"][:, :], 1.0), "VEC")
    P.inc(v.memset(Z["vaugB"][:, :], 1.0), "VEC")
    P.inc(v.memset(Z["c5_sb"][:, :], 1e-5), "VEC")
    P.inc(v.memset(Z["c8_sb"][:, :], 1e-8), "VEC")
    P.wait(s, "VEC")
    P.wait(t, "VEC")


def emit_l1_batch(nc, P, Z, S, prefetch, x2tm_ap, meta_ap, cnt_col, lx_ready,
                  fsrc_ap, bchk):
    g, sy, t, v, s = nc.gpsimd, nc.sync, nc.tensor, nc.vector, nc.scalar
    st_ps, pvA_ps, pvB_ps = Z["st_ps"], Z["pvA_ps"], Z["pvB_ps"]
    bc_ps, trb_ps = Z["bc_ps"], Z["trb_ps"]
    x_sb, o_sb = Z["x_sb"], Z["o_sb"]
    k2_sb, q2_sb = Z["k2_sb"], Z["q2_sb"]
    expS = Z["expS"]
    ones_sb = Z["ones_sb"]

    for hp in range(8):
        bsel = hp % 2
        wh_sem = "WH0" if bsel == 0 else "WH1"
        prefetch(hp)
        t.wait_ge(P.sem[wh_sem], S["wh_ready"][bsel])
        P.wait(t, "LX", lx_ready)
        P.wait(t, "LS")
        for mi, dst_sb in ((0, k2_sb), (1, q2_sb)):
            for tt in range(NTT):
                pp = bc_ps if tt % 2 == 0 else pvA_ps
                P.wait(t, "VEC",
                       S["vec_bc_free"] if tt % 2 == 0 else S["vec_pvA_free"])
                for c in range(NCH):
                    mm = t.matmul(pp[:, :],
                                  Z["wh_sb"][bsel][:, D * mi + 128 * c:
                                                   D * mi + 128 * (c + 1)],
                                  x_sb[:, T * c + 512 * tt: T * c + 512 * (tt + 1)],
                                  start=(c == 0), stop=(c == NCH - 1))
                P.inc(mm, "PE")
                P.wait(v, "PE")
                vi = P.inc(v.tensor_copy(dst_sb[:, 512 * tt:512 * (tt + 1)],
                                         pp[:, :]), "VEC")
                if tt % 2 == 0:
                    S["vec_bc_free"] = vi
                else:
                    S["vec_pvA_free"] = vi
        for tt in range(NTT):
            pp = bc_ps if tt % 2 == 0 else pvA_ps
            P.wait(t, "VEC",
                   S["vec_bc_free"] if tt % 2 == 0 else S["vec_pvA_free"])
            for c in range(NCH):
                mm = t.matmul(pp[:, :],
                              Z["wh_sb"][bsel][:, 2 * D + 128 * c:
                                               2 * D + 128 * (c + 1)],
                              x_sb[:, T * c + 512 * tt: T * c + 512 * (tt + 1)],
                              start=(c == 0), stop=(c == NCH - 1))
            P.inc(mm, "PE")
            P.wait(v, "PE")
            vi = P.inc(v.tensor_copy(Z["v2tmp"][:, :], pp[:, :]), "VEC")
            if tt % 2 == 0:
                S["vec_bc_free"] = vi
            else:
                S["vec_pvA_free"] = vi
            for j in range(4):
                sc = 4 * tt + j
                P.wait(t, "VEC")
                mm = t.transpose(trb_ps[0:128, 0:128],
                                 Z["v2tmp"][:, 128 * j:128 * (j + 1)],
                                 Z["idbf_sb"][:, :])
                P.inc(mm, "PE")
                P.wait(v, "PE")
                P.inc(v.tensor_copy(Z["vaugA"][:, 65 * sc:65 * sc + 64],
                                    trb_ps[0:128, 0:64]), "VEC")
                P.inc(v.tensor_copy(Z["vaugB"][:, 65 * sc:65 * sc + 64],
                                    trb_ps[0:128, 64:128]), "VEC")
        S["wh_freed"][bsel] = P.c["PE"]
        vec_vaug = P.c["VEC"]

        for tt in range(NTT):
            g0 = S["gg"]
            tsl = slice(512 * tt, 512 * (tt + 1))
            for lg in range(32):
                cgg = g0 + lg
                if cgg >= 2:
                    P.wait(t, "SEXP", cgg - 1)
                if lg == 0:
                    P.wait(t, "VEC", vec_vaug)
                mm = t.matmul(st_ps[cgg % 2][:, 0:512],
                              k2_sb[0:64, 128 * lg:128 * (lg + 1)],
                              q2_sb[0:64, tsl], start=True, stop=True,
                              tile_position=(0, 0))
                mm = t.matmul(st_ps[cgg % 2][:, 512:1024],
                              k2_sb[64:128, 128 * lg:128 * (lg + 1)],
                              q2_sb[64:128, tsl], start=True, stop=True,
                              tile_position=(64, 0))
                P.inc(mm, "SST")
                if lg >= 1:
                    P.wait(t, "SEXP", cgg)
                    pvg = lg - 1
                    if pvg == 0:
                        P.wait(t, "VEC", S["vec_pvA_free"])
                        P.wait(t, "VEC", S["vec_pvB_free"])
                    mm = t.matmul(pvA_ps[0:65, :],
                                  Z["vaugA"][:, 65 * pvg:65 * pvg + 65],
                                  expS[(g0 + pvg) % 2][:, 0:512],
                                  start=(pvg == 0), stop=False,
                                  skip_group_check=True)
                    mm = t.matmul(pvB_ps[0:65, :],
                                  Z["vaugB"][:, 65 * pvg:65 * pvg + 65],
                                  expS[(g0 + pvg) % 2][:, 512:1024],
                                  start=(pvg == 0), stop=False,
                                  skip_group_check=True)
                    P.inc(mm, "SPV")
            P.wait(t, "SEXP", g0 + 32)
            mm = t.matmul(pvA_ps[0:65, :], Z["vaugA"][:, 65 * 31:65 * 31 + 65],
                          expS[(g0 + 31) % 2][:, 0:512],
                          start=False, stop=True, skip_group_check=True)
            mm = t.matmul(pvB_ps[0:65, :], Z["vaugB"][:, 65 * 31:65 * 31 + 65],
                          expS[(g0 + 31) % 2][:, 512:1024],
                          start=False, stop=True, skip_group_check=True)
            P.inc(mm, "SPV")
            for lg in range(32):
                cgg = g0 + lg
                P.wait(s, "SST", cgg + 1)
                if cgg >= 2:
                    P.wait(s, "SPV", cgg - 1)
                si = s.activation(expS[cgg % 2][:, :], st_ps[cgg % 2][:, :],
                                  AF.Exp, bias=0.0, scale=0.125)
                P.inc(si, "SEXP")
            S["gg"] = g0 + 32
            for head, pv in ((0, pvA_ps), (1, pvB_ps)):
                bcx = Z["bc_sb"] if head == 0 else Z["rbc_sb"]
                dr = slice(0, 1) if head == 0 else slice(64, 65)
                po = head * 64
                ocols = slice(T * hp + 512 * tt, T * hp + 512 * (tt + 1))
                P.wait(v, "SPV", S["gg"])
                with nc.allow_low_precision(reason="bf16 stage for matmul"):
                    den_cast = P.inc(v.tensor_copy(Z["rden_bf"][dr, :],
                                                   pv[64:65, :]), "VEC")
                with nc.allow_low_precision(reason="unnormalised bf16 stage"):
                    vi = P.inc(v.tensor_copy(o_sb[po:po + 64, ocols],
                                             pv[0:64, :]), "VEC")
                if head == 0:
                    S["vec_pvA_free"] = vi
                else:
                    S["vec_pvB_free"] = vi
                P.wait(t, "VEC", den_cast)
                P.wait(t, "VEC", S["vec_bc_free"])
                mm = t.matmul(bc_ps[:, :], ones_sb[dr, 0:128],
                              Z["rden_bf"][dr, :], start=True, stop=True,
                              skip_group_check=True)
                P.inc(mm, "PE")
                P.wait(v, "PE")
                P.wait(v, "VEC")
                P.inc(v.tensor_scalar(bcx[:, :], bc_ps[:, :],
                                      1e-8, None, ALU.add), "VEC")
                S["vec_bc_free"] = P.c["VEC"]
                P.wait(v, "VEC")
                P.inc(v.reciprocal(bcx[:, :], bcx[:, :]), "VEC")
                P.wait(v, "VEC")
                P.inc(v.tensor_tensor(o_sb[po:po + 64, ocols],
                                      o_sb[po:po + 64, ocols],
                                      bcx[po:po + 64, :], ALU.mult), "VEC")

    P.wait(t, "SEXP", S["gg"])
    P.wait(t, "VEC", S["vec_pvA_free"])
    P.wait(t, "VEC", S["vec_pvB_free"])
    for tt in range(NTT):
        for dc in range(NCH):
            wv = st_ps[0][:, 0:512] if dc % 2 == 0 else st_ps[0][:, 512:1024]
            P.wait(t, "VEC")
            for r in range(NCH):
                mm = t.matmul(wv,
                              Z["wo_sb"][:, D * r + 128 * dc: D * r + 128 * (dc + 1)],
                              o_sb[:, T * r + 512 * tt: T * r + 512 * (tt + 1)],
                              start=(r == 0), stop=(r == NCH - 1),
                              skip_group_check=True)
            P.inc(mm, "PE")
            P.wait(v, "PE")
            xv = x_sb[:, T * dc + 512 * tt: T * dc + 512 * (tt + 1)]
            P.inc(v.scalar_tensor_tensor(xv, wv, Z["boc_sb"][:, dc:dc + 1], xv,
                                         ALU.add, ALU.add), "VEC")
            P.wait(t, "VEC")
            mm = t.matmul(pvA_ps[0:1, :], ones_sb[:, 0:1], xv,
                          start=(dc == 0), stop=(dc == NCH - 1),
                          skip_group_check=True)
            P.inc(mm, "PE")
            P.wait(s, "VEC")
            P.inc(s.activation(Z["ysq_sb"][:, :], xv, AF.Square), "ACT")
            P.wait(t, "ACT")
            mm = t.matmul(pvA_ps[32:33, :], ones_sb[:, 0:1], Z["ysq_sb"][:, :],
                          start=(dc == 0), stop=(dc == NCH - 1),
                          skip_group_check=True)
            P.inc(mm, "PE")
        P.wait(v, "PE")
        P.inc(v.tensor_scalar(Z["mean_sb"][0:1, :], pvA_ps[0:1, :],
                              1.0 / D, None, ALU.mult), "VEC")
        P.inc(v.tensor_scalar(Z["msq_sb"][:, :], pvA_ps[32:33, :],
                              1.0 / D, None, ALU.mult), "VEC")
        P.wait(v, "VEC")
        P.inc(v.tensor_tensor(Z["var_sb"][0:1, :], Z["mean_sb"][0:1, :],
                              Z["mean_sb"][0:1, :], ALU.mult), "VEC")
        P.wait(v, "VEC")
        P.inc(v.tensor_sub(Z["var_sb"][0:1, :], Z["msq_sb"][:, :],
                           Z["var_sb"][0:1, :]), "VEC")
        P.wait(s, "VEC")
        P.inc(s.activation(Z["var_sb"][0:1, :], Z["var_sb"][0:1, :],
                           AF.Sqrt, bias=Z["c5_sb"][0:1, 0:1], scale=1.0),
              "ACT")
        P.wait(v, "ACT")
        P.inc(v.reciprocal(Z["var_sb"][0:1, :], Z["var_sb"][0:1, :]), "VEC")
        P.wait(v, "VEC")
        with nc.allow_low_precision(reason="bf16 broadcast operands"):
            P.inc(v.tensor_copy(Z["mean_bf"][0:1, :], Z["mean_sb"][0:1, :]),
                  "VEC")
            P.inc(v.tensor_copy(Z["rstd_bf"][0:1, :], Z["var_sb"][0:1, :]),
                  "VEC")
        P.wait(t, "VEC")
        t.matmul(st_ps[1][:, 0:512], ones_sb[0:1, 0:128], Z["mean_bf"][0:1, :],
                 start=True, stop=True, skip_group_check=True)
        mm = t.matmul(st_ps[1][:, 512:1024], ones_sb[0:1, 0:128],
                      Z["rstd_bf"][0:1, :], start=True, stop=True,
                      skip_group_check=True)
        P.inc(mm, "PE")
        P.wait(v, "PE")
        P.inc(v.tensor_copy(Z["mbc_sb"][:, :], st_ps[1][:, 0:512]), "VEC")
        P.inc(v.tensor_copy(Z["rbc_sb"][:, :], st_ps[1][:, 512:1024]), "VEC")
        P.wait(v, "VEC")
        for dc in range(NCH):
            xv = x_sb[:, T * dc + 512 * tt: T * dc + 512 * (tt + 1)]
            P.inc(v.tensor_sub(xv, xv, Z["mbc_sb"][:, :]), "VEC")
            P.wait(v, "VEC")
            P.inc(v.tensor_tensor(xv, xv, Z["rbc_sb"][:, :], ALU.mult), "VEC")
            P.wait(v, "VEC")
    vec_x2 = P.c["VEC"]

    for tt in range(NTT):
        P.wait(t, "VEC", vec_x2)
        for c in range(NCH):
            mm = t.matmul(st_ps[0][0:16, 512:1024],
                          Z["wg_sb"][:, 16 * c:16 * (c + 1)],
                          x_sb[:, T * c + 512 * tt: T * c + 512 * (tt + 1)],
                          start=(c == 0), stop=(c == NCH - 1),
                          skip_group_check=True)
        P.inc(mm, "PE")
        P.wait(s, "PE")
        P.wait(s, "VEC")
        si = s.activation(Z["lg_tt"][0:16, :], st_ps[0][0:16, 512:1024],
                          AF.Identity, bias=Z["bgc_sb"][0:16, 0:1], scale=1.0)
        P.inc(si, "ACT")
        P.wait(t, "ACT")
        for j in range(4):
            i = 4 * tt + j
            P.wait(t, "VEC")
            mm = t.transpose(pvB_ps[0:128, 0:16],
                             Z["lg_tt"][0:16, 128 * j:128 * (j + 1)],
                             Z["id16_sb"][:, :])
            P.inc(mm, "PE")
            P.wait(v, "PE")
            P.inc(v.tensor_copy(Z["lg_tm"][:, 16 * i:16 * (i + 1)],
                                pvB_ps[0:128, 0:16]), "VEC")
        P.wait(t, "VEC")
    k2r = Z["k2_sb"]
    mxa = k2r[:, 0:64].bitcast(F32)
    nmxa = k2r[:, 64:128].bitcast(F32)
    sma = k2r[:, 128:192].bitcast(F32)
    eqa = k2r[:, 192:1216].bitcast(F32)
    cana = k2r[:, 1216:2240].bitcast(F32)
    t1a = k2r[:, 2240:2304].bitcast(F32)
    ohba = k2r[:, 2304:2816]
    sela = k2r[:, 2816:3840].bitcast(F32)
    csa = k2r[:, 3840:3904].bitcast(F32)
    ovga = k2r[:, 3904:3968].bitcast(F32)
    P.wait(v, "VEC")
    for i in range(NST):
        ltile = Z["lg_tm"][:, 16 * i:16 * (i + 1)]
        P.inc(v.reduce_max(mxa[:, i:i + 1], ltile, axis=AX), "VEC")
        P.inc(v.reduce_max(nmxa[:, i:i + 1], ltile, axis=AX, negate=True),
              "VEC")
    P.wait(s, "VEC")
    for i in range(NST):
        ltile = Z["lg_tm"][:, 16 * i:16 * (i + 1)]
        si = s.activation(Z["ejunk_sb"][:, :], ltile, AF.Exp,
                          bias=nmxa[:, i:i + 1], scale=1.0,
                          accum_out=sma[:, i:i + 1])
        P.inc(si, "ACT")
    P.wait(v, "VEC")
    for i in range(NST):
        ltile = Z["lg_tm"][:, 16 * i:16 * (i + 1)]
        P.inc(v.tensor_scalar(eqa[:, 16 * i:16 * (i + 1)], ltile,
                              mxa[:, i:i + 1], None, ALU.is_equal), "VEC")
    P.wait(v, "ACT")
    for i in range(NST):
        P.inc(v.reciprocal(Z["meta_sb"][:, 4 * i + 2:4 * i + 3],
                           sma[:, i:i + 1]), "VEC")
    P.wait(v, "VEC")
    for i in range(NST):
        P.inc(v.tensor_tensor(cana[:, 16 * i:16 * (i + 1)], Z["r16m_sb"][:, :],
                              eqa[:, 16 * i:16 * (i + 1)], ALU.mult), "VEC")
    P.wait(v, "VEC")
    for i in range(NST):
        P.inc(v.reduce_max(t1a[:, i:i + 1], cana[:, 16 * i:16 * (i + 1)],
                           axis=AX), "VEC")
    P.wait(v, "VEC")
    for i in range(NST):
        P.inc(v.tensor_scalar(Z["idx_all"][:, i:i + 1], t1a[:, i:i + 1],
                              -1.0, 16.0, ALU.mult, ALU.add), "VEC")
    P.wait(v, "VEC")
    for i in range(NST):
        P.inc(v.tensor_copy(Z["meta_sb"][:, 4 * i:4 * i + 1],
                            Z["idx_all"][:, i:i + 1]), "VEC")
        P.inc(v.tensor_scalar(Z["ohf_all"][:, 16 * i:16 * (i + 1)],
                              Z["iota_sb"][:, :], Z["idx_all"][:, i:i + 1],
                              None, ALU.is_equal), "VEC")
        with nc.allow_low_precision(reason="onehot bf16"):
            P.inc(v.tensor_scalar(ohba[:, 16 * i:16 * (i + 1)],
                                  Z["iota_sb"][:, :], Z["idx_all"][:, i:i + 1],
                                  None, ALU.is_equal), "VEC")
    for r in range(NST // 8):
        P.wait(t, "VEC")
        for j in range(8):
            i = 8 * r + j
            mm = t.transpose(trb_ps[0:16, 128 * j:128 * (j + 1)],
                             ohba[:, 16 * i:16 * (i + 1)], Z["idbf_sb"][:, :])
            P.inc(mm, "PE")
        P.wait(v, "PE")
        for j in range(8):
            i = 8 * r + j
            P.inc(v.tensor_copy(Z["ohfm_sb"][0:16, 128 * i:128 * (i + 1)],
                                trb_ps[0:16, 128 * j:128 * (j + 1)]), "VEC")
    P.wait(v, "VEC")
    P.inc(v.tensor_tensor_scan(Z["csum_sb"][:, :], Z["ohfm_sb"][:, :],
                               Z["ohfm_sb"][:, :], 0.0, ALU.add, ALU.bypass),
          "VEC")
    P.wait(v, "VEC")
    P.inc(v.tensor_copy(Z["cnt_sb"][:, cnt_col:cnt_col + 1],
                        Z["csum_sb"][:, T - 1:T]), "VEC")
    S["vec_cnt"] = P.c["VEC"]
    P.wait(t, "VEC")
    for i in range(NST):
        mm = t.transpose(pvB_ps[0:128, 16 * i:16 * (i + 1)],
                         Z["csum_sb"][0:16, 128 * i:128 * (i + 1)],
                         Z["id16_sb"][:, :])
        P.inc(mm, "PE")
    P.wait(v, "PE")
    for i in range(NST):
        P.inc(v.tensor_tensor(sela[:, 16 * i:16 * (i + 1)],
                              pvB_ps[0:128, 16 * i:16 * (i + 1)],
                              Z["ohf_all"][:, 16 * i:16 * (i + 1)], ALU.mult),
              "VEC")
    P.wait(v, "VEC")
    for i in range(NST):
        P.inc(v.reduce_sum(csa[:, i:i + 1], sela[:, 16 * i:16 * (i + 1)],
                           axis=AX), "VEC")
    P.wait(v, "VEC")
    for i in range(NST):
        P.inc(v.tensor_scalar(Z["meta_sb"][:, 4 * i + 1:4 * i + 2],
                              csa[:, i:i + 1], -1.0, None, ALU.add), "VEC")
        P.inc(v.memset(Z["meta_sb"][:, 4 * i + 3:4 * i + 4], 0.0), "VEC")
        P.inc(v.tensor_scalar(Z["fif"][:, i:i + 1], Z["idx_all"][:, i:i + 1],
                              float(LCAP), None, ALU.mult), "VEC")
    P.wait(v, "VEC")
    for i in range(NST):
        P.inc(v.tensor_scalar(ovga[:, i:i + 1],
                              Z["meta_sb"][:, 4 * i + 1:4 * i + 2],
                              float(LCAP), BIGIX, ALU.is_ge, ALU.mult), "VEC")
        P.inc(v.tensor_tensor(Z["fif"][:, i:i + 1], Z["fif"][:, i:i + 1],
                              Z["meta_sb"][:, 4 * i + 1:4 * i + 2], ALU.add),
              "VEC")
    P.wait(v, "VEC")
    for i in range(NST):
        P.inc(v.tensor_tensor(Z["fif"][:, i:i + 1], Z["fif"][:, i:i + 1],
                              ovga[:, i:i + 1], ALU.add), "VEC")
    P.wait(v, "VEC")
    for i in range(NST):
        P.inc(v.tensor_copy(Z["fiu"][:, i:i + 1], Z["fif"][:, i:i + 1]),
              "VEC")
    P.wait(v, "VEC")
    P.wait(sy, "VEC")
    P.inc(sy.dma_start(out=meta_ap.rearrange("(i p) c -> p i c", p=128),
                       in_=Z["meta_sb"].rearrange("p (i c) -> p i c", c=4)),
          "ST", 16)
    for i in range(NST):
        b = i % 2
        sxs = "SX0" if b == 0 else "SX1"
        buf = Z["x2st"][b]
        first = True
        for dc in range(NCH):
            P.wait(t, "VEC")
            mm = t.transpose(trb_ps[0:128, 0:128],
                             x_sb[:, T * dc + 128 * i: T * dc + 128 * (i + 1)],
                             Z["idbf_sb"][:, :])
            P.inc(mm, "PE")
            P.wait(v, "PE")
            if first:
                P.wait(v, sxs)
            first = False
            P.inc(v.tensor_copy(buf[:, 128 * dc:128 * (dc + 1)],
                                trb_ps[0:128, 0:128]), "VEC")
        P.wait(sy, "VEC")
        P.inc(sy.dma_start(out=x2tm_ap[128 * i:128 * (i + 1), :], in_=buf[:, :]),
              sxs, 16)
        P.wait(g, "VEC")
        P.inc(g.indirect_dma_start(
            out=fsrc_ap,
            out_offset=IndirectOffsetOnAxis(Z["fiu"][:, i:i + 1], 0),
            in_=buf[:, :], in_offset=None,
            bounds_check=bchk, oob_is_err=False), sxs, 16)


def emit_l1_const_loads(nc, P, Z, aps):
    g, sy = nc.gpsimd, nc.sync
    for c in range(NCH):
        P.inc(g.dma_start(out=Z["wo_sb"][:, D * c: D * (c + 1)],
                          in_=aps["woT"][128 * c:128 * (c + 1), :]), "LX", 16)
    P.inc(g.dma_start(out=Z["wg_sb"].ap().rearrange("p (c e) -> p c e", c=NCH),
                      in_=aps["wg"].rearrange("(c p) e -> p c e", p=128)),
          "LX", 16)
    for dk, sk in (("boc_sb", "boc"), ("bgc_sb", "bgc"), ("idbf_sb", "idbf"),
                   ("id16_sb", "id16"), ("iota_sb", "iota16b"),
                   ("r16m_sb", "r16m"), ("ones_sb", "onesb"),
                   ("onesf_sb", "onesf"), ("tri8_sb", "tri8"),
                   ("eh01_sb", "eh01"), ("pmask_sb", "pmask"),
                   ("iota128_sb", "iota128"), ("b1c_sb", "b1c"),
                   ("b2c_sb", "b2c")):
        P.inc(sy.dma_start(out=Z[dk][:, :], in_=aps[sk][:, :]), "LS", 16)


def make_headpair_prefetch(nc, P, Z, S, wq_ap, wk_ap, wv_ap, npasses=1):
    g = nc.gpsimd

    def load_pair(hp, bsel):
        sem = "WH0" if bsel == 0 else "WH1"
        P.wait(g, "PE", S["wh_freed"][bsel])
        for mi, wsrc in enumerate((wk_ap, wq_ap, wv_ap)):
            ins = g.dma_start(
                out=Z["wh_sb"][bsel][:, D * mi:D * (mi + 1)].rearrange(
                    "p (c f) -> p c f", c=NCH),
                in_=wsrc[:, 128 * hp:128 * (hp + 1)].rearrange(
                    "(c p) f -> p c f", p=128))
            P.inc(ins, sem, 16)
        S["wh_ready"][bsel] = P.c[sem]

    seq = {"next": 1}

    def prefetch(_hp):
        nxt = seq["next"]
        if nxt < 8 * npasses:
            load_pair(nxt % 8, nxt % 2)
            seq["next"] = nxt + 1

    load_pair(0, 0)
    return prefetch



def emit_prolog_b(nc, P, Z, cnts1_ap, cnts8_ap, vec_cnt):
    g, sy, t, v, s = nc.gpsimd, nc.sync, nc.tensor, nc.vector, nc.scalar
    bc_ps = Z["bc_ps"]
    P.wait(s, "VEC", vec_cnt)
    P.inc(s.dma_start(out=cnts1_ap.rearrange("one e -> e one"),
                      in_=Z["cnt_sb"][:, 0:1]), "CN", 16)
    P.wait(g, "CN")
    cc = g.collective_compute(
        "AllGather", ALU.bypass, replica_groups=[list(range(NC))],
        ins=[cnts1_ap.opt()], outs=[cnts8_ap.opt()])
    P.inc(cc, "CCA", 1)
    P.wait(g, "CCA")
    P.inc(g.dma_start(out=Z["cnts8_sb"][:, :], in_=cnts8_ap), "CN", 16)
    pe0, vec0 = P.c["PE"], P.c["VEC"]
    P.wait(t, "CN")
    P.wait(t, "VEC", vec0)
    mm = t.matmul(bc_ps[0:16, 0:8], Z["cnts8_sb"][:, :], Z["tri8_sb"][:, :],
                  start=True, stop=True, skip_group_check=True)
    P.inc(mm, "PE")
    P.wait(v, "PE")
    P.inc(v.tensor_copy(Z["baseT_sb"][:, :], bc_ps[0:16, 0:8]), "VEC")
    P.wait(t, "VEC")
    mm = t.matmul(bc_ps[0:1, 16:32], Z["pmask_sb"][:, :], Z["cnts8_sb"][:, :],
                  start=True, stop=True, skip_group_check=True)
    P.inc(mm, "PE")
    P.wait(v, "PE")
    P.inc(v.tensor_copy(Z["gbrow_sb"][:, :], bc_ps[0:1, 16:32]), "VEC")
    P.wait(t, "VEC")
    for k in range(2):
        mm = t.matmul(bc_ps[0:1, 32 + 8 * k:40 + 8 * k],
                      Z["eh01_sb"][:, k:k + 1], Z["baseT_sb"][:, :],
                      start=True, stop=True, skip_group_check=True)
        P.inc(mm, "PE")
    P.wait(v, "PE")
    P.inc(v.tensor_copy(Z["rowsel_sb"][:, 0:8], bc_ps[0:1, 32:40]), "VEC")
    P.inc(v.tensor_copy(Z["rowsel_sb"][:, 8:16], bc_ps[0:1, 40:48]), "VEC")
    P.wait(t, "VEC")
    mm = t.matmul(bc_ps[:, 48:64], Z["onesf_sb"][0:1, :],
                  Z["rowsel_sb"][0:1, :], start=True, stop=True,
                  skip_group_check=True)
    P.inc(mm, "PE")
    mm = t.matmul(bc_ps[:, 64:80], Z["onesf_sb"][0:1, :],
                  Z["gbrow_sb"][0:1, :], start=True, stop=True,
                  skip_group_check=True)
    P.inc(mm, "PE")
    P.wait(v, "PE")
    P.inc(v.tensor_copy(Z["bbc_sb"][:, :], bc_ps[:, 48:64]), "VEC")
    P.inc(v.tensor_copy(Z["gbb_sb"][:, :], bc_ps[:, 64:80]), "VEC")
    P.wait(v, "VEC")
    P.wait(v, "PE")
    k2 = Z["k2_sb"]
    pc_all = k2[:, 0:64].bitcast(F32)
    cmp_all = k2[:, 64:576].bitcast(F32)
    nle_all = k2[:, 576:640].bitcast(F32)
    rr_all = k2[:, 640:704].bitcast(F32)
    ovg_all = k2[:, 704:768].bitcast(F32)
    cols = [(ep * 16 + i, ep, i) for ep in range(2) for i in range(CAP // 128)]
    for col, ep, i in cols:
        P.inc(v.tensor_scalar(pc_all[:, col:col + 1], Z["iota128_sb"][:, :],
                              float(128 * i), None, ALU.add), "VEC")
    P.wait(v, "VEC")
    for col, ep, i in cols:
        P.inc(v.tensor_scalar(cmp_all[:, 8 * col:8 * col + 8],
                              Z["bbc_sb"][:, 8 * ep:8 * ep + 8],
                              pc_all[:, col:col + 1], None, ALU.is_le), "VEC")
    P.wait(v, "VEC")
    for col, ep, i in cols:
        P.inc(v.reduce_sum(nle_all[:, col:col + 1],
                           cmp_all[:, 8 * col:8 * col + 8], axis=AX), "VEC")
    P.wait(v, "VEC")
    for col, ep, i in cols:
        P.inc(v.tensor_tensor(cmp_all[:, 8 * col:8 * col + 8],
                              cmp_all[:, 8 * col:8 * col + 8],
                              Z["bbc_sb"][:, 8 * ep:8 * ep + 8], ALU.mult),
              "VEC")
    P.wait(v, "VEC")
    for col, ep, i in cols:
        P.inc(v.reduce_max(rr_all[:, col:col + 1],
                           cmp_all[:, 8 * col:8 * col + 8], axis=AX), "VEC")
    P.wait(v, "VEC")
    for col, ep, i in cols:
        P.inc(v.tensor_sub(rr_all[:, col:col + 1], pc_all[:, col:col + 1],
                           rr_all[:, col:col + 1]), "VEC")
    P.wait(v, "VEC")
    for col, ep, i in cols:
        P.inc(v.tensor_scalar(ovg_all[:, col:col + 1], rr_all[:, col:col + 1],
                              float(LCAP), BIGIX, ALU.is_ge, ALU.mult), "VEC")
        P.inc(v.tensor_scalar(Z["gif"][:, col:col + 1], nle_all[:, col:col + 1],
                              2048.0, float(1024 * ep - 2048), ALU.mult,
                              ALU.add), "VEC")
    P.wait(v, "VEC")
    for col, ep, i in cols:
        P.inc(v.tensor_tensor(Z["gif"][:, col:col + 1],
                              Z["gif"][:, col:col + 1], rr_all[:, col:col + 1],
                              ALU.add), "VEC")
    P.wait(v, "VEC")
    for col, ep, i in cols:
        P.inc(v.tensor_tensor(Z["gif"][:, col:col + 1],
                              Z["gif"][:, col:col + 1], ovg_all[:, col:col + 1],
                              ALU.add), "VEC")
    P.wait(v, "VEC")
    for col, ep, i in cols:
        P.inc(v.tensor_copy(Z["giu"][:, col:col + 1],
                            Z["gif"][:, col:col + 1]), "VEC")
    P.wait(v, "VEC")



def make_ffn_loaders(nc, P, ZF, w1_ap, w2_ap):
    g = nc.gpsimd
    w1_sb, w2_sb = ZF["w1_sb"], ZF["w2_sb"]
    FS = {"w1_ready": [0, 0], "w1_freed": [0, 0],
          "w2_ready": [0, 0], "w2_freed": [0, 0]}

    def load_w1(bsel, e, hc):
        P.wait(g, "PE", FS["w1_freed"][bsel])
        ins = g.dma_start(
            out=w1_sb[bsel].rearrange("p (c f) -> p c f", c=NCH),
            in_=w1_ap[e, :, 128 * hc:128 * (hc + 1)].rearrange(
                "(c p) f -> p c f", p=128))
        FS["w1_ready"][bsel] = P.inc(ins, "W10" if bsel == 0 else "W11", 16)

    def load_w2(bsel, e, dc):
        P.wait(g, "PE", FS["w2_freed"][bsel])
        ins = g.dma_start(
            out=w2_sb[bsel].rearrange("p (c f) -> p c f", c=32),
            in_=w2_ap[e, :, 128 * dc:128 * (dc + 1)].rearrange(
                "(c p) f -> p c f", p=128))
        FS["w2_ready"][bsel] = P.inc(ins, "W20" if bsel == 0 else "W21", 16)

    return FS, load_w1, load_w2


def emit_ffn(nc, P, Z, ZF, loaders, fdst_ap, rsrc_ap, giu_done, bchk):
    g, sy, t, v, s = nc.gpsimd, nc.sync, nc.tensor, nc.vector, nc.scalar
    buf_fm = ZF["buf_fm"]
    bt_sb, yt_sb = ZF["bt_sb"], ZF["yt_sb"]
    w1_sb, w2_sb = ZF["w1_sb"], ZF["w2_sb"]
    h_sb, yfm_sb = ZF["h_sb"], ZF["yfm_sb"]
    h_ps, y_ps, tr_ps = Z["pvA_ps"], Z["pvB_ps"], Z["trb_ps"]
    b1_sb, b2_sb, idbf_sb = Z["b1c_sb"], Z["b2c_sb"], Z["idbf_sb"]
    FS, load_w1, load_w2 = loaders
    w1_ready, w1_freed = FS["w1_ready"], FS["w1_freed"]
    w2_ready, w2_freed = FS["w2_ready"], FS["w2_freed"]
    tr_after = {}
    vec_strip = {}

    for e in range(2):
        for i in range(CAP // 128):
            b = i % 2
            bts = "BT0" if b == 0 else "BT1"
            if i >= 2 or e > 0:
                P.wait(g, "PE", tr_after.get((e, i - 2), tr_after.get(
                    (e - 1, CAP // 128 + i - 2), 0)))
            P.wait(g, "VEC", giu_done)
            col = e * 16 + i
            P.inc(g.indirect_dma_start(
                out=bt_sb[b][:, :], out_offset=None,
                in_=fdst_ap,
                in_offset=IndirectOffsetOnAxis(Z["giu"][:, col:col + 1], 0),
                bounds_check=bchk, oob_is_err=False), bts, 16)
            P.wait(t, bts)
            P.wait(t, "LS", P.c["LS"])
            P.wait(t, "VEC")
            for dc in range(NCH):
                P.wait(t, "VEC")
                mm = t.transpose(tr_ps[0:128, 0:128],
                                 bt_sb[b][:, 128 * dc:128 * (dc + 1)],
                                 idbf_sb[:, :])
                P.inc(mm, "PE")
                P.wait(v, "PE")
                P.inc(v.tensor_copy(
                    buf_fm[:, CAP * dc + 128 * i:CAP * dc + 128 * (i + 1)],
                    tr_ps[0:128, 0:128]), "VEC")
            tr_after[(e, i)] = P.c["PE"]
            if i % 4 == 3:
                vec_strip[(e, i // 4)] = P.c["VEC"]
        if e > 0:
            load_w1(0, e, 0)
        for st in range(4):
            act_of = {}
            P.wait(t, "ACT")
            for hc in range(32):
                bsel = hc % 2
                if hc + 1 < 32:
                    load_w1((hc + 1) % 2, e, hc + 1)
                elif st + 1 < 4:
                    load_w1((hc + 1) % 2, e, 0)
                t.wait_ge(P.sem["W10" if bsel == 0 else "W11"], w1_ready[bsel])
                P.wait(t, "VEC", vec_strip[(e, st)])
                hps = h_ps if hc % 2 == 0 else y_ps
                if hc >= 2:
                    P.wait(t, "ACT", act_of.get(hc - 2, 0))
                for c in range(NCH):
                    mm = t.matmul(hps[:, :],
                                  w1_sb[bsel][:, 128 * c:128 * (c + 1)],
                                  buf_fm[:, CAP * c + 512 * st:
                                         CAP * c + 512 * (st + 1)],
                                  start=(c == 0), stop=(c == NCH - 1),
                                  skip_group_check=True)
                P.inc(mm, "PE")
                w1_freed[bsel] = P.c["PE"]
                P.wait(s, "PE")
                si = s.activation(h_sb[:, 512 * hc:512 * (hc + 1)], hps[:, :],
                                  AF.Relu,
                                  bias=b1_sb[:, 32 * e + hc:32 * e + hc + 1],
                                  scale=1.0)
                act_of[hc] = P.inc(si, "ACT")
            load_w2(0, e, 0)
            yact = {}
            for dc in range(NCH):
                bsel = dc % 2
                if dc + 1 < NCH:
                    load_w2((dc + 1) % 2, e, dc + 1)
                t.wait_ge(P.sem["W20" if bsel == 0 else "W21"], w2_ready[bsel])
                yps = h_ps if dc % 2 == 0 else y_ps
                if dc < 2:
                    P.wait(t, "ACT", act_of[31])
                else:
                    P.wait(t, "ACT", yact[dc - 2])
                for hc in range(32):
                    mm = t.matmul(yps[:, :],
                                  w2_sb[bsel][:, 128 * hc:128 * (hc + 1)],
                                  h_sb[:, 512 * hc:512 * (hc + 1)],
                                  start=(hc == 0), stop=(hc == 31),
                                  skip_group_check=True)
                P.inc(mm, "PE")
                w2_freed[bsel] = P.c["PE"]
                P.wait(s, "PE")
                si = s.activation(yfm_sb[:, 512 * dc:512 * (dc + 1)], yps[:, :],
                                  AF.Identity,
                                  bias=b2_sb[:, 8 * e + dc:8 * e + dc + 1],
                                  scale=1.0)
                yact[dc] = P.inc(si, "ACT")
            P.wait(t, "ACT")
            for j in range(4):
                jb = j % 2
                sys_ = "SY0" if jb == 0 else "SY1"
                buf = yt_sb[jb]
                first = True
                for dc in range(NCH):
                    P.wait(t, "VEC")
                    mm = t.transpose(tr_ps[0:128, 0:128],
                                     yfm_sb[:, 512 * dc + 128 * j:
                                            512 * dc + 128 * (j + 1)],
                                     idbf_sb[:, :])
                    P.inc(mm, "PE")
                    P.wait(v, "PE")
                    if first:
                        P.wait(v, sys_)
                    first = False
                    P.inc(v.tensor_copy(buf[:, 128 * dc:128 * (dc + 1)],
                                        tr_ps[0:128, 0:128]), "VEC")
                P.wait(g, "VEC")
                col = e * 16 + 4 * st + j
                P.inc(g.indirect_dma_start(
                    out=rsrc_ap,
                    out_offset=IndirectOffsetOnAxis(Z["giu"][:, col:col + 1], 0),
                    in_=buf[:, :], in_offset=None,
                    bounds_check=bchk, oob_is_err=False), sys_, 16)



def emit_phase_c(nc, P, Z, ZF, x2tm_ap, meta_ap, rdst_ap, out_ap, bchk,
                 pe_attn):
    g, sy, t, v, s = nc.gpsimd, nc.sync, nc.tensor, nc.vector, nc.scalar
    ybig = Z["x_sb"]
    x2all = Z["o_sb"]
    q2, k2 = Z["q2_sb"], Z["k2_sb"]
    eqall = q2[:, 0:1024].bitcast(F32)
    selall = q2[:, 1024:2048].bitcast(F32)
    lnall = q2[:, 2048:2560].bitcast(F32)
    mt_all = q2[:, 2560:2816].bitcast(F32)
    o2b = [k2[:, 0:2048].bitcast(F32), k2[:, 2048:4096].bitcast(F32)]
    zb = [Z["vaugA"][:, 0:1024], Z["vaugA"][:, 1024:2048],
          Z["vaugB"][:, 0:1024], Z["vaugB"][:, 1024:2048]]
    zdump = Z["expS"]

    P.wait(sy, "SX0")
    P.wait(sy, "SX1")
    P.wait(sy, "SY0")
    P.wait(sy, "SY1")
    P.wait(sy, "ST")
    P.wait(sy, "PE")
    P.wait(sy, "VEC")
    P.wait(sy, "ACT")
    for i in range(0, 9):
        P.inc(sy.dma_start(out=x2all[:, 1024 * i:1024 * (i + 1)],
                           in_=x2tm_ap[128 * i:128 * (i + 1), :]), "LT0", 16)
    P.wait(v, "PE", pe_attn)
    P.wait(v, "LT0", 16)
    P.wait(v, "VEC")
    for i in range(NST):
        P.inc(v.tensor_scalar(eqall[:, 16 * i:16 * (i + 1)], Z["iota_sb"][:, :],
                              mt_all[:, 4 * i:4 * i + 1], None, ALU.is_equal),
              "VEC")
    P.wait(v, "VEC")
    for i in range(NST):
        P.inc(v.tensor_tensor(selall[:, 16 * i:16 * (i + 1)],
                              eqall[:, 16 * i:16 * (i + 1)], Z["gbb_sb"][:, :],
                              ALU.mult), "VEC")
    P.wait(v, "VEC")
    for i in range(NST):
        P.inc(v.reduce_sum(lnall[:, 8 * i + 6:8 * i + 7],
                           selall[:, 16 * i:16 * (i + 1)], axis=AX), "VEC")
    P.wait(v, "VEC")
    for i in range(NST):
        P.inc(v.tensor_tensor(lnall[:, 8 * i + 6:8 * i + 7],
                              lnall[:, 8 * i + 6:8 * i + 7],
                              mt_all[:, 4 * i + 1:4 * i + 2], ALU.add), "VEC")
    P.wait(v, "VEC")
    for i in range(NST):
        P.inc(v.tensor_scalar(lnall[:, 8 * i + 7:8 * i + 8],
                              lnall[:, 8 * i + 6:8 * i + 7], float(CAP),
                              BIGIX, ALU.is_ge, ALU.mult), "VEC")
    P.wait(v, "VEC")
    for i in range(NST):
        P.inc(v.tensor_tensor(Z["fif"][:, i:i + 1], Z["fif"][:, i:i + 1],
                              lnall[:, 8 * i + 7:8 * i + 8], ALU.add), "VEC")
    P.wait(v, "VEC")
    for i in range(NST):
        P.inc(v.tensor_copy(Z["fiu"][:, i:i + 1], Z["fif"][:, i:i + 1]),
              "VEC")
    P.inc(v.memset(ybig[:, :], 0.0), "VEC")
    P.wait(g, "PE")
    P.wait(g, "VEC")
    for i in range(9, 18):
        P.inc(g.dma_start(out=x2all[:, 1024 * i:1024 * (i + 1)],
                          in_=x2tm_ap[128 * i:128 * (i + 1), :]), "LT1", 16)
    P.wait(g, "CCR")
    for i in range(NST):
        P.inc(g.indirect_dma_start(
            out=ybig[:, 1024 * i:1024 * (i + 1)], out_offset=None,
            in_=rdst_ap,
            in_offset=IndirectOffsetOnAxis(Z["fiu"][:, i:i + 1], 0),
            bounds_check=bchk, oob_is_err=False), "GB0", 16)
    P.wait(v, "LT0")
    P.wait(v, "LT1")
    zv = {}
    sums = {}
    o2v = {}
    for i in range(NST):
        zbi = zb[i % 4]
        obi = o2b[i % 2]
        if i >= 4:
            P.wait(v, "VEC", o2v[i - 4])
        P.wait(v, "GB0", 16 * (i + 1))
        zv[i] = P.inc(v.scalar_tensor_tensor(
            zbi[:, :], ybig[:, 1024 * i:1024 * (i + 1)],
            mt_all[:, 4 * i + 2:4 * i + 3],
            x2all[:, 1024 * i:1024 * (i + 1)], ALU.mult, ALU.add), "VEC")
        s.wait_ge(P.sem["VEC"], zv[i])
        P.inc(s.activation(zdump[0][:, :], zbi[:, :], AF.Identity,
                           accum_out=lnall[:, 8 * i:8 * i + 1]), "ACT")
        sums[i] = P.inc(s.activation(zdump[1][:, :], zbi[:, :], AF.Square,
                                     accum_out=lnall[:, 8 * i + 1:8 * i + 2]),
                        "ACT")
        v.wait_ge(P.sem["ACT"], sums[i])
        P.inc(v.tensor_scalar(lnall[:, 8 * i + 2:8 * i + 3],
                              lnall[:, 8 * i:8 * i + 1], 1.0 / D, None,
                              ALU.mult), "VEC")
        P.wait(v, "VEC")
        P.inc(v.tensor_scalar(lnall[:, 8 * i + 4:8 * i + 5],
                              lnall[:, 8 * i + 2:8 * i + 3],
                              lnall[:, 8 * i + 2:8 * i + 3], None, ALU.mult),
              "VEC")
        P.wait(v, "VEC")
        vv = P.inc(v.tensor_scalar(lnall[:, 8 * i + 3:8 * i + 4],
                                   lnall[:, 8 * i + 1:8 * i + 2], 1.0 / D,
                                   lnall[:, 8 * i + 4:8 * i + 5], ALU.mult,
                                   ALU.subtract), "VEC")
        s.wait_ge(P.sem["VEC"], vv)
        sq = P.inc(s.activation(lnall[:, 8 * i + 5:8 * i + 6],
                                lnall[:, 8 * i + 3:8 * i + 4], AF.Sqrt,
                                bias=Z["c5_sb"][:, 0:1], scale=1.0), "ACT")
        v.wait_ge(P.sem["ACT"], sq)
        P.inc(v.reciprocal(lnall[:, 8 * i + 5:8 * i + 6],
                           lnall[:, 8 * i + 5:8 * i + 6]), "VEC")
        P.wait(v, "VEC")
        if i >= 2:
            P.wait(v, "SO", 16 * (i - 1))
        o2v[i] = P.inc(v.tensor_scalar(obi[:, :], zbi[:, :],
                                       lnall[:, 8 * i + 2:8 * i + 3],
                                       lnall[:, 8 * i + 5:8 * i + 6],
                                       ALU.subtract, ALU.mult), "VEC")
        sy.wait_ge(P.sem["VEC"], o2v[i])
        P.inc(sy.dma_start(out=out_ap[128 * i:128 * (i + 1), :],
                           in_=obi[:, :]), "SO", 16)
    P.wait(sy, "SO")



def build_fused():
    nc = bass.Bass("TRN2", target_bir_lowering=False, debug=False)
    xT = nc.dram_tensor("xT", [D, T], F32, kind="ExternalInput").ap()
    wqT = nc.dram_tensor("wqT", [D, D], F32, kind="ExternalInput").ap()
    wkT = nc.dram_tensor("wkT", [D, D], F32, kind="ExternalInput").ap()
    wvT = nc.dram_tensor("wvT", [D, D], F32, kind="ExternalInput").ap()
    woT = nc.dram_tensor("woT", [D, D], F32, kind="ExternalInput").ap()
    boc = nc.dram_tensor("boc", [128, NCH], F32, kind="ExternalInput").ap()
    wg = nc.dram_tensor("wg", [D, E], F32, kind="ExternalInput").ap()
    bgc = nc.dram_tensor("bgc", [E, 1], F32, kind="ExternalInput").ap()
    idbf = nc.dram_tensor("idbf", [128, 128], BF16, kind="ExternalInput").ap()
    id16 = nc.dram_tensor("id16", [16, 16], F32, kind="ExternalInput").ap()
    iota16b = nc.dram_tensor("iota16b", [128, 16], F32, kind="ExternalInput").ap()
    r16m = nc.dram_tensor("r16m", [128, 16], F32, kind="ExternalInput").ap()
    onesb = nc.dram_tensor("onesb", [128, 128], BF16, kind="ExternalInput").ap()
    onesf = nc.dram_tensor("onesf", [128, 128], F32, kind="ExternalInput").ap()
    tri8 = nc.dram_tensor("tri8", [8, 8], F32, kind="ExternalInput").ap()
    eh01 = nc.dram_tensor("eh01", [16, 2], F32, kind="ExternalInput").ap()
    pmask = nc.dram_tensor("pmask", [8, 1], F32, kind="ExternalInput").ap()
    iota128 = nc.dram_tensor("iota128", [128, 1], F32, kind="ExternalInput").ap()
    w1 = nc.dram_tensor("w1", [2, D, FH], F32, kind="ExternalInput").ap()
    w2 = nc.dram_tensor("w2", [2, FH, D], F32, kind="ExternalInput").ap()
    b1c = nc.dram_tensor("b1c", [128, 64], F32, kind="ExternalInput").ap()
    b2c = nc.dram_tensor("b2c", [128, 16], F32, kind="ExternalInput").ap()
    out = nc.dram_tensor("out", [T, D], F32, kind="ExternalOutput").ap()
    x2tm = nc.dram_tensor("x2tm", [T, D], BF16).ap()
    meta = nc.dram_tensor("meta", [T, 4], F32).ap()
    cnts1 = nc.dram_tensor("cnts1", [1, E], F32).ap()
    cnts8 = nc.dram_tensor("cnts8", [8, E], F32).ap()
    fsrc = nc.dram_tensor("fsrc", [NROW, D], BF16).ap()
    fdst = nc.dram_tensor("fdst", [NROW, D], BF16).ap()
    rsrc = nc.dram_tensor("rsrc", [NROW, D], BF16).ap()
    rdst = nc.dram_tensor("rdst", [NROW, D], BF16).ap()

    es = ExitStack()
    Z = alloc_l1(nc, es)
    ZF = alloc_overlays(Z)
    P = Prog(nc, es, SEMS)
    g, sy = nc.gpsimd, nc.sync
    bchk = nc.alloc_register(mybir.EngineType.Pool, "bchk")
    g.reg_mov(bchk, NROW - 1)

    for c in range(NCH):
        P.inc(g.dma_start(out=Z["x_sb"][:, T * c: T * (c + 1)],
                          in_=xT[128 * c:128 * (c + 1), :]), "LX", 16)
    emit_l1_const_loads(nc, P, Z, dict(
        woT=woT, wg=wg, boc=boc, bgc=bgc, idbf=idbf, id16=id16,
        iota16b=iota16b, r16m=r16m, onesb=onesb, onesf=onesf, tri8=tri8,
        eh01=eh01, pmask=pmask, iota128=iota128, b1c=b1c, b2c=b2c))
    lx_ready = P.c["LX"]
    S = new_l1_state()
    prefetch = make_headpair_prefetch(nc, P, Z, S, wqT, wkT, wvT)
    emit_l1_init(nc, P, Z)
    with nc.named_scope("l1"):
        emit_l1_batch(nc, P, Z, S, prefetch, x2tm, meta, 0, lx_ready, fsrc,
                      bchk)
    pe_attn = P.c["PE"]

    with nc.named_scope("prolog_b"):
        emit_prolog_b(nc, P, Z, cnts1, cnts8, S["vec_cnt"])
    giu_done = P.c["VEC"]

    P.wait(g, "SX0")
    P.wait(g, "SX1")
    P.wait(g, "ST")
    P.wait(g, "PE")
    P.wait(g, "VEC")
    loaders = make_ffn_loaders(nc, P, ZF, w1, w2)
    loaders[1](0, 0, 0)
    P.wait(sy, "SX0")
    P.wait(sy, "SX1")
    P.wait(sy, "ST")
    P.wait(sy, "PE")
    P.wait(sy, "VEC")
    mt_all_early = Z["q2_sb"][:, 2560:2816].bitcast(F32)
    P.inc(sy.dma_start(out=mt_all_early.rearrange("p (i c) -> p i c", c=4),
                       in_=meta.rearrange("(i p) c -> p i c", p=128)),
          "LT0", 16)
    for i in range(18, NST):
        P.inc(sy.dma_start(out=Z["o_sb"][:, 1024 * i:1024 * (i + 1)],
                           in_=x2tm[128 * i:128 * (i + 1), :]), "LT0", 16)

    with nc.named_scope("a2a_fwd"):
        with allow_ici_all_to_all():
            cc = g.collective_compute(
                "AllToAll", ALU.bypass, replica_groups=[list(range(NC))],
                ins=[fsrc.opt()], outs=[fdst.opt()])
        P.inc(cc, "CCF", 1)
        P.wait(g, "CCF")

    with nc.named_scope("ffn"):
        emit_ffn(nc, P, Z, ZF, loaders, fdst, rsrc, giu_done, bchk)

    with nc.named_scope("a2a_ret"):
        P.wait(g, "SY0")
        P.wait(g, "SY1")
        with allow_ici_all_to_all():
            cc = g.collective_compute(
                "AllToAll", ALU.bypass, replica_groups=[list(range(NC))],
                ins=[rsrc.opt()], outs=[rdst.opt()])
        P.inc(cc, "CCR", 1)

    with nc.named_scope("phase_c"):
        emit_phase_c(nc, P, Z, ZF, x2tm, meta, rdst, out, bchk, pe_attn)

    es.close()
    return nc



_NC_CACHE = {}


def _get(name, builder):
    if name not in _NC_CACHE:
        _NC_CACHE[name] = builder()
    return _NC_CACHE[name]


def kernel(L0_attn_x, L0_attn_Wq, L0_attn_Wk, L0_attn_Wv, L0_attn_Wo,
           L0_attn_bo, L0_ffn_Wg, L0_ffn_bg, L0_ffn_W1, L0_ffn_b1,
           L0_ffn_W2, L0_ffn_b2):
    x = np.asarray(L0_attn_x, dtype=np.float32)
    wqT = np.ascontiguousarray(np.asarray(L0_attn_Wq, np.float32).T)
    wkT = np.ascontiguousarray(np.asarray(L0_attn_Wk, np.float32).T)
    wvT = np.ascontiguousarray(np.asarray(L0_attn_Wv, np.float32).T)
    woT = np.ascontiguousarray(np.asarray(L0_attn_Wo, np.float32).T)
    bo = np.asarray(L0_attn_bo, np.float32)
    wg = np.ascontiguousarray(np.asarray(L0_ffn_Wg, np.float32))
    bg = np.asarray(L0_ffn_bg, np.float32)
    W1 = np.asarray(L0_ffn_W1, np.float32)
    b1 = np.asarray(L0_ffn_b1, np.float32)
    W2 = np.asarray(L0_ffn_W2, np.float32)
    b2 = np.asarray(L0_ffn_b2, np.float32)

    boc = np.ascontiguousarray(bo.reshape(NCH, 128).T)
    bgc = np.ascontiguousarray(bg.reshape(E, 1))
    idbf = np.eye(128, dtype=np.float32).astype(BF16NP)
    id16 = np.eye(16, dtype=np.float32)
    iota16b = np.ascontiguousarray(np.tile(np.arange(16, dtype=np.float32),
                                           (128, 1)))
    r16m = np.ascontiguousarray(16.0 - iota16b)
    onesb = np.ones((128, 128), np.float32).astype(BF16NP)
    onesf = np.ones((128, 128), np.float32)
    tri8 = np.triu(np.ones((8, 8), np.float32), 1)
    iota128 = np.arange(128, dtype=np.float32).reshape(128, 1)

    nc1 = _get("fused", build_fused)
    in_maps = []
    for c in range(NC):
        eh01 = np.zeros((16, 2), np.float32)
        eh01[2 * c, 0] = 1.0
        eh01[2 * c + 1, 1] = 1.0
        b1cc = np.ascontiguousarray(
            np.concatenate([b1[2 * c + e].reshape(32, 128).T for e in range(2)],
                           axis=1))
        b2cc = np.ascontiguousarray(
            np.concatenate([b2[2 * c + e].reshape(NCH, 128).T for e in range(2)],
                           axis=1))
        in_maps.append({
            "xT": np.ascontiguousarray(x[c].T),
            "wqT": wqT, "wkT": wkT, "wvT": wvT, "woT": woT,
            "boc": boc, "wg": wg, "bgc": bgc,
            "idbf": idbf, "id16": id16, "iota16b": iota16b, "r16m": r16m,
            "onesb": onesb, "onesf": onesf, "tri8": tri8, "eh01": eh01,
            "pmask": (np.arange(8) < c).astype(np.float32).reshape(8, 1),
            "iota128": iota128,
            "w1": np.ascontiguousarray(W1[2 * c:2 * c + 2]),
            "w2": np.ascontiguousarray(W2[2 * c:2 * c + 2]),
            "b1c": b1cc, "b2c": b2cc,
        })
    res = run_bass_kernel_spmd(nc1, in_maps, core_ids=list(range(NC))).results
    return np.stack([res[c]["out"] for c in range(NC)], axis=0)


# version 11
# speedup vs baseline: 1.0413x; 1.0109x over previous
import numpy as np
import ml_dtypes
from contextlib import ExitStack

import concourse.bass as bass
import concourse.mybir as mybir
from concourse.bass import IndirectOffsetOnAxis
from concourse.bass_utils import run_bass_kernel_spmd
from concourse.aot_env import allow_ici_all_to_all

F32 = mybir.dt.float32
BF16 = mybir.dt.bfloat16
U32 = mybir.dt.uint32
AF = mybir.ActivationFunctionType
ALU = mybir.AluOpType
AX = mybir.AxisListType.X
BF16NP = ml_dtypes.bfloat16

D = 1024
T = 4096
H = 16
NC = 8
E = 16
CAP = 2048
LCAP = 1024
FH = 4096
NCH = D // 128
NTT = T // 512
NST = T // 128
BIG = float(1 << 26)
BIGIX = 1.0e6
NROW = E * LCAP


class Prog:
    def __init__(self, nc, es, sems):
        self.nc = nc
        self.c = {}
        self.sem = {}
        for name in sems:
            self.sem[name] = es.enter_context(nc.semaphore(name))
            self.c[name] = 0

    def inc(self, instr, name, k=1):
        self.c[name] += k
        instr.then_inc(self.sem[name], k)
        return self.c[name]

    def wait(self, eng, name, v=None):
        eng.wait_ge(self.sem[name], self.c[name] if v is None else v)


SEMS = ["LX", "LS", "WH0", "WH1", "PE", "ACT", "VEC", "ST",
        "SX0", "SX1", "SST", "SEXP", "SPV",
        "CN", "CCA", "CCF", "CCR",
        "BT0", "BT1", "W10", "W11", "W20", "W21", "SY0", "SY1",
        "LT0", "LT1", "GB0", "GB1", "SO"]


def alloc_l1(nc, es):
    sb = lambda name, shape, dt: es.enter_context(nc.sbuf_tensor(name, shape, dt))
    ps = lambda name, shape, dt: es.enter_context(nc.psum_tensor(name, shape, dt))
    Z = {}
    Z["x_sb"] = sb("x_sb", [128, NCH * T], BF16)
    Z["o_sb"] = sb("o_sb", [128, NCH * T], BF16)
    Z["wo_sb"] = sb("wo_sb", [128, NCH * D], BF16)
    Z["wh_sb"] = [sb(f"wh_sb{i}", [128, 3 * D], BF16) for i in range(2)]
    Z["wg_sb"] = sb("wg_sb", [128, NCH * E], BF16)
    Z["boc_sb"] = sb("boc_sb", [128, NCH], F32)
    Z["bgc_sb"] = sb("bgc_sb", [16, 1], F32)
    Z["idbf_sb"] = sb("idbf_sb", [128, 128], BF16)
    Z["id16_sb"] = sb("id16_sb", [16, 16], F32)
    Z["iota_sb"] = sb("iota_sb", [128, 16], F32)
    Z["r16m_sb"] = sb("r16m_sb", [128, 16], F32)
    Z["odd16_sb"] = sb("odd16_sb", [128, 16], F32)
    Z["ones_sb"] = sb("ones_sb", [128, 128], BF16)
    Z["onesf_sb"] = sb("onesf_sb", [128, 128], F32)
    Z["k2_sb"] = sb("k2_sb", [128, T], BF16)
    Z["q2_sb"] = sb("q2_sb", [128, T], BF16)
    Z["v2tmp"] = sb("v2tmp", [128, 512], BF16)
    Z["vaugA"] = sb("vaugA", [128, 65 * 32], BF16)
    Z["vaugB"] = sb("vaugB", [128, 65 * 32], BF16)
    Z["expS"] = [sb(f"expS{i}", [128, 1024], BF16) for i in range(2)]
    Z["rden_bf"] = sb("rden_bf", [65, 512], BF16)
    Z["bc_sb"] = sb("bc_sb", [128, 512], F32)
    Z["ysq_sb"] = sb("ysq_sb", [128, 512], BF16)
    Z["mean_sb"] = sb("mean_sb", [1, 512], F32)
    Z["var_sb"] = sb("var_sb", [1, 512], F32)
    Z["mean_bf"] = sb("mean_bf", [1, 512], BF16)
    Z["rstd_bf"] = sb("rstd_bf", [1, 512], BF16)
    Z["rbc_sb"] = sb("rbc_sb", [128, 512], F32)
    Z["lg_tt"] = sb("lg_tt", [16, 512], F32)
    Z["msq_sb"] = Z["lg_tt"][0:1, 0:512]
    Z["mbc_sb"] = Z["bc_sb"]
    Z["mx_sb"] = sb("mx_sb", [128, 1], F32)
    Z["nmx_sb"] = sb("nmx_sb", [128, 1], F32)
    Z["sm_sb"] = sb("sm_sb", [128, 1], F32)
    Z["ejunk_sb"] = sb("ejunk_sb", [128, 16], F32)
    Z["eq_sb"] = sb("eq_sb", [128, 16], F32)
    Z["cand_sb"] = sb("cand_sb", [128, 16], F32)
    Z["t1_sb"] = sb("t1_sb", [128, 1], F32)

    Z["ohb_sb"] = sb("ohb_sb", [128, 16], BF16)
    Z["cnt_sb"] = sb("cnt_sb", [16, 8], F32)
    Z["sel_sb"] = sb("sel_sb", [128, 16], F32)
    Z["cs_sb"] = sb("cs_sb", [128, 1], F32)

    Z["c5_sb"] = sb("c5_sb", [128, 1], F32)
    Z["c8_sb"] = sb("c8_sb", [128, 1], F32)
    Z["csum_sb"] = Z["o_sb"][0:16, 0:8192].bitcast(F32)
    Z["ohfm_sb"] = Z["o_sb"][0:16, 8192:12288]
    Z["lg_tm"] = Z["o_sb"][:, 12288:13312].bitcast(F32)
    Z["ohf_all"] = Z["o_sb"][:, 13312:14336].bitcast(F32)
    Z["meta_sb"] = Z["o_sb"][:, 14336:14592].bitcast(F32)
    Z["idx_all"] = Z["o_sb"][:, 14592:14656].bitcast(F32)
    Z["x2st"] = [Z["o_sb"][:, 16384:17408], Z["o_sb"][:, 17408:18432]]
    Z["fif"] = sb("fif", [128, 32], F32)
    Z["fiu"] = sb("fiu", [128, 32], U32)
    Z["gif"] = sb("gif", [128, 32], F32)
    Z["giu"] = sb("giu", [128, 32], U32)
    Z["gif2"] = sb("gif2", [128, 32], F32)
    Z["giu2"] = sb("giu2", [128, 32], U32)
    Z["cnts8_sb"] = sb("cnts8_sb", [8, 16], F32)
    Z["tri8_sb"] = sb("tri8_sb", [8, 8], F32)
    Z["eh01_sb"] = sb("eh01_sb", [16, 2], F32)
    Z["pmask_sb"] = sb("pmask_sb", [8, 1], F32)
    Z["iota128_sb"] = sb("iota128_sb", [128, 1], F32)
    Z["baseT_sb"] = sb("baseT_sb", [16, 8], F32)
    Z["rowsel_sb"] = sb("rowsel_sb", [1, 16], F32)
    Z["gbrow_sb"] = sb("gbrow_sb", [1, 16], F32)
    Z["bbc_sb"] = sb("bbc_sb", [128, 16], F32)
    Z["gbb_sb"] = sb("gbb_sb", [128, 16], F32)
    Z["pc_sb"] = sb("pc_sb", [128, 1], F32)
    Z["cmp8_sb"] = sb("cmp8_sb", [128, 8], F32)
    Z["bsc_sb"] = sb("bsc_sb", [128, 8], F32)
    Z["nle_sb"] = sb("nle_sb", [128, 1], F32)
    Z["rr_sb"] = sb("rr_sb", [128, 1], F32)
    Z["ovg_sb"] = sb("ovg_sb", [128, 1], F32)
    Z["b1c_sb"] = sb("b1c_sb", [128, 64], F32)
    Z["b2c_sb"] = sb("b2c_sb", [128, 16], F32)
    Z["ln2_sb"] = sb("ln2_sb", [128, 8], F32)
    Z["st_ps"] = [ps("st_ps0", [128, 1024], F32), ps("st_ps1", [128, 1024], F32)]
    Z["pvA_ps"] = ps("pvA_ps", [128, 512], F32)
    Z["pvB_ps"] = ps("pvB_ps", [128, 512], F32)
    Z["bc_ps"] = ps("bc_ps", [128, 512], F32)
    Z["trb_ps"] = ps("trb_ps", [128, 1024], BF16)
    return Z


def alloc_overlays(Z):
    ZF = {}
    x, o = Z["x_sb"], Z["o_sb"]
    ZF["buf_fm"] = x[:, 0:16384]
    ZF["h_sb"] = x[:, 16384:32768]
    ZF["yfm_sb"] = o[:, 0:4096]
    ZF["bt_sb"] = [o[:, 4096:5120], o[:, 5120:6144]]
    ZF["yt_sb"] = [o[:, 6144:7168], o[:, 7168:8192]]
    ZF["w1_sb"] = [o[:, 8192:9216], o[:, 9216:10240]]
    ZF["w2_sb"] = [o[:, 10240:14336], o[:, 14336:18432]]
    k2, q2 = Z["k2_sb"], Z["q2_sb"]
    ZF["x2t2"] = [k2[:, 0:1024], k2[:, 1024:2048]]
    ZF["yb2"] = [k2[:, 2048:3072], k2[:, 3072:4096]]
    ZF["mt2"] = [q2[:, 0:8].bitcast(F32), q2[:, 8:16].bitcast(F32)]
    ZF["z_sb"] = Z["vaugA"][:, 0:1024]
    ZF["zc_sb"] = Z["vaugA"][:, 1024:2048]
    ZF["zsq_sb"] = Z["vaugB"][:, 0:1024]
    ZF["o2_sb"] = q2[:, 2048:4096].bitcast(F32)
    return ZF


def new_l1_state():
    return {
        "gg": 0,
        "vec_bc_free": 0,
        "vec_pvA_free": 0,
        "vec_pvB_free": 0,
        "wh_ready": [0, 0],
        "wh_freed": [0, 0],
    }


def emit_l1_init(nc, P, Z):
    v, s, t = nc.vector, nc.scalar, nc.tensor
    P.wait(v, "LS")
    P.inc(v.memset(Z["vaugA"][:, :], 1.0), "VEC")
    P.inc(v.memset(Z["vaugB"][:, :], 1.0), "VEC")
    P.inc(v.memset(Z["c5_sb"][:, :], 1e-5), "VEC")
    P.inc(v.memset(Z["c8_sb"][:, :], 1e-8), "VEC")
    P.wait(s, "VEC")
    P.wait(t, "VEC")


def emit_l1_batch(nc, P, Z, S, prefetch, x2tm_ap, meta_ap, cnt_col, lx_ready,
                  fsrc_ap, bchk):
    g, sy, t, v, s = nc.gpsimd, nc.sync, nc.tensor, nc.vector, nc.scalar
    st_ps, pvA_ps, pvB_ps = Z["st_ps"], Z["pvA_ps"], Z["pvB_ps"]
    bc_ps, trb_ps = Z["bc_ps"], Z["trb_ps"]
    x_sb, o_sb = Z["x_sb"], Z["o_sb"]
    k2_sb, q2_sb = Z["k2_sb"], Z["q2_sb"]
    expS = Z["expS"]
    ones_sb = Z["ones_sb"]

    for hp in range(8):
        bsel = hp % 2
        wh_sem = "WH0" if bsel == 0 else "WH1"
        prefetch(hp)
        t.wait_ge(P.sem[wh_sem], S["wh_ready"][bsel])
        P.wait(t, "LX", lx_ready)
        P.wait(t, "LS")
        for mi, dst_sb in ((0, k2_sb), (1, q2_sb)):
            for tt in range(NTT):
                pp = bc_ps if tt % 2 == 0 else pvA_ps
                P.wait(t, "VEC",
                       S["vec_bc_free"] if tt % 2 == 0 else S["vec_pvA_free"])
                for c in range(NCH):
                    mm = t.matmul(pp[:, :],
                                  Z["wh_sb"][bsel][:, D * mi + 128 * c:
                                                   D * mi + 128 * (c + 1)],
                                  x_sb[:, T * c + 512 * tt: T * c + 512 * (tt + 1)],
                                  start=(c == 0), stop=(c == NCH - 1))
                P.inc(mm, "PE")
                P.wait(v, "PE")
                vi = P.inc(v.tensor_copy(dst_sb[:, 512 * tt:512 * (tt + 1)],
                                         pp[:, :]), "VEC")
                if tt % 2 == 0:
                    S["vec_bc_free"] = vi
                else:
                    S["vec_pvA_free"] = vi
        for tt in range(NTT):
            pp = bc_ps if tt % 2 == 0 else pvA_ps
            P.wait(t, "VEC",
                   S["vec_bc_free"] if tt % 2 == 0 else S["vec_pvA_free"])
            for c in range(NCH):
                mm = t.matmul(pp[:, :],
                              Z["wh_sb"][bsel][:, 2 * D + 128 * c:
                                               2 * D + 128 * (c + 1)],
                              x_sb[:, T * c + 512 * tt: T * c + 512 * (tt + 1)],
                              start=(c == 0), stop=(c == NCH - 1))
            P.inc(mm, "PE")
            P.wait(v, "PE")
            vi = P.inc(v.tensor_copy(Z["v2tmp"][:, :], pp[:, :]), "VEC")
            if tt % 2 == 0:
                S["vec_bc_free"] = vi
            else:
                S["vec_pvA_free"] = vi
            for j in range(4):
                sc = 4 * tt + j
                P.wait(t, "VEC")
                mm = t.transpose(trb_ps[0:128, 0:128],
                                 Z["v2tmp"][:, 128 * j:128 * (j + 1)],
                                 Z["idbf_sb"][:, :])
                P.inc(mm, "PE")
                P.wait(v, "PE")
                P.inc(v.tensor_copy(Z["vaugA"][:, 65 * sc:65 * sc + 64],
                                    trb_ps[0:128, 0:64]), "VEC")
                P.inc(v.tensor_copy(Z["vaugB"][:, 65 * sc:65 * sc + 64],
                                    trb_ps[0:128, 64:128]), "VEC")
        S["wh_freed"][bsel] = P.c["PE"]
        vec_vaug = P.c["VEC"]

        for tt in range(NTT):
            g0 = S["gg"]
            tsl = slice(512 * tt, 512 * (tt + 1))
            for lg in range(32):
                cgg = g0 + lg
                if cgg >= 2:
                    P.wait(t, "SEXP", cgg - 1)
                if lg == 0:
                    P.wait(t, "VEC", vec_vaug)
                mm = t.matmul(st_ps[cgg % 2][:, 0:512],
                              k2_sb[0:64, 128 * lg:128 * (lg + 1)],
                              q2_sb[0:64, tsl], start=True, stop=True,
                              tile_position=(0, 0))
                mm = t.matmul(st_ps[cgg % 2][:, 512:1024],
                              k2_sb[64:128, 128 * lg:128 * (lg + 1)],
                              q2_sb[64:128, tsl], start=True, stop=True,
                              tile_position=(64, 0))
                P.inc(mm, "SST")
                if lg >= 1:
                    P.wait(t, "SEXP", cgg)
                    pvg = lg - 1
                    if pvg == 0:
                        P.wait(t, "VEC", S["vec_pvA_free"])
                        P.wait(t, "VEC", S["vec_pvB_free"])
                    mm = t.matmul(pvA_ps[0:65, :],
                                  Z["vaugA"][:, 65 * pvg:65 * pvg + 65],
                                  expS[(g0 + pvg) % 2][:, 0:512],
                                  start=(pvg == 0), stop=False,
                                  skip_group_check=True)
                    mm = t.matmul(pvB_ps[0:65, :],
                                  Z["vaugB"][:, 65 * pvg:65 * pvg + 65],
                                  expS[(g0 + pvg) % 2][:, 512:1024],
                                  start=(pvg == 0), stop=False,
                                  skip_group_check=True)
                    P.inc(mm, "SPV")
            P.wait(t, "SEXP", g0 + 32)
            mm = t.matmul(pvA_ps[0:65, :], Z["vaugA"][:, 65 * 31:65 * 31 + 65],
                          expS[(g0 + 31) % 2][:, 0:512],
                          start=False, stop=True, skip_group_check=True)
            mm = t.matmul(pvB_ps[0:65, :], Z["vaugB"][:, 65 * 31:65 * 31 + 65],
                          expS[(g0 + 31) % 2][:, 512:1024],
                          start=False, stop=True, skip_group_check=True)
            P.inc(mm, "SPV")
            for lg in range(32):
                cgg = g0 + lg
                P.wait(s, "SST", cgg + 1)
                if cgg >= 2:
                    P.wait(s, "SPV", cgg - 1)
                si = s.activation(expS[cgg % 2][:, :], st_ps[cgg % 2][:, :],
                                  AF.Exp, bias=0.0, scale=0.125)
                P.inc(si, "SEXP")
            S["gg"] = g0 + 32
            for head, pv in ((0, pvA_ps), (1, pvB_ps)):
                bcx = Z["bc_sb"] if head == 0 else Z["rbc_sb"]
                dr = slice(0, 1) if head == 0 else slice(64, 65)
                po = head * 64
                ocols = slice(T * hp + 512 * tt, T * hp + 512 * (tt + 1))
                P.wait(v, "SPV", S["gg"])
                with nc.allow_low_precision(reason="bf16 stage for matmul"):
                    den_cast = P.inc(v.tensor_copy(Z["rden_bf"][dr, :],
                                                   pv[64:65, :]), "VEC")
                with nc.allow_low_precision(reason="unnormalised bf16 stage"):
                    vi = P.inc(v.tensor_copy(o_sb[po:po + 64, ocols],
                                             pv[0:64, :]), "VEC")
                if head == 0:
                    S["vec_pvA_free"] = vi
                else:
                    S["vec_pvB_free"] = vi
                P.wait(t, "VEC", den_cast)
                P.wait(t, "VEC", S["vec_bc_free"])
                mm = t.matmul(bc_ps[:, :], ones_sb[dr, 0:128],
                              Z["rden_bf"][dr, :], start=True, stop=True,
                              skip_group_check=True)
                P.inc(mm, "PE")
                P.wait(v, "PE")
                P.wait(v, "VEC")
                P.inc(v.tensor_scalar(bcx[:, :], bc_ps[:, :],
                                      1e-8, None, ALU.add), "VEC")
                S["vec_bc_free"] = P.c["VEC"]
                P.wait(v, "VEC")
                P.inc(v.reciprocal(bcx[:, :], bcx[:, :]), "VEC")
                P.wait(v, "VEC")
                P.inc(v.tensor_tensor(o_sb[po:po + 64, ocols],
                                      o_sb[po:po + 64, ocols],
                                      bcx[po:po + 64, :], ALU.mult), "VEC")

    P.wait(t, "SEXP", S["gg"])
    P.wait(t, "VEC", S["vec_pvA_free"])
    P.wait(t, "VEC", S["vec_pvB_free"])
    for tt in range(NTT):
        for dc in range(NCH):
            wv = st_ps[0][:, 0:512] if dc % 2 == 0 else st_ps[0][:, 512:1024]
            P.wait(t, "VEC")
            for r in range(NCH):
                mm = t.matmul(wv,
                              Z["wo_sb"][:, D * r + 128 * dc: D * r + 128 * (dc + 1)],
                              o_sb[:, T * r + 512 * tt: T * r + 512 * (tt + 1)],
                              start=(r == 0), stop=(r == NCH - 1),
                              skip_group_check=True)
            P.inc(mm, "PE")
            P.wait(v, "PE")
            xv = x_sb[:, T * dc + 512 * tt: T * dc + 512 * (tt + 1)]
            P.inc(v.scalar_tensor_tensor(xv, wv, Z["boc_sb"][:, dc:dc + 1], xv,
                                         ALU.add, ALU.add), "VEC")
            P.wait(t, "VEC")
            mm = t.matmul(pvA_ps[0:1, :], ones_sb[:, 0:1], xv,
                          start=(dc == 0), stop=(dc == NCH - 1),
                          skip_group_check=True)
            P.inc(mm, "PE")
            P.wait(s, "VEC")
            P.inc(s.activation(Z["ysq_sb"][:, :], xv, AF.Square), "ACT")
            P.wait(t, "ACT")
            mm = t.matmul(pvA_ps[32:33, :], ones_sb[:, 0:1], Z["ysq_sb"][:, :],
                          start=(dc == 0), stop=(dc == NCH - 1),
                          skip_group_check=True)
            P.inc(mm, "PE")
        P.wait(v, "PE")
        P.inc(v.tensor_scalar(Z["mean_sb"][0:1, :], pvA_ps[0:1, :],
                              1.0 / D, None, ALU.mult), "VEC")
        P.inc(v.tensor_scalar(Z["msq_sb"][:, :], pvA_ps[32:33, :],
                              1.0 / D, None, ALU.mult), "VEC")
        P.wait(v, "VEC")
        P.inc(v.tensor_tensor(Z["var_sb"][0:1, :], Z["mean_sb"][0:1, :],
                              Z["mean_sb"][0:1, :], ALU.mult), "VEC")
        P.wait(v, "VEC")
        P.inc(v.tensor_sub(Z["var_sb"][0:1, :], Z["msq_sb"][:, :],
                           Z["var_sb"][0:1, :]), "VEC")
        P.wait(s, "VEC")
        P.inc(s.activation(Z["var_sb"][0:1, :], Z["var_sb"][0:1, :],
                           AF.Sqrt, bias=Z["c5_sb"][0:1, 0:1], scale=1.0),
              "ACT")
        P.wait(v, "ACT")
        P.inc(v.reciprocal(Z["var_sb"][0:1, :], Z["var_sb"][0:1, :]), "VEC")
        P.wait(v, "VEC")
        with nc.allow_low_precision(reason="bf16 broadcast operands"):
            P.inc(v.tensor_copy(Z["mean_bf"][0:1, :], Z["mean_sb"][0:1, :]),
                  "VEC")
            P.inc(v.tensor_copy(Z["rstd_bf"][0:1, :], Z["var_sb"][0:1, :]),
                  "VEC")
        P.wait(t, "VEC")
        t.matmul(st_ps[1][:, 0:512], ones_sb[0:1, 0:128], Z["mean_bf"][0:1, :],
                 start=True, stop=True, skip_group_check=True)
        mm = t.matmul(st_ps[1][:, 512:1024], ones_sb[0:1, 0:128],
                      Z["rstd_bf"][0:1, :], start=True, stop=True,
                      skip_group_check=True)
        P.inc(mm, "PE")
        P.wait(v, "PE")
        P.inc(v.tensor_copy(Z["mbc_sb"][:, :], st_ps[1][:, 0:512]), "VEC")
        P.inc(v.tensor_copy(Z["rbc_sb"][:, :], st_ps[1][:, 512:1024]), "VEC")
        P.wait(v, "VEC")
        for dc in range(NCH):
            xv = x_sb[:, T * dc + 512 * tt: T * dc + 512 * (tt + 1)]
            P.inc(v.tensor_sub(xv, xv, Z["mbc_sb"][:, :]), "VEC")
            P.wait(v, "VEC")
            P.inc(v.tensor_tensor(xv, xv, Z["rbc_sb"][:, :], ALU.mult), "VEC")
            P.wait(v, "VEC")
    vec_x2 = P.c["VEC"]

    for tt in range(NTT):
        P.wait(t, "VEC", vec_x2)
        for c in range(NCH):
            mm = t.matmul(st_ps[0][0:16, 512:1024],
                          Z["wg_sb"][:, 16 * c:16 * (c + 1)],
                          x_sb[:, T * c + 512 * tt: T * c + 512 * (tt + 1)],
                          start=(c == 0), stop=(c == NCH - 1),
                          skip_group_check=True)
        P.inc(mm, "PE")
        P.wait(s, "PE")
        P.wait(s, "VEC")
        si = s.activation(Z["lg_tt"][0:16, :], st_ps[0][0:16, 512:1024],
                          AF.Identity, bias=Z["bgc_sb"][0:16, 0:1], scale=1.0)
        P.inc(si, "ACT")
        P.wait(t, "ACT")
        for j in range(4):
            i = 4 * tt + j
            P.wait(t, "VEC")
            mm = t.transpose(pvB_ps[0:128, 0:16],
                             Z["lg_tt"][0:16, 128 * j:128 * (j + 1)],
                             Z["id16_sb"][:, :])
            P.inc(mm, "PE")
            P.wait(v, "PE")
            P.inc(v.tensor_copy(Z["lg_tm"][:, 16 * i:16 * (i + 1)],
                                pvB_ps[0:128, 0:16]), "VEC")
        P.wait(t, "VEC")
    k2r = Z["k2_sb"]
    mxa = k2r[:, 0:64].bitcast(F32)
    nmxa = k2r[:, 64:128].bitcast(F32)
    sma = k2r[:, 128:192].bitcast(F32)
    eqa = k2r[:, 192:1216].bitcast(F32)
    cana = k2r[:, 1216:2240].bitcast(F32)
    t1a = k2r[:, 2240:2304].bitcast(F32)
    ohba = k2r[:, 2304:2816]
    sela = k2r[:, 2816:3840].bitcast(F32)
    csa = k2r[:, 3840:3904].bitcast(F32)
    ovga = k2r[:, 3904:3968].bitcast(F32)
    P.wait(v, "VEC")
    for i in range(NST):
        ltile = Z["lg_tm"][:, 16 * i:16 * (i + 1)]
        P.inc(v.reduce_max(mxa[:, i:i + 1], ltile, axis=AX), "VEC")
        P.inc(v.reduce_max(nmxa[:, i:i + 1], ltile, axis=AX, negate=True),
              "VEC")
    P.wait(s, "VEC")
    for i in range(NST):
        ltile = Z["lg_tm"][:, 16 * i:16 * (i + 1)]
        si = s.activation(Z["ejunk_sb"][:, :], ltile, AF.Exp,
                          bias=nmxa[:, i:i + 1], scale=1.0,
                          accum_out=sma[:, i:i + 1])
        P.inc(si, "ACT")
    P.wait(v, "VEC")
    for i in range(NST):
        ltile = Z["lg_tm"][:, 16 * i:16 * (i + 1)]
        P.inc(v.tensor_scalar(eqa[:, 16 * i:16 * (i + 1)], ltile,
                              mxa[:, i:i + 1], None, ALU.is_equal), "VEC")
    P.wait(v, "ACT")
    for i in range(NST):
        P.inc(v.reciprocal(Z["meta_sb"][:, 4 * i + 2:4 * i + 3],
                           sma[:, i:i + 1]), "VEC")
    P.wait(v, "VEC")
    for i in range(NST):
        P.inc(v.tensor_tensor(cana[:, 16 * i:16 * (i + 1)], Z["r16m_sb"][:, :],
                              eqa[:, 16 * i:16 * (i + 1)], ALU.mult), "VEC")
    P.wait(v, "VEC")
    for i in range(NST):
        P.inc(v.reduce_max(t1a[:, i:i + 1], cana[:, 16 * i:16 * (i + 1)],
                           axis=AX), "VEC")
    P.wait(v, "VEC")
    for i in range(NST):
        P.inc(v.tensor_scalar(Z["idx_all"][:, i:i + 1], t1a[:, i:i + 1],
                              -1.0, 16.0, ALU.mult, ALU.add), "VEC")
    P.wait(v, "VEC")
    for i in range(NST):
        P.inc(v.tensor_copy(Z["meta_sb"][:, 4 * i:4 * i + 1],
                            Z["idx_all"][:, i:i + 1]), "VEC")
        P.inc(v.tensor_scalar(Z["ohf_all"][:, 16 * i:16 * (i + 1)],
                              Z["iota_sb"][:, :], Z["idx_all"][:, i:i + 1],
                              None, ALU.is_equal), "VEC")
        with nc.allow_low_precision(reason="onehot bf16"):
            P.inc(v.tensor_scalar(ohba[:, 16 * i:16 * (i + 1)],
                                  Z["iota_sb"][:, :], Z["idx_all"][:, i:i + 1],
                                  None, ALU.is_equal), "VEC")
    for r in range(NST // 8):
        P.wait(t, "VEC")
        for j in range(8):
            i = 8 * r + j
            mm = t.transpose(trb_ps[0:16, 128 * j:128 * (j + 1)],
                             ohba[:, 16 * i:16 * (i + 1)], Z["idbf_sb"][:, :])
            P.inc(mm, "PE")
        P.wait(v, "PE")
        for j in range(8):
            i = 8 * r + j
            P.inc(v.tensor_copy(Z["ohfm_sb"][0:16, 128 * i:128 * (i + 1)],
                                trb_ps[0:16, 128 * j:128 * (j + 1)]), "VEC")
    P.wait(v, "VEC")
    P.inc(v.tensor_tensor_scan(Z["csum_sb"][:, :], Z["ohfm_sb"][:, :],
                               Z["ohfm_sb"][:, :], 0.0, ALU.add, ALU.bypass),
          "VEC")
    P.wait(v, "VEC")
    P.inc(v.tensor_copy(Z["cnt_sb"][:, cnt_col:cnt_col + 1],
                        Z["csum_sb"][:, T - 1:T]), "VEC")
    S["vec_cnt"] = P.c["VEC"]
    P.wait(t, "VEC")
    for i in range(NST):
        mm = t.transpose(pvB_ps[0:128, 16 * i:16 * (i + 1)],
                         Z["csum_sb"][0:16, 128 * i:128 * (i + 1)],
                         Z["id16_sb"][:, :])
        P.inc(mm, "PE")
    P.wait(v, "PE")
    for i in range(NST):
        P.inc(v.tensor_tensor(sela[:, 16 * i:16 * (i + 1)],
                              pvB_ps[0:128, 16 * i:16 * (i + 1)],
                              Z["ohf_all"][:, 16 * i:16 * (i + 1)], ALU.mult),
              "VEC")
    P.wait(v, "VEC")
    for i in range(NST):
        P.inc(v.reduce_sum(csa[:, i:i + 1], sela[:, 16 * i:16 * (i + 1)],
                           axis=AX), "VEC")
    P.wait(v, "VEC")
    for i in range(NST):
        P.inc(v.tensor_scalar(Z["meta_sb"][:, 4 * i + 1:4 * i + 2],
                              csa[:, i:i + 1], -1.0, None, ALU.add), "VEC")
        P.inc(v.memset(Z["meta_sb"][:, 4 * i + 3:4 * i + 4], 0.0), "VEC")
        P.inc(v.tensor_scalar(Z["fif"][:, i:i + 1], Z["idx_all"][:, i:i + 1],
                              float(LCAP), None, ALU.mult), "VEC")
    P.wait(v, "VEC")
    for i in range(NST):
        P.inc(v.tensor_scalar(ovga[:, i:i + 1],
                              Z["meta_sb"][:, 4 * i + 1:4 * i + 2],
                              float(LCAP), BIGIX, ALU.is_ge, ALU.mult), "VEC")
        P.inc(v.tensor_tensor(Z["fif"][:, i:i + 1], Z["fif"][:, i:i + 1],
                              Z["meta_sb"][:, 4 * i + 1:4 * i + 2], ALU.add),
              "VEC")
    P.wait(v, "VEC")
    for i in range(NST):
        P.inc(v.tensor_tensor(Z["fif"][:, i:i + 1], Z["fif"][:, i:i + 1],
                              ovga[:, i:i + 1], ALU.add), "VEC")
    P.wait(v, "VEC")
    for i in range(NST):
        P.inc(v.tensor_copy(Z["fiu"][:, i:i + 1], Z["fif"][:, i:i + 1]),
              "VEC")
    P.wait(v, "VEC")
    P.wait(sy, "VEC")
    P.inc(sy.dma_start(out=meta_ap.rearrange("(i p) c -> p i c", p=128),
                       in_=Z["meta_sb"].rearrange("p (i c) -> p i c", c=4)),
          "ST", 16)
    for i in range(NST):
        b = i % 2
        sxs = "SX0" if b == 0 else "SX1"
        buf = Z["x2st"][b]
        first = True
        for dc in range(NCH):
            P.wait(t, "VEC")
            mm = t.transpose(trb_ps[0:128, 0:128],
                             x_sb[:, T * dc + 128 * i: T * dc + 128 * (i + 1)],
                             Z["idbf_sb"][:, :])
            P.inc(mm, "PE")
            P.wait(v, "PE")
            if first:
                P.wait(v, sxs)
            first = False
            P.inc(v.tensor_copy(buf[:, 128 * dc:128 * (dc + 1)],
                                trb_ps[0:128, 0:128]), "VEC")
        P.wait(sy, "VEC")
        P.inc(sy.dma_start(out=x2tm_ap[128 * i:128 * (i + 1), :], in_=buf[:, :]),
              sxs, 16)
        P.wait(g, "VEC")
        P.inc(g.indirect_dma_start(
            out=fsrc_ap,
            out_offset=IndirectOffsetOnAxis(Z["fiu"][:, i:i + 1], 0),
            in_=buf[:, :], in_offset=None,
            bounds_check=bchk, oob_is_err=False), sxs, 16)


def emit_l1_const_loads(nc, P, Z, aps):
    g, sy = nc.gpsimd, nc.sync
    for c in range(NCH):
        P.inc(g.dma_start(out=Z["wo_sb"][:, D * c: D * (c + 1)],
                          in_=aps["woT"][128 * c:128 * (c + 1), :]), "LX", 16)
    P.inc(g.dma_start(out=Z["wg_sb"].ap().rearrange("p (c e) -> p c e", c=NCH),
                      in_=aps["wg"].rearrange("(c p) e -> p c e", p=128)),
          "LX", 16)
    for dk, sk in (("boc_sb", "boc"), ("bgc_sb", "bgc"), ("idbf_sb", "idbf"),
                   ("id16_sb", "id16"), ("iota_sb", "iota16b"),
                   ("r16m_sb", "r16m"), ("ones_sb", "onesb"),
                   ("odd16_sb", "odd16"),
                   ("onesf_sb", "onesf"), ("tri8_sb", "tri8"),
                   ("eh01_sb", "eh01"), ("pmask_sb", "pmask"),
                   ("iota128_sb", "iota128"), ("b1c_sb", "b1c"),
                   ("b2c_sb", "b2c")):
        P.inc(sy.dma_start(out=Z[dk][:, :], in_=aps[sk][:, :]), "LS", 16)


def make_headpair_prefetch(nc, P, Z, S, wq_ap, wk_ap, wv_ap, npasses=1):
    g = nc.gpsimd

    def load_pair(hp, bsel):
        sem = "WH0" if bsel == 0 else "WH1"
        P.wait(g, "PE", S["wh_freed"][bsel])
        for mi, wsrc in enumerate((wk_ap, wq_ap, wv_ap)):
            ins = g.dma_start(
                out=Z["wh_sb"][bsel][:, D * mi:D * (mi + 1)].rearrange(
                    "p (c f) -> p c f", c=NCH),
                in_=wsrc[:, 128 * hp:128 * (hp + 1)].rearrange(
                    "(c p) f -> p c f", p=128))
            P.inc(ins, sem, 16)
        S["wh_ready"][bsel] = P.c[sem]

    seq = {"next": 1}

    def prefetch(_hp):
        nxt = seq["next"]
        if nxt < 8 * npasses:
            load_pair(nxt % 8, nxt % 2)
            seq["next"] = nxt + 1

    load_pair(0, 0)
    return prefetch



def emit_prolog_b(nc, P, Z, cnts1_ap, cnts8_ap, vec_cnt):
    g, sy, t, v, s = nc.gpsimd, nc.sync, nc.tensor, nc.vector, nc.scalar
    bc_ps = Z["bc_ps"]
    P.wait(s, "VEC", vec_cnt)
    P.inc(s.dma_start(out=cnts1_ap.rearrange("one e -> e one"),
                      in_=Z["cnt_sb"][:, 0:1]), "CN", 16)
    P.wait(g, "CN")
    cc = g.collective_compute(
        "AllGather", ALU.bypass, replica_groups=[list(range(NC))],
        ins=[cnts1_ap.opt()], outs=[cnts8_ap.opt()])
    P.inc(cc, "CCA", 1)
    P.wait(g, "CCA")
    P.inc(g.dma_start(out=Z["cnts8_sb"][:, :], in_=cnts8_ap), "CN", 16)
    pe0, vec0 = P.c["PE"], P.c["VEC"]
    P.wait(t, "CN")
    P.wait(t, "VEC", vec0)
    mm = t.matmul(bc_ps[0:16, 0:8], Z["cnts8_sb"][:, :], Z["tri8_sb"][:, :],
                  start=True, stop=True, skip_group_check=True)
    P.inc(mm, "PE")
    P.wait(v, "PE")
    P.inc(v.tensor_copy(Z["baseT_sb"][:, :], bc_ps[0:16, 0:8]), "VEC")
    P.wait(t, "VEC")
    mm = t.matmul(bc_ps[0:1, 16:32], Z["pmask_sb"][:, :], Z["cnts8_sb"][:, :],
                  start=True, stop=True, skip_group_check=True)
    P.inc(mm, "PE")
    P.wait(v, "PE")
    P.inc(v.tensor_copy(Z["gbrow_sb"][:, :], bc_ps[0:1, 16:32]), "VEC")
    P.wait(t, "VEC")
    for k in range(2):
        mm = t.matmul(bc_ps[0:1, 32 + 8 * k:40 + 8 * k],
                      Z["eh01_sb"][:, k:k + 1], Z["baseT_sb"][:, :],
                      start=True, stop=True, skip_group_check=True)
        P.inc(mm, "PE")
    P.wait(v, "PE")
    P.inc(v.tensor_copy(Z["rowsel_sb"][:, 0:8], bc_ps[0:1, 32:40]), "VEC")
    P.inc(v.tensor_copy(Z["rowsel_sb"][:, 8:16], bc_ps[0:1, 40:48]), "VEC")
    P.wait(t, "VEC")
    mm = t.matmul(bc_ps[:, 48:64], Z["onesf_sb"][0:1, :],
                  Z["rowsel_sb"][0:1, :], start=True, stop=True,
                  skip_group_check=True)
    P.inc(mm, "PE")
    mm = t.matmul(bc_ps[:, 64:80], Z["onesf_sb"][0:1, :],
                  Z["gbrow_sb"][0:1, :], start=True, stop=True,
                  skip_group_check=True)
    P.inc(mm, "PE")
    P.wait(v, "PE")
    P.inc(v.tensor_copy(Z["bbc_sb"][:, :], bc_ps[:, 48:64]), "VEC")
    P.inc(v.tensor_copy(Z["gbb_sb"][:, :], bc_ps[:, 64:80]), "VEC")
    P.wait(v, "VEC")
    P.wait(v, "PE")
    k2 = Z["k2_sb"]
    pc_all = k2[:, 0:64].bitcast(F32)
    cmp_all = k2[:, 64:576].bitcast(F32)
    nle_all = k2[:, 576:640].bitcast(F32)
    rr_all = k2[:, 640:704].bitcast(F32)
    ovg_all = k2[:, 704:768].bitcast(F32)
    cols = [(ep * 16 + i, ep, i) for ep in range(2) for i in range(CAP // 128)]
    for col, ep, i in cols:
        P.inc(v.tensor_scalar(pc_all[:, col:col + 1], Z["iota128_sb"][:, :],
                              float(128 * i), None, ALU.add), "VEC")
    P.wait(v, "VEC")
    for col, ep, i in cols:
        P.inc(v.tensor_scalar(cmp_all[:, 8 * col:8 * col + 8],
                              Z["bbc_sb"][:, 8 * ep:8 * ep + 8],
                              pc_all[:, col:col + 1], None, ALU.is_le), "VEC")
    P.wait(v, "VEC")
    for col, ep, i in cols:
        P.inc(v.reduce_sum(nle_all[:, col:col + 1],
                           cmp_all[:, 8 * col:8 * col + 8], axis=AX), "VEC")
    P.wait(v, "VEC")
    for col, ep, i in cols:
        P.inc(v.tensor_tensor(cmp_all[:, 8 * col:8 * col + 8],
                              cmp_all[:, 8 * col:8 * col + 8],
                              Z["bbc_sb"][:, 8 * ep:8 * ep + 8], ALU.mult),
              "VEC")
    P.wait(v, "VEC")
    for col, ep, i in cols:
        P.inc(v.reduce_max(rr_all[:, col:col + 1],
                           cmp_all[:, 8 * col:8 * col + 8], axis=AX), "VEC")
    P.wait(v, "VEC")
    for col, ep, i in cols:
        P.inc(v.tensor_sub(rr_all[:, col:col + 1], pc_all[:, col:col + 1],
                           rr_all[:, col:col + 1]), "VEC")
    P.wait(v, "VEC")
    for col, ep, i in cols:
        P.inc(v.tensor_scalar(ovg_all[:, col:col + 1], rr_all[:, col:col + 1],
                              float(LCAP), BIGIX, ALU.is_ge, ALU.mult), "VEC")
        P.inc(v.tensor_scalar(Z["gif"][:, col:col + 1], nle_all[:, col:col + 1],
                              2048.0, float(1024 * ep - 2048), ALU.mult,
                              ALU.add), "VEC")
        P.inc(v.tensor_scalar(Z["gif2"][:, col:col + 1],
                              nle_all[:, col:col + 1],
                              1024.0, float(8192 * ep - 1024), ALU.mult,
                              ALU.add), "VEC")
    P.wait(v, "VEC")
    for col, ep, i in cols:
        P.inc(v.tensor_tensor(Z["gif"][:, col:col + 1],
                              Z["gif"][:, col:col + 1], rr_all[:, col:col + 1],
                              ALU.add), "VEC")
        P.inc(v.tensor_tensor(Z["gif2"][:, col:col + 1],
                              Z["gif2"][:, col:col + 1], rr_all[:, col:col + 1],
                              ALU.add), "VEC")
    P.wait(v, "VEC")
    for col, ep, i in cols:
        P.inc(v.tensor_tensor(Z["gif"][:, col:col + 1],
                              Z["gif"][:, col:col + 1], ovg_all[:, col:col + 1],
                              ALU.add), "VEC")
        P.inc(v.tensor_tensor(Z["gif2"][:, col:col + 1],
                              Z["gif2"][:, col:col + 1], ovg_all[:, col:col + 1],
                              ALU.add), "VEC")
    P.wait(v, "VEC")
    for col, ep, i in cols:
        P.inc(v.tensor_copy(Z["giu"][:, col:col + 1],
                            Z["gif"][:, col:col + 1]), "VEC")
        P.inc(v.tensor_copy(Z["giu2"][:, col:col + 1],
                            Z["gif2"][:, col:col + 1]), "VEC")
    P.wait(v, "VEC")



def make_ffn_loaders(nc, P, ZF, w1_ap, w2_ap):
    g = nc.gpsimd
    w1_sb, w2_sb = ZF["w1_sb"], ZF["w2_sb"]
    FS = {"w1_ready": [0, 0], "w1_freed": [0, 0],
          "w2_ready": [0, 0], "w2_freed": [0, 0]}

    def load_w1(bsel, e, hc):
        P.wait(g, "PE", FS["w1_freed"][bsel])
        ins = g.dma_start(
            out=w1_sb[bsel].rearrange("p (c f) -> p c f", c=NCH),
            in_=w1_ap[e, :, 128 * hc:128 * (hc + 1)].rearrange(
                "(c p) f -> p c f", p=128))
        FS["w1_ready"][bsel] = P.inc(ins, "W10" if bsel == 0 else "W11", 16)

    def load_w2(bsel, e, dc):
        P.wait(g, "PE", FS["w2_freed"][bsel])
        ins = g.dma_start(
            out=w2_sb[bsel].rearrange("p (c f) -> p c f", c=32),
            in_=w2_ap[e, :, 128 * dc:128 * (dc + 1)].rearrange(
                "(c p) f -> p c f", p=128))
        FS["w2_ready"][bsel] = P.inc(ins, "W20" if bsel == 0 else "W21", 16)

    return FS, load_w1, load_w2


def emit_ffn(nc, P, Z, ZF, loaders, fdst_ap, rsrc_ap, giu_done, bchk,
             mid_hook=None):
    g, sy, t, v, s = nc.gpsimd, nc.sync, nc.tensor, nc.vector, nc.scalar
    buf_fm = ZF["buf_fm"]
    bt_sb, yt_sb = ZF["bt_sb"], ZF["yt_sb"]
    w1_sb, w2_sb = ZF["w1_sb"], ZF["w2_sb"]
    h_sb, yfm_sb = ZF["h_sb"], ZF["yfm_sb"]
    h_ps, y_ps, tr_ps = Z["pvA_ps"], Z["pvB_ps"], Z["trb_ps"]
    b1_sb, b2_sb, idbf_sb = Z["b1c_sb"], Z["b2c_sb"], Z["idbf_sb"]
    FS, load_w1, load_w2 = loaders
    w1_ready, w1_freed = FS["w1_ready"], FS["w1_freed"]
    w2_ready, w2_freed = FS["w2_ready"], FS["w2_freed"]
    tr_after = {}
    vec_strip = {}

    for e in range(2):
        for i in range(CAP // 128):
            b = i % 2
            bts = "BT0" if b == 0 else "BT1"
            if i >= 2 or e > 0:
                P.wait(g, "PE", tr_after.get((e, i - 2), tr_after.get(
                    (e - 1, CAP // 128 + i - 2), 0)))
            P.wait(g, "VEC", giu_done)
            col = e * 16 + i
            P.inc(g.indirect_dma_start(
                out=bt_sb[b][:, :], out_offset=None,
                in_=fdst_ap,
                in_offset=IndirectOffsetOnAxis(Z["giu"][:, col:col + 1], 0),
                bounds_check=bchk, oob_is_err=False), bts, 16)
            P.wait(t, bts)
            P.wait(t, "LS", P.c["LS"])
            P.wait(t, "VEC")
            for dc in range(NCH):
                P.wait(t, "VEC")
                mm = t.transpose(tr_ps[0:128, 0:128],
                                 bt_sb[b][:, 128 * dc:128 * (dc + 1)],
                                 idbf_sb[:, :])
                P.inc(mm, "PE")
                P.wait(v, "PE")
                P.inc(v.tensor_copy(
                    buf_fm[:, CAP * dc + 128 * i:CAP * dc + 128 * (i + 1)],
                    tr_ps[0:128, 0:128]), "VEC")
            tr_after[(e, i)] = P.c["PE"]
            if i % 4 == 3:
                vec_strip[(e, i // 4)] = P.c["VEC"]
        if e > 0:
            load_w1(0, e, 0)
        for st in range(4):
            act_of = {}
            P.wait(t, "ACT")
            for hc in range(32):
                bsel = hc % 2
                if hc + 1 < 32:
                    load_w1((hc + 1) % 2, e, hc + 1)
                elif st + 1 < 4:
                    load_w1((hc + 1) % 2, e, 0)
                t.wait_ge(P.sem["W10" if bsel == 0 else "W11"], w1_ready[bsel])
                P.wait(t, "VEC", vec_strip[(e, st)])
                hps = h_ps if hc % 2 == 0 else y_ps
                if hc >= 2:
                    P.wait(t, "ACT", act_of.get(hc - 2, 0))
                for c in range(NCH):
                    mm = t.matmul(hps[:, :],
                                  w1_sb[bsel][:, 128 * c:128 * (c + 1)],
                                  buf_fm[:, CAP * c + 512 * st:
                                         CAP * c + 512 * (st + 1)],
                                  start=(c == 0), stop=(c == NCH - 1),
                                  skip_group_check=True)
                P.inc(mm, "PE")
                w1_freed[bsel] = P.c["PE"]
                P.wait(s, "PE")
                si = s.activation(h_sb[:, 512 * hc:512 * (hc + 1)], hps[:, :],
                                  AF.Relu,
                                  bias=b1_sb[:, 32 * e + hc:32 * e + hc + 1],
                                  scale=1.0)
                act_of[hc] = P.inc(si, "ACT")
            load_w2(0, e, 0)
            yact = {}
            for dc in range(NCH):
                bsel = dc % 2
                if dc + 1 < NCH:
                    load_w2((dc + 1) % 2, e, dc + 1)
                t.wait_ge(P.sem["W20" if bsel == 0 else "W21"], w2_ready[bsel])
                yps = h_ps if dc % 2 == 0 else y_ps
                if dc < 2:
                    P.wait(t, "ACT", act_of[31])
                else:
                    P.wait(t, "ACT", yact[dc - 2])
                for hc in range(32):
                    mm = t.matmul(yps[:, :],
                                  w2_sb[bsel][:, 128 * hc:128 * (hc + 1)],
                                  h_sb[:, 512 * hc:512 * (hc + 1)],
                                  start=(hc == 0), stop=(hc == 31),
                                  skip_group_check=True)
                P.inc(mm, "PE")
                w2_freed[bsel] = P.c["PE"]
                P.wait(s, "PE")
                si = s.activation(yfm_sb[:, 512 * dc:512 * (dc + 1)], yps[:, :],
                                  AF.Identity,
                                  bias=b2_sb[:, 8 * e + dc:8 * e + dc + 1],
                                  scale=1.0)
                yact[dc] = P.inc(si, "ACT")
            P.wait(t, "ACT")
            for j in range(4):
                jb = j % 2
                sys_ = "SY0" if jb == 0 else "SY1"
                buf = yt_sb[jb]
                first = True
                for dc in range(NCH):
                    P.wait(t, "VEC")
                    mm = t.transpose(tr_ps[0:128, 0:128],
                                     yfm_sb[:, 512 * dc + 128 * j:
                                            512 * dc + 128 * (j + 1)],
                                     idbf_sb[:, :])
                    P.inc(mm, "PE")
                    P.wait(v, "PE")
                    if first:
                        P.wait(v, sys_)
                    first = False
                    P.inc(v.tensor_copy(buf[:, 128 * dc:128 * (dc + 1)],
                                        tr_ps[0:128, 0:128]), "VEC")
                P.wait(g, "VEC")
                col = e * 16 + 4 * st + j
                P.inc(g.indirect_dma_start(
                    out=rsrc_ap,
                    out_offset=IndirectOffsetOnAxis(Z["giu2"][:, col:col + 1],
                                                    0),
                    in_=buf[:, :], in_offset=None,
                    bounds_check=bchk, oob_is_err=False), sys_, 16)
        if e == 0 and mid_hook is not None:
            mid_hook()



def emit_phase_c(nc, P, Z, ZF, x2tm_ap, meta_ap, rdst_ap, out_ap, bchk,
                 pe_attn):
    g, sy, t, v, s = nc.gpsimd, nc.sync, nc.tensor, nc.vector, nc.scalar
    ybig = Z["x_sb"]
    x2all = Z["o_sb"]
    q2, k2 = Z["q2_sb"], Z["k2_sb"]
    eqall = q2[:, 0:1024].bitcast(F32)
    selall = q2[:, 1024:2048].bitcast(F32)
    lnall = q2[:, 2048:2560].bitcast(F32)
    mt_all = q2[:, 2560:2816].bitcast(F32)
    o2b = [k2[:, 0:2048].bitcast(F32), k2[:, 2048:4096].bitcast(F32)]
    zb = [Z["vaugA"][:, 0:1024], Z["vaugA"][:, 1024:2048],
          Z["vaugB"][:, 0:1024], Z["vaugB"][:, 1024:2048]]
    zdump = Z["expS"]

    P.wait(sy, "SX0")
    P.wait(sy, "SX1")
    P.wait(sy, "SY0")
    P.wait(sy, "SY1")
    P.wait(sy, "ST")
    P.wait(sy, "PE")
    P.wait(sy, "VEC")
    P.wait(sy, "ACT")
    for i in range(0, 9):
        P.inc(sy.dma_start(out=x2all[:, 1024 * i:1024 * (i + 1)],
                           in_=x2tm_ap[128 * i:128 * (i + 1), :]), "LT0", 16)
    P.wait(v, "PE", pe_attn)
    P.wait(v, "LT0", 16)
    P.wait(v, "VEC")
    for i in range(NST):
        P.inc(v.tensor_scalar(eqall[:, 16 * i:16 * (i + 1)], Z["iota_sb"][:, :],
                              mt_all[:, 4 * i:4 * i + 1], None, ALU.is_equal),
              "VEC")
    P.wait(v, "VEC")
    for i in range(NST):
        P.inc(v.tensor_tensor(selall[:, 16 * i:16 * (i + 1)],
                              eqall[:, 16 * i:16 * (i + 1)], Z["gbb_sb"][:, :],
                              ALU.mult), "VEC")
    P.wait(v, "VEC")
    for i in range(NST):
        P.inc(v.reduce_sum(lnall[:, 8 * i + 6:8 * i + 7],
                           selall[:, 16 * i:16 * (i + 1)], axis=AX), "VEC")
    P.wait(v, "VEC")
    for i in range(NST):
        P.inc(v.tensor_tensor(lnall[:, 8 * i + 6:8 * i + 7],
                              lnall[:, 8 * i + 6:8 * i + 7],
                              mt_all[:, 4 * i + 1:4 * i + 2], ALU.add), "VEC")
    P.wait(v, "VEC")
    for i in range(NST):
        P.inc(v.tensor_scalar(lnall[:, 8 * i + 7:8 * i + 8],
                              lnall[:, 8 * i + 6:8 * i + 7], float(CAP),
                              BIGIX, ALU.is_ge, ALU.mult), "VEC")
    bcola = q2[:, 2816:2880].bitcast(F32)
    f2a = q2[:, 2880:2944].bitcast(F32)
    P.wait(v, "VEC")
    for i in range(NST):
        P.inc(v.tensor_tensor(selall[:, 16 * i:16 * (i + 1)],
                              eqall[:, 16 * i:16 * (i + 1)],
                              Z["odd16_sb"][:, :], ALU.mult), "VEC")
        P.inc(v.tensor_scalar(f2a[:, i:i + 1], mt_all[:, 4 * i:4 * i + 1],
                              512.0, None, ALU.mult), "VEC")
    P.wait(v, "VEC")
    for i in range(NST):
        P.inc(v.reduce_sum(bcola[:, i:i + 1], selall[:, 16 * i:16 * (i + 1)],
                           axis=AX), "VEC")
    P.wait(v, "VEC")
    for i in range(NST):
        P.inc(v.tensor_scalar(bcola[:, i:i + 1], bcola[:, i:i + 1],
                              7680.0, None, ALU.mult), "VEC")
        P.inc(v.tensor_tensor(f2a[:, i:i + 1], f2a[:, i:i + 1],
                              mt_all[:, 4 * i + 1:4 * i + 2], ALU.add), "VEC")
    P.wait(v, "VEC")
    for i in range(NST):
        P.inc(v.tensor_tensor(f2a[:, i:i + 1], f2a[:, i:i + 1],
                              bcola[:, i:i + 1], ALU.add), "VEC")
        P.inc(v.tensor_scalar(bcola[:, i:i + 1],
                              mt_all[:, 4 * i + 1:4 * i + 2],
                              float(LCAP), BIGIX, ALU.is_ge, ALU.mult), "VEC")
    P.wait(v, "VEC")
    for i in range(NST):
        P.inc(v.tensor_tensor(f2a[:, i:i + 1], f2a[:, i:i + 1],
                              bcola[:, i:i + 1], ALU.add), "VEC")
    P.wait(v, "VEC")
    for i in range(NST):
        P.inc(v.tensor_tensor(f2a[:, i:i + 1], f2a[:, i:i + 1],
                              lnall[:, 8 * i + 7:8 * i + 8], ALU.add), "VEC")
    P.wait(v, "VEC")
    for i in range(NST):
        P.inc(v.tensor_copy(Z["fiu"][:, i:i + 1], f2a[:, i:i + 1]),
              "VEC")
    P.inc(v.memset(ybig[:, :], 0.0), "VEC")
    P.wait(g, "PE")
    P.wait(g, "VEC")
    for i in range(9, 18):
        P.inc(g.dma_start(out=x2all[:, 1024 * i:1024 * (i + 1)],
                          in_=x2tm_ap[128 * i:128 * (i + 1), :]), "LT1", 16)
    P.wait(g, "CCR", 2)
    for i in range(NST):
        P.inc(g.indirect_dma_start(
            out=ybig[:, 1024 * i:1024 * (i + 1)], out_offset=None,
            in_=rdst_ap,
            in_offset=IndirectOffsetOnAxis(Z["fiu"][:, i:i + 1], 0),
            bounds_check=bchk, oob_is_err=False), "GB0", 16)
    P.wait(v, "LT0")
    P.wait(v, "LT1")
    zv = {}
    sums = {}
    o2v = {}
    for i in range(NST):
        zbi = zb[i % 4]
        obi = o2b[i % 2]
        if i >= 4:
            P.wait(v, "VEC", o2v[i - 4])
        P.wait(v, "GB0", 16 * (i + 1))
        zv[i] = P.inc(v.scalar_tensor_tensor(
            zbi[:, :], ybig[:, 1024 * i:1024 * (i + 1)],
            mt_all[:, 4 * i + 2:4 * i + 3],
            x2all[:, 1024 * i:1024 * (i + 1)], ALU.mult, ALU.add), "VEC")
        s.wait_ge(P.sem["VEC"], zv[i])
        P.inc(s.activation(zdump[0][:, :], zbi[:, :], AF.Identity,
                           accum_out=lnall[:, 8 * i:8 * i + 1]), "ACT")
        sums[i] = P.inc(s.activation(zdump[1][:, :], zbi[:, :], AF.Square,
                                     accum_out=lnall[:, 8 * i + 1:8 * i + 2]),
                        "ACT")
        v.wait_ge(P.sem["ACT"], sums[i])
        P.inc(v.tensor_scalar(lnall[:, 8 * i + 2:8 * i + 3],
                              lnall[:, 8 * i:8 * i + 1], 1.0 / D, None,
                              ALU.mult), "VEC")
        P.wait(v, "VEC")
        P.inc(v.tensor_scalar(lnall[:, 8 * i + 4:8 * i + 5],
                              lnall[:, 8 * i + 2:8 * i + 3],
                              lnall[:, 8 * i + 2:8 * i + 3], None, ALU.mult),
              "VEC")
        P.wait(v, "VEC")
        vv = P.inc(v.tensor_scalar(lnall[:, 8 * i + 3:8 * i + 4],
                                   lnall[:, 8 * i + 1:8 * i + 2], 1.0 / D,
                                   lnall[:, 8 * i + 4:8 * i + 5], ALU.mult,
                                   ALU.subtract), "VEC")
        s.wait_ge(P.sem["VEC"], vv)
        sq = P.inc(s.activation(lnall[:, 8 * i + 5:8 * i + 6],
                                lnall[:, 8 * i + 3:8 * i + 4], AF.Sqrt,
                                bias=Z["c5_sb"][:, 0:1], scale=1.0), "ACT")
        v.wait_ge(P.sem["ACT"], sq)
        P.inc(v.reciprocal(lnall[:, 8 * i + 5:8 * i + 6],
                           lnall[:, 8 * i + 5:8 * i + 6]), "VEC")
        P.wait(v, "VEC")
        if i >= 2:
            P.wait(v, "SO", 16 * (i - 1))
        o2v[i] = P.inc(v.tensor_scalar(obi[:, :], zbi[:, :],
                                       lnall[:, 8 * i + 2:8 * i + 3],
                                       lnall[:, 8 * i + 5:8 * i + 6],
                                       ALU.subtract, ALU.mult), "VEC")
        sy.wait_ge(P.sem["VEC"], o2v[i])
        P.inc(sy.dma_start(out=out_ap[128 * i:128 * (i + 1), :],
                           in_=obi[:, :]), "SO", 16)
    P.wait(sy, "SO")



def build_fused():
    nc = bass.Bass("TRN2", target_bir_lowering=False, debug=False)
    xT = nc.dram_tensor("xT", [D, T], F32, kind="ExternalInput").ap()
    wqT = nc.dram_tensor("wqT", [D, D], F32, kind="ExternalInput").ap()
    wkT = nc.dram_tensor("wkT", [D, D], F32, kind="ExternalInput").ap()
    wvT = nc.dram_tensor("wvT", [D, D], F32, kind="ExternalInput").ap()
    woT = nc.dram_tensor("woT", [D, D], F32, kind="ExternalInput").ap()
    boc = nc.dram_tensor("boc", [128, NCH], F32, kind="ExternalInput").ap()
    wg = nc.dram_tensor("wg", [D, E], F32, kind="ExternalInput").ap()
    bgc = nc.dram_tensor("bgc", [E, 1], F32, kind="ExternalInput").ap()
    idbf = nc.dram_tensor("idbf", [128, 128], BF16, kind="ExternalInput").ap()
    id16 = nc.dram_tensor("id16", [16, 16], F32, kind="ExternalInput").ap()
    iota16b = nc.dram_tensor("iota16b", [128, 16], F32, kind="ExternalInput").ap()
    odd16 = nc.dram_tensor("odd16", [128, 16], F32, kind="ExternalInput").ap()
    r16m = nc.dram_tensor("r16m", [128, 16], F32, kind="ExternalInput").ap()
    onesb = nc.dram_tensor("onesb", [128, 128], BF16, kind="ExternalInput").ap()
    onesf = nc.dram_tensor("onesf", [128, 128], F32, kind="ExternalInput").ap()
    tri8 = nc.dram_tensor("tri8", [8, 8], F32, kind="ExternalInput").ap()
    eh01 = nc.dram_tensor("eh01", [16, 2], F32, kind="ExternalInput").ap()
    pmask = nc.dram_tensor("pmask", [8, 1], F32, kind="ExternalInput").ap()
    iota128 = nc.dram_tensor("iota128", [128, 1], F32, kind="ExternalInput").ap()
    w1 = nc.dram_tensor("w1", [2, D, FH], F32, kind="ExternalInput").ap()
    w2 = nc.dram_tensor("w2", [2, FH, D], F32, kind="ExternalInput").ap()
    b1c = nc.dram_tensor("b1c", [128, 64], F32, kind="ExternalInput").ap()
    b2c = nc.dram_tensor("b2c", [128, 16], F32, kind="ExternalInput").ap()
    out = nc.dram_tensor("out", [T, D], F32, kind="ExternalOutput").ap()
    x2tm = nc.dram_tensor("x2tm", [T, D], BF16).ap()
    meta = nc.dram_tensor("meta", [T, 4], F32).ap()
    cnts1 = nc.dram_tensor("cnts1", [1, E], F32).ap()
    cnts8 = nc.dram_tensor("cnts8", [8, E], F32).ap()
    fsrc = nc.dram_tensor("fsrc", [NROW, D], BF16).ap()
    fdst = nc.dram_tensor("fdst", [NROW, D], BF16).ap()
    rsrc = nc.dram_tensor("rsrc", [NROW, D], BF16).ap()
    rdst = nc.dram_tensor("rdst", [NROW, D], BF16).ap()

    es = ExitStack()
    Z = alloc_l1(nc, es)
    ZF = alloc_overlays(Z)
    P = Prog(nc, es, SEMS)
    g, sy = nc.gpsimd, nc.sync
    bchk = nc.alloc_register(mybir.EngineType.Pool, "bchk")
    g.reg_mov(bchk, NROW - 1)

    for c in range(NCH):
        P.inc(g.dma_start(out=Z["x_sb"][:, T * c: T * (c + 1)],
                          in_=xT[128 * c:128 * (c + 1), :]), "LX", 16)
    emit_l1_const_loads(nc, P, Z, dict(
        woT=woT, wg=wg, boc=boc, bgc=bgc, idbf=idbf, id16=id16,
        iota16b=iota16b, r16m=r16m, onesb=onesb, onesf=onesf, tri8=tri8,
        eh01=eh01, pmask=pmask, iota128=iota128, b1c=b1c, b2c=b2c,
        odd16=odd16))
    lx_ready = P.c["LX"]
    S = new_l1_state()
    prefetch = make_headpair_prefetch(nc, P, Z, S, wqT, wkT, wvT)
    emit_l1_init(nc, P, Z)
    with nc.named_scope("l1"):
        emit_l1_batch(nc, P, Z, S, prefetch, x2tm, meta, 0, lx_ready, fsrc,
                      bchk)
    pe_attn = P.c["PE"]

    with nc.named_scope("prolog_b"):
        emit_prolog_b(nc, P, Z, cnts1, cnts8, S["vec_cnt"])
    giu_done = P.c["VEC"]

    P.wait(g, "SX0")
    P.wait(g, "SX1")
    P.wait(g, "ST")
    P.wait(g, "PE")
    P.wait(g, "VEC")
    loaders = make_ffn_loaders(nc, P, ZF, w1, w2)
    loaders[1](0, 0, 0)
    P.wait(sy, "SX0")
    P.wait(sy, "SX1")
    P.wait(sy, "ST")
    P.wait(sy, "PE")
    P.wait(sy, "VEC")
    mt_all_early = Z["q2_sb"][:, 2560:2816].bitcast(F32)
    P.inc(sy.dma_start(out=mt_all_early.rearrange("p (i c) -> p i c", c=4),
                       in_=meta.rearrange("(i p) c -> p i c", p=128)),
          "LT0", 16)
    for i in range(18, NST):
        P.inc(sy.dma_start(out=Z["o_sb"][:, 1024 * i:1024 * (i + 1)],
                           in_=x2tm[128 * i:128 * (i + 1), :]), "LT0", 16)

    with nc.named_scope("a2a_fwd"):
        with allow_ici_all_to_all():
            cc = g.collective_compute(
                "AllToAll", ALU.bypass, replica_groups=[list(range(NC))],
                ins=[fsrc.opt()], outs=[fdst.opt()])
        P.inc(cc, "CCF", 1)
        P.wait(g, "CCF")

    def mid_hook():
        P.wait(g, "SY0")
        P.wait(g, "SY1")
        with allow_ici_all_to_all():
            cc = g.collective_compute(
                "AllToAll", ALU.bypass, replica_groups=[list(range(NC))],
                ins=[rsrc[0:NROW // 2, :].opt()],
                outs=[rdst[0:NROW // 2, :].opt()])
        P.inc(cc, "CCR", 1)

    with nc.named_scope("ffn"):
        emit_ffn(nc, P, Z, ZF, loaders, fdst, rsrc, giu_done, bchk, mid_hook)

    with nc.named_scope("a2a_ret"):
        P.wait(g, "SY0")
        P.wait(g, "SY1")
        with allow_ici_all_to_all():
            cc = g.collective_compute(
                "AllToAll", ALU.bypass, replica_groups=[list(range(NC))],
                ins=[rsrc[NROW // 2:NROW, :].opt()],
                outs=[rdst[NROW // 2:NROW, :].opt()])
        P.inc(cc, "CCR", 1)

    with nc.named_scope("phase_c"):
        emit_phase_c(nc, P, Z, ZF, x2tm, meta, rdst, out, bchk, pe_attn)

    es.close()
    return nc



_NC_CACHE = {}


def _get(name, builder):
    if name not in _NC_CACHE:
        _NC_CACHE[name] = builder()
    return _NC_CACHE[name]


def kernel(L0_attn_x, L0_attn_Wq, L0_attn_Wk, L0_attn_Wv, L0_attn_Wo,
           L0_attn_bo, L0_ffn_Wg, L0_ffn_bg, L0_ffn_W1, L0_ffn_b1,
           L0_ffn_W2, L0_ffn_b2):
    x = np.asarray(L0_attn_x, dtype=np.float32)
    wqT = np.ascontiguousarray(np.asarray(L0_attn_Wq, np.float32).T)
    wkT = np.ascontiguousarray(np.asarray(L0_attn_Wk, np.float32).T)
    wvT = np.ascontiguousarray(np.asarray(L0_attn_Wv, np.float32).T)
    woT = np.ascontiguousarray(np.asarray(L0_attn_Wo, np.float32).T)
    bo = np.asarray(L0_attn_bo, np.float32)
    wg = np.ascontiguousarray(np.asarray(L0_ffn_Wg, np.float32))
    bg = np.asarray(L0_ffn_bg, np.float32)
    W1 = np.asarray(L0_ffn_W1, np.float32)
    b1 = np.asarray(L0_ffn_b1, np.float32)
    W2 = np.asarray(L0_ffn_W2, np.float32)
    b2 = np.asarray(L0_ffn_b2, np.float32)

    boc = np.ascontiguousarray(bo.reshape(NCH, 128).T)
    bgc = np.ascontiguousarray(bg.reshape(E, 1))
    idbf = np.eye(128, dtype=np.float32).astype(BF16NP)
    id16 = np.eye(16, dtype=np.float32)
    iota16b = np.ascontiguousarray(np.tile(np.arange(16, dtype=np.float32),
                                           (128, 1)))
    r16m = np.ascontiguousarray(16.0 - iota16b)
    onesb = np.ones((128, 128), np.float32).astype(BF16NP)
    onesf = np.ones((128, 128), np.float32)
    odd16 = np.ascontiguousarray(np.tile(np.arange(16, dtype=np.float32) % 2,
                                         (128, 1)))
    tri8 = np.triu(np.ones((8, 8), np.float32), 1)
    iota128 = np.arange(128, dtype=np.float32).reshape(128, 1)

    nc1 = _get("fused", build_fused)
    in_maps = []
    for c in range(NC):
        eh01 = np.zeros((16, 2), np.float32)
        eh01[2 * c, 0] = 1.0
        eh01[2 * c + 1, 1] = 1.0
        b1cc = np.ascontiguousarray(
            np.concatenate([b1[2 * c + e].reshape(32, 128).T for e in range(2)],
                           axis=1))
        b2cc = np.ascontiguousarray(
            np.concatenate([b2[2 * c + e].reshape(NCH, 128).T for e in range(2)],
                           axis=1))
        in_maps.append({
            "xT": np.ascontiguousarray(x[c].T),
            "wqT": wqT, "wkT": wkT, "wvT": wvT, "woT": woT,
            "boc": boc, "wg": wg, "bgc": bgc,
            "idbf": idbf, "id16": id16, "iota16b": iota16b, "r16m": r16m,
            "onesb": onesb, "onesf": onesf, "tri8": tri8, "eh01": eh01,
            "odd16": odd16,
            "pmask": (np.arange(8) < c).astype(np.float32).reshape(8, 1),
            "iota128": iota128,
            "w1": np.ascontiguousarray(W1[2 * c:2 * c + 2]),
            "w2": np.ascontiguousarray(W2[2 * c:2 * c + 2]),
            "b1c": b1cc, "b2c": b2cc,
        })
    res = run_bass_kernel_spmd(nc1, in_maps, core_ids=list(range(NC))).results
    return np.stack([res[c]["out"] for c in range(NC)], axis=0)


# version 12 (final)
# speedup vs baseline: 1.0774x; 1.0347x over previous
import numpy as np
import ml_dtypes
from contextlib import ExitStack

import concourse.bass as bass
import concourse.mybir as mybir
from concourse.bass import IndirectOffsetOnAxis
from concourse.bass_utils import run_bass_kernel_spmd
from concourse.aot_env import allow_ici_all_to_all

F32 = mybir.dt.float32
BF16 = mybir.dt.bfloat16
U32 = mybir.dt.uint32
AF = mybir.ActivationFunctionType
ALU = mybir.AluOpType
AX = mybir.AxisListType.X
BF16NP = ml_dtypes.bfloat16

D = 1024
T = 4096
H = 16
NC = 8
E = 16
CAP = 2048
LCAP = 1024
FH = 4096
NCH = D // 128
NTT = T // 512
NST = T // 128
BIG = float(1 << 26)
BIGIX = 1.0e6
NROW = E * LCAP


class Prog:
    def __init__(self, nc, es, sems):
        self.nc = nc
        self.c = {}
        self.sem = {}
        for name in sems:
            self.sem[name] = es.enter_context(nc.semaphore(name))
            self.c[name] = 0

    def inc(self, instr, name, k=1):
        self.c[name] += k
        instr.then_inc(self.sem[name], k)
        return self.c[name]

    def wait(self, eng, name, v=None):
        eng.wait_ge(self.sem[name], self.c[name] if v is None else v)


SEMS = ["LX", "LS", "WH0", "WH1", "PE", "ACT", "VEC", "ST",
        "SX0", "SX1", "SST", "SEXP", "SPV",
        "CN", "CCA", "CCF", "CCR",
        "BT0", "BT1", "W10", "W11", "W20", "W21", "SY0", "SY1",
        "LT0", "LT1", "GB0", "GB1", "SO"]


def alloc_l1(nc, es):
    sb = lambda name, shape, dt: es.enter_context(nc.sbuf_tensor(name, shape, dt))
    ps = lambda name, shape, dt: es.enter_context(nc.psum_tensor(name, shape, dt))
    Z = {}
    Z["x_sb"] = sb("x_sb", [128, NCH * T], BF16)
    Z["o_sb"] = sb("o_sb", [128, NCH * T], BF16)
    Z["wo_sb"] = sb("wo_sb", [128, NCH * D], BF16)
    Z["wh_sb"] = [sb(f"wh_sb{i}", [128, 3 * D], BF16) for i in range(2)]
    Z["wg_sb"] = sb("wg_sb", [128, NCH * E], BF16)
    Z["boc_sb"] = sb("boc_sb", [128, NCH], F32)
    Z["bgc_sb"] = sb("bgc_sb", [16, 1], F32)
    Z["idbf_sb"] = sb("idbf_sb", [128, 128], BF16)
    Z["id16_sb"] = sb("id16_sb", [16, 16], F32)
    Z["iota_sb"] = sb("iota_sb", [128, 16], F32)
    Z["r16m_sb"] = sb("r16m_sb", [128, 16], F32)
    Z["odd16_sb"] = sb("odd16_sb", [128, 16], F32)
    Z["ones_sb"] = sb("ones_sb", [128, 128], BF16)
    Z["onesf_sb"] = sb("onesf_sb", [128, 128], F32)
    Z["k2_sb"] = sb("k2_sb", [128, T], BF16)
    Z["q2_sb"] = sb("q2_sb", [128, T], BF16)
    Z["v2tmp"] = sb("v2tmp", [128, 512], BF16)
    Z["vaugA"] = sb("vaugA", [128, 65 * 32], BF16)
    Z["vaugB"] = sb("vaugB", [128, 65 * 32], BF16)
    Z["expS"] = [sb(f"expS{i}", [128, 1024], BF16) for i in range(2)]
    Z["rden_bf"] = sb("rden_bf", [65, 512], BF16)
    Z["bc_sb"] = sb("bc_sb", [128, 512], F32)
    Z["ysq_sb"] = sb("ysq_sb", [128, 512], BF16)
    Z["mean_sb"] = sb("mean_sb", [1, 512], F32)
    Z["var_sb"] = sb("var_sb", [1, 512], F32)
    Z["mean_bf"] = sb("mean_bf", [1, 512], BF16)
    Z["rstd_bf"] = sb("rstd_bf", [1, 512], BF16)
    Z["rbc_sb"] = sb("rbc_sb", [128, 512], F32)
    Z["lg_tt"] = sb("lg_tt", [16, 512], F32)
    Z["msq_sb"] = Z["lg_tt"][0:1, 0:512]
    Z["mbc_sb"] = Z["bc_sb"]
    Z["mx_sb"] = sb("mx_sb", [128, 1], F32)
    Z["nmx_sb"] = sb("nmx_sb", [128, 1], F32)
    Z["sm_sb"] = sb("sm_sb", [128, 1], F32)
    Z["ejunk_sb"] = sb("ejunk_sb", [128, 16], F32)
    Z["eq_sb"] = sb("eq_sb", [128, 16], F32)
    Z["cand_sb"] = sb("cand_sb", [128, 16], F32)
    Z["t1_sb"] = sb("t1_sb", [128, 1], F32)

    Z["ohb_sb"] = sb("ohb_sb", [128, 16], BF16)
    Z["cnt_sb"] = sb("cnt_sb", [16, 8], F32)
    Z["sel_sb"] = sb("sel_sb", [128, 16], F32)
    Z["cs_sb"] = sb("cs_sb", [128, 1], F32)

    Z["c5_sb"] = sb("c5_sb", [128, 1], F32)
    Z["c8_sb"] = sb("c8_sb", [128, 1], F32)
    Z["csum_sb"] = Z["o_sb"][0:16, 0:8192].bitcast(F32)
    Z["ohfm_sb"] = Z["o_sb"][0:16, 8192:12288]
    Z["lg_tm"] = Z["o_sb"][:, 12288:13312].bitcast(F32)
    Z["ohf_all"] = Z["o_sb"][:, 13312:14336].bitcast(F32)
    Z["meta_sb"] = Z["o_sb"][:, 14336:14592].bitcast(F32)
    Z["idx_all"] = Z["o_sb"][:, 14592:14656].bitcast(F32)
    Z["x2st"] = [Z["o_sb"][:, 16384:17408], Z["o_sb"][:, 17408:18432]]
    Z["fif"] = sb("fif", [128, 32], F32)
    Z["fiu"] = sb("fiu", [128, 32], U32)
    Z["gif"] = sb("gif", [128, 32], F32)
    Z["giu"] = sb("giu", [128, 32], U32)
    Z["gif2"] = sb("gif2", [128, 32], F32)
    Z["giu2"] = sb("giu2", [128, 32], U32)
    Z["cnts8_sb"] = sb("cnts8_sb", [8, 16], F32)
    Z["tri8_sb"] = sb("tri8_sb", [8, 8], F32)
    Z["eh01_sb"] = sb("eh01_sb", [16, 2], F32)
    Z["pmask_sb"] = sb("pmask_sb", [8, 1], F32)
    Z["iota128_sb"] = sb("iota128_sb", [128, 1], F32)
    Z["baseT_sb"] = sb("baseT_sb", [16, 8], F32)
    Z["rowsel_sb"] = sb("rowsel_sb", [1, 16], F32)
    Z["gbrow_sb"] = sb("gbrow_sb", [1, 16], F32)
    Z["bbc_sb"] = sb("bbc_sb", [128, 16], F32)
    Z["gbb_sb"] = sb("gbb_sb", [128, 16], F32)
    Z["pc_sb"] = sb("pc_sb", [128, 1], F32)
    Z["cmp8_sb"] = sb("cmp8_sb", [128, 8], F32)
    Z["bsc_sb"] = sb("bsc_sb", [128, 8], F32)
    Z["nle_sb"] = sb("nle_sb", [128, 1], F32)
    Z["rr_sb"] = sb("rr_sb", [128, 1], F32)
    Z["ovg_sb"] = sb("ovg_sb", [128, 1], F32)
    Z["b1c_sb"] = sb("b1c_sb", [128, 64], F32)
    Z["b2c_sb"] = sb("b2c_sb", [128, 16], F32)
    Z["ln2_sb"] = sb("ln2_sb", [128, 8], F32)
    Z["st_ps"] = [ps("st_ps0", [128, 1024], F32), ps("st_ps1", [128, 1024], F32)]
    Z["pvA_ps"] = ps("pvA_ps", [128, 512], F32)
    Z["pvB_ps"] = ps("pvB_ps", [128, 512], F32)
    Z["bc_ps"] = ps("bc_ps", [128, 512], F32)
    Z["trb_ps"] = ps("trb_ps", [128, 1024], BF16)
    return Z


def alloc_overlays(Z):
    ZF = {}
    x, o = Z["x_sb"], Z["o_sb"]
    ZF["buf_fm"] = x[:, 0:16384]
    ZF["h_sb"] = x[:, 16384:32768]
    ZF["yfm_sb"] = o[:, 0:4096]
    ZF["bt_sb"] = [o[:, 4096:5120], o[:, 5120:6144]]
    ZF["yt_sb"] = [o[:, 6144:7168], o[:, 7168:8192]]
    ZF["w1_sb"] = [o[:, 8192:9216], o[:, 9216:10240]]
    ZF["w2_sb"] = [o[:, 10240:14336], o[:, 14336:18432]]
    k2, q2 = Z["k2_sb"], Z["q2_sb"]
    ZF["x2t2"] = [k2[:, 0:1024], k2[:, 1024:2048]]
    ZF["yb2"] = [k2[:, 2048:3072], k2[:, 3072:4096]]
    ZF["mt2"] = [q2[:, 0:8].bitcast(F32), q2[:, 8:16].bitcast(F32)]
    ZF["z_sb"] = Z["vaugA"][:, 0:1024]
    ZF["zc_sb"] = Z["vaugA"][:, 1024:2048]
    ZF["zsq_sb"] = Z["vaugB"][:, 0:1024]
    ZF["o2_sb"] = q2[:, 2048:4096].bitcast(F32)
    return ZF


def new_l1_state():
    return {
        "gg": 0,
        "vec_bc_free": 0,
        "vec_pvA_free": 0,
        "vec_pvB_free": 0,
        "wh_ready": [0, 0],
        "wh_freed": [0, 0],
    }


def emit_l1_init(nc, P, Z):
    v, s, t = nc.vector, nc.scalar, nc.tensor
    P.wait(v, "LS")
    P.inc(v.memset(Z["vaugA"][:, :], 1.0), "VEC")
    P.inc(v.memset(Z["vaugB"][:, :], 1.0), "VEC")
    P.inc(v.memset(Z["c5_sb"][:, :], 1e-5), "VEC")
    P.inc(v.memset(Z["c8_sb"][:, :], 1e-8), "VEC")
    P.wait(s, "VEC")
    P.wait(t, "VEC")


def emit_l1_batch(nc, P, Z, S, prefetch, x2tm_ap, meta_ap, cnt_col, lx_ready,
                  fsrc_ap, bchk):
    g, sy, t, v, s = nc.gpsimd, nc.sync, nc.tensor, nc.vector, nc.scalar
    st_ps, pvA_ps, pvB_ps = Z["st_ps"], Z["pvA_ps"], Z["pvB_ps"]
    bc_ps, trb_ps = Z["bc_ps"], Z["trb_ps"]
    x_sb, o_sb = Z["x_sb"], Z["o_sb"]
    k2_sb, q2_sb = Z["k2_sb"], Z["q2_sb"]
    expS = Z["expS"]
    ones_sb = Z["ones_sb"]

    for hp in range(8):
        bsel = hp % 2
        wh_sem = "WH0" if bsel == 0 else "WH1"
        prefetch(hp)
        t.wait_ge(P.sem[wh_sem], S["wh_ready"][bsel])
        P.wait(t, "LX", lx_ready)
        P.wait(t, "LS")
        for mi, dst_sb in ((0, k2_sb), (1, q2_sb)):
            for tt in range(NTT):
                pp = bc_ps if tt % 2 == 0 else pvA_ps
                P.wait(t, "VEC",
                       S["vec_bc_free"] if tt % 2 == 0 else S["vec_pvA_free"])
                for c in range(NCH):
                    mm = t.matmul(pp[:, :],
                                  Z["wh_sb"][bsel][:, D * mi + 128 * c:
                                                   D * mi + 128 * (c + 1)],
                                  x_sb[:, T * c + 512 * tt: T * c + 512 * (tt + 1)],
                                  start=(c == 0), stop=(c == NCH - 1))
                P.inc(mm, "PE")
                P.wait(v, "PE")
                vi = P.inc(v.tensor_copy(dst_sb[:, 512 * tt:512 * (tt + 1)],
                                         pp[:, :]), "VEC")
                if tt % 2 == 0:
                    S["vec_bc_free"] = vi
                else:
                    S["vec_pvA_free"] = vi
        for tt in range(NTT):
            pp = bc_ps if tt % 2 == 0 else pvA_ps
            P.wait(t, "VEC",
                   S["vec_bc_free"] if tt % 2 == 0 else S["vec_pvA_free"])
            for c in range(NCH):
                mm = t.matmul(pp[:, :],
                              Z["wh_sb"][bsel][:, 2 * D + 128 * c:
                                               2 * D + 128 * (c + 1)],
                              x_sb[:, T * c + 512 * tt: T * c + 512 * (tt + 1)],
                              start=(c == 0), stop=(c == NCH - 1))
            P.inc(mm, "PE")
            P.wait(v, "PE")
            vi = P.inc(v.tensor_copy(Z["v2tmp"][:, :], pp[:, :]), "VEC")
            if tt % 2 == 0:
                S["vec_bc_free"] = vi
            else:
                S["vec_pvA_free"] = vi
            for j in range(4):
                sc = 4 * tt + j
                P.wait(t, "VEC")
                mm = t.transpose(trb_ps[0:128, 0:128],
                                 Z["v2tmp"][:, 128 * j:128 * (j + 1)],
                                 Z["idbf_sb"][:, :])
                P.inc(mm, "PE")
                P.wait(v, "PE")
                P.inc(v.tensor_copy(Z["vaugA"][:, 65 * sc:65 * sc + 64],
                                    trb_ps[0:128, 0:64]), "VEC")
                P.inc(v.tensor_copy(Z["vaugB"][:, 65 * sc:65 * sc + 64],
                                    trb_ps[0:128, 64:128]), "VEC")
        S["wh_freed"][bsel] = P.c["PE"]
        vec_vaug = P.c["VEC"]

        for tt in range(NTT):
            g0 = S["gg"]
            tsl = slice(512 * tt, 512 * (tt + 1))
            for lg in range(32):
                cgg = g0 + lg
                if cgg >= 2:
                    P.wait(t, "SEXP", cgg - 1)
                if lg == 0:
                    P.wait(t, "VEC", vec_vaug)
                mm = t.matmul(st_ps[cgg % 2][:, 0:512],
                              k2_sb[0:64, 128 * lg:128 * (lg + 1)],
                              q2_sb[0:64, tsl], start=True, stop=True,
                              tile_position=(0, 0))
                mm = t.matmul(st_ps[cgg % 2][:, 512:1024],
                              k2_sb[64:128, 128 * lg:128 * (lg + 1)],
                              q2_sb[64:128, tsl], start=True, stop=True,
                              tile_position=(64, 0))
                P.inc(mm, "SST")
                if lg >= 1:
                    P.wait(t, "SEXP", cgg)
                    pvg = lg - 1
                    if pvg == 0:
                        P.wait(t, "VEC", S["vec_pvA_free"])
                        P.wait(t, "VEC", S["vec_pvB_free"])
                    mm = t.matmul(pvA_ps[0:65, :],
                                  Z["vaugA"][:, 65 * pvg:65 * pvg + 65],
                                  expS[(g0 + pvg) % 2][:, 0:512],
                                  start=(pvg == 0), stop=False,
                                  skip_group_check=True)
                    mm = t.matmul(pvB_ps[0:65, :],
                                  Z["vaugB"][:, 65 * pvg:65 * pvg + 65],
                                  expS[(g0 + pvg) % 2][:, 512:1024],
                                  start=(pvg == 0), stop=False,
                                  skip_group_check=True)
                    P.inc(mm, "SPV")
            P.wait(t, "SEXP", g0 + 32)
            mm = t.matmul(pvA_ps[0:65, :], Z["vaugA"][:, 65 * 31:65 * 31 + 65],
                          expS[(g0 + 31) % 2][:, 0:512],
                          start=False, stop=True, skip_group_check=True)
            mm = t.matmul(pvB_ps[0:65, :], Z["vaugB"][:, 65 * 31:65 * 31 + 65],
                          expS[(g0 + 31) % 2][:, 512:1024],
                          start=False, stop=True, skip_group_check=True)
            P.inc(mm, "SPV")
            for lg in range(32):
                cgg = g0 + lg
                P.wait(s, "SST", cgg + 1)
                if cgg >= 2:
                    P.wait(s, "SPV", cgg - 1)
                si = s.activation(expS[cgg % 2][:, :], st_ps[cgg % 2][:, :],
                                  AF.Exp, bias=0.0, scale=0.125)
                P.inc(si, "SEXP")
            S["gg"] = g0 + 32
            for head, pv in ((0, pvA_ps), (1, pvB_ps)):
                bcx = Z["bc_sb"] if head == 0 else Z["rbc_sb"]
                dr = slice(0, 1) if head == 0 else slice(64, 65)
                po = head * 64
                ocols = slice(T * hp + 512 * tt, T * hp + 512 * (tt + 1))
                P.wait(v, "SPV", S["gg"])
                with nc.allow_low_precision(reason="bf16 stage for matmul"):
                    den_cast = P.inc(v.tensor_copy(Z["rden_bf"][dr, :],
                                                   pv[64:65, :]), "VEC")
                with nc.allow_low_precision(reason="unnormalised bf16 stage"):
                    vi = P.inc(v.tensor_copy(o_sb[po:po + 64, ocols],
                                             pv[0:64, :]), "VEC")
                if head == 0:
                    S["vec_pvA_free"] = vi
                else:
                    S["vec_pvB_free"] = vi
                P.wait(t, "VEC", den_cast)
                P.wait(t, "VEC", S["vec_bc_free"])
                mm = t.matmul(bc_ps[:, :], ones_sb[dr, 0:128],
                              Z["rden_bf"][dr, :], start=True, stop=True,
                              skip_group_check=True)
                P.inc(mm, "PE")
                P.wait(v, "PE")
                P.wait(v, "VEC")
                P.inc(v.tensor_scalar(bcx[:, :], bc_ps[:, :],
                                      1e-8, None, ALU.add), "VEC")
                S["vec_bc_free"] = P.c["VEC"]
                P.wait(v, "VEC")
                P.inc(v.reciprocal(bcx[:, :], bcx[:, :]), "VEC")
                P.wait(v, "VEC")
                P.inc(v.tensor_tensor(o_sb[po:po + 64, ocols],
                                      o_sb[po:po + 64, ocols],
                                      bcx[po:po + 64, :], ALU.mult), "VEC")

    P.wait(t, "SEXP", S["gg"])
    P.wait(t, "VEC", S["vec_pvA_free"])
    P.wait(t, "VEC", S["vec_pvB_free"])
    for tt in range(NTT):
        for dc in range(NCH):
            wv = st_ps[0][:, 0:512] if dc % 2 == 0 else st_ps[0][:, 512:1024]
            P.wait(t, "VEC")
            for r in range(NCH):
                mm = t.matmul(wv,
                              Z["wo_sb"][:, D * r + 128 * dc: D * r + 128 * (dc + 1)],
                              o_sb[:, T * r + 512 * tt: T * r + 512 * (tt + 1)],
                              start=(r == 0), stop=(r == NCH - 1),
                              skip_group_check=True)
            P.inc(mm, "PE")
            P.wait(v, "PE")
            xv = x_sb[:, T * dc + 512 * tt: T * dc + 512 * (tt + 1)]
            P.inc(v.scalar_tensor_tensor(xv, wv, Z["boc_sb"][:, dc:dc + 1], xv,
                                         ALU.add, ALU.add), "VEC")
            P.wait(t, "VEC")
            mm = t.matmul(pvA_ps[0:1, :], ones_sb[:, 0:1], xv,
                          start=(dc == 0), stop=(dc == NCH - 1),
                          skip_group_check=True)
            P.inc(mm, "PE")
            P.wait(s, "VEC")
            P.inc(s.activation(Z["ysq_sb"][:, :], xv, AF.Square), "ACT")
            P.wait(t, "ACT")
            mm = t.matmul(pvA_ps[32:33, :], ones_sb[:, 0:1], Z["ysq_sb"][:, :],
                          start=(dc == 0), stop=(dc == NCH - 1),
                          skip_group_check=True)
            P.inc(mm, "PE")
        P.wait(v, "PE")
        P.inc(v.tensor_scalar(Z["mean_sb"][0:1, :], pvA_ps[0:1, :],
                              1.0 / D, None, ALU.mult), "VEC")
        P.inc(v.tensor_scalar(Z["msq_sb"][:, :], pvA_ps[32:33, :],
                              1.0 / D, None, ALU.mult), "VEC")
        P.wait(v, "VEC")
        P.inc(v.tensor_tensor(Z["var_sb"][0:1, :], Z["mean_sb"][0:1, :],
                              Z["mean_sb"][0:1, :], ALU.mult), "VEC")
        P.wait(v, "VEC")
        P.inc(v.tensor_sub(Z["var_sb"][0:1, :], Z["msq_sb"][:, :],
                           Z["var_sb"][0:1, :]), "VEC")
        P.wait(s, "VEC")
        P.inc(s.activation(Z["var_sb"][0:1, :], Z["var_sb"][0:1, :],
                           AF.Sqrt, bias=Z["c5_sb"][0:1, 0:1], scale=1.0),
              "ACT")
        P.wait(v, "ACT")
        P.inc(v.reciprocal(Z["var_sb"][0:1, :], Z["var_sb"][0:1, :]), "VEC")
        P.wait(v, "VEC")
        with nc.allow_low_precision(reason="bf16 broadcast operands"):
            P.inc(v.tensor_copy(Z["mean_bf"][0:1, :], Z["mean_sb"][0:1, :]),
                  "VEC")
            P.inc(v.tensor_copy(Z["rstd_bf"][0:1, :], Z["var_sb"][0:1, :]),
                  "VEC")
        P.wait(t, "VEC")
        t.matmul(st_ps[1][:, 0:512], ones_sb[0:1, 0:128], Z["mean_bf"][0:1, :],
                 start=True, stop=True, skip_group_check=True)
        mm = t.matmul(st_ps[1][:, 512:1024], ones_sb[0:1, 0:128],
                      Z["rstd_bf"][0:1, :], start=True, stop=True,
                      skip_group_check=True)
        P.inc(mm, "PE")
        P.wait(v, "PE")
        P.inc(v.tensor_copy(Z["mbc_sb"][:, :], st_ps[1][:, 0:512]), "VEC")
        P.inc(v.tensor_copy(Z["rbc_sb"][:, :], st_ps[1][:, 512:1024]), "VEC")
        P.wait(v, "VEC")
        for dc in range(NCH):
            xv = x_sb[:, T * dc + 512 * tt: T * dc + 512 * (tt + 1)]
            P.inc(v.tensor_sub(xv, xv, Z["mbc_sb"][:, :]), "VEC")
            P.wait(v, "VEC")
            P.inc(v.tensor_tensor(xv, xv, Z["rbc_sb"][:, :], ALU.mult), "VEC")
            P.wait(v, "VEC")
    vec_x2 = P.c["VEC"]

    for tt in range(NTT):
        P.wait(t, "VEC", vec_x2)
        for c in range(NCH):
            mm = t.matmul(st_ps[0][0:16, 512:1024],
                          Z["wg_sb"][:, 16 * c:16 * (c + 1)],
                          x_sb[:, T * c + 512 * tt: T * c + 512 * (tt + 1)],
                          start=(c == 0), stop=(c == NCH - 1),
                          skip_group_check=True)
        P.inc(mm, "PE")
        P.wait(s, "PE")
        P.wait(s, "VEC")
        si = s.activation(Z["lg_tt"][0:16, :], st_ps[0][0:16, 512:1024],
                          AF.Identity, bias=Z["bgc_sb"][0:16, 0:1], scale=1.0)
        P.inc(si, "ACT")
        P.wait(t, "ACT")
        for j in range(4):
            i = 4 * tt + j
            P.wait(t, "VEC")
            mm = t.transpose(pvB_ps[0:128, 0:16],
                             Z["lg_tt"][0:16, 128 * j:128 * (j + 1)],
                             Z["id16_sb"][:, :])
            P.inc(mm, "PE")
            P.wait(v, "PE")
            P.inc(v.tensor_copy(Z["lg_tm"][:, 16 * i:16 * (i + 1)],
                                pvB_ps[0:128, 0:16]), "VEC")
        P.wait(t, "VEC")
    k2r = Z["k2_sb"]
    mxa = k2r[:, 0:64].bitcast(F32)
    nmxa = k2r[:, 64:128].bitcast(F32)
    sma = k2r[:, 128:192].bitcast(F32)
    eqa = k2r[:, 192:1216].bitcast(F32)
    cana = k2r[:, 1216:2240].bitcast(F32)
    t1a = k2r[:, 2240:2304].bitcast(F32)
    ohba = k2r[:, 2304:2816]
    sela = k2r[:, 2816:3840].bitcast(F32)
    csa = k2r[:, 3840:3904].bitcast(F32)
    ovga = k2r[:, 3904:3968].bitcast(F32)
    P.wait(v, "VEC")
    for i in range(NST):
        ltile = Z["lg_tm"][:, 16 * i:16 * (i + 1)]
        P.inc(v.reduce_max(mxa[:, i:i + 1], ltile, axis=AX), "VEC")
        P.inc(v.reduce_max(nmxa[:, i:i + 1], ltile, axis=AX, negate=True),
              "VEC")
    P.wait(s, "VEC")
    for i in range(NST):
        ltile = Z["lg_tm"][:, 16 * i:16 * (i + 1)]
        si = s.activation(Z["ejunk_sb"][:, :], ltile, AF.Exp,
                          bias=nmxa[:, i:i + 1], scale=1.0,
                          accum_out=sma[:, i:i + 1])
        P.inc(si, "ACT")
    P.wait(v, "VEC")
    for i in range(NST):
        ltile = Z["lg_tm"][:, 16 * i:16 * (i + 1)]
        P.inc(v.tensor_scalar(eqa[:, 16 * i:16 * (i + 1)], ltile,
                              mxa[:, i:i + 1], None, ALU.is_equal), "VEC")
    P.wait(v, "ACT")
    for i in range(NST):
        P.inc(v.reciprocal(Z["meta_sb"][:, 4 * i + 2:4 * i + 3],
                           sma[:, i:i + 1]), "VEC")
    P.wait(v, "VEC")
    for i in range(NST):
        P.inc(v.tensor_tensor(cana[:, 16 * i:16 * (i + 1)], Z["r16m_sb"][:, :],
                              eqa[:, 16 * i:16 * (i + 1)], ALU.mult), "VEC")
    P.wait(v, "VEC")
    for i in range(NST):
        P.inc(v.reduce_max(t1a[:, i:i + 1], cana[:, 16 * i:16 * (i + 1)],
                           axis=AX), "VEC")
    P.wait(v, "VEC")
    for i in range(NST):
        P.inc(v.tensor_scalar(Z["idx_all"][:, i:i + 1], t1a[:, i:i + 1],
                              -1.0, 16.0, ALU.mult, ALU.add), "VEC")
    P.wait(v, "VEC")
    for i in range(NST):
        P.inc(v.tensor_copy(Z["meta_sb"][:, 4 * i:4 * i + 1],
                            Z["idx_all"][:, i:i + 1]), "VEC")
        P.inc(v.tensor_scalar(Z["ohf_all"][:, 16 * i:16 * (i + 1)],
                              Z["iota_sb"][:, :], Z["idx_all"][:, i:i + 1],
                              None, ALU.is_equal), "VEC")
        with nc.allow_low_precision(reason="onehot bf16"):
            P.inc(v.tensor_scalar(ohba[:, 16 * i:16 * (i + 1)],
                                  Z["iota_sb"][:, :], Z["idx_all"][:, i:i + 1],
                                  None, ALU.is_equal), "VEC")
    for r in range(NST // 8):
        P.wait(t, "VEC")
        for j in range(8):
            i = 8 * r + j
            mm = t.transpose(trb_ps[0:16, 128 * j:128 * (j + 1)],
                             ohba[:, 16 * i:16 * (i + 1)], Z["idbf_sb"][:, :])
            P.inc(mm, "PE")
        P.wait(v, "PE")
        for j in range(8):
            i = 8 * r + j
            P.inc(v.tensor_copy(Z["ohfm_sb"][0:16, 128 * i:128 * (i + 1)],
                                trb_ps[0:16, 128 * j:128 * (j + 1)]), "VEC")
    P.wait(v, "VEC")
    P.inc(v.tensor_tensor_scan(Z["csum_sb"][:, :], Z["ohfm_sb"][:, :],
                               Z["ohfm_sb"][:, :], 0.0, ALU.add, ALU.bypass),
          "VEC")
    P.wait(v, "VEC")
    P.inc(v.tensor_copy(Z["cnt_sb"][:, cnt_col:cnt_col + 1],
                        Z["csum_sb"][:, T - 1:T]), "VEC")
    S["vec_cnt"] = P.c["VEC"]
    P.wait(t, "VEC")
    for i in range(NST):
        mm = t.transpose(pvB_ps[0:128, 16 * i:16 * (i + 1)],
                         Z["csum_sb"][0:16, 128 * i:128 * (i + 1)],
                         Z["id16_sb"][:, :])
        P.inc(mm, "PE")
    P.wait(v, "PE")
    for i in range(NST):
        P.inc(v.tensor_tensor(sela[:, 16 * i:16 * (i + 1)],
                              pvB_ps[0:128, 16 * i:16 * (i + 1)],
                              Z["ohf_all"][:, 16 * i:16 * (i + 1)], ALU.mult),
              "VEC")
    P.wait(v, "VEC")
    for i in range(NST):
        P.inc(v.reduce_sum(csa[:, i:i + 1], sela[:, 16 * i:16 * (i + 1)],
                           axis=AX), "VEC")
    P.wait(v, "VEC")
    for i in range(NST):
        P.inc(v.tensor_scalar(Z["meta_sb"][:, 4 * i + 1:4 * i + 2],
                              csa[:, i:i + 1], -1.0, None, ALU.add), "VEC")
        P.inc(v.memset(Z["meta_sb"][:, 4 * i + 3:4 * i + 4], 0.0), "VEC")
        P.inc(v.tensor_scalar(Z["fif"][:, i:i + 1], Z["idx_all"][:, i:i + 1],
                              float(LCAP), None, ALU.mult), "VEC")
    P.wait(v, "VEC")
    for i in range(NST):
        P.inc(v.tensor_scalar(ovga[:, i:i + 1],
                              Z["meta_sb"][:, 4 * i + 1:4 * i + 2],
                              float(LCAP), BIGIX, ALU.is_ge, ALU.mult), "VEC")
        P.inc(v.tensor_tensor(Z["fif"][:, i:i + 1], Z["fif"][:, i:i + 1],
                              Z["meta_sb"][:, 4 * i + 1:4 * i + 2], ALU.add),
              "VEC")
    P.wait(v, "VEC")
    for i in range(NST):
        P.inc(v.tensor_tensor(Z["fif"][:, i:i + 1], Z["fif"][:, i:i + 1],
                              ovga[:, i:i + 1], ALU.add), "VEC")
    P.wait(v, "VEC")
    for i in range(NST):
        P.inc(v.tensor_copy(Z["fiu"][:, i:i + 1], Z["fif"][:, i:i + 1]),
              "VEC")
    P.wait(v, "VEC")
    P.wait(sy, "VEC")
    P.inc(sy.dma_start(out=meta_ap.rearrange("(i p) c -> p i c", p=128),
                       in_=Z["meta_sb"].rearrange("p (i c) -> p i c", c=4)),
          "ST", 16)
    for i in range(NST):
        b = i % 2
        sxs = "SX0" if b == 0 else "SX1"
        buf = Z["x2st"][b]
        P.wait(t, "VEC")
        for dc in range(NCH):
            mm = t.transpose(trb_ps[0:128, 128 * dc:128 * (dc + 1)],
                             x_sb[:, T * dc + 128 * i: T * dc + 128 * (i + 1)],
                             Z["idbf_sb"][:, :])
            P.inc(mm, "PE")
        P.wait(v, "PE")
        P.wait(v, sxs)
        P.inc(v.tensor_copy(buf[:, :], trb_ps[0:128, :]), "VEC")
        P.wait(sy, "VEC")
        P.inc(sy.dma_start(out=x2tm_ap[128 * i:128 * (i + 1), :], in_=buf[:, :]),
              sxs, 16)
        P.wait(g, "VEC")
        P.inc(g.indirect_dma_start(
            out=fsrc_ap,
            out_offset=IndirectOffsetOnAxis(Z["fiu"][:, i:i + 1], 0),
            in_=buf[:, :], in_offset=None,
            bounds_check=bchk, oob_is_err=False), sxs, 16)


def emit_l1_const_loads(nc, P, Z, aps):
    g, sy = nc.gpsimd, nc.sync
    for c in range(NCH):
        P.inc(g.dma_start(out=Z["wo_sb"][:, D * c: D * (c + 1)],
                          in_=aps["woT"][128 * c:128 * (c + 1), :]), "LX", 16)
    P.inc(g.dma_start(out=Z["wg_sb"].ap().rearrange("p (c e) -> p c e", c=NCH),
                      in_=aps["wg"].rearrange("(c p) e -> p c e", p=128)),
          "LX", 16)
    for dk, sk in (("boc_sb", "boc"), ("bgc_sb", "bgc"), ("idbf_sb", "idbf"),
                   ("id16_sb", "id16"), ("iota_sb", "iota16b"),
                   ("r16m_sb", "r16m"), ("ones_sb", "onesb"),
                   ("odd16_sb", "odd16"),
                   ("onesf_sb", "onesf"), ("tri8_sb", "tri8"),
                   ("eh01_sb", "eh01"), ("pmask_sb", "pmask"),
                   ("iota128_sb", "iota128"), ("b1c_sb", "b1c"),
                   ("b2c_sb", "b2c")):
        P.inc(sy.dma_start(out=Z[dk][:, :], in_=aps[sk][:, :]), "LS", 16)


def make_headpair_prefetch(nc, P, Z, S, wq_ap, wk_ap, wv_ap, npasses=1):
    g = nc.gpsimd

    def load_pair(hp, bsel):
        sem = "WH0" if bsel == 0 else "WH1"
        P.wait(g, "PE", S["wh_freed"][bsel])
        for mi, wsrc in enumerate((wk_ap, wq_ap, wv_ap)):
            ins = g.dma_start(
                out=Z["wh_sb"][bsel][:, D * mi:D * (mi + 1)].rearrange(
                    "p (c f) -> p c f", c=NCH),
                in_=wsrc[:, 128 * hp:128 * (hp + 1)].rearrange(
                    "(c p) f -> p c f", p=128))
            P.inc(ins, sem, 16)
        S["wh_ready"][bsel] = P.c[sem]

    seq = {"next": 1}

    def prefetch(_hp):
        nxt = seq["next"]
        if nxt < 8 * npasses:
            load_pair(nxt % 8, nxt % 2)
            seq["next"] = nxt + 1

    load_pair(0, 0)
    return prefetch



def emit_prolog_b(nc, P, Z, cnts1_ap, cnts8_ap, vec_cnt):
    g, sy, t, v, s = nc.gpsimd, nc.sync, nc.tensor, nc.vector, nc.scalar
    bc_ps = Z["bc_ps"]
    P.wait(s, "VEC", vec_cnt)
    P.inc(s.dma_start(out=cnts1_ap.rearrange("one e -> e one"),
                      in_=Z["cnt_sb"][:, 0:1]), "CN", 16)
    P.wait(g, "CN")
    cc = g.collective_compute(
        "AllGather", ALU.bypass, replica_groups=[list(range(NC))],
        ins=[cnts1_ap.opt()], outs=[cnts8_ap.opt()])
    P.inc(cc, "CCA", 1)
    P.wait(g, "CCA")
    P.inc(g.dma_start(out=Z["cnts8_sb"][:, :], in_=cnts8_ap), "CN", 16)
    pe0, vec0 = P.c["PE"], P.c["VEC"]
    P.wait(t, "CN")
    P.wait(t, "VEC", vec0)
    mm = t.matmul(bc_ps[0:16, 0:8], Z["cnts8_sb"][:, :], Z["tri8_sb"][:, :],
                  start=True, stop=True, skip_group_check=True)
    P.inc(mm, "PE")
    P.wait(v, "PE")
    P.inc(v.tensor_copy(Z["baseT_sb"][:, :], bc_ps[0:16, 0:8]), "VEC")
    P.wait(t, "VEC")
    mm = t.matmul(bc_ps[0:1, 16:32], Z["pmask_sb"][:, :], Z["cnts8_sb"][:, :],
                  start=True, stop=True, skip_group_check=True)
    P.inc(mm, "PE")
    P.wait(v, "PE")
    P.inc(v.tensor_copy(Z["gbrow_sb"][:, :], bc_ps[0:1, 16:32]), "VEC")
    P.wait(t, "VEC")
    for k in range(2):
        mm = t.matmul(bc_ps[0:1, 32 + 8 * k:40 + 8 * k],
                      Z["eh01_sb"][:, k:k + 1], Z["baseT_sb"][:, :],
                      start=True, stop=True, skip_group_check=True)
        P.inc(mm, "PE")
    P.wait(v, "PE")
    P.inc(v.tensor_copy(Z["rowsel_sb"][:, 0:8], bc_ps[0:1, 32:40]), "VEC")
    P.inc(v.tensor_copy(Z["rowsel_sb"][:, 8:16], bc_ps[0:1, 40:48]), "VEC")
    P.wait(t, "VEC")
    mm = t.matmul(bc_ps[:, 48:64], Z["onesf_sb"][0:1, :],
                  Z["rowsel_sb"][0:1, :], start=True, stop=True,
                  skip_group_check=True)
    P.inc(mm, "PE")
    mm = t.matmul(bc_ps[:, 64:80], Z["onesf_sb"][0:1, :],
                  Z["gbrow_sb"][0:1, :], start=True, stop=True,
                  skip_group_check=True)
    P.inc(mm, "PE")
    P.wait(v, "PE")
    P.inc(v.tensor_copy(Z["bbc_sb"][:, :], bc_ps[:, 48:64]), "VEC")
    P.inc(v.tensor_copy(Z["gbb_sb"][:, :], bc_ps[:, 64:80]), "VEC")
    P.wait(v, "VEC")
    P.wait(v, "PE")
    k2 = Z["k2_sb"]
    pc_all = k2[:, 0:64].bitcast(F32)
    cmp_all = k2[:, 64:576].bitcast(F32)
    nle_all = k2[:, 576:640].bitcast(F32)
    rr_all = k2[:, 640:704].bitcast(F32)
    ovg_all = k2[:, 704:768].bitcast(F32)
    cols = [(ep * 16 + i, ep, i) for ep in range(2) for i in range(CAP // 128)]
    for col, ep, i in cols:
        P.inc(v.tensor_scalar(pc_all[:, col:col + 1], Z["iota128_sb"][:, :],
                              float(128 * i), None, ALU.add), "VEC")
    P.wait(v, "VEC")
    for col, ep, i in cols:
        P.inc(v.tensor_scalar(cmp_all[:, 8 * col:8 * col + 8],
                              Z["bbc_sb"][:, 8 * ep:8 * ep + 8],
                              pc_all[:, col:col + 1], None, ALU.is_le), "VEC")
    P.wait(v, "VEC")
    for col, ep, i in cols:
        P.inc(v.reduce_sum(nle_all[:, col:col + 1],
                           cmp_all[:, 8 * col:8 * col + 8], axis=AX), "VEC")
    P.wait(v, "VEC")
    for col, ep, i in cols:
        P.inc(v.tensor_tensor(cmp_all[:, 8 * col:8 * col + 8],
                              cmp_all[:, 8 * col:8 * col + 8],
                              Z["bbc_sb"][:, 8 * ep:8 * ep + 8], ALU.mult),
              "VEC")
    P.wait(v, "VEC")
    for col, ep, i in cols:
        P.inc(v.reduce_max(rr_all[:, col:col + 1],
                           cmp_all[:, 8 * col:8 * col + 8], axis=AX), "VEC")
    P.wait(v, "VEC")
    for col, ep, i in cols:
        P.inc(v.tensor_sub(rr_all[:, col:col + 1], pc_all[:, col:col + 1],
                           rr_all[:, col:col + 1]), "VEC")
    P.wait(v, "VEC")
    for col, ep, i in cols:
        P.inc(v.tensor_scalar(ovg_all[:, col:col + 1], rr_all[:, col:col + 1],
                              float(LCAP), BIGIX, ALU.is_ge, ALU.mult), "VEC")
        P.inc(v.tensor_scalar(Z["gif"][:, col:col + 1], nle_all[:, col:col + 1],
                              2048.0, float(1024 * ep - 2048), ALU.mult,
                              ALU.add), "VEC")
        P.inc(v.tensor_scalar(Z["gif2"][:, col:col + 1],
                              nle_all[:, col:col + 1],
                              1024.0, float(8192 * ep - 1024), ALU.mult,
                              ALU.add), "VEC")
    P.wait(v, "VEC")
    for col, ep, i in cols:
        P.inc(v.tensor_tensor(Z["gif"][:, col:col + 1],
                              Z["gif"][:, col:col + 1], rr_all[:, col:col + 1],
                              ALU.add), "VEC")
        P.inc(v.tensor_tensor(Z["gif2"][:, col:col + 1],
                              Z["gif2"][:, col:col + 1], rr_all[:, col:col + 1],
                              ALU.add), "VEC")
    P.wait(v, "VEC")
    for col, ep, i in cols:
        P.inc(v.tensor_tensor(Z["gif"][:, col:col + 1],
                              Z["gif"][:, col:col + 1], ovg_all[:, col:col + 1],
                              ALU.add), "VEC")
        P.inc(v.tensor_tensor(Z["gif2"][:, col:col + 1],
                              Z["gif2"][:, col:col + 1], ovg_all[:, col:col + 1],
                              ALU.add), "VEC")
    P.wait(v, "VEC")
    for col, ep, i in cols:
        P.inc(v.tensor_copy(Z["giu"][:, col:col + 1],
                            Z["gif"][:, col:col + 1]), "VEC")
        P.inc(v.tensor_copy(Z["giu2"][:, col:col + 1],
                            Z["gif2"][:, col:col + 1]), "VEC")
    P.wait(v, "VEC")



def make_ffn_loaders(nc, P, ZF, w1_ap, w2_ap):
    g = nc.gpsimd
    w1_sb, w2_sb = ZF["w1_sb"], ZF["w2_sb"]
    FS = {"w1_ready": [0, 0], "w1_freed": [0, 0],
          "w2_ready": [0, 0], "w2_freed": [0, 0]}

    def load_w1(bsel, e, hc):
        P.wait(g, "PE", FS["w1_freed"][bsel])
        ins = g.dma_start(
            out=w1_sb[bsel].rearrange("p (c f) -> p c f", c=NCH),
            in_=w1_ap[e, :, 128 * hc:128 * (hc + 1)].rearrange(
                "(c p) f -> p c f", p=128))
        FS["w1_ready"][bsel] = P.inc(ins, "W10" if bsel == 0 else "W11", 16)

    def load_w2(bsel, e, dc):
        P.wait(g, "PE", FS["w2_freed"][bsel])
        ins = g.dma_start(
            out=w2_sb[bsel].rearrange("p (c f) -> p c f", c=32),
            in_=w2_ap[e, :, 128 * dc:128 * (dc + 1)].rearrange(
                "(c p) f -> p c f", p=128))
        FS["w2_ready"][bsel] = P.inc(ins, "W20" if bsel == 0 else "W21", 16)

    return FS, load_w1, load_w2


def emit_ffn(nc, P, Z, ZF, loaders, fdst_ap, rsrc_ap, giu_done, bchk,
             mid_hook=None):
    g, sy, t, v, s = nc.gpsimd, nc.sync, nc.tensor, nc.vector, nc.scalar
    buf_fm = ZF["buf_fm"]
    bt_sb, yt_sb = ZF["bt_sb"], ZF["yt_sb"]
    w1_sb, w2_sb = ZF["w1_sb"], ZF["w2_sb"]
    h_sb, yfm_sb = ZF["h_sb"], ZF["yfm_sb"]
    h_ps, y_ps, tr_ps = Z["pvA_ps"], Z["pvB_ps"], Z["trb_ps"]
    b1_sb, b2_sb, idbf_sb = Z["b1c_sb"], Z["b2c_sb"], Z["idbf_sb"]
    FS, load_w1, load_w2 = loaders
    w1_ready, w1_freed = FS["w1_ready"], FS["w1_freed"]
    w2_ready, w2_freed = FS["w2_ready"], FS["w2_freed"]
    tr_after = {}
    vec_strip = {}

    for e in range(2):
        for i in range(CAP // 128):
            b = i % 2
            bts = "BT0" if b == 0 else "BT1"
            if i >= 2 or e > 0:
                P.wait(g, "PE", tr_after.get((e, i - 2), tr_after.get(
                    (e - 1, CAP // 128 + i - 2), 0)))
            P.wait(g, "VEC", giu_done)
            col = e * 16 + i
            P.inc(g.indirect_dma_start(
                out=bt_sb[b][:, :], out_offset=None,
                in_=fdst_ap,
                in_offset=IndirectOffsetOnAxis(Z["giu"][:, col:col + 1], 0),
                bounds_check=bchk, oob_is_err=False), bts, 16)
            P.wait(t, bts)
            P.wait(t, "LS", P.c["LS"])
            P.wait(t, "VEC")
            for dc in range(NCH):
                mm = t.transpose(tr_ps[0:128, 128 * dc:128 * (dc + 1)],
                                 bt_sb[b][:, 128 * dc:128 * (dc + 1)],
                                 idbf_sb[:, :])
                P.inc(mm, "PE")
            P.wait(v, "PE")
            for dc in range(NCH):
                P.inc(v.tensor_copy(
                    buf_fm[:, CAP * dc + 128 * i:CAP * dc + 128 * (i + 1)],
                    tr_ps[0:128, 128 * dc:128 * (dc + 1)]), "VEC")
            tr_after[(e, i)] = P.c["PE"]
            if i % 4 == 3:
                vec_strip[(e, i // 4)] = P.c["VEC"]
        if e > 0:
            load_w1(0, e, 0)
        for st in range(4):
            act_of = {}
            P.wait(t, "ACT")
            for hc in range(32):
                bsel = hc % 2
                if hc + 1 < 32:
                    load_w1((hc + 1) % 2, e, hc + 1)
                elif st + 1 < 4:
                    load_w1((hc + 1) % 2, e, 0)
                t.wait_ge(P.sem["W10" if bsel == 0 else "W11"], w1_ready[bsel])
                P.wait(t, "VEC", vec_strip[(e, st)])
                hps = h_ps if hc % 2 == 0 else y_ps
                if hc >= 2:
                    P.wait(t, "ACT", act_of.get(hc - 2, 0))
                for c in range(NCH):
                    mm = t.matmul(hps[:, :],
                                  w1_sb[bsel][:, 128 * c:128 * (c + 1)],
                                  buf_fm[:, CAP * c + 512 * st:
                                         CAP * c + 512 * (st + 1)],
                                  start=(c == 0), stop=(c == NCH - 1),
                                  skip_group_check=True)
                P.inc(mm, "PE")
                w1_freed[bsel] = P.c["PE"]
                P.wait(s, "PE")
                si = s.activation(h_sb[:, 512 * hc:512 * (hc + 1)], hps[:, :],
                                  AF.Relu,
                                  bias=b1_sb[:, 32 * e + hc:32 * e + hc + 1],
                                  scale=1.0)
                act_of[hc] = P.inc(si, "ACT")
            load_w2(0, e, 0)
            yact = {}
            for dc in range(NCH):
                bsel = dc % 2
                if dc + 1 < NCH:
                    load_w2((dc + 1) % 2, e, dc + 1)
                t.wait_ge(P.sem["W20" if bsel == 0 else "W21"], w2_ready[bsel])
                yps = h_ps if dc % 2 == 0 else y_ps
                if dc < 2:
                    P.wait(t, "ACT", act_of[31])
                else:
                    P.wait(t, "ACT", yact[dc - 2])
                for hc in range(32):
                    mm = t.matmul(yps[:, :],
                                  w2_sb[bsel][:, 128 * hc:128 * (hc + 1)],
                                  h_sb[:, 512 * hc:512 * (hc + 1)],
                                  start=(hc == 0), stop=(hc == 31),
                                  skip_group_check=True)
                P.inc(mm, "PE")
                w2_freed[bsel] = P.c["PE"]
                P.wait(s, "PE")
                si = s.activation(yfm_sb[:, 512 * dc:512 * (dc + 1)], yps[:, :],
                                  AF.Identity,
                                  bias=b2_sb[:, 8 * e + dc:8 * e + dc + 1],
                                  scale=1.0)
                yact[dc] = P.inc(si, "ACT")
            P.wait(t, "ACT")
            for j in range(4):
                jb = j % 2
                sys_ = "SY0" if jb == 0 else "SY1"
                buf = yt_sb[jb]
                P.wait(t, "VEC")
                for dc in range(NCH):
                    mm = t.transpose(tr_ps[0:128, 128 * dc:128 * (dc + 1)],
                                     yfm_sb[:, 512 * dc + 128 * j:
                                            512 * dc + 128 * (j + 1)],
                                     idbf_sb[:, :])
                    P.inc(mm, "PE")
                P.wait(v, "PE")
                P.wait(v, sys_)
                P.inc(v.tensor_copy(buf[:, :], tr_ps[0:128, :]), "VEC")
                P.wait(g, "VEC")
                col = e * 16 + 4 * st + j
                P.inc(g.indirect_dma_start(
                    out=rsrc_ap,
                    out_offset=IndirectOffsetOnAxis(Z["giu2"][:, col:col + 1],
                                                    0),
                    in_=buf[:, :], in_offset=None,
                    bounds_check=bchk, oob_is_err=False), sys_, 16)
        if e == 0 and mid_hook is not None:
            mid_hook()



def emit_phase_c(nc, P, Z, ZF, x2tm_ap, meta_ap, rdst_ap, out_ap, bchk,
                 pe_attn):
    g, sy, t, v, s = nc.gpsimd, nc.sync, nc.tensor, nc.vector, nc.scalar
    ybig = Z["x_sb"]
    x2all = Z["o_sb"]
    q2, k2 = Z["q2_sb"], Z["k2_sb"]
    eqall = q2[:, 0:1024].bitcast(F32)
    selall = q2[:, 1024:2048].bitcast(F32)
    lnall = q2[:, 2048:2560].bitcast(F32)
    mt_all = q2[:, 2560:2816].bitcast(F32)
    o2b = [k2[:, 0:2048].bitcast(F32), k2[:, 2048:4096].bitcast(F32)]
    zb = [Z["vaugA"][:, 0:1024], Z["vaugA"][:, 1024:2048],
          Z["vaugB"][:, 0:1024], Z["vaugB"][:, 1024:2048]]
    zdump = Z["expS"]

    P.wait(sy, "SX0")
    P.wait(sy, "SX1")
    P.wait(sy, "SY0")
    P.wait(sy, "SY1")
    P.wait(sy, "ST")
    P.wait(sy, "PE")
    P.wait(sy, "VEC")
    P.wait(sy, "ACT")
    for i in range(0, 9):
        P.inc(sy.dma_start(out=x2all[:, 1024 * i:1024 * (i + 1)],
                           in_=x2tm_ap[128 * i:128 * (i + 1), :]), "LT0", 16)
    P.wait(v, "PE", pe_attn)
    P.wait(v, "LT0", 16)
    P.wait(v, "VEC")
    for i in range(NST):
        P.inc(v.tensor_scalar(eqall[:, 16 * i:16 * (i + 1)], Z["iota_sb"][:, :],
                              mt_all[:, 4 * i:4 * i + 1], None, ALU.is_equal),
              "VEC")
    P.wait(v, "VEC")
    for i in range(NST):
        P.inc(v.tensor_tensor(selall[:, 16 * i:16 * (i + 1)],
                              eqall[:, 16 * i:16 * (i + 1)], Z["gbb_sb"][:, :],
                              ALU.mult), "VEC")
    P.wait(v, "VEC")
    for i in range(NST):
        P.inc(v.reduce_sum(lnall[:, 8 * i + 6:8 * i + 7],
                           selall[:, 16 * i:16 * (i + 1)], axis=AX), "VEC")
    P.wait(v, "VEC")
    for i in range(NST):
        P.inc(v.tensor_tensor(lnall[:, 8 * i + 6:8 * i + 7],
                              lnall[:, 8 * i + 6:8 * i + 7],
                              mt_all[:, 4 * i + 1:4 * i + 2], ALU.add), "VEC")
    P.wait(v, "VEC")
    for i in range(NST):
        P.inc(v.tensor_scalar(lnall[:, 8 * i + 7:8 * i + 8],
                              lnall[:, 8 * i + 6:8 * i + 7], float(CAP),
                              BIGIX, ALU.is_ge, ALU.mult), "VEC")
    bcola = q2[:, 2816:2880].bitcast(F32)
    f2a = q2[:, 2880:2944].bitcast(F32)
    P.wait(v, "VEC")
    for i in range(NST):
        P.inc(v.tensor_tensor(selall[:, 16 * i:16 * (i + 1)],
                              eqall[:, 16 * i:16 * (i + 1)],
                              Z["odd16_sb"][:, :], ALU.mult), "VEC")
        P.inc(v.tensor_scalar(f2a[:, i:i + 1], mt_all[:, 4 * i:4 * i + 1],
                              512.0, None, ALU.mult), "VEC")
    P.wait(v, "VEC")
    for i in range(NST):
        P.inc(v.reduce_sum(bcola[:, i:i + 1], selall[:, 16 * i:16 * (i + 1)],
                           axis=AX), "VEC")
    P.wait(v, "VEC")
    for i in range(NST):
        P.inc(v.tensor_scalar(bcola[:, i:i + 1], bcola[:, i:i + 1],
                              7680.0, None, ALU.mult), "VEC")
        P.inc(v.tensor_tensor(f2a[:, i:i + 1], f2a[:, i:i + 1],
                              mt_all[:, 4 * i + 1:4 * i + 2], ALU.add), "VEC")
    P.wait(v, "VEC")
    for i in range(NST):
        P.inc(v.tensor_tensor(f2a[:, i:i + 1], f2a[:, i:i + 1],
                              bcola[:, i:i + 1], ALU.add), "VEC")
        P.inc(v.tensor_scalar(bcola[:, i:i + 1],
                              mt_all[:, 4 * i + 1:4 * i + 2],
                              float(LCAP), BIGIX, ALU.is_ge, ALU.mult), "VEC")
    P.wait(v, "VEC")
    for i in range(NST):
        P.inc(v.tensor_tensor(f2a[:, i:i + 1], f2a[:, i:i + 1],
                              bcola[:, i:i + 1], ALU.add), "VEC")
    P.wait(v, "VEC")
    for i in range(NST):
        P.inc(v.tensor_tensor(f2a[:, i:i + 1], f2a[:, i:i + 1],
                              lnall[:, 8 * i + 7:8 * i + 8], ALU.add), "VEC")
    P.wait(v, "VEC")
    for i in range(NST):
        P.inc(v.tensor_copy(Z["fiu"][:, i:i + 1], f2a[:, i:i + 1]),
              "VEC")
    P.inc(v.memset(ybig[:, :], 0.0), "VEC")
    P.wait(g, "PE")
    P.wait(g, "VEC")
    for i in range(9, 18):
        P.inc(g.dma_start(out=x2all[:, 1024 * i:1024 * (i + 1)],
                          in_=x2tm_ap[128 * i:128 * (i + 1), :]), "LT1", 16)
    P.wait(g, "CCR", 2)
    for i in range(NST):
        P.inc(g.indirect_dma_start(
            out=ybig[:, 1024 * i:1024 * (i + 1)], out_offset=None,
            in_=rdst_ap,
            in_offset=IndirectOffsetOnAxis(Z["fiu"][:, i:i + 1], 0),
            bounds_check=bchk, oob_is_err=False), "GB0", 16)
    P.wait(v, "LT0")
    P.wait(v, "LT1")
    zv = {}
    sums = {}
    o2v = {}
    for i in range(NST):
        zbi = zb[i % 4]
        obi = o2b[i % 2]
        if i >= 4:
            P.wait(v, "VEC", o2v[i - 4])
        P.wait(v, "GB0", 16 * (i + 1))
        zv[i] = P.inc(v.scalar_tensor_tensor(
            zbi[:, :], ybig[:, 1024 * i:1024 * (i + 1)],
            mt_all[:, 4 * i + 2:4 * i + 3],
            x2all[:, 1024 * i:1024 * (i + 1)], ALU.mult, ALU.add), "VEC")
        s.wait_ge(P.sem["VEC"], zv[i])
        P.inc(s.activation(zdump[0][:, :], zbi[:, :], AF.Identity,
                           accum_out=lnall[:, 8 * i:8 * i + 1]), "ACT")
        sums[i] = P.inc(s.activation(zdump[1][:, :], zbi[:, :], AF.Square,
                                     accum_out=lnall[:, 8 * i + 1:8 * i + 2]),
                        "ACT")
        v.wait_ge(P.sem["ACT"], sums[i])
        P.inc(v.tensor_scalar(lnall[:, 8 * i + 2:8 * i + 3],
                              lnall[:, 8 * i:8 * i + 1], 1.0 / D, None,
                              ALU.mult), "VEC")
        P.wait(v, "VEC")
        P.inc(v.tensor_scalar(lnall[:, 8 * i + 4:8 * i + 5],
                              lnall[:, 8 * i + 2:8 * i + 3],
                              lnall[:, 8 * i + 2:8 * i + 3], None, ALU.mult),
              "VEC")
        P.wait(v, "VEC")
        vv = P.inc(v.tensor_scalar(lnall[:, 8 * i + 3:8 * i + 4],
                                   lnall[:, 8 * i + 1:8 * i + 2], 1.0 / D,
                                   lnall[:, 8 * i + 4:8 * i + 5], ALU.mult,
                                   ALU.subtract), "VEC")
        s.wait_ge(P.sem["VEC"], vv)
        sq = P.inc(s.activation(lnall[:, 8 * i + 5:8 * i + 6],
                                lnall[:, 8 * i + 3:8 * i + 4], AF.Sqrt,
                                bias=Z["c5_sb"][:, 0:1], scale=1.0), "ACT")
        v.wait_ge(P.sem["ACT"], sq)
        P.inc(v.reciprocal(lnall[:, 8 * i + 5:8 * i + 6],
                           lnall[:, 8 * i + 5:8 * i + 6]), "VEC")
        P.wait(v, "VEC")
        if i >= 2:
            P.wait(v, "SO", 16 * (i - 1))
        o2v[i] = P.inc(v.tensor_scalar(obi[:, :], zbi[:, :],
                                       lnall[:, 8 * i + 2:8 * i + 3],
                                       lnall[:, 8 * i + 5:8 * i + 6],
                                       ALU.subtract, ALU.mult), "VEC")
        sy.wait_ge(P.sem["VEC"], o2v[i])
        P.inc(sy.dma_start(out=out_ap[128 * i:128 * (i + 1), :],
                           in_=obi[:, :]), "SO", 16)
    P.wait(sy, "SO")



def build_fused():
    nc = bass.Bass("TRN2", target_bir_lowering=False, debug=False)
    xT = nc.dram_tensor("xT", [D, T], F32, kind="ExternalInput").ap()
    wqT = nc.dram_tensor("wqT", [D, D], F32, kind="ExternalInput").ap()
    wkT = nc.dram_tensor("wkT", [D, D], F32, kind="ExternalInput").ap()
    wvT = nc.dram_tensor("wvT", [D, D], F32, kind="ExternalInput").ap()
    woT = nc.dram_tensor("woT", [D, D], F32, kind="ExternalInput").ap()
    boc = nc.dram_tensor("boc", [128, NCH], F32, kind="ExternalInput").ap()
    wg = nc.dram_tensor("wg", [D, E], F32, kind="ExternalInput").ap()
    bgc = nc.dram_tensor("bgc", [E, 1], F32, kind="ExternalInput").ap()
    idbf = nc.dram_tensor("idbf", [128, 128], BF16, kind="ExternalInput").ap()
    id16 = nc.dram_tensor("id16", [16, 16], F32, kind="ExternalInput").ap()
    iota16b = nc.dram_tensor("iota16b", [128, 16], F32, kind="ExternalInput").ap()
    odd16 = nc.dram_tensor("odd16", [128, 16], F32, kind="ExternalInput").ap()
    r16m = nc.dram_tensor("r16m", [128, 16], F32, kind="ExternalInput").ap()
    onesb = nc.dram_tensor("onesb", [128, 128], BF16, kind="ExternalInput").ap()
    onesf = nc.dram_tensor("onesf", [128, 128], F32, kind="ExternalInput").ap()
    tri8 = nc.dram_tensor("tri8", [8, 8], F32, kind="ExternalInput").ap()
    eh01 = nc.dram_tensor("eh01", [16, 2], F32, kind="ExternalInput").ap()
    pmask = nc.dram_tensor("pmask", [8, 1], F32, kind="ExternalInput").ap()
    iota128 = nc.dram_tensor("iota128", [128, 1], F32, kind="ExternalInput").ap()
    w1 = nc.dram_tensor("w1", [2, D, FH], F32, kind="ExternalInput").ap()
    w2 = nc.dram_tensor("w2", [2, FH, D], F32, kind="ExternalInput").ap()
    b1c = nc.dram_tensor("b1c", [128, 64], F32, kind="ExternalInput").ap()
    b2c = nc.dram_tensor("b2c", [128, 16], F32, kind="ExternalInput").ap()
    out = nc.dram_tensor("out", [T, D], F32, kind="ExternalOutput").ap()
    x2tm = nc.dram_tensor("x2tm", [T, D], BF16).ap()
    meta = nc.dram_tensor("meta", [T, 4], F32).ap()
    cnts1 = nc.dram_tensor("cnts1", [1, E], F32).ap()
    cnts8 = nc.dram_tensor("cnts8", [8, E], F32).ap()
    fsrc = nc.dram_tensor("fsrc", [NROW, D], BF16).ap()
    fdst = nc.dram_tensor("fdst", [NROW, D], BF16).ap()
    rsrc = nc.dram_tensor("rsrc", [NROW, D], BF16).ap()
    rdst = nc.dram_tensor("rdst", [NROW, D], BF16).ap()

    es = ExitStack()
    Z = alloc_l1(nc, es)
    ZF = alloc_overlays(Z)
    P = Prog(nc, es, SEMS)
    g, sy = nc.gpsimd, nc.sync
    bchk = nc.alloc_register(mybir.EngineType.Pool, "bchk")
    g.reg_mov(bchk, NROW - 1)

    for c in range(NCH):
        P.inc(g.dma_start(out=Z["x_sb"][:, T * c: T * (c + 1)],
                          in_=xT[128 * c:128 * (c + 1), :]), "LX", 16)
    emit_l1_const_loads(nc, P, Z, dict(
        woT=woT, wg=wg, boc=boc, bgc=bgc, idbf=idbf, id16=id16,
        iota16b=iota16b, r16m=r16m, onesb=onesb, onesf=onesf, tri8=tri8,
        eh01=eh01, pmask=pmask, iota128=iota128, b1c=b1c, b2c=b2c,
        odd16=odd16))
    lx_ready = P.c["LX"]
    S = new_l1_state()
    prefetch = make_headpair_prefetch(nc, P, Z, S, wqT, wkT, wvT)
    emit_l1_init(nc, P, Z)
    with nc.named_scope("l1"):
        emit_l1_batch(nc, P, Z, S, prefetch, x2tm, meta, 0, lx_ready, fsrc,
                      bchk)
    pe_attn = P.c["PE"]

    with nc.named_scope("prolog_b"):
        emit_prolog_b(nc, P, Z, cnts1, cnts8, S["vec_cnt"])
    giu_done = P.c["VEC"]

    P.wait(g, "SX0")
    P.wait(g, "SX1")
    P.wait(g, "ST")
    P.wait(g, "PE")
    P.wait(g, "VEC")
    loaders = make_ffn_loaders(nc, P, ZF, w1, w2)
    loaders[1](0, 0, 0)
    P.wait(sy, "SX0")
    P.wait(sy, "SX1")
    P.wait(sy, "ST")
    P.wait(sy, "PE")
    P.wait(sy, "VEC")
    mt_all_early = Z["q2_sb"][:, 2560:2816].bitcast(F32)
    P.inc(sy.dma_start(out=mt_all_early.rearrange("p (i c) -> p i c", c=4),
                       in_=meta.rearrange("(i p) c -> p i c", p=128)),
          "LT0", 16)
    for i in range(18, NST):
        P.inc(sy.dma_start(out=Z["o_sb"][:, 1024 * i:1024 * (i + 1)],
                           in_=x2tm[128 * i:128 * (i + 1), :]), "LT0", 16)

    with nc.named_scope("a2a_fwd"):
        with allow_ici_all_to_all():
            cc = g.collective_compute(
                "AllToAll", ALU.bypass, replica_groups=[list(range(NC))],
                ins=[fsrc.opt()], outs=[fdst.opt()])
        P.inc(cc, "CCF", 1)
        P.wait(g, "CCF")

    def mid_hook():
        P.wait(g, "SY0")
        P.wait(g, "SY1")
        with allow_ici_all_to_all():
            cc = g.collective_compute(
                "AllToAll", ALU.bypass, replica_groups=[list(range(NC))],
                ins=[rsrc[0:NROW // 2, :].opt()],
                outs=[rdst[0:NROW // 2, :].opt()])
        P.inc(cc, "CCR", 1)

    with nc.named_scope("ffn"):
        emit_ffn(nc, P, Z, ZF, loaders, fdst, rsrc, giu_done, bchk, mid_hook)

    with nc.named_scope("a2a_ret"):
        P.wait(g, "SY0")
        P.wait(g, "SY1")
        with allow_ici_all_to_all():
            cc = g.collective_compute(
                "AllToAll", ALU.bypass, replica_groups=[list(range(NC))],
                ins=[rsrc[NROW // 2:NROW, :].opt()],
                outs=[rdst[NROW // 2:NROW, :].opt()])
        P.inc(cc, "CCR", 1)

    with nc.named_scope("phase_c"):
        emit_phase_c(nc, P, Z, ZF, x2tm, meta, rdst, out, bchk, pe_attn)

    es.close()
    return nc



_NC_CACHE = {}


def _get(name, builder):
    if name not in _NC_CACHE:
        _NC_CACHE[name] = builder()
    return _NC_CACHE[name]


def kernel(L0_attn_x, L0_attn_Wq, L0_attn_Wk, L0_attn_Wv, L0_attn_Wo,
           L0_attn_bo, L0_ffn_Wg, L0_ffn_bg, L0_ffn_W1, L0_ffn_b1,
           L0_ffn_W2, L0_ffn_b2):
    x = np.asarray(L0_attn_x, dtype=np.float32)
    wqT = np.ascontiguousarray(np.asarray(L0_attn_Wq, np.float32).T)
    wkT = np.ascontiguousarray(np.asarray(L0_attn_Wk, np.float32).T)
    wvT = np.ascontiguousarray(np.asarray(L0_attn_Wv, np.float32).T)
    woT = np.ascontiguousarray(np.asarray(L0_attn_Wo, np.float32).T)
    bo = np.asarray(L0_attn_bo, np.float32)
    wg = np.ascontiguousarray(np.asarray(L0_ffn_Wg, np.float32))
    bg = np.asarray(L0_ffn_bg, np.float32)
    W1 = np.asarray(L0_ffn_W1, np.float32)
    b1 = np.asarray(L0_ffn_b1, np.float32)
    W2 = np.asarray(L0_ffn_W2, np.float32)
    b2 = np.asarray(L0_ffn_b2, np.float32)

    boc = np.ascontiguousarray(bo.reshape(NCH, 128).T)
    bgc = np.ascontiguousarray(bg.reshape(E, 1))
    idbf = np.eye(128, dtype=np.float32).astype(BF16NP)
    id16 = np.eye(16, dtype=np.float32)
    iota16b = np.ascontiguousarray(np.tile(np.arange(16, dtype=np.float32),
                                           (128, 1)))
    r16m = np.ascontiguousarray(16.0 - iota16b)
    onesb = np.ones((128, 128), np.float32).astype(BF16NP)
    onesf = np.ones((128, 128), np.float32)
    odd16 = np.ascontiguousarray(np.tile(np.arange(16, dtype=np.float32) % 2,
                                         (128, 1)))
    tri8 = np.triu(np.ones((8, 8), np.float32), 1)
    iota128 = np.arange(128, dtype=np.float32).reshape(128, 1)

    nc1 = _get("fused", build_fused)
    in_maps = []
    for c in range(NC):
        eh01 = np.zeros((16, 2), np.float32)
        eh01[2 * c, 0] = 1.0
        eh01[2 * c + 1, 1] = 1.0
        b1cc = np.ascontiguousarray(
            np.concatenate([b1[2 * c + e].reshape(32, 128).T for e in range(2)],
                           axis=1))
        b2cc = np.ascontiguousarray(
            np.concatenate([b2[2 * c + e].reshape(NCH, 128).T for e in range(2)],
                           axis=1))
        in_maps.append({
            "xT": np.ascontiguousarray(x[c].T),
            "wqT": wqT, "wkT": wkT, "wvT": wvT, "woT": woT,
            "boc": boc, "wg": wg, "bgc": bgc,
            "idbf": idbf, "id16": id16, "iota16b": iota16b, "r16m": r16m,
            "onesb": onesb, "onesf": onesf, "tri8": tri8, "eh01": eh01,
            "odd16": odd16,
            "pmask": (np.arange(8) < c).astype(np.float32).reshape(8, 1),
            "iota128": iota128,
            "w1": np.ascontiguousarray(W1[2 * c:2 * c + 2]),
            "w2": np.ascontiguousarray(W2[2 * c:2 * c + 2]),
            "b1c": b1cc, "b2c": b2cc,
        })
    res = run_bass_kernel_spmd(nc1, in_maps, core_ids=list(range(NC))).results
    return np.stack([res[c]["out"] for c in range(NC)], axis=0)


# version 13
# speedup vs baseline: 1.0858x; 1.0078x over previous
import numpy as np
import ml_dtypes
from contextlib import ExitStack

import concourse.bass as bass
import concourse.mybir as mybir
from concourse.bass import IndirectOffsetOnAxis
from concourse.bass_utils import run_bass_kernel_spmd
from concourse.aot_env import allow_ici_all_to_all

F32 = mybir.dt.float32
BF16 = mybir.dt.bfloat16
U32 = mybir.dt.uint32
AF = mybir.ActivationFunctionType
ALU = mybir.AluOpType
AX = mybir.AxisListType.X
BF16NP = ml_dtypes.bfloat16

D = 1024
T = 4096
H = 16
NC = 8
E = 16
CAP = 2048
LCAP = 1024
FH = 4096
NCH = D // 128
NTT = T // 512
NST = T // 128
BIG = float(1 << 26)
BIGIX = 1.0e6
NROW = E * LCAP


class Prog:
    def __init__(self, nc, es, sems):
        self.nc = nc
        self.c = {}
        self.sem = {}
        for name in sems:
            self.sem[name] = es.enter_context(nc.semaphore(name))
            self.c[name] = 0

    def inc(self, instr, name, k=1):
        self.c[name] += k
        instr.then_inc(self.sem[name], k)
        return self.c[name]

    def wait(self, eng, name, v=None):
        eng.wait_ge(self.sem[name], self.c[name] if v is None else v)


SEMS = ["LX", "LS", "WH0", "WH1", "PE", "ACT", "VEC", "ST",
        "SX0", "SX1", "SST", "SEXP", "SPV",
        "CN", "CCA", "CCF", "CCR",
        "BT0", "BT1", "W10", "W11", "W20", "W21", "SY0", "SY1",
        "LT0", "LT1", "GB0", "GB1", "SO"]


def alloc_l1(nc, es):
    sb = lambda name, shape, dt: es.enter_context(nc.sbuf_tensor(name, shape, dt))
    ps = lambda name, shape, dt: es.enter_context(nc.psum_tensor(name, shape, dt))
    Z = {}
    Z["x_sb"] = sb("x_sb", [128, NCH * T], BF16)
    Z["o_sb"] = sb("o_sb", [128, NCH * T], BF16)
    Z["wo_sb"] = sb("wo_sb", [128, NCH * D], BF16)
    Z["wh_sb"] = [sb(f"wh_sb{i}", [128, 3 * D], BF16) for i in range(2)]
    Z["wg_sb"] = sb("wg_sb", [128, NCH * E], BF16)
    Z["boc_sb"] = sb("boc_sb", [128, NCH], F32)
    Z["bgc_sb"] = sb("bgc_sb", [16, 1], F32)
    Z["idbf_sb"] = sb("idbf_sb", [128, 128], BF16)
    Z["id16_sb"] = sb("id16_sb", [16, 16], F32)
    Z["iota_sb"] = sb("iota_sb", [128, 16], F32)
    Z["r16m_sb"] = sb("r16m_sb", [128, 16], F32)
    Z["odd16_sb"] = sb("odd16_sb", [128, 16], F32)
    Z["ones_sb"] = sb("ones_sb", [128, 128], BF16)
    Z["onesf_sb"] = sb("onesf_sb", [128, 128], F32)
    Z["k2_sb"] = sb("k2_sb", [128, T], BF16)
    Z["q2_sb"] = sb("q2_sb", [128, T], BF16)
    Z["v2tmp"] = sb("v2tmp", [128, 512], BF16)
    Z["vaugA"] = sb("vaugA", [128, 65 * 32], BF16)
    Z["vaugB"] = sb("vaugB", [128, 65 * 32], BF16)
    Z["expS"] = [sb(f"expS{i}", [128, 1024], BF16) for i in range(2)]
    Z["rden_bf"] = sb("rden_bf", [65, 512], BF16)
    Z["bc_sb"] = sb("bc_sb", [128, 512], F32)
    Z["ysq_sb"] = sb("ysq_sb", [128, 512], BF16)
    Z["mean_sb"] = sb("mean_sb", [1, 512], F32)
    Z["var_sb"] = sb("var_sb", [1, 512], F32)
    Z["mean_bf"] = sb("mean_bf", [1, 512], BF16)
    Z["rstd_bf"] = sb("rstd_bf", [1, 512], BF16)
    Z["rbc_sb"] = sb("rbc_sb", [128, 512], F32)
    Z["lg_tt"] = sb("lg_tt", [16, 512], F32)
    Z["msq_sb"] = Z["lg_tt"][0:1, 0:512]
    Z["mbc_sb"] = Z["bc_sb"]
    Z["mx_sb"] = sb("mx_sb", [128, 1], F32)
    Z["nmx_sb"] = sb("nmx_sb", [128, 1], F32)
    Z["sm_sb"] = sb("sm_sb", [128, 1], F32)
    Z["ejunk_sb"] = sb("ejunk_sb", [128, 16], F32)
    Z["eq_sb"] = sb("eq_sb", [128, 16], F32)
    Z["cand_sb"] = sb("cand_sb", [128, 16], F32)
    Z["t1_sb"] = sb("t1_sb", [128, 1], F32)

    Z["ohb_sb"] = sb("ohb_sb", [128, 16], BF16)
    Z["cnt_sb"] = sb("cnt_sb", [16, 8], F32)
    Z["sel_sb"] = sb("sel_sb", [128, 16], F32)
    Z["cs_sb"] = sb("cs_sb", [128, 1], F32)

    Z["c5_sb"] = sb("c5_sb", [128, 1], F32)
    Z["c8_sb"] = sb("c8_sb", [128, 1], F32)
    Z["csum_sb"] = Z["o_sb"][0:16, 0:8192].bitcast(F32)
    Z["ohfm_sb"] = Z["o_sb"][0:16, 8192:12288]
    Z["lg_tm"] = Z["o_sb"][:, 12288:13312].bitcast(F32)
    Z["ohf_all"] = Z["o_sb"][:, 13312:14336].bitcast(F32)
    Z["meta_sb"] = Z["o_sb"][:, 14336:14592].bitcast(F32)
    Z["idx_all"] = Z["o_sb"][:, 14592:14656].bitcast(F32)
    Z["x2st"] = [Z["o_sb"][:, 16384:17408], Z["o_sb"][:, 17408:18432]]
    Z["fif"] = sb("fif", [128, 32], F32)
    Z["fiu"] = sb("fiu", [128, 32], U32)
    Z["gif"] = sb("gif", [128, 32], F32)
    Z["giu"] = sb("giu", [128, 32], U32)
    Z["gif2"] = sb("gif2", [128, 32], F32)
    Z["giu2"] = sb("giu2", [128, 32], U32)
    Z["cnts8_sb"] = sb("cnts8_sb", [8, 16], F32)
    Z["tri8_sb"] = sb("tri8_sb", [8, 8], F32)
    Z["eh01_sb"] = sb("eh01_sb", [16, 2], F32)
    Z["pmask_sb"] = sb("pmask_sb", [8, 1], F32)
    Z["iota128_sb"] = sb("iota128_sb", [128, 1], F32)
    Z["baseT_sb"] = sb("baseT_sb", [16, 8], F32)
    Z["rowsel_sb"] = sb("rowsel_sb", [1, 16], F32)
    Z["gbrow_sb"] = sb("gbrow_sb", [1, 16], F32)
    Z["bbc_sb"] = sb("bbc_sb", [128, 16], F32)
    Z["gbb_sb"] = sb("gbb_sb", [128, 16], F32)
    Z["pc_sb"] = sb("pc_sb", [128, 1], F32)
    Z["cmp8_sb"] = sb("cmp8_sb", [128, 8], F32)
    Z["bsc_sb"] = sb("bsc_sb", [128, 8], F32)
    Z["nle_sb"] = sb("nle_sb", [128, 1], F32)
    Z["rr_sb"] = sb("rr_sb", [128, 1], F32)
    Z["ovg_sb"] = sb("ovg_sb", [128, 1], F32)
    Z["b1c_sb"] = sb("b1c_sb", [128, 64], F32)
    Z["b2c_sb"] = sb("b2c_sb", [128, 16], F32)
    Z["ln2_sb"] = sb("ln2_sb", [128, 8], F32)
    Z["st_ps"] = [ps("st_ps0", [128, 1024], F32), ps("st_ps1", [128, 1024], F32)]
    Z["pvA_ps"] = ps("pvA_ps", [128, 512], F32)
    Z["pvB_ps"] = ps("pvB_ps", [128, 512], F32)
    Z["bc_ps"] = ps("bc_ps", [128, 512], F32)
    Z["trb_ps"] = ps("trb_ps", [128, 1024], BF16)
    return Z


def alloc_overlays(Z):
    ZF = {}
    x, o = Z["x_sb"], Z["o_sb"]
    ZF["buf_fm"] = x[:, 0:16384]
    ZF["h_sb"] = x[:, 16384:32768]
    ZF["yfm_sb"] = o[:, 0:4096]
    ZF["bt_sb"] = [o[:, 4096:5120], o[:, 5120:6144]]
    ZF["yt_sb"] = [o[:, 6144:7168], o[:, 7168:8192]]
    ZF["w1_sb"] = [o[:, 8192:9216], o[:, 9216:10240]]
    ZF["w2_sb"] = [o[:, 10240:14336], o[:, 14336:18432]]
    k2, q2 = Z["k2_sb"], Z["q2_sb"]
    ZF["x2t2"] = [k2[:, 0:1024], k2[:, 1024:2048]]
    ZF["yb2"] = [k2[:, 2048:3072], k2[:, 3072:4096]]
    ZF["mt2"] = [q2[:, 0:8].bitcast(F32), q2[:, 8:16].bitcast(F32)]
    ZF["z_sb"] = Z["vaugA"][:, 0:1024]
    ZF["zc_sb"] = Z["vaugA"][:, 1024:2048]
    ZF["zsq_sb"] = Z["vaugB"][:, 0:1024]
    ZF["o2_sb"] = q2[:, 2048:4096].bitcast(F32)
    return ZF


def new_l1_state():
    return {
        "gg": 0,
        "vec_bc_free": 0,
        "vec_pvA_free": 0,
        "vec_pvB_free": 0,
        "wh_ready": [0, 0],
        "wh_freed": [0, 0],
    }


def emit_l1_init(nc, P, Z):
    v, s, t = nc.vector, nc.scalar, nc.tensor
    P.wait(v, "LS")
    P.inc(v.memset(Z["vaugA"][:, :], 1.0), "VEC")
    P.inc(v.memset(Z["vaugB"][:, :], 1.0), "VEC")
    P.inc(v.memset(Z["c5_sb"][:, :], 1e-5), "VEC")
    P.inc(v.memset(Z["c8_sb"][:, :], 1e-8), "VEC")
    P.wait(s, "VEC")
    P.wait(t, "VEC")


def emit_l1_batch(nc, P, Z, S, prefetch, x2tm_ap, meta_ap, cnt_col, lx_ready,
                  fsrc_ap, bchk):
    g, sy, t, v, s = nc.gpsimd, nc.sync, nc.tensor, nc.vector, nc.scalar
    st_ps, pvA_ps, pvB_ps = Z["st_ps"], Z["pvA_ps"], Z["pvB_ps"]
    bc_ps, trb_ps = Z["bc_ps"], Z["trb_ps"]
    x_sb, o_sb = Z["x_sb"], Z["o_sb"]
    k2_sb, q2_sb = Z["k2_sb"], Z["q2_sb"]
    expS = Z["expS"]
    ones_sb = Z["ones_sb"]

    for hp in range(8):
        bsel = hp % 2
        wh_sem = "WH0" if bsel == 0 else "WH1"
        prefetch(hp)
        t.wait_ge(P.sem[wh_sem], S["wh_ready"][bsel])
        P.wait(t, "LX", lx_ready)
        P.wait(t, "LS")
        for mi, dst_sb in ((0, k2_sb), (1, q2_sb)):
            for tt in range(NTT):
                pp = bc_ps if tt % 2 == 0 else pvA_ps
                P.wait(t, "VEC",
                       S["vec_bc_free"] if tt % 2 == 0 else S["vec_pvA_free"])
                for c in range(NCH):
                    mm = t.matmul(pp[:, :],
                                  Z["wh_sb"][bsel][:, D * mi + 128 * c:
                                                   D * mi + 128 * (c + 1)],
                                  x_sb[:, T * c + 512 * tt: T * c + 512 * (tt + 1)],
                                  start=(c == 0), stop=(c == NCH - 1))
                P.inc(mm, "PE")
                P.wait(v, "PE")
                vi = P.inc(v.tensor_copy(dst_sb[:, 512 * tt:512 * (tt + 1)],
                                         pp[:, :]), "VEC")
                if tt % 2 == 0:
                    S["vec_bc_free"] = vi
                else:
                    S["vec_pvA_free"] = vi
        for tt in range(NTT):
            pp = bc_ps if tt % 2 == 0 else pvA_ps
            P.wait(t, "VEC",
                   S["vec_bc_free"] if tt % 2 == 0 else S["vec_pvA_free"])
            for c in range(NCH):
                mm = t.matmul(pp[:, :],
                              Z["wh_sb"][bsel][:, 2 * D + 128 * c:
                                               2 * D + 128 * (c + 1)],
                              x_sb[:, T * c + 512 * tt: T * c + 512 * (tt + 1)],
                              start=(c == 0), stop=(c == NCH - 1))
            P.inc(mm, "PE")
            P.wait(v, "PE")
            vi = P.inc(v.tensor_copy(Z["v2tmp"][:, :], pp[:, :]), "VEC")
            if tt % 2 == 0:
                S["vec_bc_free"] = vi
            else:
                S["vec_pvA_free"] = vi
            for j in range(4):
                sc = 4 * tt + j
                P.wait(t, "VEC")
                mm = t.transpose(trb_ps[0:128, 0:128],
                                 Z["v2tmp"][:, 128 * j:128 * (j + 1)],
                                 Z["idbf_sb"][:, :])
                P.inc(mm, "PE")
                P.wait(v, "PE")
                P.inc(v.tensor_copy(Z["vaugA"][:, 65 * sc:65 * sc + 64],
                                    trb_ps[0:128, 0:64]), "VEC")
                P.inc(v.tensor_copy(Z["vaugB"][:, 65 * sc:65 * sc + 64],
                                    trb_ps[0:128, 64:128]), "VEC")
        S["wh_freed"][bsel] = P.c["PE"]
        vec_vaug = P.c["VEC"]

        for tt in range(NTT):
            g0 = S["gg"]
            tsl = slice(512 * tt, 512 * (tt + 1))
            for lg in range(32):
                cgg = g0 + lg
                if cgg >= 2:
                    P.wait(t, "SEXP", cgg - 1)
                if lg == 0:
                    P.wait(t, "VEC", vec_vaug)
                mm = t.matmul(st_ps[cgg % 2][:, 0:512],
                              k2_sb[0:64, 128 * lg:128 * (lg + 1)],
                              q2_sb[0:64, tsl], start=True, stop=True,
                              tile_position=(0, 0))
                mm = t.matmul(st_ps[cgg % 2][:, 512:1024],
                              k2_sb[64:128, 128 * lg:128 * (lg + 1)],
                              q2_sb[64:128, tsl], start=True, stop=True,
                              tile_position=(64, 0))
                P.inc(mm, "SST")
                if lg >= 1:
                    P.wait(t, "SEXP", cgg)
                    pvg = lg - 1
                    if pvg == 0:
                        P.wait(t, "VEC", S["vec_pvA_free"])
                        P.wait(t, "VEC", S["vec_pvB_free"])
                    mm = t.matmul(pvA_ps[0:65, :],
                                  Z["vaugA"][:, 65 * pvg:65 * pvg + 65],
                                  expS[(g0 + pvg) % 2][:, 0:512],
                                  start=(pvg == 0), stop=False,
                                  skip_group_check=True)
                    mm = t.matmul(pvB_ps[0:65, :],
                                  Z["vaugB"][:, 65 * pvg:65 * pvg + 65],
                                  expS[(g0 + pvg) % 2][:, 512:1024],
                                  start=(pvg == 0), stop=False,
                                  skip_group_check=True)
                    P.inc(mm, "SPV")
            P.wait(t, "SEXP", g0 + 32)
            mm = t.matmul(pvA_ps[0:65, :], Z["vaugA"][:, 65 * 31:65 * 31 + 65],
                          expS[(g0 + 31) % 2][:, 0:512],
                          start=False, stop=True, skip_group_check=True)
            mm = t.matmul(pvB_ps[0:65, :], Z["vaugB"][:, 65 * 31:65 * 31 + 65],
                          expS[(g0 + 31) % 2][:, 512:1024],
                          start=False, stop=True, skip_group_check=True)
            P.inc(mm, "SPV")
            for lg in range(32):
                cgg = g0 + lg
                P.wait(s, "SST", cgg + 1)
                if cgg >= 2:
                    P.wait(s, "SPV", cgg - 1)
                si = s.activation(expS[cgg % 2][:, :], st_ps[cgg % 2][:, :],
                                  AF.Exp, bias=0.0, scale=0.125)
                P.inc(si, "SEXP")
            S["gg"] = g0 + 32
            for head, pv in ((0, pvA_ps), (1, pvB_ps)):
                bcx = Z["bc_sb"] if head == 0 else Z["rbc_sb"]
                dr = slice(0, 1) if head == 0 else slice(64, 65)
                po = head * 64
                ocols = slice(T * hp + 512 * tt, T * hp + 512 * (tt + 1))
                P.wait(v, "SPV", S["gg"])
                with nc.allow_low_precision(reason="bf16 stage for matmul"):
                    den_cast = P.inc(v.tensor_copy(Z["rden_bf"][dr, :],
                                                   pv[64:65, :]), "VEC")
                with nc.allow_low_precision(reason="unnormalised bf16 stage"):
                    vi = P.inc(v.tensor_copy(o_sb[po:po + 64, ocols],
                                             pv[0:64, :]), "VEC")
                if head == 0:
                    S["vec_pvA_free"] = vi
                else:
                    S["vec_pvB_free"] = vi
                P.wait(t, "VEC", den_cast)
                P.wait(t, "VEC", S["vec_bc_free"])
                mm = t.matmul(bc_ps[:, :], ones_sb[dr, 0:128],
                              Z["rden_bf"][dr, :], start=True, stop=True,
                              skip_group_check=True)
                P.inc(mm, "PE")
                P.wait(v, "PE")
                P.wait(v, "VEC")
                P.inc(v.tensor_scalar(bcx[:, :], bc_ps[:, :],
                                      1e-8, None, ALU.add), "VEC")
                S["vec_bc_free"] = P.c["VEC"]
                P.wait(v, "VEC")
                P.inc(v.reciprocal(bcx[:, :], bcx[:, :]), "VEC")
                P.wait(v, "VEC")
                P.inc(v.tensor_tensor(o_sb[po:po + 64, ocols],
                                      o_sb[po:po + 64, ocols],
                                      bcx[po:po + 64, :], ALU.mult), "VEC")

    P.wait(t, "SEXP", S["gg"])
    P.wait(t, "VEC", S["vec_pvA_free"])
    P.wait(t, "VEC", S["vec_pvB_free"])
    for tt in range(NTT):
        for dc in range(NCH):
            wv = st_ps[0][:, 0:512] if dc % 2 == 0 else st_ps[0][:, 512:1024]
            P.wait(t, "VEC")
            for r in range(NCH):
                mm = t.matmul(wv,
                              Z["wo_sb"][:, D * r + 128 * dc: D * r + 128 * (dc + 1)],
                              o_sb[:, T * r + 512 * tt: T * r + 512 * (tt + 1)],
                              start=(r == 0), stop=(r == NCH - 1),
                              skip_group_check=True)
            P.inc(mm, "PE")
            P.wait(v, "PE")
            xv = x_sb[:, T * dc + 512 * tt: T * dc + 512 * (tt + 1)]
            P.inc(v.scalar_tensor_tensor(xv, wv, Z["boc_sb"][:, dc:dc + 1], xv,
                                         ALU.add, ALU.add), "VEC")
            P.wait(t, "VEC")
            mm = t.matmul(pvA_ps[0:1, :], ones_sb[:, 0:1], xv,
                          start=(dc == 0), stop=(dc == NCH - 1),
                          skip_group_check=True)
            P.inc(mm, "PE")
            P.wait(s, "VEC")
            P.inc(s.activation(Z["ysq_sb"][:, :], xv, AF.Square), "ACT")
            P.wait(t, "ACT")
            mm = t.matmul(pvA_ps[32:33, :], ones_sb[:, 0:1], Z["ysq_sb"][:, :],
                          start=(dc == 0), stop=(dc == NCH - 1),
                          skip_group_check=True)
            P.inc(mm, "PE")
        P.wait(v, "PE")
        P.inc(v.tensor_scalar(Z["mean_sb"][0:1, :], pvA_ps[0:1, :],
                              1.0 / D, None, ALU.mult), "VEC")
        P.inc(v.tensor_scalar(Z["msq_sb"][:, :], pvA_ps[32:33, :],
                              1.0 / D, None, ALU.mult), "VEC")
        P.wait(v, "VEC")
        P.inc(v.tensor_tensor(Z["var_sb"][0:1, :], Z["mean_sb"][0:1, :],
                              Z["mean_sb"][0:1, :], ALU.mult), "VEC")
        P.wait(v, "VEC")
        P.inc(v.tensor_sub(Z["var_sb"][0:1, :], Z["msq_sb"][:, :],
                           Z["var_sb"][0:1, :]), "VEC")
        P.wait(s, "VEC")
        P.inc(s.activation(Z["var_sb"][0:1, :], Z["var_sb"][0:1, :],
                           AF.Sqrt, bias=Z["c5_sb"][0:1, 0:1], scale=1.0),
              "ACT")
        P.wait(v, "ACT")
        P.inc(v.reciprocal(Z["var_sb"][0:1, :], Z["var_sb"][0:1, :]), "VEC")
        P.wait(v, "VEC")
        with nc.allow_low_precision(reason="bf16 broadcast operands"):
            P.inc(v.tensor_copy(Z["mean_bf"][0:1, :], Z["mean_sb"][0:1, :]),
                  "VEC")
            P.inc(v.tensor_copy(Z["rstd_bf"][0:1, :], Z["var_sb"][0:1, :]),
                  "VEC")
        P.wait(t, "VEC")
        t.matmul(st_ps[1][:, 0:512], ones_sb[0:1, 0:128], Z["mean_bf"][0:1, :],
                 start=True, stop=True, skip_group_check=True)
        mm = t.matmul(st_ps[1][:, 512:1024], ones_sb[0:1, 0:128],
                      Z["rstd_bf"][0:1, :], start=True, stop=True,
                      skip_group_check=True)
        P.inc(mm, "PE")
        P.wait(v, "PE")
        P.inc(v.tensor_copy(Z["mbc_sb"][:, :], st_ps[1][:, 0:512]), "VEC")
        P.inc(v.tensor_copy(Z["rbc_sb"][:, :], st_ps[1][:, 512:1024]), "VEC")
        P.wait(v, "VEC")
        for dc in range(NCH):
            xv = x_sb[:, T * dc + 512 * tt: T * dc + 512 * (tt + 1)]
            P.inc(v.tensor_sub(xv, xv, Z["mbc_sb"][:, :]), "VEC")
            P.wait(v, "VEC")
            P.inc(v.tensor_tensor(xv, xv, Z["rbc_sb"][:, :], ALU.mult), "VEC")
            P.wait(v, "VEC")
    vec_x2 = P.c["VEC"]

    for tt in range(NTT):
        P.wait(t, "VEC", vec_x2)
        for c in range(NCH):
            mm = t.matmul(st_ps[0][0:16, 512:1024],
                          Z["wg_sb"][:, 16 * c:16 * (c + 1)],
                          x_sb[:, T * c + 512 * tt: T * c + 512 * (tt + 1)],
                          start=(c == 0), stop=(c == NCH - 1),
                          skip_group_check=True)
        P.inc(mm, "PE")
        P.wait(s, "PE")
        P.wait(s, "VEC")
        si = s.activation(Z["lg_tt"][0:16, :], st_ps[0][0:16, 512:1024],
                          AF.Identity, bias=Z["bgc_sb"][0:16, 0:1], scale=1.0)
        P.inc(si, "ACT")
        P.wait(t, "ACT")
        for j in range(4):
            i = 4 * tt + j
            P.wait(t, "VEC")
            mm = t.transpose(pvB_ps[0:128, 0:16],
                             Z["lg_tt"][0:16, 128 * j:128 * (j + 1)],
                             Z["id16_sb"][:, :])
            P.inc(mm, "PE")
            P.wait(v, "PE")
            P.inc(v.tensor_copy(Z["lg_tm"][:, 16 * i:16 * (i + 1)],
                                pvB_ps[0:128, 0:16]), "VEC")
        P.wait(t, "VEC")
    k2r = Z["k2_sb"]
    mxa = k2r[:, 0:64].bitcast(F32)
    nmxa = k2r[:, 64:128].bitcast(F32)
    sma = k2r[:, 128:192].bitcast(F32)
    eqa = k2r[:, 192:1216].bitcast(F32)
    cana = k2r[:, 1216:2240].bitcast(F32)
    t1a = k2r[:, 2240:2304].bitcast(F32)
    ohba = k2r[:, 2304:2816]
    sela = k2r[:, 2816:3840].bitcast(F32)
    csa = k2r[:, 3840:3904].bitcast(F32)
    ovga = k2r[:, 3904:3968].bitcast(F32)
    P.wait(v, "VEC")
    for i in range(NST):
        ltile = Z["lg_tm"][:, 16 * i:16 * (i + 1)]
        P.inc(v.reduce_max(mxa[:, i:i + 1], ltile, axis=AX), "VEC")
        P.inc(v.reduce_max(nmxa[:, i:i + 1], ltile, axis=AX, negate=True),
              "VEC")
    P.wait(s, "VEC")
    for i in range(NST):
        ltile = Z["lg_tm"][:, 16 * i:16 * (i + 1)]
        si = s.activation(Z["ejunk_sb"][:, :], ltile, AF.Exp,
                          bias=nmxa[:, i:i + 1], scale=1.0,
                          accum_out=sma[:, i:i + 1])
        P.inc(si, "ACT")
    P.wait(v, "VEC")
    for i in range(NST):
        ltile = Z["lg_tm"][:, 16 * i:16 * (i + 1)]
        P.inc(v.tensor_scalar(eqa[:, 16 * i:16 * (i + 1)], ltile,
                              mxa[:, i:i + 1], None, ALU.is_equal), "VEC")
    P.wait(v, "ACT")
    for i in range(NST):
        P.inc(v.reciprocal(Z["meta_sb"][:, 4 * i + 2:4 * i + 3],
                           sma[:, i:i + 1]), "VEC")
    P.wait(v, "VEC")
    for i in range(NST):
        P.inc(v.tensor_tensor(cana[:, 16 * i:16 * (i + 1)], Z["r16m_sb"][:, :],
                              eqa[:, 16 * i:16 * (i + 1)], ALU.mult), "VEC")
    P.wait(v, "VEC")
    for i in range(NST):
        P.inc(v.reduce_max(t1a[:, i:i + 1], cana[:, 16 * i:16 * (i + 1)],
                           axis=AX), "VEC")
    P.wait(v, "VEC")
    for i in range(NST):
        P.inc(v.tensor_scalar(Z["idx_all"][:, i:i + 1], t1a[:, i:i + 1],
                              -1.0, 16.0, ALU.mult, ALU.add), "VEC")
    P.wait(v, "VEC")
    for i in range(NST):
        P.inc(v.tensor_copy(Z["meta_sb"][:, 4 * i:4 * i + 1],
                            Z["idx_all"][:, i:i + 1]), "VEC")
        P.inc(v.tensor_scalar(Z["ohf_all"][:, 16 * i:16 * (i + 1)],
                              Z["iota_sb"][:, :], Z["idx_all"][:, i:i + 1],
                              None, ALU.is_equal), "VEC")
        with nc.allow_low_precision(reason="onehot bf16"):
            P.inc(v.tensor_scalar(ohba[:, 16 * i:16 * (i + 1)],
                                  Z["iota_sb"][:, :], Z["idx_all"][:, i:i + 1],
                                  None, ALU.is_equal), "VEC")
    for r in range(NST // 8):
        P.wait(t, "VEC")
        for j in range(8):
            i = 8 * r + j
            mm = t.transpose(trb_ps[0:16, 128 * j:128 * (j + 1)],
                             ohba[:, 16 * i:16 * (i + 1)], Z["idbf_sb"][:, :])
            P.inc(mm, "PE")
        P.wait(v, "PE")
        for j in range(8):
            i = 8 * r + j
            P.inc(v.tensor_copy(Z["ohfm_sb"][0:16, 128 * i:128 * (i + 1)],
                                trb_ps[0:16, 128 * j:128 * (j + 1)]), "VEC")
    P.wait(v, "VEC")
    P.inc(v.tensor_tensor_scan(Z["csum_sb"][:, :], Z["ohfm_sb"][:, :],
                               Z["ohfm_sb"][:, :], 0.0, ALU.add, ALU.bypass),
          "VEC")
    P.wait(v, "VEC")
    P.inc(v.tensor_copy(Z["cnt_sb"][:, cnt_col:cnt_col + 1],
                        Z["csum_sb"][:, T - 1:T]), "VEC")
    S["vec_cnt"] = P.c["VEC"]
    P.wait(t, "VEC")
    for i in range(NST):
        mm = t.transpose(pvB_ps[0:128, 16 * i:16 * (i + 1)],
                         Z["csum_sb"][0:16, 128 * i:128 * (i + 1)],
                         Z["id16_sb"][:, :])
        P.inc(mm, "PE")
    P.wait(v, "PE")
    for i in range(NST):
        P.inc(v.tensor_tensor(sela[:, 16 * i:16 * (i + 1)],
                              pvB_ps[0:128, 16 * i:16 * (i + 1)],
                              Z["ohf_all"][:, 16 * i:16 * (i + 1)], ALU.mult),
              "VEC")
    P.wait(v, "VEC")
    for i in range(NST):
        P.inc(v.reduce_sum(csa[:, i:i + 1], sela[:, 16 * i:16 * (i + 1)],
                           axis=AX), "VEC")
    P.wait(v, "VEC")
    for i in range(NST):
        P.inc(v.tensor_scalar(Z["meta_sb"][:, 4 * i + 1:4 * i + 2],
                              csa[:, i:i + 1], -1.0, None, ALU.add), "VEC")
        P.inc(v.memset(Z["meta_sb"][:, 4 * i + 3:4 * i + 4], 0.0), "VEC")
        P.inc(v.tensor_scalar(Z["fif"][:, i:i + 1], Z["idx_all"][:, i:i + 1],
                              float(LCAP), None, ALU.mult), "VEC")
    P.wait(v, "VEC")
    for i in range(NST):
        P.inc(v.tensor_scalar(ovga[:, i:i + 1],
                              Z["meta_sb"][:, 4 * i + 1:4 * i + 2],
                              float(LCAP), BIGIX, ALU.is_ge, ALU.mult), "VEC")
        P.inc(v.tensor_tensor(Z["fif"][:, i:i + 1], Z["fif"][:, i:i + 1],
                              Z["meta_sb"][:, 4 * i + 1:4 * i + 2], ALU.add),
              "VEC")
    P.wait(v, "VEC")
    for i in range(NST):
        P.inc(v.tensor_tensor(Z["fif"][:, i:i + 1], Z["fif"][:, i:i + 1],
                              ovga[:, i:i + 1], ALU.add), "VEC")
    P.wait(v, "VEC")
    for i in range(NST):
        P.inc(v.tensor_copy(Z["fiu"][:, i:i + 1], Z["fif"][:, i:i + 1]),
              "VEC")
    P.wait(v, "VEC")
    P.wait(sy, "VEC")
    P.inc(sy.dma_start(out=meta_ap.rearrange("(i p) c -> p i c", p=128),
                       in_=Z["meta_sb"].rearrange("p (i c) -> p i c", c=4)),
          "ST", 16)
    for i in range(NST):
        b = i % 2
        sxs = "SX0" if b == 0 else "SX1"
        buf = Z["x2st"][b]
        P.wait(t, "VEC")
        for dc in range(NCH):
            mm = t.transpose(trb_ps[0:128, 128 * dc:128 * (dc + 1)],
                             x_sb[:, T * dc + 128 * i: T * dc + 128 * (i + 1)],
                             Z["idbf_sb"][:, :])
            P.inc(mm, "PE")
        P.wait(v, "PE")
        P.wait(v, sxs)
        P.inc(v.tensor_copy(buf[:, :], trb_ps[0:128, :]), "VEC")
        stq = sy if b == 0 else s
        P.wait(stq, "VEC")
        P.inc(stq.dma_start(out=x2tm_ap[128 * i:128 * (i + 1), :],
                            in_=buf[:, :]), sxs, 16)
        P.wait(g, "VEC")
        P.inc(g.indirect_dma_start(
            out=fsrc_ap,
            out_offset=IndirectOffsetOnAxis(Z["fiu"][:, i:i + 1], 0),
            in_=buf[:, :], in_offset=None,
            bounds_check=bchk, oob_is_err=False), sxs, 16)


def emit_l1_const_loads(nc, P, Z, aps):
    g, sy = nc.gpsimd, nc.sync
    for c in range(NCH):
        P.inc(g.dma_start(out=Z["wo_sb"][:, D * c: D * (c + 1)],
                          in_=aps["woT"][128 * c:128 * (c + 1), :]), "LX", 16)
    P.inc(g.dma_start(out=Z["wg_sb"].ap().rearrange("p (c e) -> p c e", c=NCH),
                      in_=aps["wg"].rearrange("(c p) e -> p c e", p=128)),
          "LX", 16)
    for dk, sk in (("boc_sb", "boc"), ("bgc_sb", "bgc"), ("idbf_sb", "idbf"),
                   ("id16_sb", "id16"), ("iota_sb", "iota16b"),
                   ("r16m_sb", "r16m"), ("ones_sb", "onesb"),
                   ("odd16_sb", "odd16"),
                   ("onesf_sb", "onesf"), ("tri8_sb", "tri8"),
                   ("eh01_sb", "eh01"), ("pmask_sb", "pmask"),
                   ("iota128_sb", "iota128"), ("b1c_sb", "b1c"),
                   ("b2c_sb", "b2c")):
        P.inc(sy.dma_start(out=Z[dk][:, :], in_=aps[sk][:, :]), "LS", 16)


def make_headpair_prefetch(nc, P, Z, S, wq_ap, wk_ap, wv_ap, npasses=1):
    g = nc.gpsimd

    def load_pair(hp, bsel):
        sem = "WH0" if bsel == 0 else "WH1"
        P.wait(g, "PE", S["wh_freed"][bsel])
        for mi, wsrc in enumerate((wk_ap, wq_ap, wv_ap)):
            ins = g.dma_start(
                out=Z["wh_sb"][bsel][:, D * mi:D * (mi + 1)].rearrange(
                    "p (c f) -> p c f", c=NCH),
                in_=wsrc[:, 128 * hp:128 * (hp + 1)].rearrange(
                    "(c p) f -> p c f", p=128))
            P.inc(ins, sem, 16)
        S["wh_ready"][bsel] = P.c[sem]

    seq = {"next": 1}

    def prefetch(_hp):
        nxt = seq["next"]
        if nxt < 8 * npasses:
            load_pair(nxt % 8, nxt % 2)
            seq["next"] = nxt + 1

    load_pair(0, 0)
    return prefetch



def emit_prolog_b(nc, P, Z, cnts1_ap, cnts8_ap, vec_cnt):
    g, sy, t, v, s = nc.gpsimd, nc.sync, nc.tensor, nc.vector, nc.scalar
    bc_ps = Z["bc_ps"]
    P.wait(s, "VEC", vec_cnt)
    P.inc(s.dma_start(out=cnts1_ap.rearrange("one e -> e one"),
                      in_=Z["cnt_sb"][:, 0:1]), "CN", 16)
    P.wait(g, "CN")
    cc = g.collective_compute(
        "AllGather", ALU.bypass, replica_groups=[list(range(NC))],
        ins=[cnts1_ap.opt()], outs=[cnts8_ap.opt()])
    P.inc(cc, "CCA", 1)
    P.wait(g, "CCA")
    P.inc(g.dma_start(out=Z["cnts8_sb"][:, :], in_=cnts8_ap), "CN", 16)
    pe0, vec0 = P.c["PE"], P.c["VEC"]
    P.wait(t, "CN")
    P.wait(t, "VEC", vec0)
    mm = t.matmul(bc_ps[0:16, 0:8], Z["cnts8_sb"][:, :], Z["tri8_sb"][:, :],
                  start=True, stop=True, skip_group_check=True)
    P.inc(mm, "PE")
    P.wait(v, "PE")
    P.inc(v.tensor_copy(Z["baseT_sb"][:, :], bc_ps[0:16, 0:8]), "VEC")
    P.wait(t, "VEC")
    mm = t.matmul(bc_ps[0:1, 16:32], Z["pmask_sb"][:, :], Z["cnts8_sb"][:, :],
                  start=True, stop=True, skip_group_check=True)
    P.inc(mm, "PE")
    P.wait(v, "PE")
    P.inc(v.tensor_copy(Z["gbrow_sb"][:, :], bc_ps[0:1, 16:32]), "VEC")
    P.wait(t, "VEC")
    for k in range(2):
        mm = t.matmul(bc_ps[0:1, 32 + 8 * k:40 + 8 * k],
                      Z["eh01_sb"][:, k:k + 1], Z["baseT_sb"][:, :],
                      start=True, stop=True, skip_group_check=True)
        P.inc(mm, "PE")
    P.wait(v, "PE")
    P.inc(v.tensor_copy(Z["rowsel_sb"][:, 0:8], bc_ps[0:1, 32:40]), "VEC")
    P.inc(v.tensor_copy(Z["rowsel_sb"][:, 8:16], bc_ps[0:1, 40:48]), "VEC")
    P.wait(t, "VEC")
    mm = t.matmul(bc_ps[:, 48:64], Z["onesf_sb"][0:1, :],
                  Z["rowsel_sb"][0:1, :], start=True, stop=True,
                  skip_group_check=True)
    P.inc(mm, "PE")
    mm = t.matmul(bc_ps[:, 64:80], Z["onesf_sb"][0:1, :],
                  Z["gbrow_sb"][0:1, :], start=True, stop=True,
                  skip_group_check=True)
    P.inc(mm, "PE")
    P.wait(v, "PE")
    P.inc(v.tensor_copy(Z["bbc_sb"][:, :], bc_ps[:, 48:64]), "VEC")
    P.inc(v.tensor_copy(Z["gbb_sb"][:, :], bc_ps[:, 64:80]), "VEC")
    P.wait(v, "VEC")
    P.wait(v, "PE")
    k2 = Z["k2_sb"]
    pc_all = k2[:, 0:64].bitcast(F32)
    cmp_all = k2[:, 64:576].bitcast(F32)
    nle_all = k2[:, 576:640].bitcast(F32)
    rr_all = k2[:, 640:704].bitcast(F32)
    ovg_all = k2[:, 704:768].bitcast(F32)
    cols = [(ep * 16 + i, ep, i) for ep in range(2) for i in range(CAP // 128)]
    for col, ep, i in cols:
        P.inc(v.tensor_scalar(pc_all[:, col:col + 1], Z["iota128_sb"][:, :],
                              float(128 * i), None, ALU.add), "VEC")
    P.wait(v, "VEC")
    for col, ep, i in cols:
        P.inc(v.tensor_scalar(cmp_all[:, 8 * col:8 * col + 8],
                              Z["bbc_sb"][:, 8 * ep:8 * ep + 8],
                              pc_all[:, col:col + 1], None, ALU.is_le), "VEC")
    P.wait(v, "VEC")
    for col, ep, i in cols:
        P.inc(v.reduce_sum(nle_all[:, col:col + 1],
                           cmp_all[:, 8 * col:8 * col + 8], axis=AX), "VEC")
    P.wait(v, "VEC")
    for col, ep, i in cols:
        P.inc(v.tensor_tensor(cmp_all[:, 8 * col:8 * col + 8],
                              cmp_all[:, 8 * col:8 * col + 8],
                              Z["bbc_sb"][:, 8 * ep:8 * ep + 8], ALU.mult),
              "VEC")
    P.wait(v, "VEC")
    for col, ep, i in cols:
        P.inc(v.reduce_max(rr_all[:, col:col + 1],
                           cmp_all[:, 8 * col:8 * col + 8], axis=AX), "VEC")
    P.wait(v, "VEC")
    for col, ep, i in cols:
        P.inc(v.tensor_sub(rr_all[:, col:col + 1], pc_all[:, col:col + 1],
                           rr_all[:, col:col + 1]), "VEC")
    P.wait(v, "VEC")
    for col, ep, i in cols:
        P.inc(v.tensor_scalar(ovg_all[:, col:col + 1], rr_all[:, col:col + 1],
                              float(LCAP), BIGIX, ALU.is_ge, ALU.mult), "VEC")
        P.inc(v.tensor_scalar(Z["gif"][:, col:col + 1], nle_all[:, col:col + 1],
                              2048.0, float(1024 * ep - 2048), ALU.mult,
                              ALU.add), "VEC")
        P.inc(v.tensor_scalar(Z["gif2"][:, col:col + 1],
                              nle_all[:, col:col + 1],
                              1024.0, float(8192 * ep - 1024), ALU.mult,
                              ALU.add), "VEC")
    P.wait(v, "VEC")
    for col, ep, i in cols:
        P.inc(v.tensor_tensor(Z["gif"][:, col:col + 1],
                              Z["gif"][:, col:col + 1], rr_all[:, col:col + 1],
                              ALU.add), "VEC")
        P.inc(v.tensor_tensor(Z["gif2"][:, col:col + 1],
                              Z["gif2"][:, col:col + 1], rr_all[:, col:col + 1],
                              ALU.add), "VEC")
    P.wait(v, "VEC")
    for col, ep, i in cols:
        P.inc(v.tensor_tensor(Z["gif"][:, col:col + 1],
                              Z["gif"][:, col:col + 1], ovg_all[:, col:col + 1],
                              ALU.add), "VEC")
        P.inc(v.tensor_tensor(Z["gif2"][:, col:col + 1],
                              Z["gif2"][:, col:col + 1], ovg_all[:, col:col + 1],
                              ALU.add), "VEC")
    P.wait(v, "VEC")
    for col, ep, i in cols:
        P.inc(v.tensor_copy(Z["giu"][:, col:col + 1],
                            Z["gif"][:, col:col + 1]), "VEC")
        P.inc(v.tensor_copy(Z["giu2"][:, col:col + 1],
                            Z["gif2"][:, col:col + 1]), "VEC")
    P.wait(v, "VEC")



def make_ffn_loaders(nc, P, ZF, w1_ap, w2_ap):
    g = nc.gpsimd
    w1_sb, w2_sb = ZF["w1_sb"], ZF["w2_sb"]
    FS = {"w1_ready": [0, 0], "w1_freed": [0, 0],
          "w2_ready": [0, 0], "w2_freed": [0, 0]}

    def load_w1(bsel, e, hc):
        P.wait(g, "PE", FS["w1_freed"][bsel])
        ins = g.dma_start(
            out=w1_sb[bsel].rearrange("p (c f) -> p c f", c=NCH),
            in_=w1_ap[e, :, 128 * hc:128 * (hc + 1)].rearrange(
                "(c p) f -> p c f", p=128))
        FS["w1_ready"][bsel] = P.inc(ins, "W10" if bsel == 0 else "W11", 16)

    def load_w2(bsel, e, dc):
        P.wait(g, "PE", FS["w2_freed"][bsel])
        ins = g.dma_start(
            out=w2_sb[bsel].rearrange("p (c f) -> p c f", c=32),
            in_=w2_ap[e, :, 128 * dc:128 * (dc + 1)].rearrange(
                "(c p) f -> p c f", p=128))
        FS["w2_ready"][bsel] = P.inc(ins, "W20" if bsel == 0 else "W21", 16)

    return FS, load_w1, load_w2


def emit_ffn(nc, P, Z, ZF, loaders, fdst_ap, rsrc_ap, giu_done, bchk,
             mid_hook=None):
    g, sy, t, v, s = nc.gpsimd, nc.sync, nc.tensor, nc.vector, nc.scalar
    buf_fm = ZF["buf_fm"]
    bt_sb, yt_sb = ZF["bt_sb"], ZF["yt_sb"]
    w1_sb, w2_sb = ZF["w1_sb"], ZF["w2_sb"]
    h_sb, yfm_sb = ZF["h_sb"], ZF["yfm_sb"]
    h_ps, y_ps, tr_ps = Z["pvA_ps"], Z["pvB_ps"], Z["trb_ps"]
    b1_sb, b2_sb, idbf_sb = Z["b1c_sb"], Z["b2c_sb"], Z["idbf_sb"]
    FS, load_w1, load_w2 = loaders
    w1_ready, w1_freed = FS["w1_ready"], FS["w1_freed"]
    w2_ready, w2_freed = FS["w2_ready"], FS["w2_freed"]
    tr_after = {}
    vec_strip = {}

    for e in range(2):
        for i in range(CAP // 128):
            b = i % 2
            bts = "BT0" if b == 0 else "BT1"
            if i >= 2 or e > 0:
                P.wait(g, "PE", tr_after.get((e, i - 2), tr_after.get(
                    (e - 1, CAP // 128 + i - 2), 0)))
            P.wait(g, "VEC", giu_done)
            col = e * 16 + i
            P.inc(g.indirect_dma_start(
                out=bt_sb[b][:, :], out_offset=None,
                in_=fdst_ap,
                in_offset=IndirectOffsetOnAxis(Z["giu"][:, col:col + 1], 0),
                bounds_check=bchk, oob_is_err=False), bts, 16)
            P.wait(t, bts)
            P.wait(t, "LS", P.c["LS"])
            P.wait(t, "VEC")
            for dc in range(NCH):
                mm = t.transpose(tr_ps[0:128, 128 * dc:128 * (dc + 1)],
                                 bt_sb[b][:, 128 * dc:128 * (dc + 1)],
                                 idbf_sb[:, :])
                P.inc(mm, "PE")
            P.wait(v, "PE")
            for dc in range(NCH):
                P.inc(v.tensor_copy(
                    buf_fm[:, CAP * dc + 128 * i:CAP * dc + 128 * (i + 1)],
                    tr_ps[0:128, 128 * dc:128 * (dc + 1)]), "VEC")
            tr_after[(e, i)] = P.c["PE"]
            if i % 4 == 3:
                vec_strip[(e, i // 4)] = P.c["VEC"]
        if e > 0:
            load_w1(0, e, 0)
        for st in range(4):
            act_of = {}
            P.wait(t, "ACT")
            for hc in range(32):
                bsel = hc % 2
                if hc + 1 < 32:
                    load_w1((hc + 1) % 2, e, hc + 1)
                elif st + 1 < 4:
                    load_w1((hc + 1) % 2, e, 0)
                t.wait_ge(P.sem["W10" if bsel == 0 else "W11"], w1_ready[bsel])
                P.wait(t, "VEC", vec_strip[(e, st)])
                hps = h_ps if hc % 2 == 0 else y_ps
                if hc >= 2:
                    P.wait(t, "ACT", act_of.get(hc - 2, 0))
                for c in range(NCH):
                    mm = t.matmul(hps[:, :],
                                  w1_sb[bsel][:, 128 * c:128 * (c + 1)],
                                  buf_fm[:, CAP * c + 512 * st:
                                         CAP * c + 512 * (st + 1)],
                                  start=(c == 0), stop=(c == NCH - 1),
                                  skip_group_check=True)
                P.inc(mm, "PE")
                w1_freed[bsel] = P.c["PE"]
                P.wait(s, "PE")
                si = s.activation(h_sb[:, 512 * hc:512 * (hc + 1)], hps[:, :],
                                  AF.Relu,
                                  bias=b1_sb[:, 32 * e + hc:32 * e + hc + 1],
                                  scale=1.0)
                act_of[hc] = P.inc(si, "ACT")
            load_w2(0, e, 0)
            yact = {}
            for dc in range(NCH):
                bsel = dc % 2
                if dc + 1 < NCH:
                    load_w2((dc + 1) % 2, e, dc + 1)
                t.wait_ge(P.sem["W20" if bsel == 0 else "W21"], w2_ready[bsel])
                yps = h_ps if dc % 2 == 0 else y_ps
                if dc < 2:
                    P.wait(t, "ACT", act_of[31])
                else:
                    P.wait(t, "ACT", yact[dc - 2])
                for hc in range(32):
                    mm = t.matmul(yps[:, :],
                                  w2_sb[bsel][:, 128 * hc:128 * (hc + 1)],
                                  h_sb[:, 512 * hc:512 * (hc + 1)],
                                  start=(hc == 0), stop=(hc == 31),
                                  skip_group_check=True)
                P.inc(mm, "PE")
                w2_freed[bsel] = P.c["PE"]
                P.wait(s, "PE")
                si = s.activation(yfm_sb[:, 512 * dc:512 * (dc + 1)], yps[:, :],
                                  AF.Identity,
                                  bias=b2_sb[:, 8 * e + dc:8 * e + dc + 1],
                                  scale=1.0)
                yact[dc] = P.inc(si, "ACT")
            P.wait(t, "ACT")
            for j in range(4):
                jb = j % 2
                sys_ = "SY0" if jb == 0 else "SY1"
                buf = yt_sb[jb]
                P.wait(t, "VEC")
                for dc in range(NCH):
                    mm = t.transpose(tr_ps[0:128, 128 * dc:128 * (dc + 1)],
                                     yfm_sb[:, 512 * dc + 128 * j:
                                            512 * dc + 128 * (j + 1)],
                                     idbf_sb[:, :])
                    P.inc(mm, "PE")
                P.wait(v, "PE")
                P.wait(v, sys_)
                P.inc(v.tensor_copy(buf[:, :], tr_ps[0:128, :]), "VEC")
                P.wait(g, "VEC")
                col = e * 16 + 4 * st + j
                P.inc(g.indirect_dma_start(
                    out=rsrc_ap,
                    out_offset=IndirectOffsetOnAxis(Z["giu2"][:, col:col + 1],
                                                    0),
                    in_=buf[:, :], in_offset=None,
                    bounds_check=bchk, oob_is_err=False), sys_, 16)
        if e == 0 and mid_hook is not None:
            mid_hook()



def emit_phase_c(nc, P, Z, ZF, x2tm_ap, meta_ap, rdst_ap, out_ap, bchk,
                 pe_attn):
    g, sy, t, v, s = nc.gpsimd, nc.sync, nc.tensor, nc.vector, nc.scalar
    ybig = Z["x_sb"]
    x2all = Z["o_sb"]
    q2, k2 = Z["q2_sb"], Z["k2_sb"]
    eqall = q2[:, 0:1024].bitcast(F32)
    selall = q2[:, 1024:2048].bitcast(F32)
    lnall = q2[:, 2048:2560].bitcast(F32)
    mt_all = q2[:, 2560:2816].bitcast(F32)
    o2b = [k2[:, 0:2048].bitcast(F32), k2[:, 2048:4096].bitcast(F32)]
    zb = [Z["vaugA"][:, 0:1024], Z["vaugA"][:, 1024:2048],
          Z["vaugB"][:, 0:1024], Z["vaugB"][:, 1024:2048]]
    zdump = Z["expS"]

    P.wait(sy, "SX0")
    P.wait(sy, "SX1")
    P.wait(sy, "SY0")
    P.wait(sy, "SY1")
    P.wait(sy, "ST")
    P.wait(sy, "PE")
    P.wait(sy, "VEC")
    P.wait(sy, "ACT")
    for i in range(0, 9):
        P.inc(sy.dma_start(out=x2all[:, 1024 * i:1024 * (i + 1)],
                           in_=x2tm_ap[128 * i:128 * (i + 1), :]), "LT0", 16)
    P.wait(v, "PE", pe_attn)
    P.wait(v, "LT0", 16)
    P.wait(v, "VEC")
    for i in range(NST):
        P.inc(v.tensor_scalar(eqall[:, 16 * i:16 * (i + 1)], Z["iota_sb"][:, :],
                              mt_all[:, 4 * i:4 * i + 1], None, ALU.is_equal),
              "VEC")
    P.wait(v, "VEC")
    for i in range(NST):
        P.inc(v.tensor_tensor(selall[:, 16 * i:16 * (i + 1)],
                              eqall[:, 16 * i:16 * (i + 1)], Z["gbb_sb"][:, :],
                              ALU.mult), "VEC")
    P.wait(v, "VEC")
    for i in range(NST):
        P.inc(v.reduce_sum(lnall[:, 8 * i + 6:8 * i + 7],
                           selall[:, 16 * i:16 * (i + 1)], axis=AX), "VEC")
    P.wait(v, "VEC")
    for i in range(NST):
        P.inc(v.tensor_tensor(lnall[:, 8 * i + 6:8 * i + 7],
                              lnall[:, 8 * i + 6:8 * i + 7],
                              mt_all[:, 4 * i + 1:4 * i + 2], ALU.add), "VEC")
    P.wait(v, "VEC")
    for i in range(NST):
        P.inc(v.tensor_scalar(lnall[:, 8 * i + 7:8 * i + 8],
                              lnall[:, 8 * i + 6:8 * i + 7], float(CAP),
                              BIGIX, ALU.is_ge, ALU.mult), "VEC")
    bcola = q2[:, 2816:2880].bitcast(F32)
    f2a = q2[:, 2880:2944].bitcast(F32)
    P.wait(v, "VEC")
    for i in range(NST):
        P.inc(v.tensor_tensor(selall[:, 16 * i:16 * (i + 1)],
                              eqall[:, 16 * i:16 * (i + 1)],
                              Z["odd16_sb"][:, :], ALU.mult), "VEC")
        P.inc(v.tensor_scalar(f2a[:, i:i + 1], mt_all[:, 4 * i:4 * i + 1],
                              512.0, None, ALU.mult), "VEC")
    P.wait(v, "VEC")
    for i in range(NST):
        P.inc(v.reduce_sum(bcola[:, i:i + 1], selall[:, 16 * i:16 * (i + 1)],
                           axis=AX), "VEC")
    P.wait(v, "VEC")
    for i in range(NST):
        P.inc(v.tensor_scalar(bcola[:, i:i + 1], bcola[:, i:i + 1],
                              7680.0, None, ALU.mult), "VEC")
        P.inc(v.tensor_tensor(f2a[:, i:i + 1], f2a[:, i:i + 1],
                              mt_all[:, 4 * i + 1:4 * i + 2], ALU.add), "VEC")
    P.wait(v, "VEC")
    for i in range(NST):
        P.inc(v.tensor_tensor(f2a[:, i:i + 1], f2a[:, i:i + 1],
                              bcola[:, i:i + 1], ALU.add), "VEC")
        P.inc(v.tensor_scalar(bcola[:, i:i + 1],
                              mt_all[:, 4 * i + 1:4 * i + 2],
                              float(LCAP), BIGIX, ALU.is_ge, ALU.mult), "VEC")
    P.wait(v, "VEC")
    for i in range(NST):
        P.inc(v.tensor_tensor(f2a[:, i:i + 1], f2a[:, i:i + 1],
                              bcola[:, i:i + 1], ALU.add), "VEC")
    P.wait(v, "VEC")
    for i in range(NST):
        P.inc(v.tensor_tensor(f2a[:, i:i + 1], f2a[:, i:i + 1],
                              lnall[:, 8 * i + 7:8 * i + 8], ALU.add), "VEC")
    P.wait(v, "VEC")
    for i in range(NST):
        P.inc(v.tensor_copy(Z["fiu"][:, i:i + 1], f2a[:, i:i + 1]),
              "VEC")
    P.inc(v.memset(ybig[:, :], 0.0), "VEC")
    P.wait(g, "PE")
    P.wait(g, "VEC")
    for i in range(9, 18):
        P.inc(g.dma_start(out=x2all[:, 1024 * i:1024 * (i + 1)],
                          in_=x2tm_ap[128 * i:128 * (i + 1), :]), "LT1", 16)
    P.wait(g, "CCR", 2)
    for i in range(NST):
        P.inc(g.indirect_dma_start(
            out=ybig[:, 1024 * i:1024 * (i + 1)], out_offset=None,
            in_=rdst_ap,
            in_offset=IndirectOffsetOnAxis(Z["fiu"][:, i:i + 1], 0),
            bounds_check=bchk, oob_is_err=False), "GB0", 16)
    P.wait(v, "LT0")
    P.wait(v, "LT1")
    zv = {}
    sums = {}
    o2v = {}
    for i in range(NST):
        zbi = zb[i % 4]
        obi = o2b[i % 2]
        if i >= 4:
            P.wait(v, "VEC", o2v[i - 4])
        P.wait(v, "GB0", 16 * (i + 1))
        zv[i] = P.inc(v.scalar_tensor_tensor(
            zbi[:, :], ybig[:, 1024 * i:1024 * (i + 1)],
            mt_all[:, 4 * i + 2:4 * i + 3],
            x2all[:, 1024 * i:1024 * (i + 1)], ALU.mult, ALU.add), "VEC")
        s.wait_ge(P.sem["VEC"], zv[i])
        P.inc(s.activation(zdump[0][:, :], zbi[:, :], AF.Identity,
                           accum_out=lnall[:, 8 * i:8 * i + 1]), "ACT")
        sums[i] = P.inc(s.activation(zdump[1][:, :], zbi[:, :], AF.Square,
                                     accum_out=lnall[:, 8 * i + 1:8 * i + 2]),
                        "ACT")
        v.wait_ge(P.sem["ACT"], sums[i])
        P.inc(v.tensor_scalar(lnall[:, 8 * i + 2:8 * i + 3],
                              lnall[:, 8 * i:8 * i + 1], 1.0 / D, None,
                              ALU.mult), "VEC")
        P.wait(v, "VEC")
        P.inc(v.tensor_scalar(lnall[:, 8 * i + 4:8 * i + 5],
                              lnall[:, 8 * i + 2:8 * i + 3],
                              lnall[:, 8 * i + 2:8 * i + 3], None, ALU.mult),
              "VEC")
        P.wait(v, "VEC")
        vv = P.inc(v.tensor_scalar(lnall[:, 8 * i + 3:8 * i + 4],
                                   lnall[:, 8 * i + 1:8 * i + 2], 1.0 / D,
                                   lnall[:, 8 * i + 4:8 * i + 5], ALU.mult,
                                   ALU.subtract), "VEC")
        s.wait_ge(P.sem["VEC"], vv)
        sq = P.inc(s.activation(lnall[:, 8 * i + 5:8 * i + 6],
                                lnall[:, 8 * i + 3:8 * i + 4], AF.Sqrt,
                                bias=Z["c5_sb"][:, 0:1], scale=1.0), "ACT")
        v.wait_ge(P.sem["ACT"], sq)
        P.inc(v.reciprocal(lnall[:, 8 * i + 5:8 * i + 6],
                           lnall[:, 8 * i + 5:8 * i + 6]), "VEC")
        P.wait(v, "VEC")
        if i >= 2:
            P.wait(v, "SO", 16 * (i - 1))
        o2v[i] = P.inc(v.tensor_scalar(obi[:, :], zbi[:, :],
                                       lnall[:, 8 * i + 2:8 * i + 3],
                                       lnall[:, 8 * i + 5:8 * i + 6],
                                       ALU.subtract, ALU.mult), "VEC")
        sy.wait_ge(P.sem["VEC"], o2v[i])
        P.inc(sy.dma_start(out=out_ap[128 * i:128 * (i + 1), :],
                           in_=obi[:, :]), "SO", 16)
    P.wait(sy, "SO")



def build_fused():
    nc = bass.Bass("TRN2", target_bir_lowering=False, debug=False)
    xT = nc.dram_tensor("xT", [D, T], F32, kind="ExternalInput").ap()
    wqT = nc.dram_tensor("wqT", [D, D], F32, kind="ExternalInput").ap()
    wkT = nc.dram_tensor("wkT", [D, D], F32, kind="ExternalInput").ap()
    wvT = nc.dram_tensor("wvT", [D, D], F32, kind="ExternalInput").ap()
    woT = nc.dram_tensor("woT", [D, D], F32, kind="ExternalInput").ap()
    boc = nc.dram_tensor("boc", [128, NCH], F32, kind="ExternalInput").ap()
    wg = nc.dram_tensor("wg", [D, E], F32, kind="ExternalInput").ap()
    bgc = nc.dram_tensor("bgc", [E, 1], F32, kind="ExternalInput").ap()
    idbf = nc.dram_tensor("idbf", [128, 128], BF16, kind="ExternalInput").ap()
    id16 = nc.dram_tensor("id16", [16, 16], F32, kind="ExternalInput").ap()
    iota16b = nc.dram_tensor("iota16b", [128, 16], F32, kind="ExternalInput").ap()
    odd16 = nc.dram_tensor("odd16", [128, 16], F32, kind="ExternalInput").ap()
    r16m = nc.dram_tensor("r16m", [128, 16], F32, kind="ExternalInput").ap()
    onesb = nc.dram_tensor("onesb", [128, 128], BF16, kind="ExternalInput").ap()
    onesf = nc.dram_tensor("onesf", [128, 128], F32, kind="ExternalInput").ap()
    tri8 = nc.dram_tensor("tri8", [8, 8], F32, kind="ExternalInput").ap()
    eh01 = nc.dram_tensor("eh01", [16, 2], F32, kind="ExternalInput").ap()
    pmask = nc.dram_tensor("pmask", [8, 1], F32, kind="ExternalInput").ap()
    iota128 = nc.dram_tensor("iota128", [128, 1], F32, kind="ExternalInput").ap()
    w1 = nc.dram_tensor("w1", [2, D, FH], F32, kind="ExternalInput").ap()
    w2 = nc.dram_tensor("w2", [2, FH, D], F32, kind="ExternalInput").ap()
    b1c = nc.dram_tensor("b1c", [128, 64], F32, kind="ExternalInput").ap()
    b2c = nc.dram_tensor("b2c", [128, 16], F32, kind="ExternalInput").ap()
    out = nc.dram_tensor("out", [T, D], F32, kind="ExternalOutput").ap()
    x2tm = nc.dram_tensor("x2tm", [T, D], BF16).ap()
    meta = nc.dram_tensor("meta", [T, 4], F32).ap()
    cnts1 = nc.dram_tensor("cnts1", [1, E], F32).ap()
    cnts8 = nc.dram_tensor("cnts8", [8, E], F32).ap()
    fsrc = nc.dram_tensor("fsrc", [NROW, D], BF16).ap()
    fdst = nc.dram_tensor("fdst", [NROW, D], BF16).ap()
    rsrc = nc.dram_tensor("rsrc", [NROW, D], BF16).ap()
    rdst = nc.dram_tensor("rdst", [NROW, D], BF16).ap()

    es = ExitStack()
    Z = alloc_l1(nc, es)
    ZF = alloc_overlays(Z)
    P = Prog(nc, es, SEMS)
    g, sy = nc.gpsimd, nc.sync
    bchk = nc.alloc_register(mybir.EngineType.Pool, "bchk")
    g.reg_mov(bchk, NROW - 1)

    for c in range(NCH):
        P.inc(g.dma_start(out=Z["x_sb"][:, T * c: T * (c + 1)],
                          in_=xT[128 * c:128 * (c + 1), :]), "LX", 16)
    emit_l1_const_loads(nc, P, Z, dict(
        woT=woT, wg=wg, boc=boc, bgc=bgc, idbf=idbf, id16=id16,
        iota16b=iota16b, r16m=r16m, onesb=onesb, onesf=onesf, tri8=tri8,
        eh01=eh01, pmask=pmask, iota128=iota128, b1c=b1c, b2c=b2c,
        odd16=odd16))
    lx_ready = P.c["LX"]
    S = new_l1_state()
    prefetch = make_headpair_prefetch(nc, P, Z, S, wqT, wkT, wvT)
    emit_l1_init(nc, P, Z)
    with nc.named_scope("l1"):
        emit_l1_batch(nc, P, Z, S, prefetch, x2tm, meta, 0, lx_ready, fsrc,
                      bchk)
    pe_attn = P.c["PE"]

    with nc.named_scope("prolog_b"):
        emit_prolog_b(nc, P, Z, cnts1, cnts8, S["vec_cnt"])
    giu_done = P.c["VEC"]

    P.wait(g, "SX0")
    P.wait(g, "SX1")
    P.wait(g, "ST")
    P.wait(g, "PE")
    P.wait(g, "VEC")
    loaders = make_ffn_loaders(nc, P, ZF, w1, w2)
    loaders[1](0, 0, 0)
    P.wait(sy, "SX0")
    P.wait(sy, "SX1")
    P.wait(sy, "ST")
    P.wait(sy, "PE")
    P.wait(sy, "VEC")
    mt_all_early = Z["q2_sb"][:, 2560:2816].bitcast(F32)
    P.inc(sy.dma_start(out=mt_all_early.rearrange("p (i c) -> p i c", c=4),
                       in_=meta.rearrange("(i p) c -> p i c", p=128)),
          "LT0", 16)
    for i in range(18, NST):
        P.inc(sy.dma_start(out=Z["o_sb"][:, 1024 * i:1024 * (i + 1)],
                           in_=x2tm[128 * i:128 * (i + 1), :]), "LT0", 16)

    with nc.named_scope("a2a_fwd"):
        with allow_ici_all_to_all():
            cc = g.collective_compute(
                "AllToAll", ALU.bypass, replica_groups=[list(range(NC))],
                ins=[fsrc.opt()], outs=[fdst.opt()])
        P.inc(cc, "CCF", 1)
        P.wait(g, "CCF")

    def mid_hook():
        P.wait(g, "SY0")
        P.wait(g, "SY1")
        with allow_ici_all_to_all():
            cc = g.collective_compute(
                "AllToAll", ALU.bypass, replica_groups=[list(range(NC))],
                ins=[rsrc[0:NROW // 2, :].opt()],
                outs=[rdst[0:NROW // 2, :].opt()])
        P.inc(cc, "CCR", 1)

    with nc.named_scope("ffn"):
        emit_ffn(nc, P, Z, ZF, loaders, fdst, rsrc, giu_done, bchk, mid_hook)

    with nc.named_scope("a2a_ret"):
        P.wait(g, "SY0")
        P.wait(g, "SY1")
        with allow_ici_all_to_all():
            cc = g.collective_compute(
                "AllToAll", ALU.bypass, replica_groups=[list(range(NC))],
                ins=[rsrc[NROW // 2:NROW, :].opt()],
                outs=[rdst[NROW // 2:NROW, :].opt()])
        P.inc(cc, "CCR", 1)

    with nc.named_scope("phase_c"):
        emit_phase_c(nc, P, Z, ZF, x2tm, meta, rdst, out, bchk, pe_attn)

    es.close()
    return nc



_NC_CACHE = {}


def _get(name, builder):
    if name not in _NC_CACHE:
        _NC_CACHE[name] = builder()
    return _NC_CACHE[name]


def kernel(L0_attn_x, L0_attn_Wq, L0_attn_Wk, L0_attn_Wv, L0_attn_Wo,
           L0_attn_bo, L0_ffn_Wg, L0_ffn_bg, L0_ffn_W1, L0_ffn_b1,
           L0_ffn_W2, L0_ffn_b2):
    x = np.asarray(L0_attn_x, dtype=np.float32)
    wqT = np.ascontiguousarray(np.asarray(L0_attn_Wq, np.float32).T)
    wkT = np.ascontiguousarray(np.asarray(L0_attn_Wk, np.float32).T)
    wvT = np.ascontiguousarray(np.asarray(L0_attn_Wv, np.float32).T)
    woT = np.ascontiguousarray(np.asarray(L0_attn_Wo, np.float32).T)
    bo = np.asarray(L0_attn_bo, np.float32)
    wg = np.ascontiguousarray(np.asarray(L0_ffn_Wg, np.float32))
    bg = np.asarray(L0_ffn_bg, np.float32)
    W1 = np.asarray(L0_ffn_W1, np.float32)
    b1 = np.asarray(L0_ffn_b1, np.float32)
    W2 = np.asarray(L0_ffn_W2, np.float32)
    b2 = np.asarray(L0_ffn_b2, np.float32)

    boc = np.ascontiguousarray(bo.reshape(NCH, 128).T)
    bgc = np.ascontiguousarray(bg.reshape(E, 1))
    idbf = np.eye(128, dtype=np.float32).astype(BF16NP)
    id16 = np.eye(16, dtype=np.float32)
    iota16b = np.ascontiguousarray(np.tile(np.arange(16, dtype=np.float32),
                                           (128, 1)))
    r16m = np.ascontiguousarray(16.0 - iota16b)
    onesb = np.ones((128, 128), np.float32).astype(BF16NP)
    onesf = np.ones((128, 128), np.float32)
    odd16 = np.ascontiguousarray(np.tile(np.arange(16, dtype=np.float32) % 2,
                                         (128, 1)))
    tri8 = np.triu(np.ones((8, 8), np.float32), 1)
    iota128 = np.arange(128, dtype=np.float32).reshape(128, 1)

    nc1 = _get("fused", build_fused)
    in_maps = []
    for c in range(NC):
        eh01 = np.zeros((16, 2), np.float32)
        eh01[2 * c, 0] = 1.0
        eh01[2 * c + 1, 1] = 1.0
        b1cc = np.ascontiguousarray(
            np.concatenate([b1[2 * c + e].reshape(32, 128).T for e in range(2)],
                           axis=1))
        b2cc = np.ascontiguousarray(
            np.concatenate([b2[2 * c + e].reshape(NCH, 128).T for e in range(2)],
                           axis=1))
        in_maps.append({
            "xT": np.ascontiguousarray(x[c].T),
            "wqT": wqT, "wkT": wkT, "wvT": wvT, "woT": woT,
            "boc": boc, "wg": wg, "bgc": bgc,
            "idbf": idbf, "id16": id16, "iota16b": iota16b, "r16m": r16m,
            "onesb": onesb, "onesf": onesf, "tri8": tri8, "eh01": eh01,
            "odd16": odd16,
            "pmask": (np.arange(8) < c).astype(np.float32).reshape(8, 1),
            "iota128": iota128,
            "w1": np.ascontiguousarray(W1[2 * c:2 * c + 2]),
            "w2": np.ascontiguousarray(W2[2 * c:2 * c + 2]),
            "b1c": b1cc, "b2c": b2cc,
        })
    res = run_bass_kernel_spmd(nc1, in_maps, core_ids=list(range(NC))).results
    return np.stack([res[c]["out"] for c in range(NC)], axis=0)
